# version 04
# baseline (speedup 1.0000x reference)
import numpy as np
import ml_dtypes
from contextlib import ExitStack
import concourse.bass as bass
import concourse.mybir as mybir
from concourse.bass_utils import run_bass_kernel_spmd

F32 = mybir.dt.float32
BF16 = mybir.dt.bfloat16
I8 = mybir.dt.int8
AF = mybir.ActivationFunctionType
ALU = mybir.AluOpType
AX = mybir.AxisListType
bfdt = ml_dtypes.bfloat16

D = 2048
T = 2048
NCORE = 8
SCALE = float(1.0 / np.sqrt(128.0))
NEG = -1e30
PAIRS = [[0, 1], [2, 3], [4, 5], [6, 7]]


class Ct:
    def __init__(self, sem):
        self.sem = sem
        self.n = 0
    def inc(self, k=1):
        self.n += k
        return self.n


class _RecInstr:
    __slots__ = ("rec",)
    def __init__(self, rec):
        self.rec = rec
    def then_inc(self, ct, k=1):
        self.rec[3].append((ct.sem, k))
        ct.inc(k)
        return self
    def then_inc_default(self, ct):
        self.rec[3].append((ct.sem, None))
        ct.inc(1)
        return self


class Rec:
    def __init__(self, sched, eng):
        self._q = sched.q[eng]
    def wait(self, ct, n=None):
        sem = ct.sem
        val = ct.n if n is None else n
        if val > 0:
            self._q.append(("w", sem, val, None))
    def __getattr__(self, name):
        q = self._q
        def call(*a, **kw):
            rec = ("c", name, (a, kw), [])
            q.append(rec)
            return _RecInstr(rec)
        return call


class Blk:
    ORDER = ("y", "g", "t", "v", "s")
    ATTR = {"y": "sync", "g": "gpsimd", "t": "tensor", "v": "vector", "s": "scalar"}
    def __init__(self):
        self.q = {e: [] for e in self.ORDER}
    def engines(self):
        return tuple(Rec(self, e) for e in ("y", "g", "t", "v", "s"))
    def run(self, nc):
        with nc.Block() as b:
            for e in self.ORDER:
                if not self.q[e]:
                    continue
                def make(recs):
                    def body(eng):
                        for r in recs:
                            if r[0] == "w":
                                eng.wait_ge(r[1], r[2])
                            else:
                                a, kw = r[2]
                                inst = getattr(eng, r[1])(*a, **kw)
                                for sem, k in r[3]:
                                    if k is None:
                                        inst.then_inc(sem)
                                    else:
                                        inst.then_inc(sem, k)
                    return body
                getattr(b, self.ATTR[e])(make(self.q[e]))



_PERM128 = np.concatenate([np.arange(0, 128, 2), np.arange(1, 128, 2)])

def _perm256(base):
    return np.concatenate([base + _PERM128, base + 128 + _PERM128])


def _host_consts():
    c = {}
    c["ident_bf"] = np.eye(128, dtype=np.float32).astype(bfdt)
    c["ident_f32"] = np.eye(128, dtype=np.float32)
    m = np.zeros((4, 128, 512), np.float32)
    for j in range(4):
        for r in range(128):
            m[j, r, 128 * j + r + 1:] = NEG
    c["maskA"] = m
    inv = 1.0 / 10000.0 ** (np.arange(0, 128, 2, dtype=np.float64) / 128.0)
    fr = np.arange(T, dtype=np.float64)[None, :] * inv[:, None]
    cosv = np.cos(fr).astype(np.float32)
    sinv = np.sin(fr).astype(np.float32)
    c["cs1T"] = np.concatenate([cosv, sinv], axis=0)
    c["cs2T"] = np.concatenate([sinv, cosv], axis=0)
    r1 = np.zeros((128, 128), np.float32)
    r2 = np.zeros((128, 128), np.float32)
    for m in range(64):
        r1[m, m] = 1.0
        r1[m + 64, m] = -1.0
        r2[m, m + 64] = 1.0
        r2[m + 64, m + 64] = 1.0
    c["R1"] = r1.astype(bfdt)
    c["R2"] = r2.astype(bfdt)
    mt = np.zeros((2, 128, 256), np.float32)
    for kt in range(2):
        for kk in range(128):
            mt[kt, kk, :kt * 128 + kk] = NEG
    c["maskT"] = mt
    sel = np.zeros((16, 16, 128), np.float32)
    for h in range(16):
        sel[h, h, :] = 1.0
    c["sel16"] = sel
    sel8 = np.zeros((8, 8, 128), np.float32)
    for f in range(8):
        sel8[f, f, :] = 1.0
    c["sel8"] = sel8
    s127 = np.zeros((128, 128), np.float32)
    s127[127, :] = 1.0
    c["sel127"] = s127
    c["ones_bf"] = np.ones((128, 1), np.float32).astype(bfdt)
    c["ones_f32"] = np.ones((128, 1), np.float32)
    c["onesrow_f32"] = np.ones((1, 128), np.float32)
    return c


def host_prep(inputs):
    consts = _host_consts()
    x = np.asarray(inputs["L0_attn_x"])
    pc = []
    for c in range(NCORE):
        b, h2 = c // 2, c % 2
        d = dict(consts)
        d["xT"] = np.ascontiguousarray(x[b].T)
        d["xTr"] = np.ascontiguousarray(x[b].T[:, h2 * 1024:(h2 + 1) * 1024])
        Wq = np.asarray(inputs["L0_attn_Wq"])
        cols = np.concatenate([_perm256(256 * h) for h in range(8 * h2, 8 * h2 + 8)])
        d["qw4"] = np.ascontiguousarray(Wq[:, cols]).reshape(16, 128, 4, 512).transpose(2, 1, 0, 3).copy()
        Wk = np.asarray(inputs["L0_attn_Wk"])
        kcols = np.concatenate([_perm256(256 * kv) for kv in (2 * h2, 2 * h2 + 1)])
        d["kw"] = np.ascontiguousarray(Wk[:, kcols]).reshape(16, 128, 512).transpose(1, 0, 2).copy()
        Wv = np.asarray(inputs["L0_attn_Wv"])
        d["vw"] = np.ascontiguousarray(Wv[:, 512 * h2:512 * h2 + 512]).reshape(16, 128, 512).transpose(1, 0, 2).copy()
        Wo = np.asarray(inputs["L0_attn_Wo"])
        woa = Wo[h2 * 2048:(h2 + 1) * 2048, :]
        d["gnw_row"] = np.ascontiguousarray(np.asarray(inputs["L0_attn_gn_w"])[h2 * 2048:(h2 + 1) * 2048].reshape(1, 2048))
        d["gnb_row"] = np.ascontiguousarray(np.asarray(inputs["L0_attn_gn_b"])[h2 * 2048:(h2 + 1) * 2048].reshape(1, 2048))
        for nm in ("lambda_q1", "lambda_k1", "lambda_q2", "lambda_k2"):
            d[nm.replace("lambda_", "lam")] = np.ascontiguousarray(np.asarray(inputs["L0_attn_" + nm]).reshape(128, 1))
        rows = np.arange(1024 * h2, 1024 * h2 + 1024)
        d["fqw"] = np.ascontiguousarray(np.asarray(inputs["L0_ffn_Wq"])[rows].T).reshape(16, 128, 2, 512).transpose(2, 1, 0, 3).copy()
        d["fkw"] = np.ascontiguousarray(np.asarray(inputs["L0_ffn_Wk"])[rows].T).reshape(16, 128, 2, 512).transpose(2, 1, 0, 3).copy()
        d["fvw"] = np.ascontiguousarray(np.asarray(inputs["L0_ffn_Wv"])[rows].T).reshape(16, 128, 2, 512).transpose(2, 1, 0, 3).copy()
        wof = np.ascontiguousarray(np.asarray(inputs["L0_ffn_Wo"]).T[rows])
        woall = np.concatenate([woa, wof], axis=0)
        d["wo4"] = woall.reshape(24, 128, 4, 512).transpose(2, 1, 0, 3).copy()
        d["ln1w_col"] = np.ascontiguousarray(np.asarray(inputs["L0_ffn_ln1_w"]).reshape(16, 128).T)
        d["ln2w_col"] = np.ascontiguousarray(np.asarray(inputs["L0_ffn_ln2_w"]).reshape(16, 128).T)
        d["A_t"] = np.asarray(inputs["L0_ffn_A"]).reshape(16, 128, 16).copy()
        d["Bmat"] = np.asarray(inputs["L0_ffn_B"]).copy()
        d["wi4"] = np.ascontiguousarray(np.asarray(inputs["L0_ffn_W_in_base"]).T).reshape(16, 128, 16, 512).transpose(2, 1, 0, 3).copy()
        d["wu4"] = np.ascontiguousarray(np.asarray(inputs["L0_ffn_W_up_base"]).T).reshape(16, 128, 16, 512).transpose(2, 1, 0, 3).copy()
        d["wd4"] = np.ascontiguousarray(np.asarray(inputs["L0_ffn_W_down"]).T).reshape(64, 128, 16, 128).transpose(2, 1, 0, 3).copy()
        Wi = np.asarray(inputs["L1_attn_W_in"])
        zrows = np.arange(2048 * h2, 2048 * h2 + 2048)
        zpad64 = np.zeros((64, 2048), Wi.dtype)
        w1a = np.concatenate([Wi[4096 + zrows],
                              Wi[8192:8256], zpad64,
                              Wi[8256:8320], zpad64,
                              Wi[8320 + 16 * h2:8320 + 16 * h2 + 16],
                              np.zeros((112, 2048), Wi.dtype)], axis=0)
        d["w1a"] = np.ascontiguousarray(w1a.T).reshape(16, 128, 19, 128).transpose(2, 1, 0, 3).copy()
        d["w1z"] = np.ascontiguousarray(Wi[zrows].T).reshape(16, 128, 16, 128).transpose(2, 1, 0, 3).copy()
        cw = np.asarray(inputs["L1_attn_conv_w"])
        cb = np.asarray(inputs["L1_attn_conv_b"])
        cwp = np.concatenate([cw[2048 * h2:2048 * h2 + 2048],
                              cw[4096:4160], np.zeros((64, 4), cw.dtype),
                              cw[4160:4224], np.zeros((64, 4), cw.dtype)], axis=0)
        cbp = np.concatenate([cb[2048 * h2:2048 * h2 + 2048],
                              cb[4096:4160], np.zeros(64, cb.dtype),
                              cb[4160:4224], np.zeros(64, cb.dtype)])
        d["convw"] = np.ascontiguousarray(cwp).reshape(18, 128, 4)
        d["convb"] = np.ascontiguousarray(cbp).reshape(18, 128, 1)
        hrows = np.arange(16 * h2, 16 * h2 + 16)
        d["Alog"] = np.ascontiguousarray(np.asarray(inputs["L1_attn_A_log"])[hrows].reshape(16, 1))
        d["dtb"] = np.ascontiguousarray(np.asarray(inputs["L1_attn_dt_bias"])[hrows].reshape(16, 1))
        d["Dvec"] = np.ascontiguousarray(np.asarray(inputs["L1_attn_D"])[hrows].reshape(16, 1))
        d["wot"] = np.ascontiguousarray(np.asarray(inputs["L1_attn_W_out"]).T).reshape(32, 128, 16, 128).transpose(2, 1, 0, 3).copy()
        d["wg8"] = np.asarray(inputs["L1_ffn_w_gate"]).reshape(16, 128, 16, 512).transpose(2, 1, 0, 3).copy()
        d["wu8"] = np.asarray(inputs["L1_ffn_w_up"]).reshape(16, 128, 16, 512).transpose(2, 1, 0, 3).copy()
        d["wd8"] = np.asarray(inputs["L1_ffn_w_down"]).reshape(64, 128, 16, 128).transpose(2, 1, 0, 3).copy()
        d["sgt"] = np.ascontiguousarray(np.asarray(inputs["L1_ffn_s_gate"]).reshape(64, 128).T)
        d["sut"] = np.ascontiguousarray(np.asarray(inputs["L1_ffn_s_up"]).reshape(64, 128).T)
        d["sdt"] = np.ascontiguousarray(np.asarray(inputs["L1_ffn_s_down"]).reshape(16, 128).T)
        pc.append(d)
    return pc


def _declare(nc, names_shapes, kind):
    out = {}
    for nm, shape, dt in names_shapes:
        out[nm] = nc.dram_tensor(nm, list(shape), dt, kind=kind).ap()
    return out



STAGEA_IN = [
    ("xT", (D, T), F32),
    ("qw4", (4, 128, 16, 512), F32),
    ("kw", (128, 16, 512), F32),
    ("vw", (128, 16, 512), F32),
    ("fqw", (2, 128, 16, 512), F32),
    ("fkw", (2, 128, 16, 512), F32),
    ("fvw", (2, 128, 16, 512), F32),
    ("wo4", (4, 128, 24, 512), F32),
    ("gnw_row", (1, 2048), F32), ("gnb_row", (1, 2048), F32),
    ("lamq1", (128, 1), F32), ("lamk1", (128, 1), F32),
    ("lamq2", (128, 1), F32), ("lamk2", (128, 1), F32),
    ("ln1w_col", (128, 16), F32),
    ("cs1T", (128, T), F32), ("cs2T", (128, T), F32),
    ("R1", (128, 128), BF16), ("R2", (128, 128), BF16),
    ("maskA", (4, 128, 512), F32),
    ("ident_bf", (128, 128), BF16),
    ("ones_bf", (128, 1), BF16),
    ("ones_f32", (128, 1), F32),
    ("onesrow_f32", (1, 128), F32),
]


def emit_stageA(nc, st, ex, part_out, partF_out, sp):
    A = lambda nm, sh, dt, side="left": st.enter_context(nc.sbuf_tensor(nm, list(sh), dt, side=side))

    ident = A("idA", (128, 128), BF16)
    onesb = A("onbA", (128, 1), BF16)
    onesf = A("onfA", (128, 1), F32)
    onesrow = A("onrA", (1, 128), F32)
    ln1w = A("l1wA", (128, 16), F32)
    lamv = A("lamA", (128, 4), F32)
    lam_b = A("lamB", (128, 1), F32)
    small = A("smlA", (1, 32), F32)
    epsb = A("epsbA", (128, 4), F32)
    bc0 = A("bc0A", (128, T), F32)

    sd = Ct(nc.alloc_semaphore("sdA"))
    sm = Ct(nc.alloc_semaphore("smA"))
    sv = Ct(nc.alloc_semaphore("svA"))
    ss = Ct(nc.alloc_semaphore("ssA"))
    sg = Ct(nc.alloc_semaphore("sgA"))

    bk = Blk(); y, g, t, v, s = bk.engines()
    y.dma_start(ident[:], ex["ident_bf"][:]).then_inc(sd, 16)
    y.dma_start(onesb[:], ex["ones_bf"][:]).then_inc(sd, 16)
    y.dma_start(onesf[:], ex["ones_f32"][:]).then_inc(sd, 16)
    y.dma_start(onesrow[:], ex["onesrow_f32"][:]).then_inc(sd, 16)
    y.dma_start(ln1w[:], ex["ln1w_col"][:]).then_inc(sd, 16)
    for j, nm in enumerate(("lamq1", "lamk1", "lamq2", "lamk2")):
        y.dma_start(lamv[:, j:j + 1], ex[nm][:]).then_inc(sd, 16)
    y.wait(sd)
    v.memset(epsb[:, 0:1], 1e-5)
    v.memset(epsb[:, 1:2], 1e-6)
    v.memset(epsb[:, 2:3], 1.1920929e-07)
    v.memset(epsb[:, 3:4], 0.0)
    bk.run(nc)

    with nc.psum_tensor("psA1", [128, 8], F32) as ps1:
        bk = Blk(); y, g, t, v, s = bk.engines()
        v.tensor_tensor(bc0[:, 0:1], lamv[:, 0:1], lamv[:, 1:2], ALU.mult)
        v.tensor_tensor(bc0[:, 1:2], lamv[:, 2:3], lamv[:, 3:4], ALU.mult).then_inc(sv, 1)
        t.wait(sv)
        t.matmul(ps1[0:1, 0:1], bc0[:, 0:1], onesf[:], start=True, stop=True)
        t.matmul(ps1[0:1, 1:2], bc0[:, 1:2], onesf[:], start=True, stop=True).then_inc(sm, 1)
        s.wait(sm)
        s.activation(small[0:1, 0:2], ps1[0:1, 0:2], AF.Exp).then_inc(ss, 1)
        v.wait(ss)
        v.tensor_tensor(small[0:1, 2:3], small[0:1, 0:1], small[0:1, 1:2], ALU.subtract).then_inc(sv, 1)
        g.wait(sv)
        g.tensor_scalar(small[0:1, 3:4], small[0:1, 2:3], 0.2, None, ALU.add).then_inc(sg, 1)
        t.wait(sg)
        t.matmul(ps1[:, 2:3], onesrow[:], small[0:1, 3:4], start=True, stop=True).then_inc(sm, 1)
        v.wait(sm)
        v.tensor_copy(lam_b[:], ps1[:, 2:3])
        bk.run(nc)

    qkv_stack = ExitStack()
    q_r = qkv_stack.enter_context(nc.sbuf_tensor("qrA", [128, 16 * T], BF16, side="left"))
    k_r = qkv_stack.enter_context(nc.sbuf_tensor("krA", [128, 4 * T], BF16, side="left"))
    v_tm = qkv_stack.enter_context(nc.sbuf_tensor("vtA", [128, 16 * 512], BF16, side="left"))

    with nc.sbuf_tensor("xnA", [128, 16 * T], BF16, side="right") as xn, \
         nc.sbuf_tensor("swA", [1, T], F32, side="right") as srow, \
         nc.sbuf_tensor("scr0A", [128, T], BF16, side="right") as scr0:

      with nc.sbuf_tensor("mrA", [1, T], F32, side="right") as mean_row, \
           nc.sbuf_tensor("bc1A", [128, T], BF16, side="right") as bc1, \
           nc.sbuf_tensor("xldA", [128, T], F32, side="right") as xld:
        with nc.psum_tensor("psSumA", [1, T], F32) as ps_sum, \
             nc.psum_tensor("psSqA", [1, T], F32) as ps_sq:
            bk = Blk(); y, g, t, v, s = bk.engines()
            for cc in range(16):
                if cc >= 1:
                    y.wait(sv); y.wait(ss)
                y.dma_start(xld[:], ex["xT"][cc * 128:(cc + 1) * 128, :]).then_inc(sd, 16)
                v.wait(sd)
                if cc >= 1:
                    v.wait(sm)
                v.tensor_copy(scr0[:], xld[:]).then_inc(sv, 1)
                s.wait(sd)
                s.activation(xn[:, cc * T:(cc + 1) * T], xld[:], AF.Square).then_inc(ss, 1)
                t.wait(sv); t.wait(ss)
                xb = scr0
                for j in range(4):
                    t.matmul(ps_sum[0:1, j * 512:(j + 1) * 512], onesb[:], xb[:, j * 512:(j + 1) * 512],
                             start=(cc == 0), stop=(cc == 15))
                mmi = None
                for j in range(4):
                    mmi = t.matmul(ps_sq[0:1, j * 512:(j + 1) * 512], onesb[:],
                                   xn[:, cc * T + j * 512:cc * T + (j + 1) * 512],
                                   start=(cc == 0), stop=(cc == 15))
                mmi.then_inc(sm, 1)
            bk.run(nc)

            bk = Blk(); y, g, t, v, s = bk.engines()
            v.tensor_scalar(mean_row[:], ps_sum[0:1, 0:T], 1.0 / 2048.0, None, ALU.mult)
            v.tensor_scalar(srow[:], ps_sq[0:1, 0:T], 1.0 / 2048.0, None, ALU.mult).then_inc(sv, 1)
            bk.run(nc)
            bk = Blk(); y, g, t, v, s = bk.engines()
            v.tensor_tensor(ps_sum[0:1, 0:T], mean_row[:], mean_row[:], ALU.mult).then_inc(sv, 1)
            bk.run(nc)
            bk = Blk(); y, g, t, v, s = bk.engines()
            v.tensor_tensor(srow[:], srow[:], ps_sum[0:1, 0:T], ALU.subtract).then_inc(sv, 1)
            s.wait(sv)
            s.activation(ps_sum[0:1, 0:T], srow[:], AF.Sqrt, bias=epsb[0:1, 0:1]).then_inc(ss, 1)
            bk.run(nc)
            bk = Blk(); y, g, t, v, s = bk.engines()
            v.reciprocal(srow[:], ps_sum[0:1, 0:T]).then_inc(sv, 1)
            bk.run(nc)

        with nc.psum_tensor("psBcA", [128, 1024], F32) as ps_bc:
            bk = Blk(); y, g, t, v, s = bk.engines()
            for j in range(4):
                t.wait(sv)
                t.matmul(ps_bc[:, 0:512], onesrow[:], mean_row[0:1, j * 512:(j + 1) * 512], start=True, stop=True)
                t.matmul(ps_bc[:, 512:1024], onesrow[:], srow[0:1, j * 512:(j + 1) * 512], start=True, stop=True).then_inc(sm, 1)
                v.wait(sm)
                v.tensor_copy(bc0[:, j * 512:(j + 1) * 512], ps_bc[:, 0:512])
                v.tensor_copy(bc1[:, j * 512:(j + 1) * 512], ps_bc[:, 512:1024]).then_inc(sv, 1)
            bk.run(nc)

            bk = Blk(); y, g, t, v, s = bk.engines()
            for cc in range(16):
                if cc >= 1:
                    y.wait(sg)
                y.dma_start(xld[:], ex["xT"][cc * 128:(cc + 1) * 128, :]).then_inc(sd, 16)
                v.wait(sd)
                v.tensor_tensor(xld[:], xld[:], bc0[:], ALU.subtract).then_inc(sv, 1)
                g.wait(sv)
                g.tensor_tensor(xn[:, cc * T:(cc + 1) * T], xld[:], bc1[:], ALU.mult).then_inc(sg, 1)
            bk.run(nc)

      if True:
        with ExitStack() as st3, \
             nc.psum_tensor("psqA0", [128, 512], F32) as pq0, \
             nc.psum_tensor("psqA1", [128, 512], F32) as pq1, \
             nc.psum_tensor("pstrA", [128, 512], BF16) as pstr, \
             nc.psum_tensor("psS1A", [1, T], F32) as ps_s1:
            A3 = lambda nm, sh, dt: st3.enter_context(nc.sbuf_tensor(nm, list(sh), dt, side="right"))
            wbuf = A3("wbA", (128, 16 * 512), BF16)
            vtmp = A3("vtmpA", (128, 512), BF16)
            stq = A3("stqA", (128, 2048), BF16)
            psq = [pq0, pq1]

            def proj_block(bk_eng, w_grp_ap, oc_width, dst_fn):
                y, g, t, v, s = bk_eng
                g.wait(sm)
                g.dma_start(wbuf[:, 0:16 * oc_width].rearrange("p (k c) -> p k c", k=16),
                            w_grp_ap).then_inc(sd, 16)
                t.wait(sd)
                noc = oc_width // 128
                for oc in range(noc):
                    for tb in range(4):
                        par = (oc * 4 + tb) % 2
                        t.wait(sv)
                        mmi = None
                        for kc in range(16):
                            base = kc * oc_width + oc * 128
                            mmi = t.matmul(psq[par][:], wbuf[:, base:base + 128],
                                           xn[:, kc * T + tb * 512:kc * T + (tb + 1) * 512],
                                           start=(kc == 0), stop=(kc == 15))
                        mmi.then_inc(sm, 1)
                        dst_fn(oc, tb, psq[par])

            for grp in range(4):
                bk = Blk(); engs = bk.engines(); y, g, t, v, s = engs
                def dq(oc, tb, ps, grp=grp, v=v):
                    v.wait(sm)
                    cc = grp * 4 + oc
                    v.tensor_copy(q_r[:, cc * T + tb * 512:cc * T + (tb + 1) * 512], ps[:]).then_inc(sv, 1)
                proj_block(engs, ex["qw4"][grp], 512, dq)
                bk.run(nc)
            bk = Blk(); engs = bk.engines(); y, g, t, v, s = engs
            def dk(oc, tb, ps, v=v):
                v.wait(sm)
                v.tensor_copy(k_r[:, oc * T + tb * 512:oc * T + (tb + 1) * 512], ps[:]).then_inc(sv, 1)
            proj_block(engs, ex["kw"][:], 512, dk)
            bk.run(nc)
            bk = Blk(); engs = bk.engines(); y, g, t, v, s = engs
            def dv(oc, tb, ps, t=t, v=v):
                v.wait(sm)
                v.tensor_copy(vtmp[:], ps[:]).then_inc(sv, 1)
                t.wait(sv)
                tri = None
                for j in range(4):
                    tri = t.transpose(pstr[:, j * 128:(j + 1) * 128], vtmp[:, j * 128:(j + 1) * 128], ident[:])
                tri.then_inc(sm, 1)
                v.wait(sm)
                cpi = None
                for j in range(4):
                    kt = tb * 4 + j
                    cpi = v.tensor_copy(v_tm[:, kt * 512 + oc * 128:kt * 512 + (oc + 1) * 128],
                                        pstr[:, j * 128:(j + 1) * 128])
                cpi.then_inc(sv, 1)
            proj_block(engs, ex["vw"][:], 512, dv)
            bk.run(nc)

            bk = Blk(); y, g, t, v, s = bk.engines()
            sqm_evt = []
            for cc in range(16):
                if cc >= 1:
                    s.wait(sm, sqm_evt[cc - 1])
                sb = scr0
                s.activation(sb[:], xn[:, cc * T:(cc + 1) * T], AF.Square).then_inc(ss, 1)
                t.wait(ss)
                mmi = None
                for j in range(4):
                    mmi = t.matmul(ps_s1[0:1, j * 512:(j + 1) * 512], onesb[:], sb[:, j * 512:(j + 1) * 512],
                                   start=(cc == 0), stop=(cc == 15))
                mmi.then_inc(sm, 1)
                sqm_evt.append(sm.n)
            bk.run(nc)
            bk = Blk(); y, g, t, v, s = bk.engines()
            v.tensor_scalar(srow[:], ps_s1[0:1, 0:T], 1.0 / 2048.0, None, ALU.mult).then_inc(sv, 1)
            s.wait(sv)
            s.activation(ps_s1[0:1, 0:T], srow[:], AF.Sqrt, bias=epsb[0:1, 1:2]).then_inc(ss, 1)
            bk.run(nc)
            bk = Blk(); y, g, t, v, s = bk.engines()
            v.wait(ss)
            v.reciprocal(srow[:], ps_s1[0:1, 0:T]).then_inc(sv, 1)
            for j in range(4):
                t.wait(sv)
                t.matmul(psq[1][:], onesrow[:], srow[0:1, j * 512:(j + 1) * 512], start=True, stop=True).then_inc(sm, 1)
                v.wait(sm)
                v.tensor_copy(bc0[:, j * 512:(j + 1) * 512], psq[1][:]).then_inc(sv, 1)
            bk.run(nc)
            bk = Blk(); y, g, t, v, s = bk.engines()
            for cc in range(16):
                sb = scr0
                if cc >= 1:
                    v.wait(sg)
                v.tensor_scalar(sb[:], xn[:, cc * T:(cc + 1) * T], ln1w[:, cc:cc + 1], None, ALU.mult).then_inc(sv, 1)
                g.wait(sv)
                g.tensor_tensor(xn[:, cc * T:(cc + 1) * T], sb[:], bc0[:], ALU.mult).then_inc(sg, 1)
            bk.run(nc)

            for wnm, spnm in (("fqw", "sp_qf"), ("fkw", "sp_kf")):
                for grp in range(2):
                    bk = Blk(); engs = bk.engines(); y2, g2, t2, v2, s2 = engs
                    def dqf(oc, tb, ps, grp=grp, spnm=spnm, y2=y2, v2=v2):
                        v2.wait(sm)
                        if tb == 0:
                            v2.wait(sd)
                        v2.tensor_copy(stq[:, tb * 512:(tb + 1) * 512], ps[:]).then_inc(sv, 1)
                        if tb == 3:
                            y2.wait(sv)
                            row = (grp * 4 + oc) * 128
                            y2.dma_start(sp[spnm][row:row + 128, :], stq[:]).then_inc(sd, 16)
                    proj_block(engs, ex[wnm][grp], 512, dqf)
                    bk.run(nc)
            for grp in range(2):
                bk = Blk(); engs = bk.engines(); y2, g2, t2, v2, s2 = engs
                def dvf(oc, tb, ps, grp=grp, t2=t2, v2=v2, y2=y2):
                    h = grp * 4 + oc
                    v2.wait(sm)
                    v2.tensor_copy(vtmp[:], ps[:]).then_inc(sv, 1)
                    t2.wait(sv)
                    tri = None
                    for j in range(4):
                        tri = t2.transpose(pstr[:, j * 128:(j + 1) * 128], vtmp[:, j * 128:(j + 1) * 128], ident[:])
                    tri.then_inc(sm, 1)
                    v2.wait(sm)
                    v2.wait(sd)
                    cpi = None
                    for j in range(4):
                        cpi = v2.tensor_copy(stq[:, j * 128:(j + 1) * 128], pstr[:, j * 128:(j + 1) * 128])
                    cpi.then_inc(sv, 1)
                    y2.wait(sv)
                    for j in range(4):
                        kt = tb * 4 + j
                        y2.dma_start(sp["sp_vtm"][kt * 128:(kt + 1) * 128, h * 128:(h + 1) * 128],
                                     stq[:, j * 128:(j + 1) * 128]).then_inc(sd, 16)
                proj_block(engs, ex["fvw"][grp], 512, dvf)
                bk.run(nc)
            bk = Blk(); y, g, t, v, s = bk.engines()
            y.wait(sd)
            bk.run(nc)

    with nc.sbuf_tensor("cs1A", [128, T], F32, side="right") as cs1, \
         nc.sbuf_tensor("cs2A", [128, T], F32, side="right") as cs2, \
         nc.sbuf_tensor("rR1A", [128, 128], BF16, side="right") as R1, \
         nc.sbuf_tensor("rR2A", [128, 128], BF16, side="right") as R2, \
         nc.sbuf_tensor("rpA", [128, T], BF16, side="right") as scrA, \
         nc.sbuf_tensor("rpB", [128, T], BF16, side="right") as scrB, \
         nc.psum_tensor("psRopeA", [128, 2048], F32) as ps_ro:
        bk = Blk(); y, g, t, v, s = bk.engines()
        y.dma_start(cs1[:], ex["cs1T"][:]).then_inc(sd, 16)
        y.dma_start(cs2[:], ex["cs2T"][:]).then_inc(sd, 16)
        y.dma_start(R1[:], ex["R1"][:]).then_inc(sd, 16)
        y.dma_start(R2[:], ex["R2"][:]).then_inc(sd, 16)
        for i in range(20):
            buf = q_r if i < 16 else k_r
            cc = i if i < 16 else i - 16
            ch = buf[:, cc * T:(cc + 1) * T]
            if i == 0:
                v.wait(sd); g.wait(sd)
            if i > 0:
                v.wait(sm)
                g.wait(sm)
            v.tensor_tensor(scrA[:], ch, cs1[:], ALU.mult).then_inc(sv, 1)
            g.tensor_tensor(scrB[:], ch, cs2[:], ALU.mult).then_inc(sg, 1)
            t.wait(sv); t.wait(sg)
            if i > 0:
                t.wait(sv)
            mmi = None
            for tb in range(4):
                t.matmul(ps_ro[:, tb * 512:(tb + 1) * 512], R1[:], scrA[:, tb * 512:(tb + 1) * 512],
                         start=True, stop=False)
                mmi = t.matmul(ps_ro[:, tb * 512:(tb + 1) * 512], R2[:], scrB[:, tb * 512:(tb + 1) * 512],
                               start=False, stop=True)
            mmi.then_inc(sm, 1)
            v.wait(sm)
            cpi = None
            for tb in range(4):
                cpi = v.tensor_copy(ch[:, tb * 512:(tb + 1) * 512] if False else buf[:, cc * T + tb * 512:cc * T + (tb + 1) * 512], ps_ro[:, tb * 512:(tb + 1) * 512])
            cpi.then_inc(sv, 1)
        bk.run(nc)

    with nc.sbuf_tensor("mskA", [128, 4 * 512], BF16, side="right") as masks:
        bk = Blk(); y, g, t, v, s = bk.engines()
        for j in range(4):
            g.dma_start(masks[:, j * 512:(j + 1) * 512], ex["maskA"][j]).then_inc(sd, 16)
        g.wait(sd)
        bk.run(nc)

        with ExitStack() as st5, \
             nc.psum_tensor("psScA", [128, 2048], F32) as ps_sc, \
             nc.psum_tensor("psTrA", [128, 512], BF16) as ps_tr, \
             nc.psum_tensor("psAv0", [128, 256], F32) as ps_av1, \
             nc.psum_tensor("psAv1", [128, 256], F32) as ps_av2, \
             nc.psum_tensor("psEpA", [128, 512], F32) as ps_ep:
            A5 = lambda nm, sh, dt: st5.enter_context(nc.sbuf_tensor(nm, list(sh), dt, side="right"))
            e_sb = A5("esbA", (128, 2048), BF16)
            eT_sb = A5("etbA", (128, 2048), BF16)
            avd = A5("avdA", (128, 16 * 256), F32)
            mxn = A5("mxnA", (128, 1), F32)
            biasq = A5("biasA", (128, 1), F32)
            ssum = A5("ssumA", (128, 2), F32)
            rrec = A5("rrecA", (128, 3), F32)
            sums_b = A5("sumsA", (128, 16), F32)
            sqs_b = A5("sqsA", (128, 16), F32)
            avc = A5("avcA", (128, 512), F32)
            abrow = A5("abrowA", (1, 512), F32)
            marow = A5("marowA", (1, 256), F32)
            gnrow = A5("gnrowA", (1, 512), F32)
            scrq = A5("scrqA", (128, 256), BF16)
            avn_tm = A5("avntmA", (128, 256), BF16)
            stow = A5("stowA", (128, 256), BF16)

            for h in range(8):
                kvl = h // 4
                bk = Blk(); y, g, t, v, s = bk.engines()
                y.dma_start(gnrow[0:1, 0:256], ex["gnw_row"][0:1, h * 256:(h + 1) * 256]).then_inc(sd, 16)
                y.dma_start(gnrow[0:1, 256:512], ex["gnb_row"][0:1, h * 256:(h + 1) * 256]).then_inc(sd, 16)
                for qt in range(16):
                    nkb = qt // 4 + 1
                    for m in (1, 2):
                        qcc = 2 * h + (m - 1)
                        kcc = 2 * kvl + (m - 1)
                        t.wait(ss)
                        mmi = None
                        for kb in range(nkb):
                            mmi = t.matmul(ps_sc[:, kb * 512:(kb + 1) * 512],
                                           q_r[:, qcc * T + qt * 128:qcc * T + (qt + 1) * 128],
                                           k_r[:, kcc * T + kb * 512:kcc * T + (kb + 1) * 512],
                                           start=True, stop=True)
                        mmi.then_inc(sm, 1)
                        v.wait(sm)
                        v.tensor_tensor(ps_sc[:, (nkb - 1) * 512:nkb * 512], ps_sc[:, (nkb - 1) * 512:nkb * 512],
                                        masks[:, (qt % 4) * 512:(qt % 4 + 1) * 512], ALU.add)
                        v.reduce_max(mxn[:], ps_sc[:, 0:nkb * 512], axis=AX.X, negate=True).then_inc(sv, 1)
                        g.wait(sv)
                        g.tensor_scalar(biasq[:], mxn[:], SCALE, None, ALU.mult).then_inc(sg, 1)
                        s.wait(sg)
                        s.wait(sm)
                        s.activation(e_sb[:, 0:nkb * 512], ps_sc[:, 0:nkb * 512], AF.Exp,
                                     bias=biasq[:], scale=SCALE,
                                     accum_out=ssum[:, m - 1:m]).then_inc(ss, 1)
                        for grp in range(nkb):
                            t.wait(ss)
                            t.wait(sv)
                            tri = None
                            for j in range(4):
                                tri = t.transpose(ps_tr[:, j * 128:(j + 1) * 128],
                                                  e_sb[:, grp * 512 + j * 128:grp * 512 + (j + 1) * 128], ident[:])
                            tri.then_inc(sm, 1)
                            v.wait(sm)
                            v.tensor_copy(eT_sb[:, grp * 512:(grp + 1) * 512], ps_tr[:]).then_inc(sv, 1)
                        ps_av = ps_av1 if m == 1 else ps_av2
                        t.wait(sv)
                        t.wait(ss)
                        mmi = None
                        for kt in range(nkb * 4):
                            mmi = t.matmul(ps_av[:], eT_sb[:, kt * 128:(kt + 1) * 128],
                                           v_tm[:, kt * 512 + kvl * 256:kt * 512 + (kvl + 1) * 256],
                                           start=(kt == 0), stop=(kt == nkb * 4 - 1))
                        mmi.then_inc(sm, 1)
                    v.wait(ss)
                    v.reciprocal(rrec[:, 0:1], ssum[:, 0:1])
                    v.reciprocal(rrec[:, 1:2], ssum[:, 1:2]).then_inc(sv, 1)
                    g.wait(sv)
                    g.tensor_tensor(rrec[:, 2:3], rrec[:, 1:2], lam_b[:], ALU.mult).then_inc(sg, 1)
                    s.wait(sm); s.wait(sg)
                    s.activation(avc[:, 0:256], ps_av1[:], AF.Copy, scale=rrec[:, 0:1])
                    s.activation(avc[:, 256:512], ps_av2[:], AF.Copy, scale=rrec[:, 2:3]).then_inc(ss, 1)
                    v.wait(ss)
                    v.tensor_tensor(avd[:, qt * 256:(qt + 1) * 256], avc[:, 0:256], avc[:, 256:512],
                                    ALU.subtract).then_inc(sv, 1)
                    v.tensor_scalar(avc[:, 0:256], avd[:, qt * 256:(qt + 1) * 256], 1.0, None, ALU.mult,
                                    op1=ALU.add, accum_out=sums_b[:, qt:qt + 1]).then_inc(sv, 1)
                    s.wait(sv)
                    s.activation(scrq[:], avd[:, qt * 256:(qt + 1) * 256], AF.Square,
                                 accum_out=sqs_b[:, qt:qt + 1]).then_inc(ss, 1)
                bk.run(nc)

                bk = Blk(); y, g, t, v, s = bk.engines()
                v.wait(ss)
                v.reduce_sum(rrec[:, 0:1], sums_b[:], axis=AX.X)
                v.reduce_sum(rrec[:, 1:2], sqs_b[:], axis=AX.X).then_inc(sv, 1)
                t.wait(sv)
                t.matmul(ps_ep[0:1, 0:1], rrec[:, 0:1], onesf[:], start=True, stop=True)
                t.matmul(ps_ep[0:1, 1:2], rrec[:, 1:2], onesf[:], start=True, stop=True).then_inc(sm, 1)
                v.wait(sm)
                v.tensor_scalar(small[0:1, 4:5], ps_ep[0:1, 0:1], 1.0 / (2048.0 * 256.0), None, ALU.mult)
                v.tensor_scalar(small[0:1, 5:6], ps_ep[0:1, 1:2], 1.0 / (2048.0 * 256.0), None, ALU.mult).then_inc(sv, 1)
                g.wait(sv)
                g.tensor_tensor(small[0:1, 6:7], small[0:1, 4:5], small[0:1, 4:5], ALU.mult).then_inc(sg, 1)
                v.wait(sg)
                v.tensor_tensor(small[0:1, 7:8], small[0:1, 5:6], small[0:1, 6:7], ALU.subtract).then_inc(sv, 1)
                s.wait(sv)
                s.activation(small[0:1, 8:9], small[0:1, 7:8], AF.Sqrt, bias=epsb[0:1, 0:1]).then_inc(ss, 1)
                v.wait(ss)
                v.reciprocal(small[0:1, 9:10], small[0:1, 8:9]).then_inc(sv, 1)
                v.wait(sd)
                v.tensor_scalar(abrow[0:1, 0:256], gnrow[0:1, 0:256], small[0:1, 9:10], 0.8,
                                ALU.mult, op1=ALU.mult).then_inc(sv, 1)
                g.wait(sv)
                g.tensor_scalar(marow[0:1, 0:256], abrow[0:1, 0:256], small[0:1, 4:5], None, ALU.mult).then_inc(sg, 1)
                v.tensor_scalar(abrow[0:1, 256:512], gnrow[0:1, 256:512], 0.8, None, ALU.mult)
                v.wait(sg)
                v.tensor_tensor(abrow[0:1, 256:512], abrow[0:1, 256:512], marow[0:1, 0:256], ALU.subtract).then_inc(sv, 1)
                t.wait(sv)
                t.matmul(ps_ep[:, 0:512], onesrow[:], abrow[0:1, 0:512], start=True, stop=True).then_inc(sm, 1)
                v.wait(sm)
                v.tensor_copy(bc0[:, 0:512], ps_ep[:, 0:512]).then_inc(sv, 1)
                bk.run(nc)

                bk = Blk(); y, g, t, v, s = bk.engines()
                for qt in range(16):
                    if qt > 0:
                        v.wait(sg)
                    v.tensor_tensor(avc[:, 0:256], avd[:, qt * 256:(qt + 1) * 256], bc0[:, 0:256], ALU.mult).then_inc(sv, 1)
                    g.wait(sv)
                    g.tensor_tensor(avn_tm[:], avc[:, 0:256], bc0[:, 256:512], ALU.add).then_inc(sg, 1)
                    t.wait(sg)
                    tri = None
                    for j in range(2):
                        tri = t.transpose(ps_tr[:, j * 128:(j + 1) * 128], avn_tm[:, j * 128:(j + 1) * 128], ident[:])
                    tri.then_inc(sm, 1)
                    v.wait(sm)
                    v.wait(sd)
                    v.tensor_copy(stow[:], ps_tr[:, 0:256]).then_inc(sv, 1)
                    y.wait(sv)
                    y.dma_start(sp["sp_avnT"][qt, 2 * h], stow[:, 0:128]).then_inc(sd, 16)
                    y.dma_start(sp["sp_avnT"][qt, 2 * h + 1], stow[:, 128:256]).then_inc(sd, 16)
                bk.run(nc)
    qkv_stack.close()

    with nc.sbuf_tensor("mskF", [128, 4 * 512], BF16, side="right") as masks:
        bk = Blk(); y, g, t, v, s = bk.engines()
        for j in range(4):
            g.dma_start(masks[:, j * 512:(j + 1) * 512], ex["maskA"][j]).then_inc(sd, 16)
        g.wait(sd)
        bk.run(nc)
        with nc.sbuf_tensor("qfA", [128, 8 * T], BF16, side="right") as q_f, \
             nc.sbuf_tensor("kfA", [128, 8 * T], BF16, side="right") as k_f, \
             nc.sbuf_tensor("vfA", [128, 16 * 1024], BF16, side="right") as v_tf, \
             ExitStack() as st6, \
             nc.psum_tensor("psScF", [128, 2048], F32) as ps_sc, \
             nc.psum_tensor("psTrF", [128, 512], BF16) as ps_tr, \
             nc.psum_tensor("psAvF", [128, 128], F32) as ps_av:
            A6 = lambda nm, sh, dt: st6.enter_context(nc.sbuf_tensor(nm, list(sh), dt, side="right"))
            e_sb = A6("esbF", (128, 2048), BF16)
            eT_sb = A6("etbF", (128, 2048), BF16)
            mxn = A6("mxnF", (128, 1), F32)
            biasq = A6("biasF", (128, 1), F32)
            ssum = A6("ssumF", (128, 1), F32)
            rrec = A6("rrecF", (128, 1), F32)
            avn_tm = A6("avntmF", (128, 128), BF16)
            stow = A6("stowF", (128, 128), BF16)

            bk = Blk(); y, g, t, v, s = bk.engines()
            for cc in range(8):
                y.dma_start(q_f[:, cc * T:(cc + 1) * T], sp["sp_qf"][cc * 128:(cc + 1) * 128, :]).then_inc(sd, 16)
                y.dma_start(k_f[:, cc * T:(cc + 1) * T], sp["sp_kf"][cc * 128:(cc + 1) * 128, :]).then_inc(sd, 16)
            for kt in range(16):
                y.dma_start(v_tf[:, kt * 1024:(kt + 1) * 1024], sp["sp_vtm"][kt * 128:(kt + 1) * 128, :]).then_inc(sd, 16)
            y.wait(sd)
            bk.run(nc)

            for h in range(8):
                bk = Blk(); y, g, t, v, s = bk.engines()
                for qt in range(16):
                    nkb = qt // 4 + 1
                    t.wait(ss)
                    mmi = None
                    for kb in range(nkb):
                        mmi = t.matmul(ps_sc[:, kb * 512:(kb + 1) * 512],
                                       q_f[:, h * T + qt * 128:h * T + (qt + 1) * 128],
                                       k_f[:, h * T + kb * 512:h * T + (kb + 1) * 512],
                                       start=True, stop=True)
                    mmi.then_inc(sm, 1)
                    v.wait(sm)
                    v.tensor_tensor(ps_sc[:, (nkb - 1) * 512:nkb * 512], ps_sc[:, (nkb - 1) * 512:nkb * 512],
                                    masks[:, (qt % 4) * 512:(qt % 4 + 1) * 512], ALU.add)
                    v.reduce_max(mxn[:], ps_sc[:, 0:nkb * 512], axis=AX.X, negate=True).then_inc(sv, 1)
                    g.wait(sv)
                    g.tensor_scalar(biasq[:], mxn[:], SCALE, None, ALU.mult).then_inc(sg, 1)
                    s.wait(sg)
                    s.wait(sm)
                    s.activation(e_sb[:, 0:nkb * 512], ps_sc[:, 0:nkb * 512], AF.Exp,
                                 bias=biasq[:], scale=SCALE, accum_out=ssum[:]).then_inc(ss, 1)
                    for grp in range(nkb):
                        t.wait(ss)
                        t.wait(sv)
                        tri = None
                        for j in range(4):
                            tri = t.transpose(ps_tr[:, j * 128:(j + 1) * 128],
                                              e_sb[:, grp * 512 + j * 128:grp * 512 + (j + 1) * 128], ident[:])
                        tri.then_inc(sm, 1)
                        v.wait(sm)
                        v.tensor_copy(eT_sb[:, grp * 512:(grp + 1) * 512], ps_tr[:]).then_inc(sv, 1)
                    t.wait(sv)
                    t.wait(ss)
                    mmi = None
                    for kt in range(nkb * 4):
                        mmi = t.matmul(ps_av[:], eT_sb[:, kt * 128:(kt + 1) * 128],
                                       v_tf[:, kt * 1024 + h * 128:kt * 1024 + (h + 1) * 128],
                                       start=(kt == 0), stop=(kt == nkb * 4 - 1))
                    mmi.then_inc(sm, 1)
                    v.wait(ss)
                    v.reciprocal(rrec[:], ssum[:]).then_inc(sv, 1)
                    s.wait(sm); s.wait(sv)
                    s.activation(avn_tm[:], ps_av[:], AF.Copy, scale=rrec[:]).then_inc(ss, 1)
                    t.wait(ss)
                    tri = t.transpose(ps_tr[:, 0:128], avn_tm[:], ident[:])
                    tri.then_inc(sm, 1)
                    v.wait(sm)
                    v.wait(sd)
                    v.tensor_copy(stow[:], ps_tr[:, 0:128]).then_inc(sv, 1)
                    y.wait(sv)
                    y.dma_start(sp["sp_avnT"][qt, 16 + h], stow[:]).then_inc(sd, 16)
                bk.run(nc)

    with nc.sbuf_tensor("woA", [128, 24 * 512], BF16, side="right") as wo_sb, \
         nc.sbuf_tensor("avqA", [128, 2 * 24 * 128], BF16, side="right") as avq, \
         nc.sbuf_tensor("stoA", [128, 2 * 512], BF16, side="right") as sto, \
         nc.sbuf_tensor("stoF", [128, 2 * 512], BF16, side="right") as stoF, \
         nc.psum_tensor("psOA", [128, 512], F32) as ps_o0, \
         nc.psum_tensor("psOB", [128, 512], F32) as ps_o1, \
         nc.psum_tensor("psFA", [128, 512], F32) as ps_f0, \
         nc.psum_tensor("psFB", [128, 512], F32) as ps_f1:
        ps_os = [ps_o0, ps_o1]
        ps_fs = [ps_f0, ps_f1]
        for db in range(4):
            bk = Blk(); y, g, t, v, s = bk.engines()
            g.wait(sm)
            g.dma_start(wo_sb[:].rearrange("p (k c) -> p k c", k=24), ex["wo4"][db]).then_inc(sd, 16)
            for qt in range(16):
                par = qt % 2
                abase = par * 24 * 128
                y.wait(sm)
                for cc in range(24):
                    y.dma_start(avq[:, abase + cc * 128:abase + (cc + 1) * 128],
                                sp["sp_avnT"][qt, cc]).then_inc(sd, 16)
                t.wait(sd)
                t.wait(sv)
                mmi = None
                for cc in range(16):
                    mmi = t.matmul(ps_os[par][:], avq[:, abase + cc * 128:abase + (cc + 1) * 128],
                                   wo_sb[:, cc * 512:(cc + 1) * 512],
                                   start=(cc == 0), stop=(cc == 15))
                mmi.then_inc(sm, 1)
                mmi = None
                for cc in range(16, 24):
                    mmi = t.matmul(ps_fs[par][:], avq[:, abase + cc * 128:abase + (cc + 1) * 128],
                                   wo_sb[:, cc * 512:(cc + 1) * 512],
                                   start=(cc == 16), stop=(cc == 23))
                mmi.then_inc(sm, 1)
                v.wait(sm)
                v.wait(sd)
                v.tensor_copy(stoF[:, par * 512:(par + 1) * 512], ps_fs[par][:]).then_inc(sv, 1)
                v.tensor_copy(sto[:, par * 512:(par + 1) * 512], ps_os[par][:]).then_inc(sv, 1)
                y.wait(sv)
                y.dma_start(partF_out[qt * 128:(qt + 1) * 128, db * 512:(db + 1) * 512],
                            stoF[:, par * 512:(par + 1) * 512]).then_inc(sd, 16)
                y.dma_start(part_out[qt * 128:(qt + 1) * 128, db * 512:(db + 1) * 512],
                            sto[:, par * 512:(par + 1) * 512]).then_inc(sd, 16)
            y.wait(sd)
            bk.run(nc)



def build_stageA():
    nc = bass.Bass("TRN2", target_bir_lowering=False)
    ex = _declare(nc, STAGEA_IN, "ExternalInput")
    part = nc.dram_tensor("part", [T, 2048], BF16, kind="ExternalOutput").ap()
    partF = nc.dram_tensor("partF", [T, 2048], BF16, kind="ExternalOutput").ap()
    sp = {
        "sp_qf": nc.dram_tensor("sp_qf", [8 * 128, T], BF16).ap(),
        "sp_kf": nc.dram_tensor("sp_kf", [8 * 128, T], BF16).ap(),
        "sp_vtm": nc.dram_tensor("sp_vtm", [16 * 128, 1024], BF16).ap(),
        "sp_avnT": nc.dram_tensor("sp_avnT", [16, 24, 128, 128], BF16).ap(),
    }
    with ExitStack() as st:
        emit_stageA(nc, st, ex, part, partF, sp)
    nc.finalize()
    return nc




STAGEB_IN = [
    ("xTr", (D, 1024), F32),
    ("partAr", (1024, 2048), BF16),
    ("partBr", (1024, 2048), BF16),
    ("pfAr", (1024, 2048), BF16),
    ("pfBr", (1024, 2048), BF16),
    ("A_t", (16, 128, 16), F32),
    ("Bmat", (16, 8192), F32),
    ("ln2w_col", (128, 16), F32),
    ("wi4", (16, 128, 16, 512), F32),
    ("wu4", (16, 128, 16, 512), F32),
    ("wd4", (16, 128, 64, 128), F32),
    ("ones_bf", (128, 1), BF16),
    ("ones_f32", (128, 1), F32),
    ("onesrow_f32", (1, 128), F32),
]

TB = 1024
B_UPTO = 99


def emit_stageB(nc, st, ex, x1_out, fused=False, x1bf_out=None):
    A = lambda nm, sh, dt, side="left": st.enter_context(nc.sbuf_tensor(nm, list(sh), dt, side=side))
    onesb = A("onbB", (128, 1), BF16)
    onesf = A("onfB", (128, 1), F32)
    onesrow = A("onrB", (1, 128), F32)
    epsb = A("epsbB", (128, 2), F32)
    ln2w = A("l2wB", (128, 16), F32)
    mean_row = A("mrB", (1, TB), F32)
    rstd_row = A("rstdB", (1, TB), F32)
    srow = A("srowB", (1, TB), F32)
    mod_row = A("modB", (16, TB), F32)
    modp = A("modpB", (16, TB), BF16)
    h_bf = A("hB", (128, 16 * TB), BF16)
    bcA = A("bcAB", (128, TB), F32)
    bcB = A("bcBB", (128, TB), F32)
    bcC = A("bcCB", (128, TB), F32)

    sd = Ct(nc.alloc_semaphore("sdB"))
    sm = Ct(nc.alloc_semaphore("smB"))
    sv = Ct(nc.alloc_semaphore("svB"))
    ss = Ct(nc.alloc_semaphore("ssB"))
    sg = Ct(nc.alloc_semaphore("sgB"))

    bk = Blk(); y, g, t, v, s = bk.engines()
    y.dma_start(onesb[:], ex["ones_bf"][:]).then_inc(sd, 16)
    y.dma_start(onesf[:], ex["ones_f32"][:]).then_inc(sd, 16)
    y.dma_start(onesrow[:], ex["onesrow_f32"][:]).then_inc(sd, 16)
    y.dma_start(ln2w[:], ex["ln2w_col"][:]).then_inc(sd, 16)
    v.memset(epsb[:, 0:1], 1e-5)
    v.memset(epsb[:, 1:2], 1e-6)
    y.wait(sd)
    bk.run(nc)

    with nc.sbuf_tensor("tp0B", [128, TB], BF16, side="right") as tp0, \
         nc.sbuf_tensor("tp1B", [128, TB], BF16, side="right") as tp1, \
         nc.sbuf_tensor("tpFB", [128, TB], BF16, side="right") as tpF, \
         nc.sbuf_tensor("AtB", [128, 16 * 16], BF16, side="right") as A_sb, \
         nc.psum_tensor("psModB", [16, TB], F32) as ps_mod:
        bk = Blk(); y, g, t, v, s = bk.engines()
        for cc in range(16):
            g.dma_start(A_sb[:, cc * 16:(cc + 1) * 16], ex["A_t"][cc]).then_inc(sd, 16)
        for cc in range(16):
            if cc >= 1:
                y.wait(sm)
            if fused:
                y.dma_start(tpF[:], ex["rs_pf"][:, cc * 128:(cc + 1) * 128], transpose=True).then_inc(sd, 16)
                t.wait(sd)
            else:
                y.dma_start(tp0[:], ex["pfAr"][:, cc * 128:(cc + 1) * 128], transpose=True).then_inc(sd, 16)
                y.dma_start(tp1[:], ex["pfBr"][:, cc * 128:(cc + 1) * 128], transpose=True).then_inc(sd, 16)
                v.wait(sd)
                if cc >= 1:
                    v.wait(sm)
                v.tensor_tensor(tpF[:], tp0[:], tp1[:], ALU.add).then_inc(sv, 1)
                t.wait(sv)
            mmi = None
            for half in range(2):
                mmi = t.matmul(ps_mod[:, half * 512:(half + 1) * 512],
                               A_sb[:, cc * 16:(cc + 1) * 16],
                               tpF[:, half * 512:(half + 1) * 512],
                               start=(cc == 0), stop=(cc == 15))
            mmi.then_inc(sm, 1)
        bk.run(nc)
        bk = Blk(); y, g, t, v, s = bk.engines()
        s.activation(mod_row[:], ps_mod[:], AF.Tanh).then_inc(ss, 1)
        g.wait(ss)
        g.tensor_scalar(mod_row[:], mod_row[:], 0.1, None, ALU.mult).then_inc(sg, 1)
        bk.run(nc)

    if B_UPTO <= 1:
        return
    with nc.sbuf_tensor("xldB", [128, TB], F32, side="right") as xld, \
         nc.sbuf_tensor("scrB0", [128, TB], BF16, side="right") as scr0, \
         nc.sbuf_tensor("scrB1", [128, TB], BF16, side="right") as scr1:
        with nc.psum_tensor("psSumB", [1, TB], F32) as ps_sum, \
             nc.psum_tensor("psSqB", [1, TB], F32) as ps_sq:
            bk = Blk(); y, g, t, v, s = bk.engines()
            for cc in range(16):
                if cc >= 1:
                    y.wait(sv); y.wait(ss)
                y.dma_start(xld[:], ex["xTr"][cc * 128:(cc + 1) * 128, :]).then_inc(sd, 16)
                v.wait(sd)
                if cc >= 1:
                    v.wait(sm)
                v.tensor_copy(scr0[:], xld[:]).then_inc(sv, 1)
                s.wait(sd)
                if cc >= 1:
                    s.wait(sm)
                s.activation(scr1[:], xld[:], AF.Square).then_inc(ss, 1)
                t.wait(sv); t.wait(ss)
                for j in range(2):
                    t.matmul(ps_sum[0:1, j * 512:(j + 1) * 512], onesb[:], scr0[:, j * 512:(j + 1) * 512],
                             start=(cc == 0), stop=(cc == 15))
                mmi = None
                for j in range(2):
                    mmi = t.matmul(ps_sq[0:1, j * 512:(j + 1) * 512], onesb[:], scr1[:, j * 512:(j + 1) * 512],
                                   start=(cc == 0), stop=(cc == 15))
                mmi.then_inc(sm, 1)
            bk.run(nc)
            bk = Blk(); y, g, t, v, s = bk.engines()
            v.tensor_scalar(mean_row[:], ps_sum[0:1, :], 1.0 / 2048.0, None, ALU.mult)
            v.tensor_scalar(srow[:], ps_sq[0:1, :], 1.0 / 2048.0, None, ALU.mult).then_inc(sv, 1)
            bk.run(nc)
            bk = Blk(); y, g, t, v, s = bk.engines()
            v.tensor_tensor(ps_sum[0:1, :], mean_row[:], mean_row[:], ALU.mult).then_inc(sv, 1)
            bk.run(nc)
            bk = Blk(); y, g, t, v, s = bk.engines()
            v.tensor_tensor(srow[:], srow[:], ps_sum[0:1, :], ALU.subtract).then_inc(sv, 1)
            s.wait(sv)
            s.activation(ps_sum[0:1, :], srow[:], AF.Sqrt, bias=epsb[0:1, 0:1]).then_inc(ss, 1)
            bk.run(nc)
            bk = Blk(); y, g, t, v, s = bk.engines()
            v.reciprocal(rstd_row[:], ps_sum[0:1, :]).then_inc(sv, 1)
            bk.run(nc)
        with nc.psum_tensor("psBcB", [128, 1024], F32) as ps_bc:
            bk = Blk(); y, g, t, v, s = bk.engines()
            for j in range(2):
                t.wait(sv)
                t.matmul(ps_bc[:, 0:512], onesrow[:], mean_row[0:1, j * 512:(j + 1) * 512], start=True, stop=True)
                t.matmul(ps_bc[:, 512:1024], onesrow[:], rstd_row[0:1, j * 512:(j + 1) * 512], start=True, stop=True).then_inc(sm, 1)
                v.wait(sm)
                v.tensor_copy(bcA[:, j * 512:(j + 1) * 512], ps_bc[:, 0:512])
                v.tensor_copy(bcB[:, j * 512:(j + 1) * 512], ps_bc[:, 512:1024]).then_inc(sv, 1)
            bk.run(nc)

            with nc.psum_tensor("psS2B", [1, TB], F32) as ps_s2:
                bk = Blk(); y, g, t, v, s = bk.engines()
                for cc in range(16):
                    if cc >= 1:
                        y.wait(sg)
                    y.dma_start(xld[:], ex["xTr"][cc * 128:(cc + 1) * 128, :]).then_inc(sd, 16)
                    v.wait(sd)
                    if cc >= 1:
                        v.wait(sm)
                    v.tensor_tensor(xld[:], xld[:], bcA[:], ALU.subtract).then_inc(sv, 1)
                    g.wait(sv)
                    if cc >= 1:
                        g.wait(ss)
                    g.tensor_tensor(scr0[:], xld[:], bcB[:], ALU.mult).then_inc(sg, 1)
                    s.wait(sg)
                    if cc >= 1:
                        s.wait(sm)
                    s.activation(scr1[:], scr0[:], AF.Square).then_inc(ss, 1)
                    t.wait(ss)
                    mmi = None
                    for j in range(2):
                        mmi = t.matmul(ps_s2[0:1, j * 512:(j + 1) * 512], onesb[:], scr1[:, j * 512:(j + 1) * 512],
                                       start=(cc == 0), stop=(cc == 15))
                    mmi.then_inc(sm, 1)
                bk.run(nc)
                bk = Blk(); y, g, t, v, s = bk.engines()
                v.tensor_scalar(srow[:], ps_s2[0:1, :], 1.0 / 2048.0, None, ALU.mult).then_inc(sv, 1)
                s.wait(sv)
                s.activation(ps_s2[0:1, :], srow[:], AF.Sqrt, bias=epsb[0:1, 1:2]).then_inc(ss, 1)
                bk.run(nc)
                bk = Blk(); y, g, t, v, s = bk.engines()
                v.reciprocal(srow[:], ps_s2[0:1, :]).then_inc(sv, 1)
                for j in range(2):
                    t.wait(sv)
                    t.matmul(ps_bc[:, 0:512], onesrow[:], srow[0:1, j * 512:(j + 1) * 512], start=True, stop=True).then_inc(sm, 1)
                    v.wait(sm)
                    v.tensor_copy(bcC[:, j * 512:(j + 1) * 512], ps_bc[:, 0:512]).then_inc(sv, 1)
                bk.run(nc)

            bk = Blk(); y, g, t, v, s = bk.engines()
            for cc in range(16):
                if cc >= 1:
                    y.wait(sg)
                y.dma_start(xld[:], ex["xTr"][cc * 128:(cc + 1) * 128, :]).then_inc(sd, 16)
                v.wait(sd)
                v.tensor_tensor(xld[:], xld[:], bcA[:], ALU.subtract).then_inc(sv, 1)
                g.wait(sv)
                g.tensor_tensor(scr0[:], xld[:], bcB[:], ALU.mult).then_inc(sg, 1)
                v.wait(sg)
                v.tensor_scalar(scr1[:], scr0[:], ln2w[:, cc:cc + 1], None, ALU.mult).then_inc(sv, 1)
                g.wait(sv)
                g.tensor_tensor(h_bf[:, cc * TB:(cc + 1) * TB], scr1[:], bcC[:], ALU.mult).then_inc(sg, 1)
            bk.run(nc)

        if B_UPTO <= 14:
            return
        bk = Blk(); y, g, t, v, s = bk.engines()
        v.tensor_scalar(srow[:], rstd_row[:], 1.0, 1.0, ALU.mult, op1=ALU.add).then_inc(sv, 1)
        g.tensor_tensor(mean_row[:], mean_row[:], rstd_row[:], ALU.mult).then_inc(sg, 1)
        bk.run(nc)
        with nc.psum_tensor("psBcB2", [128, 1024], F32) as ps_bc2:
            bk = Blk(); y, g, t, v, s = bk.engines()
            for j in range(2):
                t.matmul(ps_bc2[:, 0:512], onesrow[:], srow[0:1, j * 512:(j + 1) * 512], start=True, stop=True)
                t.matmul(ps_bc2[:, 512:1024], onesrow[:], mean_row[0:1, j * 512:(j + 1) * 512], start=True, stop=True).then_inc(sm, 1)
                v.wait(sm)
                v.tensor_copy(bcA[:, j * 512:(j + 1) * 512], ps_bc2[:, 0:512])
                v.tensor_copy(bcB[:, j * 512:(j + 1) * 512], ps_bc2[:, 512:1024]).then_inc(sv, 1)
                if j == 0:
                    t.wait(sv)
            bk.run(nc)

    if B_UPTO <= 2:
        return
    with nc.sbuf_tensor("AtB2", [128, 16 * 16], BF16, side="right") as A_sb, \
         nc.psum_tensor("psHaB", [16, TB], F32) as ps_ha:
        bk = Blk(); y, g, t, v, s = bk.engines()
        for cc in range(16):
            g.dma_start(A_sb[:, cc * 16:(cc + 1) * 16], ex["A_t"][cc]).then_inc(sd, 16)
        t.wait(sd)
        mmi = None
        for half in range(2):
            for cc in range(16):
                mmi = t.matmul(ps_ha[:, half * 512:(half + 1) * 512],
                               A_sb[:, cc * 16:(cc + 1) * 16],
                               h_bf[:, cc * TB + half * 512:cc * TB + (half + 1) * 512],
                               start=(cc == 0), stop=(cc == 15))
        mmi.then_inc(sm, 1)
        v.wait(sm)
        v.tensor_tensor(modp[:], mod_row[:], ps_ha[:], ALU.mult).then_inc(sv, 1)
        bk.run(nc)

    if B_UPTO <= 3:
        return
    with nc.sbuf_tensor("fuseB", [128, 64 * 512], BF16, side="right") as fuse, \
         nc.sbuf_tensor("wiB", [128, 16 * 512], BF16, side="right") as wbi, \
         nc.sbuf_tensor("wuB", [128, 16 * 512], BF16, side="right") as wbu, \
         nc.sbuf_tensor("wdB", [128, 64 * 128], BF16, side="right") as wbd, \
         nc.sbuf_tensor("BmB", [16, 512], BF16, side="right") as Bm_sb, \
         nc.sbuf_tensor("gsbB", [128, 512], BF16, side="right") as gate_sb, \
         nc.sbuf_tensor("t0B", [128, 512], F32, side="right") as t0f, \
         nc.sbuf_tensor("t1B", [128, 512], F32, side="right") as t1f, \
         nc.sbuf_tensor("xslB", [128, 512], F32, side="right") as xsl, \
         nc.sbuf_tensor("atB", [128, 2 * 512], BF16, side="right") as atp, \
         nc.sbuf_tensor("stg1B", [128, 512], F32, side="right") as stg1, \
         nc.psum_tensor("psGB", [128, 512], F32) as ps_g, \
         nc.psum_tensor("psUB", [128, 512], F32) as ps_u, \
         nc.psum_tensor("psDB", [128, 512], F32) as ps_d0, \
         nc.psum_tensor("psDB2", [128, 512], F32) as ps_d1:
        ps_ds = [ps_d0, ps_d1]
        for sl in range(2):
            for og in range(16):
                bk = Blk(); y, g, t, v, s = bk.engines()
                g.wait(sm)
                g.dma_start(wbi[:].rearrange("p (k c) -> p k c", k=16), ex["wi4"][og]).then_inc(sd, 16)
                g.dma_start(wbu[:].rearrange("p (k c) -> p k c", k=16), ex["wu4"][og]).then_inc(sd, 16)
                g.dma_start(Bm_sb[:], ex["Bmat"][:, og * 512:(og + 1) * 512]).then_inc(sd, 16)
                t.wait(sd)
                for oc4 in range(4):
                    oc = og * 4 + oc4
                    t.wait(ss)
                    mmi = None
                    for kc in range(16):
                        mmi = t.matmul(ps_g[:], wbi[:, kc * 512 + oc4 * 128:kc * 512 + (oc4 + 1) * 128],
                                       h_bf[:, kc * TB + sl * 512:kc * TB + (sl + 1) * 512],
                                       start=(kc == 0), stop=False)
                    mmi = t.matmul(ps_g[:], Bm_sb[0:16, oc4 * 128:(oc4 + 1) * 128],
                                   modp[0:16, sl * 512:(sl + 1) * 512], start=False, stop=True)
                    mmi.then_inc(sm, 1)
                    t.wait(sv)
                    mmi = None
                    for kc in range(16):
                        mmi = t.matmul(ps_u[:], wbu[:, kc * 512 + oc4 * 128:kc * 512 + (oc4 + 1) * 128],
                                       h_bf[:, kc * TB + sl * 512:kc * TB + (sl + 1) * 512],
                                       start=(kc == 0), stop=(kc == 15))
                    mmi.then_inc(sm, 1)
                    s.wait(sm)
                    s.activation(gate_sb[:], ps_g[:], AF.Silu).then_inc(ss, 1)
                    v.wait(ss)
                    v.tensor_tensor(fuse[:, oc * 512:(oc + 1) * 512], gate_sb[:], ps_u[:], ALU.mult).then_inc(sv, 1)
                bk.run(nc)
            for dc in range(16):
                bk = Blk(); y, g, t, v, s = bk.engines()
                g.wait(sm)
                g.dma_start(wbd[:].rearrange("p (k c) -> p k c", k=64), ex["wd4"][dc]).then_inc(sd, 16)
                y.dma_start(xsl[:], ex["xTr"][dc * 128:(dc + 1) * 128, sl * 512:(sl + 1) * 512]).then_inc(sd, 16)
                if fused:
                    y.dma_start(atp[:, 0:512], ex["rs_part"][sl * 512:(sl + 1) * 512, dc * 128:(dc + 1) * 128],
                                transpose=True).then_inc(sd, 16)
                else:
                    y.dma_start(atp[:, 0:512], ex["partAr"][sl * 512:(sl + 1) * 512, dc * 128:(dc + 1) * 128],
                                transpose=True).then_inc(sd, 16)
                    y.dma_start(atp[:, 512:1024], ex["partBr"][sl * 512:(sl + 1) * 512, dc * 128:(dc + 1) * 128],
                                transpose=True).then_inc(sd, 16)
                par = dc % 2
                t.wait(sd)
                t.wait(sv)
                mmi = None
                for oc in range(64):
                    mmi = t.matmul(ps_ds[par][:], wbd[:, oc * 128:(oc + 1) * 128],
                                   fuse[:, oc * 512:(oc + 1) * 512],
                                   start=(oc == 0), stop=(oc == 63))
                mmi.then_inc(sm, 1)
                v.wait(sm)
                v.tensor_tensor(t0f[:], ps_ds[par][:], atp[:, 0:512], ALU.add).then_inc(sv, 1)
                if not fused:
                    g.wait(sv)
                    g.tensor_tensor(t0f[:], t0f[:], atp[:, 512:1024], ALU.add).then_inc(sg, 1)
                g.wait(sv)
                g.tensor_tensor(t1f[:], xsl[:], bcA[:, sl * 512:(sl + 1) * 512], ALU.mult).then_inc(sg, 1)
                v.wait(sg)
                v.tensor_tensor(t0f[:], t0f[:], t1f[:], ALU.add).then_inc(sv, 1)
                g.wait(sv)
                g.wait(sd)
                g.tensor_tensor(stg1[:], t0f[:], bcB[:, sl * 512:(sl + 1) * 512], ALU.subtract).then_inc(sg, 1)
                y.wait(sg)
                y.dma_start(x1_out[dc * 128:(dc + 1) * 128, sl * 512:(sl + 1) * 512], stg1[:]).then_inc(sd, 16)
                if fused:
                    v.wait(sg)
                    v.wait(sd)
                    v.tensor_copy(atp[:, 512:1024], stg1[:]).then_inc(sv, 1)
                    y.wait(sv)
                    y.dma_start(x1bf_out[dc * 128:(dc + 1) * 128, sl * 512:(sl + 1) * 512],
                                atp[:, 512:1024]).then_inc(sd, 16)
                bk.run(nc)
        bk = Blk(); y, g, t, v, s = bk.engines()
        y.wait(sd)
        bk.run(nc)


def build_stageB():
    nc = bass.Bass("TRN2", target_bir_lowering=False)
    ex = _declare(nc, STAGEB_IN, "ExternalInput")
    x1r = nc.dram_tensor("x1r", [D, 1024], F32, kind="ExternalOutput").ap()
    with ExitStack() as st:
        emit_stageB(nc, st, ex, x1r)
    nc.finalize()
    return nc



STAGEC_IN = [
    ("x1T", (D, T), F32),
    ("w1a", (19, 128, 16, 128), F32),
    ("w1z", (16, 128, 16, 128), F32),
    ("convw", (18, 128, 4), F32),
    ("convb", (18, 128, 1), F32),
    ("Alog", (16, 1), F32),
    ("dtb", (16, 1), F32),
    ("Dvec", (16, 1), F32),
    ("sel16", (16, 16, 128), F32),
    ("sel127", (128, 128), F32),
    ("maskT", (2, 128, 256), F32),
    ("ident_bf", (128, 128), BF16),
    ("ident_f32", (128, 128), F32),
    ("ones_bf", (128, 1), BF16),
    ("ones_f32", (128, 1), F32),
    ("onesrow_f32", (1, 128), F32),
]

TP = 2051


def emit_stageC(nc, st, ex, yg_out, ssq_out, sp, fused=False):
    A = lambda nm, sh, dt, side="left": st.enter_context(nc.sbuf_tensor(nm, list(sh), dt, side=side))
    identb = A("idbC", (128, 128), BF16)
    identf = A("idfC", (128, 128), F32)
    onesb = A("onbC", (128, 1), BF16)
    onesf = A("onfC", (128, 1), F32)
    onesrow = A("onrC", (1, 128), F32)
    epsb = A("epsbC", (128, 4), F32)
    sel16 = A("selC", (16, 16 * 128), F32)
    sel127 = A("s127C", (128, 128), F32)
    maskT = A("mskC", (128, 2 * 256), F32)
    avec = A("avecC", (16, 4), F32)
    DT_b = A("dtbbC", (128, 16), F32)
    r0 = A("r0C", (16, T), F32)
    r1 = A("r1C", (16, T), F32)
    AcsT = A("acstC", (128, 256), F32)
    nAcsT = A("nacstC", (128, 256), F32)
    dtT = A("dttC", (128, 256), F32)
    S_prev = A("sprevC", (64, 16 * 128), F32)
    S_prev_bf = A("sprevbC", (64, 16 * 128), BF16)
    S_tmp = A("stmpC", (64, 128), F32)

    sd = Ct(nc.alloc_semaphore("sdC"))
    sm = Ct(nc.alloc_semaphore("smC"))
    sv = Ct(nc.alloc_semaphore("svC"))
    ss = Ct(nc.alloc_semaphore("ssC"))
    sg = Ct(nc.alloc_semaphore("sgC"))

    bk = Blk(); y, g, t, v, s = bk.engines()
    y.dma_start(identb[:], ex["ident_bf"][:]).then_inc(sd, 16)
    y.dma_start(identf[:], ex["ident_f32"][:]).then_inc(sd, 16)
    y.dma_start(onesb[:], ex["ones_bf"][:]).then_inc(sd, 16)
    y.dma_start(onesf[:], ex["ones_f32"][:]).then_inc(sd, 16)
    y.dma_start(onesrow[:], ex["onesrow_f32"][:]).then_inc(sd, 16)
    for h in range(16):
        y.dma_start(sel16[:, h * 128:(h + 1) * 128], ex["sel16"][h]).then_inc(sd, 16)
    y.dma_start(sel127[:], ex["sel127"][:]).then_inc(sd, 16)
    y.dma_start(maskT[:, 0:256], ex["maskT"][0]).then_inc(sd, 16)
    y.dma_start(maskT[:, 256:512], ex["maskT"][1]).then_inc(sd, 16)
    y.dma_start(avec[:, 0:1], ex["Alog"][:]).then_inc(sd, 16)
    y.dma_start(avec[:, 1:2], ex["dtb"][:]).then_inc(sd, 16)
    y.dma_start(avec[:, 3:4], ex["Dvec"][:]).then_inc(sd, 16)
    v.memset(epsb[:, 0:1], 1e-5)
    v.memset(epsb[:, 1:2], 1.0)
    v.memset(epsb[:, 2:3], 1.1920929e-07)
    y.wait(sd)
    bk.run(nc)

    with nc.psum_tensor("psC0", [128, 32], F32) as ps0:
        bk = Blk(); y, g, t, v, s = bk.engines()
        s.activation(avec[0:16, 2:3], avec[0:16, 0:1], AF.Exp).then_inc(ss, 1)
        g.wait(ss)
        g.tensor_scalar(avec[0:16, 2:3], avec[0:16, 2:3], -1.0, None, ALU.mult).then_inc(sg, 1)
        t.wait(sg)
        t.transpose(ps0[0:1, 0:16], avec[0:16, 3:4], identf[0:16, 0:16]).then_inc(sm, 1)
        v.wait(sm)
        v.tensor_copy(r0[0:1, 0:16], ps0[0:1, 0:16]).then_inc(sv, 1)
        t.wait(sv)
        t.matmul(ps0[:, 16:32], onesrow[:], r0[0:1, 0:16], start=True, stop=True).then_inc(sm, 1)
        v.wait(sm)
        v.tensor_copy(DT_b[:], ps0[:, 16:32]).then_inc(sv, 1)
        bk.run(nc)

    xsbc_stack = ExitStack()
    xs_c = xsbc_stack.enter_context(nc.sbuf_tensor("xscC", [128, 16 * TP], BF16, side="right"))
    Bp = xsbc_stack.enter_context(nc.sbuf_tensor("bpC", [128, TP], BF16, side="right"))
    Cp = xsbc_stack.enter_context(nc.sbuf_tensor("cpC", [128, TP], BF16, side="right"))

    with nc.sbuf_tensor("xn2C", [128, 16 * T], BF16, side="right") as xn2:
        with nc.sbuf_tensor("mrC", [1, T], BF16, side="right") as mean_row, \
             nc.sbuf_tensor("orbC", [1, 128], BF16, side="right") as onesrow_bf, \
             nc.sbuf_tensor("swC", [1, T], F32, side="right") as srow, \
             nc.sbuf_tensor("bcAC", [128, T], BF16, side="right") as bcA, \
             nc.sbuf_tensor("bcBC", [128, T], BF16, side="right") as bcB, \
             nc.sbuf_tensor("xldC", [128, T], BF16 if fused else F32, side="right") as xld:
            with nc.psum_tensor("psSumC", [1, T], F32) as ps_sum, \
                 nc.psum_tensor("psSqC", [1, T], F32) as ps_sq:
                bk = Blk(); y, g, t, v, s = bk.engines()
                g.dma_start(onesrow_bf[:], ex["onesrow_f32"][:]).then_inc(sd, 16)
                for cc in range(16):
                    if cc >= 1:
                        y.wait(sv); y.wait(ss)
                    if fused:
                        y.dma_start(xld[:, 0:1024], ex["ag1"][cc * 128:(cc + 1) * 128, :]).then_inc(sd, 16)
                        y.dma_start(xld[:, 1024:2048], ex["ag1"][2048 + cc * 128:2048 + (cc + 1) * 128, :]).then_inc(sd, 16)
                    else:
                        y.dma_start(xld[:], ex["x1T"][cc * 128:(cc + 1) * 128, :]).then_inc(sd, 16)
                    v.wait(sd)
                    if cc >= 1:
                        v.wait(sm)
                    v.tensor_copy(bcB[:], xld[:]).then_inc(sv, 1)
                    s.wait(sd)
                    if cc >= 1:
                        s.wait(sm)
                    s.activation(xn2[:, cc * T:(cc + 1) * T], xld[:], AF.Square).then_inc(ss, 1)
                    t.wait(sv); t.wait(ss)
                    for j in range(4):
                        t.matmul(ps_sum[0:1, j * 512:(j + 1) * 512], onesb[:], bcB[:, j * 512:(j + 1) * 512],
                                 start=(cc == 0), stop=(cc == 15))
                    mmi = None
                    for j in range(4):
                        mmi = t.matmul(ps_sq[0:1, j * 512:(j + 1) * 512], onesb[:],
                                       xn2[:, cc * T + j * 512:cc * T + (j + 1) * 512],
                                       start=(cc == 0), stop=(cc == 15))
                    mmi.then_inc(sm, 1)
                bk.run(nc)
                bk = Blk(); y, g, t, v, s = bk.engines()
                v.tensor_scalar(mean_row[:], ps_sum[0:1, 0:T], 1.0 / 2048.0, None, ALU.mult)
                v.tensor_scalar(srow[:], ps_sq[0:1, 0:T], 1.0 / 2048.0, None, ALU.mult).then_inc(sv, 1)
                bk.run(nc)
                bk = Blk(); y, g, t, v, s = bk.engines()
                v.tensor_tensor(ps_sum[0:1, 0:T], mean_row[:], mean_row[:], ALU.mult).then_inc(sv, 1)
                bk.run(nc)
                bk = Blk(); y, g, t, v, s = bk.engines()
                v.tensor_tensor(srow[:], srow[:], ps_sum[0:1, 0:T], ALU.subtract).then_inc(sv, 1)
                s.wait(sv)
                s.activation(ps_sum[0:1, 0:T], srow[:], AF.Sqrt, bias=epsb[0:1, 0:1]).then_inc(ss, 1)
                bk.run(nc)
                bk = Blk(); y, g, t, v, s = bk.engines()
                v.reciprocal(srow[:], ps_sum[0:1, 0:T]).then_inc(sv, 1)
                bk.run(nc)
            with nc.psum_tensor("psBcC", [128, 1024], F32) as ps_bc:
                bk = Blk(); y, g, t, v, s = bk.engines()
                for j in range(4):
                    t.wait(sv)
                    t.matmul(ps_bc[:, 0:512], onesrow_bf[:], mean_row[0:1, j * 512:(j + 1) * 512], start=True, stop=True)
                    t.matmul(ps_bc[:, 512:1024], onesrow[:], srow[0:1, j * 512:(j + 1) * 512], start=True, stop=True).then_inc(sm, 1)
                    v.wait(sm)
                    v.tensor_copy(bcA[:, j * 512:(j + 1) * 512], ps_bc[:, 0:512])
                    v.tensor_copy(bcB[:, j * 512:(j + 1) * 512], ps_bc[:, 512:1024]).then_inc(sv, 1)
                bk.run(nc)
                bk = Blk(); y, g, t, v, s = bk.engines()
                for cc in range(16):
                    if cc >= 1:
                        y.wait(sg)
                    if fused:
                        y.dma_start(xld[:, 0:1024], ex["ag1"][cc * 128:(cc + 1) * 128, :]).then_inc(sd, 16)
                        y.dma_start(xld[:, 1024:2048], ex["ag1"][2048 + cc * 128:2048 + (cc + 1) * 128, :]).then_inc(sd, 16)
                    else:
                        y.dma_start(xld[:], ex["x1T"][cc * 128:(cc + 1) * 128, :]).then_inc(sd, 16)
                    v.wait(sd)
                    v.tensor_tensor(xld[:], xld[:], bcA[:], ALU.subtract).then_inc(sv, 1)
                    g.wait(sv)
                    g.tensor_tensor(xn2[:, cc * T:(cc + 1) * T], xld[:], bcB[:], ALU.mult).then_inc(sg, 1)
                bk.run(nc)

        with ExitStack() as st3, \
             nc.psum_tensor("psqC0", [128, 512], F32) as pq0, \
             nc.psum_tensor("psqC1", [128, 512], F32) as pq1:
            A3 = lambda nm, sh, dt: st3.enter_context(nc.sbuf_tensor(nm, list(sh), dt, side="right"))
            wbuf = A3("wbC", (128, 16 * 128), BF16)
            cwb = A3("cwbC", (128, 4), F32)
            cbb = A3("cbbC", (128, 1), F32)
            t1f = A3("t1fC", (128, T), F32)
            t2f = A3("t2fC", (128, T), F32)
            stq = A3("stqC", (128, T), BF16)

            for oc in range(19):
                bk = Blk(); engs = bk.engines(); y, g, t, v, s = engs
                g.wait(sm)
                g.dma_start(wbuf[:].rearrange("p (k c) -> p k c", k=16), ex["w1a"][oc]).then_inc(sd, 16)
                if oc < 18:
                    if oc >= 1:
                        y.wait(ss)
                    y.dma_start(cwb[:], ex["convw"][oc]).then_inc(sd, 16)
                    y.dma_start(cbb[:], ex["convb"][oc]).then_inc(sd, 16)
                if oc < 16:
                    dst = xs_c[:, oc * TP:(oc + 1) * TP]
                elif oc == 16:
                    dst = Bp[:]
                elif oc == 17:
                    dst = Cp[:]
                t.wait(sd)
                for tb in range(4):
                    par = tb % 2
                    t.wait(sv)
                    mmi = None
                    for kc in range(16):
                        mmi = t.matmul([pq0, pq1][par][:], wbuf[:, kc * 128:(kc + 1) * 128],
                                       xn2[:, kc * T + tb * 512:kc * T + (tb + 1) * 512],
                                       start=(kc == 0), stop=(kc == 15))
                    mmi.then_inc(sm, 1)
                    v.wait(sm)
                    if oc < 18:
                        v.tensor_copy(dst[:, 3 + tb * 512:3 + (tb + 1) * 512], [pq0, pq1][par][:]).then_inc(sv, 1)
                    else:
                        v.tensor_copy(r0[0:16, tb * 512:(tb + 1) * 512], [pq0, pq1][par][0:16, :]).then_inc(sv, 1)
                if oc < 18:
                    v.wait(sv)
                    v.wait(sd)
                    v.memset(dst[:, 0:3], 0.0)
                    v.tensor_scalar(t1f[:], dst[:, 0:T], cwb[:, 0:1], None, ALU.mult).then_inc(sv, 1)
                    g.wait(sv)
                    g.tensor_scalar(t2f[:], dst[:, 1:1 + T], cwb[:, 1:2], None, ALU.mult).then_inc(sg, 1)
                    v.wait(sg)
                    v.tensor_tensor(t1f[:], t1f[:], t2f[:], ALU.add).then_inc(sv, 1)
                    g.wait(sv)
                    g.tensor_scalar(t2f[:], dst[:, 2:2 + T], cwb[:, 2:3], None, ALU.mult).then_inc(sg, 1)
                    v.wait(sg)
                    v.tensor_tensor(t1f[:], t1f[:], t2f[:], ALU.add).then_inc(sv, 1)
                    g.wait(sv)
                    g.tensor_scalar(t2f[:], dst[:, 3:3 + T], cwb[:, 3:4], None, ALU.mult).then_inc(sg, 1)
                    v.wait(sg)
                    v.tensor_tensor(t1f[:], t1f[:], t2f[:], ALU.add).then_inc(sv, 1)
                    s.wait(sv)
                    s.activation(dst[:, 3:3 + T], t1f[:], AF.Silu, bias=cbb[:]).then_inc(ss, 1)
                    v.wait(ss)
                bk.run(nc)

            for oc in range(16):
                bk = Blk(); engs = bk.engines(); y, g, t, v, s = engs
                g.wait(sm)
                g.dma_start(wbuf[:].rearrange("p (k c) -> p k c", k=16), ex["w1z"][oc]).then_inc(sd, 16)
                t.wait(sd)
                for tb in range(4):
                    par = tb % 2
                    t.wait(sv)
                    mmi = None
                    for kc in range(16):
                        mmi = t.matmul([pq0, pq1][par][:], wbuf[:, kc * 128:(kc + 1) * 128],
                                       xn2[:, kc * T + tb * 512:kc * T + (tb + 1) * 512],
                                       start=(kc == 0), stop=(kc == 15))
                    mmi.then_inc(sm, 1)
                    v.wait(sm)
                    if tb == 0:
                        v.wait(sd)
                    v.tensor_copy(stq[:, tb * 512:(tb + 1) * 512], [pq0, pq1][par][:]).then_inc(sv, 1)
                y.wait(sv)
                y.dma_start(sp["sp_z"][oc * 128:(oc + 1) * 128, :], stq[:]).then_inc(sd, 16)
                bk.run(nc)
            bk = Blk(); y, g, t, v, s = bk.engines()
            y.wait(sd)
            bk.run(nc)

    with nc.sbuf_tensor("r2C", [16, T], F32, side="right") as r2, \
         nc.sbuf_tensor("r3C", [16, T], F32, side="right") as r3:
        bk = Blk(); y, g, t, v, s = bk.engines()
        v.tensor_scalar(r1[0:16, :], r0[0:16, :], avec[0:16, 1:2], None, ALU.add).then_inc(sv, 1)
        g.wait(sv)
        g.tensor_scalar(r2[0:16, :], r1[0:16, :], 0.0, None, ALU.max).then_inc(sg, 1)
        v.wait(sg)
        v.tensor_scalar(r3[0:16, :], r2[0:16, :], 2.0, None, ALU.mult).then_inc(sv, 1)
        g.wait(sv)
        g.tensor_tensor(r3[0:16, :], r1[0:16, :], r3[0:16, :], ALU.subtract).then_inc(sg, 1)
        s.wait(sg)
        s.activation(r3[0:16, :], r3[0:16, :], AF.Exp).then_inc(ss, 1)
        bk.run(nc)
        bk = Blk(); y, g, t, v, s = bk.engines()
        s.activation(r3[0:16, :], r3[0:16, :], AF.Ln, bias=epsb[0:16, 1:2]).then_inc(ss, 1)
        bk.run(nc)
        bk = Blk(); y, g, t, v, s = bk.engines()
        v.tensor_tensor(r0[0:16, :], r2[0:16, :], r3[0:16, :], ALU.add).then_inc(sv, 1)
        g.wait(sv)
        g.tensor_scalar(r1[0:16, :], r0[0:16, :], avec[0:16, 2:3], None, ALU.mult).then_inc(sg, 1)
        bk.run(nc)
        src, dst_ = r1, r2
        for si, sh in enumerate((1, 2, 4, 8, 16, 32, 64, 128)):
            bk = Blk(); y, g, t, v, s = bk.engines()
            for c in range(8):
                base = c * 256
                v.tensor_tensor(dst_[0:16, base + sh:base + 256], src[0:16, base + sh:base + 256],
                                src[0:16, base:base + 256 - sh], ALU.add)
                v.tensor_copy(dst_[0:16, base:base + sh], src[0:16, base:base + sh])
            bk.run(nc)
            src, dst_ = dst_, src
        Acs = src

        with nc.psum_tensor("psTrC", [128, 64], F32) as ps_trf:
            bk = Blk(); y, g, t, v, s = bk.engines()
            for ts_ in range(16):
                if ts_ >= 1:
                    t.wait(sv)
                t.transpose(ps_trf[:, 0:16], Acs[0:16, ts_ * 128:(ts_ + 1) * 128], identf[0:16, 0:16])
                tri = t.transpose(ps_trf[:, 16:32], r0[0:16, ts_ * 128:(ts_ + 1) * 128], identf[0:16, 0:16])
                tri.then_inc(sm, 1)
                v.wait(sm)
                v.tensor_copy(AcsT[:, ts_ * 16:(ts_ + 1) * 16], ps_trf[:, 0:16])
                v.tensor_copy(dtT[:, ts_ * 16:(ts_ + 1) * 16], ps_trf[:, 16:32]).then_inc(sv, 1)
            bk.run(nc)
        bk = Blk(); y, g, t, v, s = bk.engines()
        v.tensor_scalar(nAcsT[:], AcsT[:], -1.0, None, ALU.mult).then_inc(sv, 1)
        bk.run(nc)



    y_buf = A("ybufC", (128, 16 * T), BF16)
    bk = Blk(); y, g, t, v, s = bk.engines()
    v.memset(S_prev[:], 0.0)
    v.memset(S_prev_bf[:], 0.0)
    bk.run(nc)

    with ExitStack() as st4, \
         nc.psum_tensor("psLC", [128, 512], F32) as ps_L, \
         nc.psum_tensor("psYC", [128, 256], F32) as ps_Y, \
         nc.psum_tensor("psYoC", [128, 256], F32) as ps_Yo, \
         nc.psum_tensor("psSC", [64, 128], F32) as ps_S, \
         nc.psum_tensor("psTbC", [128, 256], BF16) as ps_tb, \
         nc.psum_tensor("psMsC", [128, 64], F32) as ps_ms:
        A4 = lambda nm, sh, dt: st4.enter_context(nc.sbuf_tensor(nm, list(sh), dt, side="right"))
        B_tm = A4("btmC", (128, 2 * 64), BF16)
        CBt = A4("cbtC", (128, 512), BF16)
        L_sb = A4("lsbC", (128, 512), BF16)
        d_T = A4("dtTC2", (128, 512), BF16)
        xh_tm = A4("xhC", (128, 256), BF16)
        xd_tm = A4("xdC", (128, 256), BF16)
        xdd_tm = A4("xddC", (128, 256), BF16)
        decT = A4("decC", (128, 32), F32)
        cdec = A4("cdecC", (128, 16), F32)
        expT = A4("expTC", (128, 32), F32)
        yos = A4("yosC", (128, 256), F32)
        ycm = A4("ycmC", (128, 256), F32)
        xDq = A4("xdqC", (128, 256), BF16)
        st_sb = A4("stsbC", (64, 128), F32)

        for c in range(8):
            base = 3 + c * 256
            bk = Blk(); y, g, t, v, s = bk.engines()
            t.wait(sv); t.wait(ss)
            tri = None
            for kt in range(2):
                tri = t.transpose(ps_tb[:, kt * 64:(kt + 1) * 64],
                                  Bp[0:64, base + kt * 128:base + (kt + 1) * 128], identb[0:64, 0:64])
            tri.then_inc(sm, 1)
            v.wait(sm)
            v.tensor_copy(B_tm[:], ps_tb[:, 0:128]).then_inc(sv, 1)
            t.wait(sv)
            mmi = None
            for kt in range(2):
                mmi = t.matmul(ps_L[:, kt * 256:(kt + 1) * 256], Bp[0:64, base + kt * 128:base + (kt + 1) * 128],
                               Cp[0:64, base:base + 256], start=True, stop=True)
            mmi.then_inc(sm, 1)
            v.wait(sm)
            v.tensor_copy(CBt[:], ps_L[:]).then_inc(sv, 1)
            t.wait(sv)
            t.matmul(ps_ms[:, 0:16], sel127[:], AcsT[:, (2 * c + 1) * 16:(2 * c + 2) * 16],
                     start=True, stop=True).then_inc(sm, 1)
            v.wait(sm)
            for kt in range(2):
                v.tensor_tensor(decT[:, kt * 16:(kt + 1) * 16], ps_ms[:, 0:16],
                                AcsT[:, (2 * c + kt) * 16:(2 * c + kt + 1) * 16], ALU.subtract)
            v.tensor_copy(cdec[:], ps_ms[:, 0:16]).then_inc(sv, 1)
            s.wait(sv)
            s.activation(decT[:], decT[:], AF.Exp)
            s.activation(cdec[:], cdec[:], AF.Exp)
            for kt in range(2):
                s.activation(expT[:, kt * 16:(kt + 1) * 16], AcsT[:, (2 * c + kt) * 16:(2 * c + kt + 1) * 16],
                             AF.Exp)
            bk.run(nc)

            for h in range(16):
                bk = Blk(); y, g, t, v, s = bk.engines()
                t.wait(ss)
                mmi = None
                for kt in range(2):
                    t.matmul(ps_L[:, kt * 256:(kt + 1) * 256], sel16[:, h * 128:(h + 1) * 128],
                             Acs[0:16, c * 256:(c + 1) * 256], start=True, stop=False)
                    mmi = t.matmul(ps_L[:, kt * 256:(kt + 1) * 256], identf[:],
                                   maskT[:, kt * 256:(kt + 1) * 256], start=False, stop=True)
                mmi.then_inc(sm, 1)
                s.wait(sm)
                for kt in range(2):
                    s.activation(L_sb[:, kt * 256:(kt + 1) * 256], ps_L[:, kt * 256:(kt + 1) * 256], AF.Exp,
                                 bias=nAcsT[:, (2 * c + kt) * 16 + h:(2 * c + kt) * 16 + h + 1])
                s.activation(L_sb[0:1, 0:1], L_sb[0:1, 0:1], AF.Copy).then_inc(ss, 1)
                v.wait(ss)
                v.tensor_tensor(d_T[:], L_sb[:], CBt[:], ALU.mult).then_inc(sv, 1)
                t.wait(sg); t.wait(sv)
                tri = None
                for kt in range(2):
                    tri = t.transpose(ps_tb[:, kt * 128:(kt + 1) * 128],
                                      xs_c[:, h * TP + base + kt * 128:h * TP + base + (kt + 1) * 128], identb[:])
                tri.then_inc(sm, 1)
                v.wait(sm)
                v.tensor_copy(xh_tm[:], ps_tb[:, 0:256]).then_inc(sv, 1)
                g.wait(sv)
                for kt in range(2):
                    g.tensor_scalar(xd_tm[:, kt * 128:(kt + 1) * 128], xh_tm[:, kt * 128:(kt + 1) * 128],
                                    dtT[:, (2 * c + kt) * 16 + h:(2 * c + kt) * 16 + h + 1], None, ALU.mult)
                bk.run(nc)
                bk = Blk(); y, g, t, v, s = bk.engines()
                for kt in range(2):
                    v.tensor_scalar(xdd_tm[:, kt * 128:(kt + 1) * 128], xd_tm[:, kt * 128:(kt + 1) * 128],
                                    decT[:, kt * 16 + h:kt * 16 + h + 1], None, ALU.mult)
                bk.run(nc)
                bk = Blk(); y, g, t, v, s = bk.engines()
                mmi = None
                for qt in range(2):
                    for kt in range(qt + 1):
                        mmi = t.matmul(ps_Y[:, qt * 128:(qt + 1) * 128],
                                       d_T[:, kt * 256 + qt * 128:kt * 256 + (qt + 1) * 128],
                                       xd_tm[:, kt * 128:(kt + 1) * 128],
                                       start=(kt == 0), stop=(kt == qt))
                mmi.then_inc(sm, 1)
                mmi = None
                for kt in range(2):
                    mmi = t.matmul(ps_S[:], B_tm[:, kt * 64:(kt + 1) * 64], xdd_tm[:, kt * 128:(kt + 1) * 128],
                                   start=(kt == 0), stop=(kt == 1))
                mmi.then_inc(sm, 1)
                mmi = None
                for qt in range(2):
                    mmi = t.matmul(ps_Yo[:, qt * 128:(qt + 1) * 128],
                                   Cp[0:64, base + qt * 128:base + (qt + 1) * 128],
                                   S_prev_bf[:, h * 128:(h + 1) * 128], start=True, stop=True)
                mmi.then_inc(sm, 1)
                s.wait(sm)
                for qt in range(2):
                    s.activation(yos[:, qt * 128:(qt + 1) * 128], ps_Yo[:, qt * 128:(qt + 1) * 128], AF.Copy,
                                 scale=expT[:, qt * 16 + h:qt * 16 + h + 1])
                s.activation(yos[0:1, 0:1], yos[0:1, 0:1], AF.Copy).then_inc(ss, 1)
                v.wait(ss)
                v.tensor_tensor(ycm[:], ps_Y[:], yos[:], ALU.add).then_inc(sv, 1)
                g.wait(sv)
                for kt in range(2):
                    g.tensor_scalar(xDq[:, kt * 128:(kt + 1) * 128], xh_tm[:, kt * 128:(kt + 1) * 128],
                                    DT_b[:, h:h + 1], None, ALU.mult)
                g.tensor_scalar(xDq[0:1, 0:1], xDq[0:1, 0:1], 1.0, None, ALU.mult).then_inc(sg, 1)
                v.wait(sg)
                for qt in range(2):
                    v.tensor_tensor(y_buf[:, (2 * c + qt) * T + h * 128:(2 * c + qt) * T + (h + 1) * 128],
                                    ycm[:, qt * 128:(qt + 1) * 128], xDq[:, qt * 128:(qt + 1) * 128], ALU.add)
                v.tensor_copy(st_sb[:], ps_S[:]).then_inc(sv, 1)
                g.wait(sv)
                g.tensor_scalar(S_tmp[:], S_prev[:, h * 128:(h + 1) * 128], cdec[0:64, h:h + 1], None, ALU.mult).then_inc(sg, 1)
                v.wait(sg)
                v.tensor_tensor(S_prev[:, h * 128:(h + 1) * 128], S_tmp[:], st_sb[:], ALU.add)
                v.tensor_copy(S_prev_bf[:, h * 128:(h + 1) * 128], S_prev[:, h * 128:(h + 1) * 128]).then_inc(sv, 1)
                bk.run(nc)

    xsbc_stack.close()

    with nc.sbuf_tensor("ssqC", [128, 16], F32, side="right") as ssq_cols, \
         nc.sbuf_tensor("scr2C", [128, T], BF16, side="right") as scr2, \
         nc.sbuf_tensor("ssqoC", [16, 128], F32, side="right") as ssq_o, \
         nc.psum_tensor("psTr2C", [16, 128], F32) as ps_tr2:
        bk = Blk(); y, g, t, v, s = bk.engines()
        for tt in range(16):
            s.activation(scr2[:], y_buf[:, tt * T:(tt + 1) * T], AF.Square,
                         accum_out=ssq_cols[:, tt:tt + 1]).then_inc(ss, 1)
        t.wait(ss)
        t.transpose(ps_tr2[:], ssq_cols[:], identf[:]).then_inc(sm, 1)
        v.wait(sm)
        v.tensor_copy(ssq_o[:], ps_tr2[:]).then_inc(sv, 1)
        y.wait(sv)
        y.dma_start(ssq_out[:, :], ssq_o[:]).then_inc(sd, 16)
        bk.run(nc)

    if fused:
        ccq = Ct(nc.alloc_semaphore("ccqC"))
        bk = Blk(); y, g, t, v, s = bk.engines()
        g.wait(sd)
        g.collective_compute("AllGather", ALU.bypass, replica_groups=PAIRS,
                             ins=[ex["cc_ssq_in"]], outs=[ex["cc_ssq_out"]]).then_inc_default(ccq)
        g.wait(ccq, 1)
        bk.run(nc)
        rst_cols = st.enter_context(nc.sbuf_tensor("rstcC", [128, 16], F32, side="right"))
        with nc.sbuf_tensor("rstC", [16, 256], F32, side="right") as rst, \
             nc.psum_tensor("psRstC", [128, 16], F32) as ps_rst:
            bk = Blk(); y, g, t, v, s = bk.engines()
            y.dma_start(rst[0:16, 0:128], ex["ag3"][0:16, :]).then_inc(sd, 16)
            y.dma_start(rst[0:16, 128:256], ex["ag3"][16:32, :]).then_inc(sd, 16)
            v.wait(sd)
            v.tensor_tensor(rst[0:16, 0:128], rst[0:16, 0:128], rst[0:16, 128:256], ALU.add)
            v.tensor_scalar(rst[0:16, 0:128], rst[0:16, 0:128], 1.0 / 4096.0, None, ALU.mult).then_inc(sv, 1)
            s.wait(sv)
            s.activation(rst[0:16, 128:256], rst[0:16, 0:128], AF.Sqrt, bias=epsb[0:16, 2:3]).then_inc(ss, 1)
            v.wait(ss)
            v.reciprocal(rst[0:16, 0:128], rst[0:16, 128:256]).then_inc(sv, 1)
            t.wait(sv)
            t.transpose(ps_rst[:], rst[0:16, 0:128], identf[0:16, 0:16]).then_inc(sm, 1)
            v.wait(sm)
            v.tensor_copy(rst_cols[:], ps_rst[:]).then_inc(sv, 1)
            bk.run(nc)

        with nc.sbuf_tensor("ztmC", [128, 16 * T], BF16, side="right") as z_tm, \
             nc.sbuf_tensor("zldC", [128, T], BF16, side="right") as zld, \
             nc.sbuf_tensor("zslC", [128, T], BF16, side="right") as zsil, \
             nc.sbuf_tensor("stg2C", [128, T], BF16, side="right") as stg2, \
             nc.psum_tensor("psTr3C", [128, 512], BF16) as ps_tr3:
            for cc in range(16):
                bk = Blk(); y, g, t, v, s = bk.engines()
                y.dma_start(zld[:], sp["sp_z"][cc * 128:(cc + 1) * 128, :]).then_inc(sd, 16)
                t.wait(sd)
                for tg in range(4):
                    if tg > 0 or cc > 0:
                        t.wait(sv)
                    tri = None
                    for j2 in range(4):
                        tri = t.transpose(ps_tr3[:, j2 * 128:(j2 + 1) * 128],
                                          zld[:, (tg * 4 + j2) * 128:(tg * 4 + j2 + 1) * 128], identb[:])
                    tri.then_inc(sm, 1)
                    v.wait(sm)
                    cpi = None
                    for j2 in range(4):
                        tt = tg * 4 + j2
                        cpi = v.tensor_copy(z_tm[:, tt * T + cc * 128:tt * T + (cc + 1) * 128],
                                            ps_tr3[:, j2 * 128:(j2 + 1) * 128])
                    cpi.then_inc(sv, 1)
                bk.run(nc)
            for tt in range(16):
                bk = Blk(); y, g, t, v, s = bk.engines()
                s.activation(zsil[:], z_tm[:, tt * T:(tt + 1) * T], AF.Silu).then_inc(ss, 1)
                v.tensor_scalar(stg2[:], y_buf[:, tt * T:(tt + 1) * T], rst_cols[:, tt:tt + 1], None, ALU.mult).then_inc(sv, 1)
                g.wait(ss); g.wait(sv)
                g.tensor_tensor(y_buf[:, tt * T:(tt + 1) * T], stg2[:], zsil[:], ALU.mult).then_inc(sg, 1)
                bk.run(nc)

        with nc.sbuf_tensor("yfmC", [128, 16 * T], BF16, side="right") as y_fm, \
             nc.psum_tensor("psTr4C", [128, 512], BF16) as ps_tr4:
            for tt in range(16):
                bk = Blk(); y, g, t, v, s = bk.engines()
                for cg in range(4):
                    if tt > 0 or cg > 0:
                        t.wait(sv)
                    tri = None
                    for j2 in range(4):
                        cc = cg * 4 + j2
                        tri = t.transpose(ps_tr4[:, j2 * 128:(j2 + 1) * 128],
                                          y_buf[:, tt * T + cc * 128:tt * T + (cc + 1) * 128], identb[:])
                    tri.then_inc(sm, 1)
                    v.wait(sm)
                    cpi = None
                    for j2 in range(4):
                        cc = cg * 4 + j2
                        cpi = v.tensor_copy(y_fm[:, cc * T + tt * 128:cc * T + (tt + 1) * 128],
                                            ps_tr4[:, j2 * 128:(j2 + 1) * 128])
                    cpi.then_inc(sv, 1)
                bk.run(nc)
        with nc.sbuf_tensor("woC", [128, 16 * 512], BF16, side="right") as wo_sb, \
             nc.sbuf_tensor("stoC", [128, 2 * 512], BF16, side="right") as sto, \
             nc.psum_tensor("psOC", [128, 512], F32) as ps_o0, \
             nc.psum_tensor("psOC2", [128, 512], F32) as ps_o1:
            ps_os = [ps_o0, ps_o1]
            for db in range(4):
                bk = Blk(); y, g, t, v, s = bk.engines()
                g.wait(sm)
                g.dma_start(wo_sb[:].rearrange("p (k c) -> p k c", k=16), ex["wotC"][db]).then_inc(sd, 16)
                t.wait(sd)
                for qt in range(16):
                    par = qt % 2
                    t.wait(sv)
                    mmi = None
                    for cc in range(16):
                        mmi = t.matmul(ps_os[par][:], y_fm[:, cc * T + qt * 128:cc * T + (qt + 1) * 128],
                                       wo_sb[:, cc * 512:(cc + 1) * 512],
                                       start=(cc == 0), stop=(cc == 15))
                    mmi.then_inc(sm, 1)
                    v.wait(sm)
                    v.wait(sd)
                    v.tensor_copy(sto[:, par * 512:(par + 1) * 512], ps_os[par][:]).then_inc(sv, 1)
                    y.wait(sv)
                    y.dma_start(ex["rs2in"][qt * 128:(qt + 1) * 128, db * 512:(db + 1) * 512],
                                sto[:, par * 512:(par + 1) * 512]).then_inc(sd, 16)
                y.wait(sd)
                bk.run(nc)
    else:
        with nc.sbuf_tensor("ztmC", [128, 16 * T], BF16, side="right") as z_tm, \
             nc.sbuf_tensor("zldC", [128, T], BF16, side="right") as zld, \
             nc.sbuf_tensor("zslC", [128, T], BF16, side="right") as zsil, \
             nc.sbuf_tensor("stg2C", [128, T], BF16, side="right") as stg2, \
             nc.psum_tensor("psTr3C", [128, 512], BF16) as ps_tr3:
            for cc in range(16):
                bk = Blk(); y, g, t, v, s = bk.engines()
                y.dma_start(zld[:], sp["sp_z"][cc * 128:(cc + 1) * 128, :]).then_inc(sd, 16)
                t.wait(sd)
                for tg in range(4):
                    if tg > 0 or cc > 0:
                        t.wait(sv)
                    tri = None
                    for j in range(4):
                        tri = t.transpose(ps_tr3[:, j * 128:(j + 1) * 128],
                                          zld[:, (tg * 4 + j) * 128:(tg * 4 + j + 1) * 128], identb[:])
                    tri.then_inc(sm, 1)
                    v.wait(sm)
                    cpi = None
                    for j in range(4):
                        tt = tg * 4 + j
                        cpi = v.tensor_copy(z_tm[:, tt * T + cc * 128:tt * T + (cc + 1) * 128],
                                            ps_tr3[:, j * 128:(j + 1) * 128])
                    cpi.then_inc(sv, 1)
                bk.run(nc)
            for tt in range(16):
                bk = Blk(); y, g, t, v, s = bk.engines()
                s.activation(zsil[:], z_tm[:, tt * T:(tt + 1) * T], AF.Silu).then_inc(ss, 1)
                v.wait(ss)
                v.wait(sd)
                v.tensor_tensor(stg2[:], y_buf[:, tt * T:(tt + 1) * T], zsil[:], ALU.mult).then_inc(sv, 1)
                y.wait(sv)
                y.dma_start(yg_out[tt * 128:(tt + 1) * 128, :], stg2[:]).then_inc(sd, 16)
                bk.run(nc)
            bk = Blk(); y, g, t, v, s = bk.engines()
            y.wait(sd)
            bk.run(nc)



def build_stageC():
    nc = bass.Bass("TRN2", target_bir_lowering=False)
    ex = _declare(nc, STAGEC_IN, "ExternalInput")
    yg = nc.dram_tensor("ygated", [T, 2048], BF16, kind="ExternalOutput").ap()
    ssq = nc.dram_tensor("ssqout", [16, 128], F32, kind="ExternalOutput").ap()
    sp = {"sp_z": nc.dram_tensor("sp_z", [16 * 128, T], BF16).ap()}
    with ExitStack() as st:
        emit_stageC(nc, st, ex, yg, ssq, sp)
    nc.finalize()
    return nc



STAGED_IN = [
    ("x1rD", (D, 1024), F32),
    ("yq0", (1024, 2048), BF16),
    ("yq1", (1024, 2048), BF16),
    ("ssqa", (8, 128), F32),
    ("ssqb", (8, 128), F32),
    ("wot", (16, 128, 32, 128), F32),
    ("wg8", (16, 128, 16, 512), I8),
    ("wu8", (16, 128, 16, 512), I8),
    ("wd8", (16, 128, 64, 128), I8),
    ("sgt", (128, 64), F32),
    ("sut", (128, 64), F32),
    ("sdt", (128, 16), F32),
    ("sel8", (8, 8, 128), F32),
    ("ident_f32", (128, 128), F32),
    ("ones_bf", (128, 1), BF16),
    ("onesrow_f32", (1, 128), F32),
]


def emit_stageD(nc, st, ex, out_r, fused=False):
    A = lambda nm, sh, dt, side="left": st.enter_context(nc.sbuf_tensor(nm, list(sh), dt, side=side))
    identf = A("idfD", (128, 128), F32)
    onesb = A("onbD", (128, 1), BF16)
    onesrow = A("onrD", (1, 128), F32)
    epsb = A("epsbD", (128, 2), F32)
    sgt = A("sgtD", (128, 64), F32)
    sut = A("sutD", (128, 64), F32)
    sdt = A("sdtD", (128, 16), F32)
    ssq2 = A("ssq2D", (8, 256), F32)
    out_acc = A("oaccD", (128, 16 * 1024), F32)
    if not fused:
        d2_stack = ExitStack()
        sel8 = d2_stack.enter_context(nc.sbuf_tensor("sel8D", [8, 8 * 128], F32, side="right"))
        rstd_b = d2_stack.enter_context(nc.sbuf_tensor("rstdbD", [128, 1024], F32, side="right"))

    sd = Ct(nc.alloc_semaphore("sdD"))
    sm = Ct(nc.alloc_semaphore("smD"))
    sv = Ct(nc.alloc_semaphore("svD"))
    ss = Ct(nc.alloc_semaphore("ssD"))
    sg = Ct(nc.alloc_semaphore("sgD"))

    bk = Blk(); y, g, t, v, s = bk.engines()
    y.dma_start(identf[:], ex["ident_f32"][:]).then_inc(sd, 16)
    y.dma_start(onesb[:], ex["ones_bf"][:]).then_inc(sd, 16)
    y.dma_start(onesrow[:], ex["onesrow_f32"][:]).then_inc(sd, 16)
    if not fused:
        for f in range(8):
            y.dma_start(sel8[:, f * 128:(f + 1) * 128], ex["sel8"][f]).then_inc(sd, 16)
        y.dma_start(ssq2[0:8, 0:128], ex["ssqa"][:]).then_inc(sd, 16)
        y.dma_start(ssq2[0:8, 128:256], ex["ssqb"][:]).then_inc(sd, 16)
    y.dma_start(sgt[:], ex["sgt"][:]).then_inc(sd, 16)
    y.dma_start(sut[:], ex["sut"][:]).then_inc(sd, 16)
    y.dma_start(sdt[:], ex["sdt"][:]).then_inc(sd, 16)
    v.memset(epsb[:, 0:1], 1e-5)
    v.memset(epsb[:, 1:2], 1.1920929e-07)
    y.wait(sd)
    bk.run(nc)

    if fused:
        with nc.sbuf_tensor("tpD", [128, 2 * 1024], BF16, side="right") as tp, \
             nc.sbuf_tensor("xslD0", [128, 2 * 1024], F32, side="right") as xsl:
            bk = Blk(); y, g, t, v, s = bk.engines()
            for Dc in range(16):
                par = Dc % 2
                if Dc >= 2:
                    y.wait(sv)
                y.dma_start(tp[:, par * 1024:(par + 1) * 1024],
                            ex["rs2"][:, Dc * 128:(Dc + 1) * 128], transpose=True).then_inc(sd, 16)
                y.dma_start(xsl[:, par * 1024:(par + 1) * 1024],
                            ex["x1rD"][Dc * 128:(Dc + 1) * 128, :]).then_inc(sd, 16)
                v.wait(sd)
                v.tensor_tensor(out_acc[:, Dc * 1024:(Dc + 1) * 1024], tp[:, par * 1024:(par + 1) * 1024],
                                xsl[:, par * 1024:(par + 1) * 1024], ALU.add).then_inc(sv, 1)
            bk.run(nc)
    else:
        with nc.psum_tensor("psD1", [128, 128], F32) as ps1:
            bk = Blk(); y, g, t, v, s = bk.engines()
            v.tensor_tensor(ssq2[0:8, 0:128], ssq2[0:8, 0:128], ssq2[0:8, 128:256], ALU.add)
            v.tensor_scalar(ssq2[0:8, 0:128], ssq2[0:8, 0:128], 1.0 / 4096.0, None, ALU.mult).then_inc(sv, 1)
            s.wait(sv)
            s.activation(ssq2[0:8, 128:256], ssq2[0:8, 0:128], AF.Sqrt, bias=epsb[0:8, 1:2]).then_inc(ss, 1)
            v.wait(ss)
            v.reciprocal(ssq2[0:8, 0:128], ssq2[0:8, 128:256]).then_inc(sv, 1)
            for f in range(8):
                t.wait(sv)
                t.matmul(ps1[:], sel8[0:8, f * 128:(f + 1) * 128], ssq2[0:8, 0:128], start=True, stop=True).then_inc(sm, 1)
                v.wait(sm)
                v.tensor_copy(rstd_b[:, f * 128:(f + 1) * 128], ps1[:]).then_inc(sv, 1)
            bk.run(nc)

        with nc.sbuf_tensor("yfmD", [128, 32 * 1024], BF16, side="right") as y_fm, \
             nc.sbuf_tensor("wotD", [128, 32 * 128], BF16, side="right") as wbo, \
             nc.sbuf_tensor("xslD", [128, 1024], F32, side="right") as xsl, \
             nc.psum_tensor("psOD", [128, 512], F32) as ps_o0, \
             nc.psum_tensor("psOD2", [128, 512], F32) as ps_o1:
            bk = Blk(); y, g, t, v, s = bk.engines()
            for cc in range(32):
                src = ex["yq0"] if cc < 16 else ex["yq1"]
                y.dma_start(y_fm[:, cc * 1024:(cc + 1) * 1024], src[:, (cc % 16) * 128:((cc % 16) + 1) * 128],
                            transpose=True).then_inc(sd, 16)
            v.wait(sd)
            for cc in range(32):
                v.tensor_tensor(y_fm[:, cc * 1024:(cc + 1) * 1024], y_fm[:, cc * 1024:(cc + 1) * 1024],
                                rstd_b[:], ALU.mult)
            v.tensor_scalar(rstd_b[0:1, 0:1], rstd_b[0:1, 0:1], 1.0, None, ALU.mult).then_inc(sv, 1)
            bk.run(nc)
            for Dc in range(16):
                bk = Blk(); y, g, t, v, s = bk.engines()
                g.wait(sm)
                g.dma_start(wbo[:].rearrange("p (k c) -> p k c", k=32), ex["wot"][Dc]).then_inc(sd, 16)
                y.wait(sv)
                y.dma_start(xsl[:], ex["x1rD"][Dc * 128:(Dc + 1) * 128, :]).then_inc(sd, 16)
                t.wait(sd)
                for tb in range(2):
                    par = tb % 2
                    t.wait(sv)
                    mmi = None
                    for cc in range(32):
                        mmi = t.matmul([ps_o0, ps_o1][par][:], wbo[:, cc * 128:(cc + 1) * 128],
                                       y_fm[:, cc * 1024 + tb * 512:cc * 1024 + (tb + 1) * 512],
                                       start=(cc == 0), stop=(cc == 31))
                    mmi.then_inc(sm, 1)
                    v.wait(sm)
                    v.tensor_tensor(out_acc[:, Dc * 1024 + tb * 512:Dc * 1024 + (tb + 1) * 512],
                                    [ps_o0, ps_o1][par][:], xsl[:, tb * 512:(tb + 1) * 512], ALU.add).then_inc(sv, 1)
                bk.run(nc)
        d2_stack.close()


    xn2 = A("xn2D", (128, 16 * 1024), BF16)
    with nc.sbuf_tensor("mrD", [1, 1024], BF16, side="right") as mean_row, \
         nc.sbuf_tensor("orbD", [1, 128], BF16, side="right") as onesrow_bf, \
         nc.sbuf_tensor("swD", [1, 1024], F32, side="right") as srow, \
         nc.sbuf_tensor("bcAD", [128, 1024], BF16, side="right") as bcA, \
         nc.sbuf_tensor("bcBD", [128, 1024], BF16, side="right") as bcB, \
         nc.sbuf_tensor("xldD", [128, 1024], F32, side="right") as xld:
        with nc.psum_tensor("psSumD", [1, 1024], F32) as ps_sum, \
             nc.psum_tensor("psSqD", [1, 1024], F32) as ps_sq:
            bk = Blk(); y, g, t, v, s = bk.engines()
            g.dma_start(onesrow_bf[:], ex["onesrow_f32"][:]).then_inc(sd, 16)
            for cc in range(16):
                if cc >= 1:
                    y.wait(sv); y.wait(ss)
                y.dma_start(xld[:], ex["x1rD"][cc * 128:(cc + 1) * 128, :]).then_inc(sd, 16)
                v.wait(sd)
                if cc >= 1:
                    v.wait(sm)
                v.tensor_copy(bcB[:], xld[:]).then_inc(sv, 1)
                s.wait(sd)
                if cc >= 1:
                    s.wait(sm)
                s.activation(xn2[:, cc * 1024:(cc + 1) * 1024], xld[:], AF.Square).then_inc(ss, 1)
                t.wait(sv); t.wait(ss)
                for j in range(2):
                    t.matmul(ps_sum[0:1, j * 512:(j + 1) * 512], onesb[:], bcB[:, j * 512:(j + 1) * 512],
                             start=(cc == 0), stop=(cc == 15))
                mmi = None
                for j in range(2):
                    mmi = t.matmul(ps_sq[0:1, j * 512:(j + 1) * 512], onesb[:],
                                   xn2[:, cc * 1024 + j * 512:cc * 1024 + (j + 1) * 512],
                                   start=(cc == 0), stop=(cc == 15))
                mmi.then_inc(sm, 1)
            bk.run(nc)
            bk = Blk(); y, g, t, v, s = bk.engines()
            v.tensor_scalar(mean_row[:], ps_sum[0:1, :], 1.0 / 2048.0, None, ALU.mult)
            v.tensor_scalar(srow[:], ps_sq[0:1, :], 1.0 / 2048.0, None, ALU.mult).then_inc(sv, 1)
            bk.run(nc)
            bk = Blk(); y, g, t, v, s = bk.engines()
            v.tensor_tensor(ps_sum[0:1, :], mean_row[:], mean_row[:], ALU.mult).then_inc(sv, 1)
            bk.run(nc)
            bk = Blk(); y, g, t, v, s = bk.engines()
            v.tensor_tensor(srow[:], srow[:], ps_sum[0:1, :], ALU.subtract).then_inc(sv, 1)
            s.wait(sv)
            s.activation(ps_sum[0:1, :], srow[:], AF.Sqrt, bias=epsb[0:1, 0:1]).then_inc(ss, 1)
            bk.run(nc)
            bk = Blk(); y, g, t, v, s = bk.engines()
            v.reciprocal(srow[:], ps_sum[0:1, :]).then_inc(sv, 1)
            bk.run(nc)
        with nc.psum_tensor("psBcD", [128, 1024], F32) as ps_bc:
            bk = Blk(); y, g, t, v, s = bk.engines()
            for j in range(2):
                t.wait(sv)
                t.matmul(ps_bc[:, 0:512], onesrow_bf[:], mean_row[0:1, j * 512:(j + 1) * 512], start=True, stop=True)
                t.matmul(ps_bc[:, 512:1024], onesrow[:], srow[0:1, j * 512:(j + 1) * 512], start=True, stop=True).then_inc(sm, 1)
                v.wait(sm)
                v.tensor_copy(bcA[:, j * 512:(j + 1) * 512], ps_bc[:, 0:512])
                v.tensor_copy(bcB[:, j * 512:(j + 1) * 512], ps_bc[:, 512:1024]).then_inc(sv, 1)
            bk.run(nc)
            bk = Blk(); y, g, t, v, s = bk.engines()
            for cc in range(16):
                if cc >= 1:
                    y.wait(sg)
                y.dma_start(xld[:], ex["x1rD"][cc * 128:(cc + 1) * 128, :]).then_inc(sd, 16)
                v.wait(sd)
                v.tensor_tensor(xld[:], xld[:], bcA[:], ALU.subtract).then_inc(sv, 1)
                g.wait(sv)
                g.tensor_tensor(xn2[:, cc * 1024:(cc + 1) * 1024], xld[:], bcB[:], ALU.mult).then_inc(sg, 1)
            bk.run(nc)

    with nc.sbuf_tensor("fuseD", [128, 64 * 512], BF16, side="right") as fuse, \
         nc.sbuf_tensor("gsbD", [128, 512], BF16, side="right") as gate_sb, \
         nc.sbuf_tensor("usbD", [128, 512], BF16, side="right") as up_sb, \
         nc.sbuf_tensor("t0D", [128, 512], F32, side="right") as t0f, \
         nc.psum_tensor("psGD", [128, 512], F32) as ps_g, \
         nc.psum_tensor("psUD", [128, 512], F32) as ps_u, \
         nc.psum_tensor("psDD", [128, 512], F32) as ps_d0, \
         nc.psum_tensor("psDD2", [128, 512], F32) as ps_d1:
        for sl in range(2):
          with nc.sbuf_tensor(f"wgD{sl}", [128, 16 * 512], BF16, side="right") as wbg, \
               nc.sbuf_tensor(f"wuD{sl}", [128, 16 * 512], BF16, side="right") as wbu:
            for og in range(16):
                bk = Blk(); y, g, t, v, s = bk.engines()
                g.wait(sm)
                g.dma_start(wbg[:].rearrange("p (k c) -> p k c", k=16), ex["wg8"][og]).then_inc(sd, 16)
                g.dma_start(wbu[:].rearrange("p (k c) -> p k c", k=16), ex["wu8"][og]).then_inc(sd, 16)
                t.wait(sd)
                for oc4 in range(4):
                    oc = og * 4 + oc4
                    t.wait(ss)
                    mmi = None
                    for kc in range(16):
                        mmi = t.matmul(ps_g[:], wbg[:, kc * 512 + oc4 * 128:kc * 512 + (oc4 + 1) * 128],
                                       xn2[:, kc * 1024 + sl * 512:kc * 1024 + (sl + 1) * 512],
                                       start=(kc == 0), stop=(kc == 15))
                    mmi.then_inc(sm, 1)
                    t.wait(sv)
                    mmi = None
                    for kc in range(16):
                        mmi = t.matmul(ps_u[:], wbu[:, kc * 512 + oc4 * 128:kc * 512 + (oc4 + 1) * 128],
                                       xn2[:, kc * 1024 + sl * 512:kc * 1024 + (sl + 1) * 512],
                                       start=(kc == 0), stop=(kc == 15))
                    mmi.then_inc(sm, 1)
                    s.wait(sm)
                    s.activation(gate_sb[:], ps_g[:], AF.Silu, scale=sgt[:, oc:oc + 1]).then_inc(ss, 1)
                    v.wait(sm)
                    v.tensor_scalar(up_sb[:], ps_u[:], sut[:, oc:oc + 1], None, ALU.mult).then_inc(sv, 1)
                    g.wait(ss); g.wait(sv)
                    g.tensor_tensor(fuse[:, oc * 512:(oc + 1) * 512], gate_sb[:], up_sb[:], ALU.mult).then_inc(sg, 1)
                bk.run(nc)
          with nc.sbuf_tensor(f"wdD{sl}", [128, 64 * 128], BF16, side="right") as wbd:
            for dc in range(16):
                bk = Blk(); y, g, t, v, s = bk.engines()
                g.wait(sm)
                g.dma_start(wbd[:].rearrange("p (k c) -> p k c", k=64), ex["wd8"][dc]).then_inc(sd, 16)
                par = dc % 2
                t.wait(sd)
                t.wait(sg)
                t.wait(sv)
                mmi = None
                for oc in range(64):
                    mmi = t.matmul([ps_d0, ps_d1][par][:], wbd[:, oc * 128:(oc + 1) * 128],
                                   fuse[:, oc * 512:(oc + 1) * 512],
                                   start=(oc == 0), stop=(oc == 63))
                mmi.then_inc(sm, 1)
                v.wait(sm)
                v.tensor_scalar(t0f[:], [ps_d0, ps_d1][par][:], sdt[:, dc:dc + 1], None, ALU.mult).then_inc(sv, 1)
                g.wait(sv)
                g.tensor_tensor(out_acc[:, dc * 1024 + sl * 512:dc * 1024 + (sl + 1) * 512],
                                out_acc[:, dc * 1024 + sl * 512:dc * 1024 + (sl + 1) * 512],
                                t0f[:], ALU.add).then_inc(sg, 1)
                bk.run(nc)

    with nc.sbuf_tensor("stgD", [128, 2048], F32, side="right") as stg, \
         nc.psum_tensor("psTD", [128, 256], F32) as ps_t:
        for tt in range(8):
            bk = Blk(); y, g, t, v, s = bk.engines()
            if tt > 0:
                t.wait(sd)
            for Dc in range(16):
                par = Dc % 2
                t.wait(sv)
                tri = t.transpose(ps_t[:, par * 128:(par + 1) * 128],
                                  out_acc[:, Dc * 1024 + tt * 128:Dc * 1024 + (tt + 1) * 128], identf[:])
                tri.then_inc(sm, 1)
                v.wait(sm)
                v.tensor_copy(stg[:, Dc * 128:(Dc + 1) * 128], ps_t[:, par * 128:(par + 1) * 128]).then_inc(sv, 1)
            y.wait(sv)
            y.dma_start(out_r[tt * 128:(tt + 1) * 128, :], stg[:]).then_inc(sd, 16)
            bk.run(nc)
        bk = Blk(); y, g, t, v, s = bk.engines()
        y.wait(sd)
        bk.run(nc)


def build_stageD():
    nc = bass.Bass("TRN2", target_bir_lowering=False)
    ex = _declare(nc, STAGED_IN, "ExternalInput")
    outr = nc.dram_tensor("outr", [1024, 2048], F32, kind="ExternalOutput").ap()
    with ExitStack() as st:
        emit_stageD(nc, st, ex, outr)
    nc.finalize()
    return nc


def _kernel_unfused_impl(**inputs):
    pc = host_prep(inputs)

    nc = build_stageA()
    names = [nm for nm, _, _ in STAGEA_IN]
    in_maps = [{nm: np.ascontiguousarray(pc[c][nm]) for nm in names} for c in range(NCORE)]
    resA = run_bass_kernel_spmd(nc, in_maps, core_ids=list(range(NCORE)))
    parts = [resA.results[c]["part"] for c in range(NCORE)]
    partFs = [resA.results[c]["partF"] for c in range(NCORE)]

    nc = build_stageB()
    names = [nm for nm, _, _ in STAGEB_IN]
    in_maps = []
    for c in range(NCORE):
        b, h2 = c // 2, c % 2
        d = {nm: np.ascontiguousarray(pc[c][nm]) for nm in names
             if nm not in ("partAr", "partBr", "pfAr", "pfBr")}
        rows = slice(h2 * 1024, (h2 + 1) * 1024)
        d["partAr"] = np.ascontiguousarray(parts[2 * b][rows])
        d["partBr"] = np.ascontiguousarray(parts[2 * b + 1][rows])
        d["pfAr"] = np.ascontiguousarray(partFs[2 * b][rows])
        d["pfBr"] = np.ascontiguousarray(partFs[2 * b + 1][rows])
        in_maps.append(d)
    resB = run_bass_kernel_spmd(nc, in_maps, core_ids=list(range(NCORE)))
    x1r = [resB.results[c]["x1r"] for c in range(NCORE)]

    nc = build_stageC()
    names = [nm for nm, _, _ in STAGEC_IN]
    in_maps = []
    for c in range(NCORE):
        b = c // 2
        d = {nm: np.ascontiguousarray(pc[c][nm]) for nm in names if nm != "x1T"}
        d["x1T"] = np.ascontiguousarray(np.concatenate([x1r[2 * b], x1r[2 * b + 1]], axis=1))
        in_maps.append(d)
    resC = run_bass_kernel_spmd(nc, in_maps, core_ids=list(range(NCORE)))
    ygs = [resC.results[c]["ygated"] for c in range(NCORE)]
    ssqs = [resC.results[c]["ssqout"] for c in range(NCORE)]

    nc = build_stageD()
    names = [nm for nm, _, _ in STAGED_IN]
    in_maps = []
    for c in range(NCORE):
        b, h2 = c // 2, c % 2
        d = {nm: np.ascontiguousarray(pc[c][nm]) for nm in names
             if nm not in ("x1rD", "yq0", "yq1", "ssqa", "ssqb")}
        rows = slice(h2 * 1024, (h2 + 1) * 1024)
        d["x1rD"] = np.ascontiguousarray(x1r[c])
        d["yq0"] = np.ascontiguousarray(ygs[2 * b][rows])
        d["yq1"] = np.ascontiguousarray(ygs[2 * b + 1][rows])
        d["ssqa"] = np.ascontiguousarray(ssqs[2 * b][h2 * 8:(h2 + 1) * 8])
        d["ssqb"] = np.ascontiguousarray(ssqs[2 * b + 1][h2 * 8:(h2 + 1) * 8])
        in_maps.append(d)
    resD = run_bass_kernel_spmd(nc, in_maps, core_ids=list(range(NCORE)))

    out = np.empty((4, T, D), np.float32)
    for c in range(NCORE):
        b, h2 = c // 2, c % 2
        out[b, h2 * 1024:(h2 + 1) * 1024, :] = resD.results[c]["outr"]
    return out


def build_fused(upto='D'):
    nc = bass.Bass("TRN2", target_bir_lowering=False)
    removed = {"partAr", "partBr", "pfAr", "pfBr", "x1T", "yq0", "yq1", "ssqa", "ssqb",
               "sel8", "wot", "x1rD"}
    seen = {}
    for lst in (STAGEA_IN, STAGEB_IN, STAGEC_IN, STAGED_IN):
        for nm, sh, dt in lst:
            if nm in removed or nm in seen:
                continue
            seen[nm] = (sh, dt)
    seen["wotC"] = ((4, 128, 16, 512), F32)
    ex = {}
    for nm, (sh, dt) in seen.items():
        ex[nm] = nc.dram_tensor(nm, list(sh), dt, kind="ExternalInput").ap()
    outr = nc.dram_tensor("outr", [1024, 2048], F32, kind="ExternalOutput").ap()

    part_i = nc.dram_tensor("part_i", [T, 2048], BF16)
    partF_i = nc.dram_tensor("partF_i", [T, 2048], BF16)
    rs_part = nc.dram_tensor("rs_part_o", [1024, 2048], BF16)
    rs_pf = nc.dram_tensor("rs_pf_o", [1024, 2048], BF16)
    x1r_i = nc.dram_tensor("x1r_i", [D, 1024], F32)
    x1bf_i = nc.dram_tensor("x1bf_i", [D, 1024], BF16)
    ag1_o = nc.dram_tensor("ag1_o", [2 * D, 1024], BF16)
    ssq_i = nc.dram_tensor("ssq_i", [16, 128], F32)
    ag3_o = nc.dram_tensor("ag3_o", [32, 128], F32)
    rs2_i = nc.dram_tensor("rs2_i", [T, 2048], BF16)
    rs2_o = nc.dram_tensor("rs2_o", [1024, 2048], BF16)
    sp = {
        "sp_qf": nc.dram_tensor("sp_qf", [8 * 128, T], BF16).ap(),
        "sp_kf": nc.dram_tensor("sp_kf", [8 * 128, T], BF16).ap(),
        "sp_vtm": nc.dram_tensor("sp_vtm", [16 * 128, 1024], BF16).ap(),
        "sp_avnT": nc.dram_tensor("sp_avnT", [16, 24, 128, 128], BF16).ap(),
        "sp_z": nc.dram_tensor("sp_z", [16 * 128, T], BF16).ap(),
    }

    ccn = [0]

    def collective(kind, tin, tout):
        ccn[0] += 1
        cc = Ct(nc.alloc_semaphore(f"ccF{ccn[0]}"))
        bk = Blk(); y, g, t, v, s = bk.engines()
        g.collective_compute(kind, ALU.add if kind == "ReduceScatter" else ALU.bypass,
                             replica_groups=PAIRS,
                             ins=[tin.ap().opt()], outs=[tout.ap().opt()]).then_inc_default(cc)
        g.wait(cc, 1)
        bk.run(nc)

    with ExitStack() as stA:
        emit_stageA(nc, stA, ex, part_i.ap(), partF_i.ap(), sp)
    collective("ReduceScatter", part_i, rs_part)
    collective("ReduceScatter", partF_i, rs_pf)
    if upto == 'A':
        nc.finalize()
        return nc

    exB = dict(ex)
    exB["rs_part"] = rs_part.ap()
    exB["rs_pf"] = rs_pf.ap()
    with ExitStack() as stB:
        emit_stageB(nc, stB, exB, x1r_i.ap(), fused=True, x1bf_out=x1bf_i.ap())
    collective("AllGather", x1bf_i, ag1_o)
    if upto == 'B':
        nc.finalize()
        return nc

    exC = dict(ex)
    exC["ag1"] = ag1_o.ap()
    exC["ag3"] = ag3_o.ap()
    exC["rs2in"] = rs2_i.ap()
    exC["cc_ssq_in"] = ssq_i.ap().opt()
    exC["cc_ssq_out"] = ag3_o.ap().opt()
    with ExitStack() as stC:
        emit_stageC(nc, stC, exC, None, ssq_i.ap(), sp, fused=True)
    collective("ReduceScatter", rs2_i, rs2_o)
    if upto == 'C':
        nc.finalize()
        return nc

    exD = dict(ex)
    exD["rs2"] = rs2_o.ap()
    exD["x1rD"] = x1r_i.ap()
    with ExitStack() as stD:
        emit_stageD(nc, stD, exD, outr, fused=True)
    nc.finalize()
    return nc


def kernel(**inputs):
    pc = host_prep(inputs)
    Wout = np.asarray(inputs["L1_attn_W_out"])
    for c in range(NCORE):
        h2 = c % 2
        wc = np.ascontiguousarray(Wout.T[2048 * h2:2048 * (h2 + 1)])
        pc[c]["wotC"] = wc.reshape(16, 128, 4, 512).transpose(2, 1, 0, 3).copy()
    nc = build_fused()
    names = [t.name for t in []]
    removed = {"partAr", "partBr", "pfAr", "pfBr", "x1T", "yq0", "yq1", "ssqa", "ssqb",
               "sel8", "wot", "x1rD"}
    innames = []
    seenn = set()
    for lst in (STAGEA_IN, STAGEB_IN, STAGEC_IN, STAGED_IN):
        for nm, sh, dt in lst:
            if nm in removed or nm in seenn:
                continue
            seenn.add(nm)
            innames.append(nm)
    innames.append("wotC")
    in_maps = [{nm: np.ascontiguousarray(pc[c][nm]) for nm in innames} for c in range(NCORE)]
    res = run_bass_kernel_spmd(nc, in_maps, core_ids=list(range(NCORE)))
    out = np.empty((4, T, D), np.float32)
    for c in range(NCORE):
        b, h2 = c // 2, c % 2
        out[b, h2 * 1024:(h2 + 1) * 1024, :] = res.results[c]["outr"]
    return out


def kernel_unfused(**inputs):
    return _kernel_unfused_impl(**inputs)


# version 05
# speedup vs baseline: 1.1799x; 1.1799x over previous
import numpy as np
import ml_dtypes
from contextlib import ExitStack
import concourse.bass as bass
import concourse.mybir as mybir
from concourse.bass_utils import run_bass_kernel_spmd

F32 = mybir.dt.float32
BF16 = mybir.dt.bfloat16
I8 = mybir.dt.int8
AF = mybir.ActivationFunctionType
ALU = mybir.AluOpType
AX = mybir.AxisListType
bfdt = ml_dtypes.bfloat16

D = 2048
T = 2048
NCORE = 8
SCALE = float(1.0 / np.sqrt(128.0))
NEG = -1e30
PAIRS = [[0, 1], [2, 3], [4, 5], [6, 7]]


class Ct:
    def __init__(self, sem):
        self.sem = sem
        self.n = 0
    def inc(self, k=1):
        self.n += k
        return self.n


class _RecInstr:
    __slots__ = ("rec",)
    def __init__(self, rec):
        self.rec = rec
    def then_inc(self, ct, k=1):
        self.rec[3].append((ct.sem, k))
        ct.inc(k)
        return self
    def then_inc_default(self, ct):
        self.rec[3].append((ct.sem, None))
        ct.inc(1)
        return self


class Rec:
    def __init__(self, sched, eng):
        self._q = sched.q[eng]
    def wait(self, ct, n=None):
        sem = ct.sem
        val = ct.n if n is None else n
        if val > 0:
            self._q.append(("w", sem, val, None))
    def __getattr__(self, name):
        q = self._q
        def call(*a, **kw):
            rec = ("c", name, (a, kw), [])
            q.append(rec)
            return _RecInstr(rec)
        return call


class Blk:
    ORDER = ("y", "g", "t", "v", "s")
    ATTR = {"y": "sync", "g": "gpsimd", "t": "tensor", "v": "vector", "s": "scalar"}
    def __init__(self):
        self.q = {e: [] for e in self.ORDER}
    def engines(self):
        return tuple(Rec(self, e) for e in ("y", "g", "t", "v", "s"))
    def run(self, nc):
        with nc.Block() as b:
            for e in self.ORDER:
                if not self.q[e]:
                    continue
                def make(recs):
                    def body(eng):
                        for r in recs:
                            if r[0] == "w":
                                eng.wait_ge(r[1], r[2])
                            else:
                                a, kw = r[2]
                                inst = getattr(eng, r[1])(*a, **kw)
                                for sem, k in r[3]:
                                    if k is None:
                                        inst.then_inc(sem)
                                    else:
                                        inst.then_inc(sem, k)
                    return body
                getattr(b, self.ATTR[e])(make(self.q[e]))



_PERM128 = np.concatenate([np.arange(0, 128, 2), np.arange(1, 128, 2)])

def _perm256(base):
    return np.concatenate([base + _PERM128, base + 128 + _PERM128])


def _host_consts():
    c = {}
    c["ident_bf"] = np.eye(128, dtype=np.float32).astype(bfdt)
    c["ident_f32"] = np.eye(128, dtype=np.float32)
    m = np.zeros((4, 128, 512), np.float32)
    for j in range(4):
        for r in range(128):
            m[j, r, 128 * j + r + 1:] = NEG
    c["maskA"] = m
    inv = 1.0 / 10000.0 ** (np.arange(0, 128, 2, dtype=np.float64) / 128.0)
    fr = np.arange(T, dtype=np.float64)[None, :] * inv[:, None]
    cosv = np.cos(fr).astype(np.float32)
    sinv = np.sin(fr).astype(np.float32)
    c["cs1T"] = np.concatenate([cosv, sinv], axis=0)
    c["cs2T"] = np.concatenate([sinv, cosv], axis=0)
    r1 = np.zeros((128, 128), np.float32)
    r2 = np.zeros((128, 128), np.float32)
    for m in range(64):
        r1[m, m] = 1.0
        r1[m + 64, m] = -1.0
        r2[m, m + 64] = 1.0
        r2[m + 64, m + 64] = 1.0
    c["R1"] = r1.astype(bfdt)
    c["R2"] = r2.astype(bfdt)
    mt = np.zeros((2, 128, 256), np.float32)
    for kt in range(2):
        for kk in range(128):
            mt[kt, kk, :kt * 128 + kk] = NEG
    c["maskT"] = mt
    sel = np.zeros((16, 16, 128), np.float32)
    for h in range(16):
        sel[h, h, :] = 1.0
    c["sel16"] = sel
    sel8 = np.zeros((8, 8, 128), np.float32)
    for f in range(8):
        sel8[f, f, :] = 1.0
    c["sel8"] = sel8
    s127 = np.zeros((128, 128), np.float32)
    s127[127, :] = 1.0
    c["sel127"] = s127
    c["ones_bf"] = np.ones((128, 1), np.float32).astype(bfdt)
    c["ones_f32"] = np.ones((128, 1), np.float32)
    c["onesrow_f32"] = np.ones((1, 128), np.float32)
    return c


def host_prep(inputs):
    consts = _host_consts()
    x = np.asarray(inputs["L0_attn_x"])
    pc = []
    for c in range(NCORE):
        b, h2 = c // 2, c % 2
        d = dict(consts)
        d["xT"] = np.ascontiguousarray(x[b].T)
        d["xTr"] = np.ascontiguousarray(x[b].T[:, h2 * 1024:(h2 + 1) * 1024])
        Wq = np.asarray(inputs["L0_attn_Wq"])
        cols = np.concatenate([_perm256(256 * h) for h in range(8 * h2, 8 * h2 + 8)])
        d["qw4"] = np.ascontiguousarray(Wq[:, cols]).reshape(16, 128, 4, 512).transpose(2, 1, 0, 3).copy()
        Wk = np.asarray(inputs["L0_attn_Wk"])
        kcols = np.concatenate([_perm256(256 * kv) for kv in (2 * h2, 2 * h2 + 1)])
        d["kw"] = np.ascontiguousarray(Wk[:, kcols]).reshape(16, 128, 512).transpose(1, 0, 2).copy()
        Wv = np.asarray(inputs["L0_attn_Wv"])
        d["vw"] = np.ascontiguousarray(Wv[:, 512 * h2:512 * h2 + 512]).reshape(16, 128, 512).transpose(1, 0, 2).copy()
        Wo = np.asarray(inputs["L0_attn_Wo"])
        woa = Wo[h2 * 2048:(h2 + 1) * 2048, :]
        d["gnw_row"] = np.ascontiguousarray(np.asarray(inputs["L0_attn_gn_w"])[h2 * 2048:(h2 + 1) * 2048].reshape(1, 2048))
        d["gnb_row"] = np.ascontiguousarray(np.asarray(inputs["L0_attn_gn_b"])[h2 * 2048:(h2 + 1) * 2048].reshape(1, 2048))
        for nm in ("lambda_q1", "lambda_k1", "lambda_q2", "lambda_k2"):
            d[nm.replace("lambda_", "lam")] = np.ascontiguousarray(np.asarray(inputs["L0_attn_" + nm]).reshape(128, 1))
        rows = np.arange(1024 * h2, 1024 * h2 + 1024)
        d["fqw"] = np.ascontiguousarray(np.asarray(inputs["L0_ffn_Wq"])[rows].T).reshape(16, 128, 2, 512).transpose(2, 1, 0, 3).copy()
        d["fkw"] = np.ascontiguousarray(np.asarray(inputs["L0_ffn_Wk"])[rows].T).reshape(16, 128, 2, 512).transpose(2, 1, 0, 3).copy()
        d["fvw"] = np.ascontiguousarray(np.asarray(inputs["L0_ffn_Wv"])[rows].T).reshape(16, 128, 2, 512).transpose(2, 1, 0, 3).copy()
        wof = np.ascontiguousarray(np.asarray(inputs["L0_ffn_Wo"]).T[rows])
        woall = np.concatenate([woa, wof], axis=0)
        d["wo4"] = woall.reshape(24, 128, 4, 512).transpose(2, 1, 0, 3).copy()
        d["ln1w_col"] = np.ascontiguousarray(np.asarray(inputs["L0_ffn_ln1_w"]).reshape(16, 128).T)
        d["ln2w_col"] = np.ascontiguousarray(np.asarray(inputs["L0_ffn_ln2_w"]).reshape(16, 128).T)
        d["A_t"] = np.asarray(inputs["L0_ffn_A"]).reshape(16, 128, 16).copy()
        d["Bmat"] = np.asarray(inputs["L0_ffn_B"]).copy()
        d["wi4"] = np.ascontiguousarray(np.asarray(inputs["L0_ffn_W_in_base"]).T).reshape(16, 128, 16, 512).transpose(2, 1, 0, 3).copy()
        d["wu4"] = np.ascontiguousarray(np.asarray(inputs["L0_ffn_W_up_base"]).T).reshape(16, 128, 16, 512).transpose(2, 1, 0, 3).copy()
        d["wd4"] = np.ascontiguousarray(np.asarray(inputs["L0_ffn_W_down"]).T).reshape(64, 128, 16, 128).transpose(2, 1, 0, 3).copy()
        Wi = np.asarray(inputs["L1_attn_W_in"])
        zrows = np.arange(2048 * h2, 2048 * h2 + 2048)
        zpad64 = np.zeros((64, 2048), Wi.dtype)
        w1a = np.concatenate([Wi[4096 + zrows],
                              Wi[8192:8256], zpad64,
                              Wi[8256:8320], zpad64,
                              Wi[8320 + 16 * h2:8320 + 16 * h2 + 16],
                              np.zeros((112, 2048), Wi.dtype)], axis=0)
        d["w1a"] = np.ascontiguousarray(w1a.T).reshape(16, 128, 19, 128).transpose(2, 1, 0, 3).copy()
        d["w1z"] = np.ascontiguousarray(Wi[zrows].T).reshape(16, 128, 16, 128).transpose(2, 1, 0, 3).copy()
        cw = np.asarray(inputs["L1_attn_conv_w"])
        cb = np.asarray(inputs["L1_attn_conv_b"])
        cwp = np.concatenate([cw[2048 * h2:2048 * h2 + 2048],
                              cw[4096:4160], np.zeros((64, 4), cw.dtype),
                              cw[4160:4224], np.zeros((64, 4), cw.dtype)], axis=0)
        cbp = np.concatenate([cb[2048 * h2:2048 * h2 + 2048],
                              cb[4096:4160], np.zeros(64, cb.dtype),
                              cb[4160:4224], np.zeros(64, cb.dtype)])
        d["convw"] = np.ascontiguousarray(cwp).reshape(18, 128, 4)
        d["convb"] = np.ascontiguousarray(cbp).reshape(18, 128, 1)
        hrows = np.arange(16 * h2, 16 * h2 + 16)
        d["Alog"] = np.ascontiguousarray(np.asarray(inputs["L1_attn_A_log"])[hrows].reshape(16, 1))
        d["dtb"] = np.ascontiguousarray(np.asarray(inputs["L1_attn_dt_bias"])[hrows].reshape(16, 1))
        d["Dvec"] = np.ascontiguousarray(np.asarray(inputs["L1_attn_D"])[hrows].reshape(16, 1))
        d["wot"] = np.ascontiguousarray(np.asarray(inputs["L1_attn_W_out"]).T).reshape(32, 128, 16, 128).transpose(2, 1, 0, 3).copy()
        d["wg8"] = np.asarray(inputs["L1_ffn_w_gate"]).reshape(16, 128, 16, 512).transpose(2, 1, 0, 3).copy()
        d["wu8"] = np.asarray(inputs["L1_ffn_w_up"]).reshape(16, 128, 16, 512).transpose(2, 1, 0, 3).copy()
        d["wd8"] = np.asarray(inputs["L1_ffn_w_down"]).reshape(64, 128, 16, 128).transpose(2, 1, 0, 3).copy()
        d["sgt"] = np.ascontiguousarray(np.asarray(inputs["L1_ffn_s_gate"]).reshape(64, 128).T)
        d["sut"] = np.ascontiguousarray(np.asarray(inputs["L1_ffn_s_up"]).reshape(64, 128).T)
        d["sdt"] = np.ascontiguousarray(np.asarray(inputs["L1_ffn_s_down"]).reshape(16, 128).T)
        pc.append(d)
    return pc


def _declare(nc, names_shapes, kind):
    out = {}
    for nm, shape, dt in names_shapes:
        out[nm] = nc.dram_tensor(nm, list(shape), dt, kind=kind).ap()
    return out



STAGEA_IN = [
    ("xT", (D, T), F32),
    ("qw4", (4, 128, 16, 512), F32),
    ("kw", (128, 16, 512), F32),
    ("vw", (128, 16, 512), F32),
    ("fqw", (2, 128, 16, 512), F32),
    ("fkw", (2, 128, 16, 512), F32),
    ("fvw", (2, 128, 16, 512), F32),
    ("wo4", (4, 128, 24, 512), F32),
    ("gnw_row", (1, 2048), F32), ("gnb_row", (1, 2048), F32),
    ("lamq1", (128, 1), F32), ("lamk1", (128, 1), F32),
    ("lamq2", (128, 1), F32), ("lamk2", (128, 1), F32),
    ("ln1w_col", (128, 16), F32),
    ("cs1T", (128, T), F32), ("cs2T", (128, T), F32),
    ("R1", (128, 128), BF16), ("R2", (128, 128), BF16),
    ("maskA", (4, 128, 512), F32),
    ("ident_bf", (128, 128), BF16),
    ("ones_bf", (128, 1), BF16),
    ("ones_f32", (128, 1), F32),
    ("onesrow_f32", (1, 128), F32),
]


def emit_stageA(nc, st, ex, part_out, partF_out, sp):
    A = lambda nm, sh, dt, side="left": st.enter_context(nc.sbuf_tensor(nm, list(sh), dt, side=side))

    ident = A("idA", (128, 128), BF16)
    onesb = A("onbA", (128, 1), BF16)
    onesf = A("onfA", (128, 1), F32)
    onesrow = A("onrA", (1, 128), F32)
    ln1w = A("l1wA", (128, 16), F32)
    lamv = A("lamA", (128, 4), F32)
    lam_b = A("lamB", (128, 1), F32)
    small = A("smlA", (1, 32), F32)
    epsb = A("epsbA", (128, 4), F32)
    bc0 = A("bc0A", (128, T), F32)

    sd = Ct(nc.alloc_semaphore("sdA"))
    sm = Ct(nc.alloc_semaphore("smA"))
    sv = Ct(nc.alloc_semaphore("svA"))
    ss = Ct(nc.alloc_semaphore("ssA"))
    sg = Ct(nc.alloc_semaphore("sgA"))

    bk = Blk(); y, g, t, v, s = bk.engines()
    y.dma_start(ident[:], ex["ident_bf"][:]).then_inc(sd, 16)
    y.dma_start(onesb[:], ex["ones_bf"][:]).then_inc(sd, 16)
    y.dma_start(onesf[:], ex["ones_f32"][:]).then_inc(sd, 16)
    y.dma_start(onesrow[:], ex["onesrow_f32"][:]).then_inc(sd, 16)
    y.dma_start(ln1w[:], ex["ln1w_col"][:]).then_inc(sd, 16)
    for j, nm in enumerate(("lamq1", "lamk1", "lamq2", "lamk2")):
        y.dma_start(lamv[:, j:j + 1], ex[nm][:]).then_inc(sd, 16)
    y.wait(sd)
    v.memset(epsb[:, 0:1], 1e-5)
    v.memset(epsb[:, 1:2], 1e-6)
    v.memset(epsb[:, 2:3], 1.1920929e-07)
    v.memset(epsb[:, 3:4], 0.0)
    bk.run(nc)

    with nc.psum_tensor("psA1", [128, 8], F32) as ps1:
        bk = Blk(); y, g, t, v, s = bk.engines()
        v.tensor_tensor(bc0[:, 0:1], lamv[:, 0:1], lamv[:, 1:2], ALU.mult)
        v.tensor_tensor(bc0[:, 1:2], lamv[:, 2:3], lamv[:, 3:4], ALU.mult).then_inc(sv, 1)
        t.wait(sv)
        t.matmul(ps1[0:1, 0:1], bc0[:, 0:1], onesf[:], start=True, stop=True)
        t.matmul(ps1[0:1, 1:2], bc0[:, 1:2], onesf[:], start=True, stop=True).then_inc(sm, 1)
        s.wait(sm)
        s.activation(small[0:1, 0:2], ps1[0:1, 0:2], AF.Exp).then_inc(ss, 1)
        v.wait(ss)
        v.tensor_tensor(small[0:1, 2:3], small[0:1, 0:1], small[0:1, 1:2], ALU.subtract).then_inc(sv, 1)
        g.wait(sv)
        g.tensor_scalar(small[0:1, 3:4], small[0:1, 2:3], 0.2, 1.0, ALU.add, op1=ALU.mult).then_inc(sg, 1)
        t.wait(sg)
        t.matmul(ps1[:, 2:3], onesrow[:], small[0:1, 3:4], start=True, stop=True).then_inc(sm, 1)
        v.wait(sm)
        v.tensor_copy(lam_b[:], ps1[:, 2:3])
        bk.run(nc)

    qkv_stack = ExitStack()
    q_r = qkv_stack.enter_context(nc.sbuf_tensor("qrA", [128, 16 * T], BF16, side="left"))
    k_r = qkv_stack.enter_context(nc.sbuf_tensor("krA", [128, 4 * T], BF16, side="left"))
    v_tm = qkv_stack.enter_context(nc.sbuf_tensor("vtA", [128, 16 * 512], BF16, side="left"))

    with nc.sbuf_tensor("xnA", [128, 16 * T], BF16, side="right") as xn, \
         nc.sbuf_tensor("swA", [1, T], F32, side="right") as srow, \
         nc.sbuf_tensor("scr0A", [128, T], BF16, side="right") as scr0:

      with nc.sbuf_tensor("mrA", [1, T], F32, side="right") as mean_row, \
           nc.sbuf_tensor("bc1A", [128, T], BF16, side="right") as bc1, \
           nc.sbuf_tensor("xldA", [128, T], F32, side="right") as xld:
        with nc.psum_tensor("psSumA", [1, T], F32) as ps_sum, \
             nc.psum_tensor("psSqA", [1, T], F32) as ps_sq:
            bk = Blk(); y, g, t, v, s = bk.engines()
            for cc in range(16):
                if cc >= 1:
                    y.wait(sv); y.wait(ss)
                y.dma_start(xld[:], ex["xT"][cc * 128:(cc + 1) * 128, :]).then_inc(sd, 16)
                v.wait(sd)
                if cc >= 1:
                    v.wait(sm)
                v.tensor_copy(scr0[:], xld[:]).then_inc(sv, 1)
                s.wait(sd)
                s.activation(xn[:, cc * T:(cc + 1) * T], xld[:], AF.Square).then_inc(ss, 1)
                t.wait(sv); t.wait(ss)
                xb = scr0
                for j in range(4):
                    t.matmul(ps_sum[0:1, j * 512:(j + 1) * 512], onesb[:], xb[:, j * 512:(j + 1) * 512],
                             start=(cc == 0), stop=(cc == 15))
                mmi = None
                for j in range(4):
                    mmi = t.matmul(ps_sq[0:1, j * 512:(j + 1) * 512], onesb[:],
                                   xn[:, cc * T + j * 512:cc * T + (j + 1) * 512],
                                   start=(cc == 0), stop=(cc == 15))
                mmi.then_inc(sm, 1)
            bk.run(nc)

            bk = Blk(); y, g, t, v, s = bk.engines()
            v.tensor_scalar(mean_row[:], ps_sum[0:1, 0:T], 1.0 / 2048.0, None, ALU.mult)
            v.tensor_scalar(srow[:], ps_sq[0:1, 0:T], 1.0 / 2048.0, None, ALU.mult).then_inc(sv, 1)
            bk.run(nc)
            bk = Blk(); y, g, t, v, s = bk.engines()
            v.tensor_tensor(ps_sum[0:1, 0:T], mean_row[:], mean_row[:], ALU.mult).then_inc(sv, 1)
            bk.run(nc)
            bk = Blk(); y, g, t, v, s = bk.engines()
            v.tensor_tensor(srow[:], srow[:], ps_sum[0:1, 0:T], ALU.subtract).then_inc(sv, 1)
            s.wait(sv)
            s.activation(ps_sum[0:1, 0:T], srow[:], AF.Sqrt, bias=epsb[0:1, 0:1]).then_inc(ss, 1)
            bk.run(nc)
            bk = Blk(); y, g, t, v, s = bk.engines()
            v.reciprocal(srow[:], ps_sum[0:1, 0:T]).then_inc(sv, 1)
            bk.run(nc)

        with nc.psum_tensor("psBcA", [128, 1024], F32) as ps_bc:
            bk = Blk(); y, g, t, v, s = bk.engines()
            for j in range(4):
                t.wait(sv)
                t.matmul(ps_bc[:, 0:512], onesrow[:], mean_row[0:1, j * 512:(j + 1) * 512], start=True, stop=True)
                t.matmul(ps_bc[:, 512:1024], onesrow[:], srow[0:1, j * 512:(j + 1) * 512], start=True, stop=True).then_inc(sm, 1)
                v.wait(sm)
                v.tensor_copy(bc0[:, j * 512:(j + 1) * 512], ps_bc[:, 0:512])
                v.tensor_copy(bc1[:, j * 512:(j + 1) * 512], ps_bc[:, 512:1024]).then_inc(sv, 1)
            bk.run(nc)

            bk = Blk(); y, g, t, v, s = bk.engines()
            for cc in range(16):
                if cc >= 1:
                    y.wait(sg)
                y.dma_start(xld[:], ex["xT"][cc * 128:(cc + 1) * 128, :]).then_inc(sd, 16)
                v.wait(sd)
                v.tensor_tensor(xld[:], xld[:], bc0[:], ALU.subtract).then_inc(sv, 1)
                g.wait(sv)
                g.tensor_tensor(xn[:, cc * T:(cc + 1) * T], xld[:], bc1[:], ALU.mult).then_inc(sg, 1)
            bk.run(nc)

      if True:
        with ExitStack() as st3, \
             nc.psum_tensor("psqA0", [128, 512], F32) as pq0, \
             nc.psum_tensor("psqA1", [128, 512], F32) as pq1, \
             nc.psum_tensor("pstrA", [128, 512], BF16) as pstr, \
             nc.psum_tensor("psS1A", [1, T], F32) as ps_s1:
            A3 = lambda nm, sh, dt: st3.enter_context(nc.sbuf_tensor(nm, list(sh), dt, side="right"))
            wbuf = A3("wbA", (128, 16 * 512), BF16)
            vtmp = A3("vtmpA", (128, 512), BF16)
            stq = A3("stqA", (128, 2048), BF16)
            psq = [pq0, pq1]

            def proj_block(bk_eng, grp_aps, oc_width, dst_fn):
                y, g, t, v, s = bk_eng
                evts = []
                for gi, w_ap in enumerate(grp_aps):
                    wb = wbuf[:, 0:16 * oc_width]
                    if gi >= 1:
                        g.wait(sm, evts[gi - 1])
                    les = []
                    for q4 in range(4):
                        g.dma_start(wb[:, q4 * 4 * oc_width:(q4 + 1) * 4 * oc_width]
                                    .rearrange("p (k c) -> p k c", k=4),
                                    w_ap[:, q4 * 4:(q4 + 1) * 4, :]).then_inc(sd, 16)
                        les.append(sd.n)
                    noc = oc_width // 128
                    for oc in range(noc):
                        for tb in range(4):
                            par = (oc * 4 + tb) % 2
                            t.wait(sv)
                            mmi = None
                            for kc in range(16):
                                if kc % 4 == 0:
                                    t.wait(sd, les[kc // 4])
                                base = kc * oc_width + oc * 128
                                mmi = t.matmul(psq[par][:], wb[:, base:base + 128],
                                               xn[:, kc * T + tb * 512:kc * T + (tb + 1) * 512],
                                               start=(kc == 0), stop=(kc == 15))
                            mmi.then_inc(sm, 1)
                            dst_fn(gi, oc, tb, psq[par])
                    evts.append(sm.n)

            bk = Blk(); engs = bk.engines(); y, g, t, v, s = engs
            def dq(gi, oc, tb, ps, v=v):
                v.wait(sm)
                cc = gi * 4 + oc
                v.tensor_copy(q_r[:, cc * T + tb * 512:cc * T + (tb + 1) * 512], ps[:]).then_inc(sv, 1)
            proj_block(engs, [ex["qw4"][grp] for grp in range(4)], 512, dq)
            bk.run(nc)
            bk = Blk(); engs = bk.engines(); y, g, t, v, s = engs
            def dk(gi, oc, tb, ps, v=v):
                v.wait(sm)
                v.tensor_copy(k_r[:, oc * T + tb * 512:oc * T + (tb + 1) * 512], ps[:]).then_inc(sv, 1)
            proj_block(engs, [ex["kw"][:]], 512, dk)
            bk.run(nc)
            bk = Blk(); engs = bk.engines(); y, g, t, v, s = engs
            def dv(gi, oc, tb, ps, t=t, v=v):
                v.wait(sm)
                v.tensor_copy(vtmp[:], ps[:]).then_inc(sv, 1)
                t.wait(sv)
                tri = None
                for j in range(4):
                    tri = t.transpose(pstr[:, j * 128:(j + 1) * 128], vtmp[:, j * 128:(j + 1) * 128], ident[:])
                tri.then_inc(sm, 1)
                v.wait(sm)
                cpi = None
                for j in range(4):
                    kt = tb * 4 + j
                    cpi = v.tensor_copy(v_tm[:, kt * 512 + oc * 128:kt * 512 + (oc + 1) * 128],
                                        pstr[:, j * 128:(j + 1) * 128])
                cpi.then_inc(sv, 1)
            proj_block(engs, [ex["vw"][:]], 512, dv)
            bk.run(nc)

            bk = Blk(); y, g, t, v, s = bk.engines()
            sqm_evt = []
            for cc in range(16):
                if cc >= 1:
                    s.wait(sm, sqm_evt[cc - 1])
                sb = scr0
                s.activation(sb[:], xn[:, cc * T:(cc + 1) * T], AF.Square).then_inc(ss, 1)
                t.wait(ss)
                mmi = None
                for j in range(4):
                    mmi = t.matmul(ps_s1[0:1, j * 512:(j + 1) * 512], onesb[:], sb[:, j * 512:(j + 1) * 512],
                                   start=(cc == 0), stop=(cc == 15))
                mmi.then_inc(sm, 1)
                sqm_evt.append(sm.n)
            bk.run(nc)
            bk = Blk(); y, g, t, v, s = bk.engines()
            v.tensor_scalar(srow[:], ps_s1[0:1, 0:T], 1.0 / 2048.0, None, ALU.mult).then_inc(sv, 1)
            s.wait(sv)
            s.activation(ps_s1[0:1, 0:T], srow[:], AF.Sqrt, bias=epsb[0:1, 1:2]).then_inc(ss, 1)
            bk.run(nc)
            bk = Blk(); y, g, t, v, s = bk.engines()
            v.wait(ss)
            v.reciprocal(srow[:], ps_s1[0:1, 0:T]).then_inc(sv, 1)
            for j in range(4):
                t.wait(sv)
                t.matmul(psq[1][:], onesrow[:], srow[0:1, j * 512:(j + 1) * 512], start=True, stop=True).then_inc(sm, 1)
                v.wait(sm)
                v.tensor_copy(bc0[:, j * 512:(j + 1) * 512], psq[1][:]).then_inc(sv, 1)
            bk.run(nc)
            bk = Blk(); y, g, t, v, s = bk.engines()
            for cc in range(16):
                sb = scr0
                if cc >= 1:
                    v.wait(sg)
                v.tensor_scalar(sb[:], xn[:, cc * T:(cc + 1) * T], ln1w[:, cc:cc + 1], None, ALU.mult).then_inc(sv, 1)
                g.wait(sv)
                g.tensor_tensor(xn[:, cc * T:(cc + 1) * T], sb[:], bc0[:], ALU.mult).then_inc(sg, 1)
            bk.run(nc)

            for wnm, spnm in (("fqw", "sp_qf"), ("fkw", "sp_kf")):
                bk = Blk(); engs = bk.engines(); y2, g2, t2, v2, s2 = engs
                def dqf(gi, oc, tb, ps, spnm=spnm, y2=y2, v2=v2):
                    v2.wait(sm)
                    if tb == 0:
                        v2.wait(sd)
                    v2.tensor_copy(stq[:, tb * 512:(tb + 1) * 512], ps[:]).then_inc(sv, 1)
                    if tb == 3:
                        y2.wait(sv)
                        row = (gi * 4 + oc) * 128
                        y2.dma_start(sp[spnm][row:row + 128, :], stq[:]).then_inc(sd, 16)
                proj_block(engs, [ex[wnm][grp] for grp in range(2)], 512, dqf)
                bk.run(nc)
            bk = Blk(); engs = bk.engines(); y2, g2, t2, v2, s2 = engs
            def dvf(gi, oc, tb, ps, t2=t2, v2=v2, y2=y2):
                h = gi * 4 + oc
                v2.wait(sm)
                v2.tensor_copy(vtmp[:], ps[:]).then_inc(sv, 1)
                t2.wait(sv)
                tri = None
                for j in range(4):
                    tri = t2.transpose(pstr[:, j * 128:(j + 1) * 128], vtmp[:, j * 128:(j + 1) * 128], ident[:])
                tri.then_inc(sm, 1)
                v2.wait(sm)
                v2.wait(sd)
                cpi = None
                for j in range(4):
                    cpi = v2.tensor_copy(stq[:, j * 128:(j + 1) * 128], pstr[:, j * 128:(j + 1) * 128])
                cpi.then_inc(sv, 1)
                y2.wait(sv)
                for j in range(4):
                    kt = tb * 4 + j
                    y2.dma_start(sp["sp_vtm"][kt * 128:(kt + 1) * 128, h * 128:(h + 1) * 128],
                                 stq[:, j * 128:(j + 1) * 128]).then_inc(sd, 16)
            proj_block(engs, [ex["fvw"][grp] for grp in range(2)], 512, dvf)
            bk.run(nc)
            bk = Blk(); y, g, t, v, s = bk.engines()
            y.wait(sd)
            bk.run(nc)

    with nc.sbuf_tensor("cs1A", [128, T], F32, side="right") as cs1, \
         nc.sbuf_tensor("cs2A", [128, T], F32, side="right") as cs2, \
         nc.sbuf_tensor("rR1A", [128, 128], BF16, side="right") as R1, \
         nc.sbuf_tensor("rR2A", [128, 128], BF16, side="right") as R2, \
         nc.sbuf_tensor("rpA", [128, T], BF16, side="right") as scrA, \
         nc.sbuf_tensor("rpB", [128, T], BF16, side="right") as scrB, \
         nc.psum_tensor("psRopeA", [128, 2048], F32) as ps_ro:
        bk = Blk(); y, g, t, v, s = bk.engines()
        y.dma_start(cs1[:], ex["cs1T"][:]).then_inc(sd, 16)
        y.dma_start(cs2[:], ex["cs2T"][:]).then_inc(sd, 16)
        y.dma_start(R1[:], ex["R1"][:]).then_inc(sd, 16)
        y.dma_start(R2[:], ex["R2"][:]).then_inc(sd, 16)
        for i in range(20):
            buf = q_r if i < 16 else k_r
            cc = i if i < 16 else i - 16
            ch = buf[:, cc * T:(cc + 1) * T]
            if i == 0:
                v.wait(sd); g.wait(sd)
            if i > 0:
                v.wait(sm)
                g.wait(sm)
            v.tensor_tensor(scrA[:], ch, cs1[:], ALU.mult).then_inc(sv, 1)
            g.tensor_tensor(scrB[:], ch, cs2[:], ALU.mult).then_inc(sg, 1)
            t.wait(sv); t.wait(sg)
            if i > 0:
                t.wait(sv)
            mmi = None
            for tb in range(4):
                t.matmul(ps_ro[:, tb * 512:(tb + 1) * 512], R1[:], scrA[:, tb * 512:(tb + 1) * 512],
                         start=True, stop=False)
                mmi = t.matmul(ps_ro[:, tb * 512:(tb + 1) * 512], R2[:], scrB[:, tb * 512:(tb + 1) * 512],
                               start=False, stop=True)
            mmi.then_inc(sm, 1)
            v.wait(sm)
            cpi = None
            for tb in range(4):
                cpi = v.tensor_copy(ch[:, tb * 512:(tb + 1) * 512] if False else buf[:, cc * T + tb * 512:cc * T + (tb + 1) * 512], ps_ro[:, tb * 512:(tb + 1) * 512])
            cpi.then_inc(sv, 1)
        bk.run(nc)

    with nc.sbuf_tensor("mskA", [128, 4 * 512], BF16, side="right") as masks:
        bk = Blk(); y, g, t, v, s = bk.engines()
        for j in range(4):
            g.dma_start(masks[:, j * 512:(j + 1) * 512], ex["maskA"][j]).then_inc(sd, 16)
        g.wait(sd)
        bk.run(nc)

        with ExitStack() as st5, \
             nc.psum_tensor("psScA", [128, 2048], F32) as ps_sc, \
             nc.psum_tensor("psTrA", [128, 512], BF16) as ps_tr, \
             nc.psum_tensor("psAv0", [128, 256], F32) as ps_av1, \
             nc.psum_tensor("psAv1", [128, 256], F32) as ps_av2, \
             nc.psum_tensor("psEpA", [128, 512], F32) as ps_ep:
            A5 = lambda nm, sh, dt: st5.enter_context(nc.sbuf_tensor(nm, list(sh), dt, side="right"))
            e_sb = A5("esbA", (128, 2048), BF16)
            eT_sb = A5("etbA", (128, 2048), BF16)
            avd = A5("avdA", (128, 16 * 256), F32)
            mxn = A5("mxnA", (128, 1), F32)
            biasq = A5("biasA", (128, 1), F32)
            ssum = A5("ssumA", (128, 2), F32)
            rrec = A5("rrecA", (128, 3), F32)
            sums_b = A5("sumsA", (128, 16), F32)
            sqs_b = A5("sqsA", (128, 16), F32)
            avc = A5("avcA", (128, 512), F32)
            abrow = A5("abrowA", (1, 512), F32)
            marow = A5("marowA", (1, 256), F32)
            gnrow = A5("gnrowA", (1, 512), F32)
            scrq = A5("scrqA", (128, 256), BF16)
            avn_tm = A5("avntmA", (128, 256), BF16)
            stow = A5("stowA", (128, 256), BF16)

            for h in range(8):
                kvl = h // 4
                bk = Blk(); y, g, t, v, s = bk.engines()
                y.dma_start(gnrow[0:1, 0:256], ex["gnw_row"][0:1, h * 256:(h + 1) * 256]).then_inc(sd, 16)
                y.dma_start(gnrow[0:1, 256:512], ex["gnb_row"][0:1, h * 256:(h + 1) * 256]).then_inc(sd, 16)
                for qt in range(16):
                    nkb = qt // 4 + 1
                    for m in (1, 2):
                        qcc = 2 * h + (m - 1)
                        kcc = 2 * kvl + (m - 1)
                        t.wait(ss)
                        mmi = None
                        for kb in range(nkb):
                            mmi = t.matmul(ps_sc[:, kb * 512:(kb + 1) * 512],
                                           q_r[:, qcc * T + qt * 128:qcc * T + (qt + 1) * 128],
                                           k_r[:, kcc * T + kb * 512:kcc * T + (kb + 1) * 512],
                                           start=True, stop=True)
                        mmi.then_inc(sm, 1)
                        v.wait(sm)
                        v.tensor_tensor(ps_sc[:, (nkb - 1) * 512:nkb * 512], ps_sc[:, (nkb - 1) * 512:nkb * 512],
                                        masks[:, (qt % 4) * 512:(qt % 4 + 1) * 512], ALU.add)
                        v.reduce_max(mxn[:], ps_sc[:, 0:nkb * 512], axis=AX.X, negate=True).then_inc(sv, 1)
                        g.wait(sv)
                        g.tensor_scalar(biasq[:], mxn[:], SCALE, 1.0, ALU.mult, op1=ALU.mult).then_inc(sg, 1)
                        s.wait(sg)
                        s.wait(sm)
                        s.activation(e_sb[:, 0:nkb * 512], ps_sc[:, 0:nkb * 512], AF.Exp,
                                     bias=biasq[:], scale=SCALE,
                                     accum_out=ssum[:, m - 1:m]).then_inc(ss, 1)
                        for grp in range(nkb):
                            t.wait(ss)
                            t.wait(sv)
                            tri = None
                            for j in range(4):
                                tri = t.transpose(ps_tr[:, j * 128:(j + 1) * 128],
                                                  e_sb[:, grp * 512 + j * 128:grp * 512 + (j + 1) * 128], ident[:])
                            tri.then_inc(sm, 1)
                            v.wait(sm)
                            v.tensor_copy(eT_sb[:, grp * 512:(grp + 1) * 512], ps_tr[:]).then_inc(sv, 1)
                        ps_av = ps_av1 if m == 1 else ps_av2
                        t.wait(sv)
                        t.wait(ss)
                        mmi = None
                        for kt in range(nkb * 4):
                            mmi = t.matmul(ps_av[:], eT_sb[:, kt * 128:(kt + 1) * 128],
                                           v_tm[:, kt * 512 + kvl * 256:kt * 512 + (kvl + 1) * 256],
                                           start=(kt == 0), stop=(kt == nkb * 4 - 1))
                        mmi.then_inc(sm, 1)
                    v.wait(ss)
                    v.reciprocal(rrec[:, 0:1], ssum[:, 0:1])
                    v.reciprocal(rrec[:, 1:2], ssum[:, 1:2]).then_inc(sv, 1)
                    g.wait(sv)
                    g.tensor_tensor(rrec[:, 2:3], rrec[:, 1:2], lam_b[:], ALU.mult).then_inc(sg, 1)
                    s.wait(sm); s.wait(sg)
                    s.activation(avc[:, 0:256], ps_av1[:], AF.Copy, scale=rrec[:, 0:1])
                    s.activation(avc[:, 256:512], ps_av2[:], AF.Copy, scale=rrec[:, 2:3]).then_inc(ss, 1)
                    v.wait(ss)
                    v.tensor_tensor(avd[:, qt * 256:(qt + 1) * 256], avc[:, 0:256], avc[:, 256:512],
                                    ALU.subtract).then_inc(sv, 1)
                    v.tensor_scalar(avc[:, 0:256], avd[:, qt * 256:(qt + 1) * 256], 1.0, None, ALU.mult,
                                    op1=ALU.add, accum_out=sums_b[:, qt:qt + 1]).then_inc(sv, 1)
                    s.wait(sv)
                    s.activation(scrq[:], avd[:, qt * 256:(qt + 1) * 256], AF.Square,
                                 accum_out=sqs_b[:, qt:qt + 1]).then_inc(ss, 1)
                bk.run(nc)

                bk = Blk(); y, g, t, v, s = bk.engines()
                v.wait(ss)
                v.reduce_sum(rrec[:, 0:1], sums_b[:], axis=AX.X)
                v.reduce_sum(rrec[:, 1:2], sqs_b[:], axis=AX.X).then_inc(sv, 1)
                t.wait(sv)
                t.matmul(ps_ep[0:1, 0:1], rrec[:, 0:1], onesf[:], start=True, stop=True)
                t.matmul(ps_ep[0:1, 1:2], rrec[:, 1:2], onesf[:], start=True, stop=True).then_inc(sm, 1)
                v.wait(sm)
                v.tensor_scalar(small[0:1, 4:5], ps_ep[0:1, 0:1], 1.0 / (2048.0 * 256.0), None, ALU.mult)
                v.tensor_scalar(small[0:1, 5:6], ps_ep[0:1, 1:2], 1.0 / (2048.0 * 256.0), None, ALU.mult).then_inc(sv, 1)
                g.wait(sv)
                g.tensor_tensor(small[0:1, 6:7], small[0:1, 4:5], small[0:1, 4:5], ALU.mult).then_inc(sg, 1)
                v.wait(sg)
                v.tensor_tensor(small[0:1, 7:8], small[0:1, 5:6], small[0:1, 6:7], ALU.subtract).then_inc(sv, 1)
                s.wait(sv)
                s.activation(small[0:1, 8:9], small[0:1, 7:8], AF.Sqrt, bias=epsb[0:1, 0:1]).then_inc(ss, 1)
                v.wait(ss)
                v.reciprocal(small[0:1, 9:10], small[0:1, 8:9]).then_inc(sv, 1)
                v.wait(sd)
                v.tensor_scalar(abrow[0:1, 0:256], gnrow[0:1, 0:256], small[0:1, 9:10], 0.8,
                                ALU.mult, op1=ALU.mult).then_inc(sv, 1)
                g.wait(sv)
                g.tensor_scalar(marow[0:1, 0:256], abrow[0:1, 0:256], small[0:1, 4:5], 1.0, ALU.mult, op1=ALU.mult).then_inc(sg, 1)
                v.tensor_scalar(abrow[0:1, 256:512], gnrow[0:1, 256:512], 0.8, None, ALU.mult)
                v.wait(sg)
                v.tensor_tensor(abrow[0:1, 256:512], abrow[0:1, 256:512], marow[0:1, 0:256], ALU.subtract).then_inc(sv, 1)
                t.wait(sv)
                t.matmul(ps_ep[:, 0:512], onesrow[:], abrow[0:1, 0:512], start=True, stop=True).then_inc(sm, 1)
                v.wait(sm)
                v.tensor_copy(bc0[:, 0:512], ps_ep[:, 0:512]).then_inc(sv, 1)
                bk.run(nc)

                bk = Blk(); y, g, t, v, s = bk.engines()
                for qt in range(16):
                    if qt > 0:
                        v.wait(sg)
                    v.tensor_tensor(avc[:, 0:256], avd[:, qt * 256:(qt + 1) * 256], bc0[:, 0:256], ALU.mult).then_inc(sv, 1)
                    g.wait(sv)
                    g.tensor_tensor(avn_tm[:], avc[:, 0:256], bc0[:, 256:512], ALU.add).then_inc(sg, 1)
                    t.wait(sg)
                    tri = None
                    for j in range(2):
                        tri = t.transpose(ps_tr[:, j * 128:(j + 1) * 128], avn_tm[:, j * 128:(j + 1) * 128], ident[:])
                    tri.then_inc(sm, 1)
                    v.wait(sm)
                    v.wait(sd)
                    v.tensor_copy(stow[:], ps_tr[:, 0:256]).then_inc(sv, 1)
                    y.wait(sv)
                    y.dma_start(sp["sp_avnT"][qt, 2 * h], stow[:, 0:128]).then_inc(sd, 16)
                    y.dma_start(sp["sp_avnT"][qt, 2 * h + 1], stow[:, 128:256]).then_inc(sd, 16)
                bk.run(nc)
    qkv_stack.close()

    with nc.sbuf_tensor("mskF", [128, 4 * 512], BF16, side="right") as masks:
        bk = Blk(); y, g, t, v, s = bk.engines()
        for j in range(4):
            g.dma_start(masks[:, j * 512:(j + 1) * 512], ex["maskA"][j]).then_inc(sd, 16)
        g.wait(sd)
        bk.run(nc)
        with nc.sbuf_tensor("qfA", [128, 8 * T], BF16, side="right") as q_f, \
             nc.sbuf_tensor("kfA", [128, 8 * T], BF16, side="right") as k_f, \
             nc.sbuf_tensor("vfA", [128, 16 * 1024], BF16, side="right") as v_tf, \
             ExitStack() as st6, \
             nc.psum_tensor("psScF", [128, 2048], F32) as ps_sc, \
             nc.psum_tensor("psTrF", [128, 512], BF16) as ps_tr, \
             nc.psum_tensor("psAvF", [128, 128], F32) as ps_av:
            A6 = lambda nm, sh, dt: st6.enter_context(nc.sbuf_tensor(nm, list(sh), dt, side="right"))
            e_sb = A6("esbF", (128, 2048), BF16)
            eT_sb = A6("etbF", (128, 2048), BF16)
            mxn = A6("mxnF", (128, 1), F32)
            biasq = A6("biasF", (128, 1), F32)
            ssum = A6("ssumF", (128, 1), F32)
            rrec = A6("rrecF", (128, 1), F32)
            avn_tm = A6("avntmF", (128, 128), BF16)
            stow = A6("stowF", (128, 128), BF16)

            bk = Blk(); y, g, t, v, s = bk.engines()
            for cc in range(8):
                y.dma_start(q_f[:, cc * T:(cc + 1) * T], sp["sp_qf"][cc * 128:(cc + 1) * 128, :]).then_inc(sd, 16)
                y.dma_start(k_f[:, cc * T:(cc + 1) * T], sp["sp_kf"][cc * 128:(cc + 1) * 128, :]).then_inc(sd, 16)
            for kt in range(16):
                y.dma_start(v_tf[:, kt * 1024:(kt + 1) * 1024], sp["sp_vtm"][kt * 128:(kt + 1) * 128, :]).then_inc(sd, 16)
            y.wait(sd)
            bk.run(nc)

            for h in range(8):
                bk = Blk(); y, g, t, v, s = bk.engines()
                for qt in range(16):
                    nkb = qt // 4 + 1
                    t.wait(ss)
                    mmi = None
                    for kb in range(nkb):
                        mmi = t.matmul(ps_sc[:, kb * 512:(kb + 1) * 512],
                                       q_f[:, h * T + qt * 128:h * T + (qt + 1) * 128],
                                       k_f[:, h * T + kb * 512:h * T + (kb + 1) * 512],
                                       start=True, stop=True)
                    mmi.then_inc(sm, 1)
                    v.wait(sm)
                    v.tensor_tensor(ps_sc[:, (nkb - 1) * 512:nkb * 512], ps_sc[:, (nkb - 1) * 512:nkb * 512],
                                    masks[:, (qt % 4) * 512:(qt % 4 + 1) * 512], ALU.add)
                    v.reduce_max(mxn[:], ps_sc[:, 0:nkb * 512], axis=AX.X, negate=True).then_inc(sv, 1)
                    g.wait(sv)
                    g.tensor_scalar(biasq[:], mxn[:], SCALE, 1.0, ALU.mult, op1=ALU.mult).then_inc(sg, 1)
                    s.wait(sg)
                    s.wait(sm)
                    s.activation(e_sb[:, 0:nkb * 512], ps_sc[:, 0:nkb * 512], AF.Exp,
                                 bias=biasq[:], scale=SCALE, accum_out=ssum[:]).then_inc(ss, 1)
                    for grp in range(nkb):
                        t.wait(ss)
                        t.wait(sv)
                        tri = None
                        for j in range(4):
                            tri = t.transpose(ps_tr[:, j * 128:(j + 1) * 128],
                                              e_sb[:, grp * 512 + j * 128:grp * 512 + (j + 1) * 128], ident[:])
                        tri.then_inc(sm, 1)
                        v.wait(sm)
                        v.tensor_copy(eT_sb[:, grp * 512:(grp + 1) * 512], ps_tr[:]).then_inc(sv, 1)
                    t.wait(sv)
                    t.wait(ss)
                    mmi = None
                    for kt in range(nkb * 4):
                        mmi = t.matmul(ps_av[:], eT_sb[:, kt * 128:(kt + 1) * 128],
                                       v_tf[:, kt * 1024 + h * 128:kt * 1024 + (h + 1) * 128],
                                       start=(kt == 0), stop=(kt == nkb * 4 - 1))
                    mmi.then_inc(sm, 1)
                    v.wait(ss)
                    v.reciprocal(rrec[:], ssum[:]).then_inc(sv, 1)
                    s.wait(sm); s.wait(sv)
                    s.activation(avn_tm[:], ps_av[:], AF.Copy, scale=rrec[:]).then_inc(ss, 1)
                    t.wait(ss)
                    tri = t.transpose(ps_tr[:, 0:128], avn_tm[:], ident[:])
                    tri.then_inc(sm, 1)
                    v.wait(sm)
                    v.wait(sd)
                    v.tensor_copy(stow[:], ps_tr[:, 0:128]).then_inc(sv, 1)
                    y.wait(sv)
                    y.dma_start(sp["sp_avnT"][qt, 16 + h], stow[:]).then_inc(sd, 16)
                bk.run(nc)

    with nc.sbuf_tensor("woA", [128, 24 * 512], BF16, side="right") as wo_sb, \
         nc.sbuf_tensor("avqA", [128, 2 * 24 * 128], BF16, side="right") as avq, \
         nc.sbuf_tensor("stoA", [128, 2 * 512], BF16, side="right") as sto, \
         nc.sbuf_tensor("stoF", [128, 2 * 512], BF16, side="right") as stoF, \
         nc.psum_tensor("psOA", [128, 512], F32) as ps_o0, \
         nc.psum_tensor("psOB", [128, 512], F32) as ps_o1, \
         nc.psum_tensor("psFA", [128, 512], F32) as ps_f0, \
         nc.psum_tensor("psFB", [128, 512], F32) as ps_f1:
        ps_os = [ps_o0, ps_o1]
        ps_fs = [ps_f0, ps_f1]
        for db in range(4):
            bk = Blk(); y, g, t, v, s = bk.engines()
            g.wait(sm)
            g.dma_start(wo_sb[:].rearrange("p (k c) -> p k c", k=24), ex["wo4"][db]).then_inc(sd, 16)
            for qt in range(16):
                par = qt % 2
                abase = par * 24 * 128
                y.wait(sm)
                for cc in range(24):
                    y.dma_start(avq[:, abase + cc * 128:abase + (cc + 1) * 128],
                                sp["sp_avnT"][qt, cc]).then_inc(sd, 16)
                t.wait(sd)
                t.wait(sv)
                mmi = None
                for cc in range(16):
                    mmi = t.matmul(ps_os[par][:], avq[:, abase + cc * 128:abase + (cc + 1) * 128],
                                   wo_sb[:, cc * 512:(cc + 1) * 512],
                                   start=(cc == 0), stop=(cc == 15))
                mmi.then_inc(sm, 1)
                mmi = None
                for cc in range(16, 24):
                    mmi = t.matmul(ps_fs[par][:], avq[:, abase + cc * 128:abase + (cc + 1) * 128],
                                   wo_sb[:, cc * 512:(cc + 1) * 512],
                                   start=(cc == 16), stop=(cc == 23))
                mmi.then_inc(sm, 1)
                v.wait(sm)
                v.wait(sd)
                v.tensor_copy(stoF[:, par * 512:(par + 1) * 512], ps_fs[par][:]).then_inc(sv, 1)
                v.tensor_copy(sto[:, par * 512:(par + 1) * 512], ps_os[par][:]).then_inc(sv, 1)
                y.wait(sv)
                y.dma_start(partF_out[qt * 128:(qt + 1) * 128, db * 512:(db + 1) * 512],
                            stoF[:, par * 512:(par + 1) * 512]).then_inc(sd, 16)
                y.dma_start(part_out[qt * 128:(qt + 1) * 128, db * 512:(db + 1) * 512],
                            sto[:, par * 512:(par + 1) * 512]).then_inc(sd, 16)
            y.wait(sd)
            bk.run(nc)



def build_stageA():
    nc = bass.Bass("TRN2", target_bir_lowering=False)
    ex = _declare(nc, STAGEA_IN, "ExternalInput")
    part = nc.dram_tensor("part", [T, 2048], BF16, kind="ExternalOutput").ap()
    partF = nc.dram_tensor("partF", [T, 2048], BF16, kind="ExternalOutput").ap()
    sp = {
        "sp_qf": nc.dram_tensor("sp_qf", [8 * 128, T], BF16).ap(),
        "sp_kf": nc.dram_tensor("sp_kf", [8 * 128, T], BF16).ap(),
        "sp_vtm": nc.dram_tensor("sp_vtm", [16 * 128, 1024], BF16).ap(),
        "sp_avnT": nc.dram_tensor("sp_avnT", [16, 24, 128, 128], BF16).ap(),
    }
    with ExitStack() as st:
        emit_stageA(nc, st, ex, part, partF, sp)
    nc.finalize()
    return nc




STAGEB_IN = [
    ("xTr", (D, 1024), F32),
    ("partAr", (1024, 2048), BF16),
    ("partBr", (1024, 2048), BF16),
    ("pfAr", (1024, 2048), BF16),
    ("pfBr", (1024, 2048), BF16),
    ("A_t", (16, 128, 16), F32),
    ("Bmat", (16, 8192), F32),
    ("ln2w_col", (128, 16), F32),
    ("wi4", (16, 128, 16, 512), F32),
    ("wu4", (16, 128, 16, 512), F32),
    ("wd4", (16, 128, 64, 128), F32),
    ("ones_bf", (128, 1), BF16),
    ("ones_f32", (128, 1), F32),
    ("onesrow_f32", (1, 128), F32),
]

TB = 1024
B_UPTO = 99


def emit_stageB(nc, st, ex, x1_out, fused=False, x1bf_out=None):
    A = lambda nm, sh, dt, side="left": st.enter_context(nc.sbuf_tensor(nm, list(sh), dt, side=side))
    onesb = A("onbB", (128, 1), BF16)
    onesf = A("onfB", (128, 1), F32)
    onesrow = A("onrB", (1, 128), F32)
    epsb = A("epsbB", (128, 2), F32)
    ln2w = A("l2wB", (128, 16), F32)
    mean_row = A("mrB", (1, TB), F32)
    rstd_row = A("rstdB", (1, TB), F32)
    srow = A("srowB", (1, TB), F32)
    mod_row = A("modB", (16, TB), F32)
    modp = A("modpB", (16, TB), BF16)
    h_bf = A("hB", (128, 16 * TB), BF16)
    bcA = A("bcAB", (128, TB), F32)
    bcB = A("bcBB", (128, TB), F32)
    bcC = A("bcCB", (128, TB), F32)

    sd = Ct(nc.alloc_semaphore("sdB"))
    sm = Ct(nc.alloc_semaphore("smB"))
    sv = Ct(nc.alloc_semaphore("svB"))
    ss = Ct(nc.alloc_semaphore("ssB"))
    sg = Ct(nc.alloc_semaphore("sgB"))

    bk = Blk(); y, g, t, v, s = bk.engines()
    y.dma_start(onesb[:], ex["ones_bf"][:]).then_inc(sd, 16)
    y.dma_start(onesf[:], ex["ones_f32"][:]).then_inc(sd, 16)
    y.dma_start(onesrow[:], ex["onesrow_f32"][:]).then_inc(sd, 16)
    y.dma_start(ln2w[:], ex["ln2w_col"][:]).then_inc(sd, 16)
    v.memset(epsb[:, 0:1], 1e-5)
    v.memset(epsb[:, 1:2], 1e-6)
    y.wait(sd)
    bk.run(nc)

    with nc.sbuf_tensor("tp0B", [128, TB], BF16, side="right") as tp0, \
         nc.sbuf_tensor("tp1B", [128, TB], BF16, side="right") as tp1, \
         nc.sbuf_tensor("tpFB", [128, TB], BF16, side="right") as tpF, \
         nc.sbuf_tensor("AtB", [128, 16 * 16], BF16, side="right") as A_sb, \
         nc.psum_tensor("psModB", [16, TB], F32) as ps_mod:
        bk = Blk(); y, g, t, v, s = bk.engines()
        for cc in range(16):
            g.dma_start(A_sb[:, cc * 16:(cc + 1) * 16], ex["A_t"][cc]).then_inc(sd, 16)
        for cc in range(16):
            if cc >= 1:
                y.wait(sm)
            if fused:
                y.dma_start(tpF[:], ex["rs_pf"][:, cc * 128:(cc + 1) * 128], transpose=True).then_inc(sd, 16)
                t.wait(sd)
            else:
                y.dma_start(tp0[:], ex["pfAr"][:, cc * 128:(cc + 1) * 128], transpose=True).then_inc(sd, 16)
                y.dma_start(tp1[:], ex["pfBr"][:, cc * 128:(cc + 1) * 128], transpose=True).then_inc(sd, 16)
                v.wait(sd)
                if cc >= 1:
                    v.wait(sm)
                v.tensor_tensor(tpF[:], tp0[:], tp1[:], ALU.add).then_inc(sv, 1)
                t.wait(sv)
            mmi = None
            for half in range(2):
                mmi = t.matmul(ps_mod[:, half * 512:(half + 1) * 512],
                               A_sb[:, cc * 16:(cc + 1) * 16],
                               tpF[:, half * 512:(half + 1) * 512],
                               start=(cc == 0), stop=(cc == 15))
            mmi.then_inc(sm, 1)
        bk.run(nc)
        bk = Blk(); y, g, t, v, s = bk.engines()
        s.activation(mod_row[:], ps_mod[:], AF.Tanh).then_inc(ss, 1)
        g.wait(ss)
        g.tensor_scalar(mod_row[:], mod_row[:], 0.1, 1.0, ALU.mult, op1=ALU.mult).then_inc(sg, 1)
        bk.run(nc)

    if B_UPTO <= 1:
        return
    with nc.sbuf_tensor("xldB", [128, TB], F32, side="right") as xld, \
         nc.sbuf_tensor("scrB0", [128, TB], BF16, side="right") as scr0, \
         nc.sbuf_tensor("scrB1", [128, TB], BF16, side="right") as scr1:
        with nc.psum_tensor("psSumB", [1, TB], F32) as ps_sum, \
             nc.psum_tensor("psSqB", [1, TB], F32) as ps_sq:
            bk = Blk(); y, g, t, v, s = bk.engines()
            for cc in range(16):
                if cc >= 1:
                    y.wait(sv); y.wait(ss)
                y.dma_start(xld[:], ex["xTr"][cc * 128:(cc + 1) * 128, :]).then_inc(sd, 16)
                v.wait(sd)
                if cc >= 1:
                    v.wait(sm)
                v.tensor_copy(scr0[:], xld[:]).then_inc(sv, 1)
                s.wait(sd)
                if cc >= 1:
                    s.wait(sm)
                s.activation(scr1[:], xld[:], AF.Square).then_inc(ss, 1)
                t.wait(sv); t.wait(ss)
                for j in range(2):
                    t.matmul(ps_sum[0:1, j * 512:(j + 1) * 512], onesb[:], scr0[:, j * 512:(j + 1) * 512],
                             start=(cc == 0), stop=(cc == 15))
                mmi = None
                for j in range(2):
                    mmi = t.matmul(ps_sq[0:1, j * 512:(j + 1) * 512], onesb[:], scr1[:, j * 512:(j + 1) * 512],
                                   start=(cc == 0), stop=(cc == 15))
                mmi.then_inc(sm, 1)
            bk.run(nc)
            bk = Blk(); y, g, t, v, s = bk.engines()
            v.tensor_scalar(mean_row[:], ps_sum[0:1, :], 1.0 / 2048.0, None, ALU.mult)
            v.tensor_scalar(srow[:], ps_sq[0:1, :], 1.0 / 2048.0, None, ALU.mult).then_inc(sv, 1)
            bk.run(nc)
            bk = Blk(); y, g, t, v, s = bk.engines()
            v.tensor_tensor(ps_sum[0:1, :], mean_row[:], mean_row[:], ALU.mult).then_inc(sv, 1)
            bk.run(nc)
            bk = Blk(); y, g, t, v, s = bk.engines()
            v.tensor_tensor(srow[:], srow[:], ps_sum[0:1, :], ALU.subtract).then_inc(sv, 1)
            s.wait(sv)
            s.activation(ps_sum[0:1, :], srow[:], AF.Sqrt, bias=epsb[0:1, 0:1]).then_inc(ss, 1)
            bk.run(nc)
            bk = Blk(); y, g, t, v, s = bk.engines()
            v.reciprocal(rstd_row[:], ps_sum[0:1, :]).then_inc(sv, 1)
            bk.run(nc)
        with nc.psum_tensor("psBcB", [128, 1024], F32) as ps_bc:
            bk = Blk(); y, g, t, v, s = bk.engines()
            for j in range(2):
                t.wait(sv)
                t.matmul(ps_bc[:, 0:512], onesrow[:], mean_row[0:1, j * 512:(j + 1) * 512], start=True, stop=True)
                t.matmul(ps_bc[:, 512:1024], onesrow[:], rstd_row[0:1, j * 512:(j + 1) * 512], start=True, stop=True).then_inc(sm, 1)
                v.wait(sm)
                v.tensor_copy(bcA[:, j * 512:(j + 1) * 512], ps_bc[:, 0:512])
                v.tensor_copy(bcB[:, j * 512:(j + 1) * 512], ps_bc[:, 512:1024]).then_inc(sv, 1)
            bk.run(nc)

            with nc.psum_tensor("psS2B", [1, TB], F32) as ps_s2:
                bk = Blk(); y, g, t, v, s = bk.engines()
                for cc in range(16):
                    if cc >= 1:
                        y.wait(sg)
                    y.dma_start(xld[:], ex["xTr"][cc * 128:(cc + 1) * 128, :]).then_inc(sd, 16)
                    v.wait(sd)
                    if cc >= 1:
                        v.wait(sm)
                    v.tensor_tensor(xld[:], xld[:], bcA[:], ALU.subtract).then_inc(sv, 1)
                    g.wait(sv)
                    if cc >= 1:
                        g.wait(ss)
                    g.tensor_tensor(scr0[:], xld[:], bcB[:], ALU.mult).then_inc(sg, 1)
                    s.wait(sg)
                    if cc >= 1:
                        s.wait(sm)
                    s.activation(scr1[:], scr0[:], AF.Square).then_inc(ss, 1)
                    t.wait(ss)
                    mmi = None
                    for j in range(2):
                        mmi = t.matmul(ps_s2[0:1, j * 512:(j + 1) * 512], onesb[:], scr1[:, j * 512:(j + 1) * 512],
                                       start=(cc == 0), stop=(cc == 15))
                    mmi.then_inc(sm, 1)
                bk.run(nc)
                bk = Blk(); y, g, t, v, s = bk.engines()
                v.tensor_scalar(srow[:], ps_s2[0:1, :], 1.0 / 2048.0, None, ALU.mult).then_inc(sv, 1)
                s.wait(sv)
                s.activation(ps_s2[0:1, :], srow[:], AF.Sqrt, bias=epsb[0:1, 1:2]).then_inc(ss, 1)
                bk.run(nc)
                bk = Blk(); y, g, t, v, s = bk.engines()
                v.reciprocal(srow[:], ps_s2[0:1, :]).then_inc(sv, 1)
                for j in range(2):
                    t.wait(sv)
                    t.matmul(ps_bc[:, 0:512], onesrow[:], srow[0:1, j * 512:(j + 1) * 512], start=True, stop=True).then_inc(sm, 1)
                    v.wait(sm)
                    v.tensor_copy(bcC[:, j * 512:(j + 1) * 512], ps_bc[:, 0:512]).then_inc(sv, 1)
                bk.run(nc)

            bk = Blk(); y, g, t, v, s = bk.engines()
            for cc in range(16):
                if cc >= 1:
                    y.wait(sg)
                y.dma_start(xld[:], ex["xTr"][cc * 128:(cc + 1) * 128, :]).then_inc(sd, 16)
                v.wait(sd)
                v.tensor_tensor(xld[:], xld[:], bcA[:], ALU.subtract).then_inc(sv, 1)
                g.wait(sv)
                g.tensor_tensor(scr0[:], xld[:], bcB[:], ALU.mult).then_inc(sg, 1)
                v.wait(sg)
                v.tensor_scalar(scr1[:], scr0[:], ln2w[:, cc:cc + 1], None, ALU.mult).then_inc(sv, 1)
                g.wait(sv)
                g.tensor_tensor(h_bf[:, cc * TB:(cc + 1) * TB], scr1[:], bcC[:], ALU.mult).then_inc(sg, 1)
            bk.run(nc)

        if B_UPTO <= 14:
            return
        bk = Blk(); y, g, t, v, s = bk.engines()
        v.tensor_scalar(srow[:], rstd_row[:], 1.0, 1.0, ALU.mult, op1=ALU.add).then_inc(sv, 1)
        g.tensor_tensor(mean_row[:], mean_row[:], rstd_row[:], ALU.mult).then_inc(sg, 1)
        bk.run(nc)
        with nc.psum_tensor("psBcB2", [128, 1024], F32) as ps_bc2:
            bk = Blk(); y, g, t, v, s = bk.engines()
            for j in range(2):
                t.matmul(ps_bc2[:, 0:512], onesrow[:], srow[0:1, j * 512:(j + 1) * 512], start=True, stop=True)
                t.matmul(ps_bc2[:, 512:1024], onesrow[:], mean_row[0:1, j * 512:(j + 1) * 512], start=True, stop=True).then_inc(sm, 1)
                v.wait(sm)
                v.tensor_copy(bcA[:, j * 512:(j + 1) * 512], ps_bc2[:, 0:512])
                v.tensor_copy(bcB[:, j * 512:(j + 1) * 512], ps_bc2[:, 512:1024]).then_inc(sv, 1)
                if j == 0:
                    t.wait(sv)
            bk.run(nc)

    if B_UPTO <= 2:
        return
    with nc.sbuf_tensor("AtB2", [128, 16 * 16], BF16, side="right") as A_sb, \
         nc.psum_tensor("psHaB", [16, TB], F32) as ps_ha:
        bk = Blk(); y, g, t, v, s = bk.engines()
        for cc in range(16):
            g.dma_start(A_sb[:, cc * 16:(cc + 1) * 16], ex["A_t"][cc]).then_inc(sd, 16)
        t.wait(sd)
        mmi = None
        for half in range(2):
            for cc in range(16):
                mmi = t.matmul(ps_ha[:, half * 512:(half + 1) * 512],
                               A_sb[:, cc * 16:(cc + 1) * 16],
                               h_bf[:, cc * TB + half * 512:cc * TB + (half + 1) * 512],
                               start=(cc == 0), stop=(cc == 15))
        mmi.then_inc(sm, 1)
        v.wait(sm)
        v.tensor_tensor(modp[:], mod_row[:], ps_ha[:], ALU.mult).then_inc(sv, 1)
        bk.run(nc)

    if B_UPTO <= 3:
        return
    with nc.sbuf_tensor("fuseB", [128, 64 * 512], BF16, side="right") as fuse, \
         nc.sbuf_tensor("gsbB", [128, 512], BF16, side="right") as gate_sb, \
         nc.psum_tensor("psGB", [128, 512], F32) as ps_g, \
         nc.psum_tensor("psUB", [128, 512], F32) as ps_u, \
         nc.psum_tensor("psDB", [128, 512], F32) as ps_d0, \
         nc.psum_tensor("psDB2", [128, 512], F32) as ps_d1:
        ps_ds = [ps_d0, ps_d1]
        for sl in range(2):
          with nc.sbuf_tensor(f"wiB{sl}", [128, 2 * 16 * 512], BF16, side="right") as wbi2, \
               nc.sbuf_tensor(f"wuB{sl}", [128, 2 * 16 * 512], BF16, side="right") as wbu2, \
               nc.sbuf_tensor(f"BmB{sl}", [16, 2 * 512], BF16, side="right") as Bm2:
              bk = Blk(); y, g, t, v, s = bk.engines()
              mm_evt = []
              for og in range(16):
                  par = og % 2
                  wbi = wbi2[:, par * 16 * 512:(par + 1) * 16 * 512]
                  wbu = wbu2[:, par * 16 * 512:(par + 1) * 16 * 512]
                  Bm_sb = Bm2[0:16, par * 512:(par + 1) * 512]
                  if og >= 2:
                      g.wait(sm, mm_evt[og - 2])
                  g.dma_start(wbi.rearrange("p (k c) -> p k c", k=16), ex["wi4"][og]).then_inc(sd, 16)
                  g.dma_start(wbu.rearrange("p (k c) -> p k c", k=16), ex["wu4"][og]).then_inc(sd, 16)
                  g.dma_start(Bm2[0:16, par * 512:(par + 1) * 512], ex["Bmat"][:, og * 512:(og + 1) * 512]).then_inc(sd, 16)
                  load_evt = sd.n
                  t.wait(sd, load_evt)
                  for oc4 in range(4):
                      oc = og * 4 + oc4
                      t.wait(ss)
                      mmi = None
                      for kc in range(16):
                          mmi = t.matmul(ps_g[:], wbi[:, kc * 512 + oc4 * 128:kc * 512 + (oc4 + 1) * 128],
                                         h_bf[:, kc * TB + sl * 512:kc * TB + (sl + 1) * 512],
                                         start=(kc == 0), stop=False)
                      mmi = t.matmul(ps_g[:], Bm_sb[0:16, oc4 * 128:(oc4 + 1) * 128],
                                     modp[0:16, sl * 512:(sl + 1) * 512], start=False, stop=True)
                      mmi.then_inc(sm, 1)
                      t.wait(sv)
                      mmi = None
                      for kc in range(16):
                          mmi = t.matmul(ps_u[:], wbu[:, kc * 512 + oc4 * 128:kc * 512 + (oc4 + 1) * 128],
                                         h_bf[:, kc * TB + sl * 512:kc * TB + (sl + 1) * 512],
                                         start=(kc == 0), stop=(kc == 15))
                      mmi.then_inc(sm, 1)
                      s.wait(sm)
                      s.activation(gate_sb[:], ps_g[:], AF.Silu).then_inc(ss, 1)
                      v.wait(ss)
                      v.tensor_tensor(fuse[:, oc * 512:(oc + 1) * 512], gate_sb[:], ps_u[:], ALU.mult).then_inc(sv, 1)
                  mm_evt.append(sm.n)
              bk.run(nc)
          with nc.sbuf_tensor(f"wdB{sl}", [128, 64 * 128], BF16, side="right") as wbd, \
               nc.sbuf_tensor(f"t0B{sl}", [128, 512], F32, side="right") as t0f, \
               nc.sbuf_tensor(f"t1B{sl}", [128, 512], F32, side="right") as t1f, \
               nc.sbuf_tensor(f"xslB{sl}", [128, 512], F32, side="right") as xsl, \
               nc.sbuf_tensor(f"atB{sl}", [128, 2 * 512], BF16, side="right") as atp, \
               nc.sbuf_tensor(f"stg1B{sl}", [128, 512], F32, side="right") as stg1:
            for dc in range(16):
                bk = Blk(); y, g, t, v, s = bk.engines()
                g.wait(sm)
                g.dma_start(wbd[:].rearrange("p (k c) -> p k c", k=64), ex["wd4"][dc]).then_inc(sd, 16)
                y.dma_start(xsl[:], ex["xTr"][dc * 128:(dc + 1) * 128, sl * 512:(sl + 1) * 512]).then_inc(sd, 16)
                if fused:
                    y.dma_start(atp[:, 0:512], ex["rs_part"][sl * 512:(sl + 1) * 512, dc * 128:(dc + 1) * 128],
                                transpose=True).then_inc(sd, 16)
                else:
                    y.dma_start(atp[:, 0:512], ex["partAr"][sl * 512:(sl + 1) * 512, dc * 128:(dc + 1) * 128],
                                transpose=True).then_inc(sd, 16)
                    y.dma_start(atp[:, 512:1024], ex["partBr"][sl * 512:(sl + 1) * 512, dc * 128:(dc + 1) * 128],
                                transpose=True).then_inc(sd, 16)
                par = dc % 2
                t.wait(sd)
                t.wait(sv)
                mmi = None
                for oc in range(64):
                    mmi = t.matmul(ps_ds[par][:], wbd[:, oc * 128:(oc + 1) * 128],
                                   fuse[:, oc * 512:(oc + 1) * 512],
                                   start=(oc == 0), stop=(oc == 63))
                mmi.then_inc(sm, 1)
                v.wait(sm)
                v.tensor_tensor(t0f[:], ps_ds[par][:], atp[:, 0:512], ALU.add).then_inc(sv, 1)
                if not fused:
                    g.wait(sv)
                    g.tensor_tensor(t0f[:], t0f[:], atp[:, 512:1024], ALU.add).then_inc(sg, 1)
                g.wait(sv)
                g.tensor_tensor(t1f[:], xsl[:], bcA[:, sl * 512:(sl + 1) * 512], ALU.mult).then_inc(sg, 1)
                v.wait(sg)
                v.tensor_tensor(t0f[:], t0f[:], t1f[:], ALU.add).then_inc(sv, 1)
                g.wait(sv)
                g.wait(sd)
                g.tensor_tensor(stg1[:], t0f[:], bcB[:, sl * 512:(sl + 1) * 512], ALU.subtract).then_inc(sg, 1)
                y.wait(sg)
                y.dma_start(x1_out[dc * 128:(dc + 1) * 128, sl * 512:(sl + 1) * 512], stg1[:]).then_inc(sd, 16)
                if fused:
                    v.wait(sg)
                    v.wait(sd)
                    v.tensor_copy(atp[:, 512:1024], stg1[:]).then_inc(sv, 1)
                    y.wait(sv)
                    y.dma_start(x1bf_out[dc * 128:(dc + 1) * 128, sl * 512:(sl + 1) * 512],
                                atp[:, 512:1024]).then_inc(sd, 16)
                bk.run(nc)
        bk = Blk(); y, g, t, v, s = bk.engines()
        y.wait(sd)
        bk.run(nc)


def build_stageB():
    nc = bass.Bass("TRN2", target_bir_lowering=False)
    ex = _declare(nc, STAGEB_IN, "ExternalInput")
    x1r = nc.dram_tensor("x1r", [D, 1024], F32, kind="ExternalOutput").ap()
    with ExitStack() as st:
        emit_stageB(nc, st, ex, x1r)
    nc.finalize()
    return nc



STAGEC_IN = [
    ("x1T", (D, T), F32),
    ("w1a", (19, 128, 16, 128), F32),
    ("w1z", (16, 128, 16, 128), F32),
    ("convw", (18, 128, 4), F32),
    ("convb", (18, 128, 1), F32),
    ("Alog", (16, 1), F32),
    ("dtb", (16, 1), F32),
    ("Dvec", (16, 1), F32),
    ("sel16", (16, 16, 128), F32),
    ("sel127", (128, 128), F32),
    ("maskT", (2, 128, 256), F32),
    ("ident_bf", (128, 128), BF16),
    ("ident_f32", (128, 128), F32),
    ("ones_bf", (128, 1), BF16),
    ("ones_f32", (128, 1), F32),
    ("onesrow_f32", (1, 128), F32),
]

TP = 2051


def emit_stageC(nc, st, ex, yg_out, ssq_out, sp, fused=False):
    A = lambda nm, sh, dt, side="left": st.enter_context(nc.sbuf_tensor(nm, list(sh), dt, side=side))
    identb = A("idbC", (128, 128), BF16)
    identf = A("idfC", (128, 128), F32)
    onesb = A("onbC", (128, 1), BF16)
    onesf = A("onfC", (128, 1), F32)
    onesrow = A("onrC", (1, 128), F32)
    epsb = A("epsbC", (128, 4), F32)
    sel16 = A("selC", (16, 16 * 128), F32)
    sel127 = A("s127C", (128, 128), F32)
    maskT = A("mskC", (128, 2 * 256), F32)
    avec = A("avecC", (16, 4), F32)
    DT_b = A("dtbbC", (128, 16), F32)
    r0 = A("r0C", (16, T), F32)
    r1 = A("r1C", (16, T), F32)
    AcsT = A("acstC", (128, 256), F32)
    nAcsT = A("nacstC", (128, 256), F32)
    dtT = A("dttC", (128, 256), F32)
    S_prev = A("sprevC", (64, 16 * 128), F32)
    S_prev_bf = A("sprevbC", (64, 16 * 128), BF16)
    S_tmp = A("stmpC", (64, 128), F32)

    sd = Ct(nc.alloc_semaphore("sdC"))
    sm = Ct(nc.alloc_semaphore("smC"))
    sv = Ct(nc.alloc_semaphore("svC"))
    ss = Ct(nc.alloc_semaphore("ssC"))
    sg = Ct(nc.alloc_semaphore("sgC"))

    bk = Blk(); y, g, t, v, s = bk.engines()
    y.dma_start(identb[:], ex["ident_bf"][:]).then_inc(sd, 16)
    y.dma_start(identf[:], ex["ident_f32"][:]).then_inc(sd, 16)
    y.dma_start(onesb[:], ex["ones_bf"][:]).then_inc(sd, 16)
    y.dma_start(onesf[:], ex["ones_f32"][:]).then_inc(sd, 16)
    y.dma_start(onesrow[:], ex["onesrow_f32"][:]).then_inc(sd, 16)
    for h in range(16):
        y.dma_start(sel16[:, h * 128:(h + 1) * 128], ex["sel16"][h]).then_inc(sd, 16)
    y.dma_start(sel127[:], ex["sel127"][:]).then_inc(sd, 16)
    y.dma_start(maskT[:, 0:256], ex["maskT"][0]).then_inc(sd, 16)
    y.dma_start(maskT[:, 256:512], ex["maskT"][1]).then_inc(sd, 16)
    y.dma_start(avec[:, 0:1], ex["Alog"][:]).then_inc(sd, 16)
    y.dma_start(avec[:, 1:2], ex["dtb"][:]).then_inc(sd, 16)
    y.dma_start(avec[:, 3:4], ex["Dvec"][:]).then_inc(sd, 16)
    v.memset(epsb[:, 0:1], 1e-5)
    v.memset(epsb[:, 1:2], 1.0)
    v.memset(epsb[:, 2:3], 1.1920929e-07)
    y.wait(sd)
    bk.run(nc)

    with nc.psum_tensor("psC0", [128, 32], F32) as ps0:
        bk = Blk(); y, g, t, v, s = bk.engines()
        s.activation(avec[0:16, 2:3], avec[0:16, 0:1], AF.Exp).then_inc(ss, 1)
        g.wait(ss)
        g.tensor_scalar(avec[0:16, 2:3], avec[0:16, 2:3], -1.0, 1.0, ALU.mult, op1=ALU.mult).then_inc(sg, 1)
        t.wait(sg)
        t.transpose(ps0[0:1, 0:16], avec[0:16, 3:4], identf[0:16, 0:16]).then_inc(sm, 1)
        v.wait(sm)
        v.tensor_copy(r0[0:1, 0:16], ps0[0:1, 0:16]).then_inc(sv, 1)
        t.wait(sv)
        t.matmul(ps0[:, 16:32], onesrow[:], r0[0:1, 0:16], start=True, stop=True).then_inc(sm, 1)
        v.wait(sm)
        v.tensor_copy(DT_b[:], ps0[:, 16:32]).then_inc(sv, 1)
        bk.run(nc)

    xsbc_stack = ExitStack()
    xs_c = xsbc_stack.enter_context(nc.sbuf_tensor("xscC", [128, 16 * TP], BF16, side="right"))
    Bp = xsbc_stack.enter_context(nc.sbuf_tensor("bpC", [128, TP], BF16, side="right"))
    Cp = xsbc_stack.enter_context(nc.sbuf_tensor("cpC", [128, TP], BF16, side="right"))

    with nc.sbuf_tensor("xn2C", [128, 16 * T], BF16, side="right") as xn2:
        with nc.sbuf_tensor("mrC", [1, T], BF16, side="right") as mean_row, \
             nc.sbuf_tensor("orbC", [1, 128], BF16, side="right") as onesrow_bf, \
             nc.sbuf_tensor("swC", [1, T], F32, side="right") as srow, \
             nc.sbuf_tensor("bcAC", [128, T], BF16, side="right") as bcA, \
             nc.sbuf_tensor("bcBC", [128, T], BF16, side="right") as bcB, \
             nc.sbuf_tensor("xldC", [128, T], BF16 if fused else F32, side="right") as xld:
            with nc.psum_tensor("psSumC", [1, T], F32) as ps_sum, \
                 nc.psum_tensor("psSqC", [1, T], F32) as ps_sq:
                bk = Blk(); y, g, t, v, s = bk.engines()
                g.dma_start(onesrow_bf[:], ex["onesrow_f32"][:]).then_inc(sd, 16)
                for cc in range(16):
                    if cc >= 1:
                        y.wait(sv); y.wait(ss)
                    if fused:
                        y.dma_start(xld[:, 0:1024], ex["ag1"][cc * 128:(cc + 1) * 128, :]).then_inc(sd, 16)
                        y.dma_start(xld[:, 1024:2048], ex["ag1"][2048 + cc * 128:2048 + (cc + 1) * 128, :]).then_inc(sd, 16)
                    else:
                        y.dma_start(xld[:], ex["x1T"][cc * 128:(cc + 1) * 128, :]).then_inc(sd, 16)
                    v.wait(sd)
                    if cc >= 1:
                        v.wait(sm)
                    v.tensor_copy(bcB[:], xld[:]).then_inc(sv, 1)
                    s.wait(sd)
                    if cc >= 1:
                        s.wait(sm)
                    s.activation(xn2[:, cc * T:(cc + 1) * T], xld[:], AF.Square).then_inc(ss, 1)
                    t.wait(sv); t.wait(ss)
                    for j in range(4):
                        t.matmul(ps_sum[0:1, j * 512:(j + 1) * 512], onesb[:], bcB[:, j * 512:(j + 1) * 512],
                                 start=(cc == 0), stop=(cc == 15))
                    mmi = None
                    for j in range(4):
                        mmi = t.matmul(ps_sq[0:1, j * 512:(j + 1) * 512], onesb[:],
                                       xn2[:, cc * T + j * 512:cc * T + (j + 1) * 512],
                                       start=(cc == 0), stop=(cc == 15))
                    mmi.then_inc(sm, 1)
                bk.run(nc)
                bk = Blk(); y, g, t, v, s = bk.engines()
                v.tensor_scalar(mean_row[:], ps_sum[0:1, 0:T], 1.0 / 2048.0, None, ALU.mult)
                v.tensor_scalar(srow[:], ps_sq[0:1, 0:T], 1.0 / 2048.0, None, ALU.mult).then_inc(sv, 1)
                bk.run(nc)
                bk = Blk(); y, g, t, v, s = bk.engines()
                v.tensor_tensor(ps_sum[0:1, 0:T], mean_row[:], mean_row[:], ALU.mult).then_inc(sv, 1)
                bk.run(nc)
                bk = Blk(); y, g, t, v, s = bk.engines()
                v.tensor_tensor(srow[:], srow[:], ps_sum[0:1, 0:T], ALU.subtract).then_inc(sv, 1)
                s.wait(sv)
                s.activation(ps_sum[0:1, 0:T], srow[:], AF.Sqrt, bias=epsb[0:1, 0:1]).then_inc(ss, 1)
                bk.run(nc)
                bk = Blk(); y, g, t, v, s = bk.engines()
                v.reciprocal(srow[:], ps_sum[0:1, 0:T]).then_inc(sv, 1)
                bk.run(nc)
            with nc.psum_tensor("psBcC", [128, 1024], F32) as ps_bc:
                bk = Blk(); y, g, t, v, s = bk.engines()
                for j in range(4):
                    t.wait(sv)
                    t.matmul(ps_bc[:, 0:512], onesrow_bf[:], mean_row[0:1, j * 512:(j + 1) * 512], start=True, stop=True)
                    t.matmul(ps_bc[:, 512:1024], onesrow[:], srow[0:1, j * 512:(j + 1) * 512], start=True, stop=True).then_inc(sm, 1)
                    v.wait(sm)
                    v.tensor_copy(bcA[:, j * 512:(j + 1) * 512], ps_bc[:, 0:512])
                    v.tensor_copy(bcB[:, j * 512:(j + 1) * 512], ps_bc[:, 512:1024]).then_inc(sv, 1)
                bk.run(nc)
                bk = Blk(); y, g, t, v, s = bk.engines()
                for cc in range(16):
                    if cc >= 1:
                        y.wait(sg)
                    if fused:
                        y.dma_start(xld[:, 0:1024], ex["ag1"][cc * 128:(cc + 1) * 128, :]).then_inc(sd, 16)
                        y.dma_start(xld[:, 1024:2048], ex["ag1"][2048 + cc * 128:2048 + (cc + 1) * 128, :]).then_inc(sd, 16)
                    else:
                        y.dma_start(xld[:], ex["x1T"][cc * 128:(cc + 1) * 128, :]).then_inc(sd, 16)
                    v.wait(sd)
                    v.tensor_tensor(xld[:], xld[:], bcA[:], ALU.subtract).then_inc(sv, 1)
                    g.wait(sv)
                    g.tensor_tensor(xn2[:, cc * T:(cc + 1) * T], xld[:], bcB[:], ALU.mult).then_inc(sg, 1)
                bk.run(nc)

        with ExitStack() as st3, \
             nc.psum_tensor("psqC0", [128, 512], F32) as pq0, \
             nc.psum_tensor("psqC1", [128, 512], F32) as pq1:
            A3 = lambda nm, sh, dt: st3.enter_context(nc.sbuf_tensor(nm, list(sh), dt, side="right"))
            wbuf = A3("wbC", (128, 16 * 128), BF16)
            cwb = A3("cwbC", (128, 4), F32)
            cbb = A3("cbbC", (128, 1), F32)
            t1f = A3("t1fC", (128, T), F32)
            t2f = A3("t2fC", (128, T), F32)
            stq = A3("stqC", (128, T), BF16)

            for oc in range(19):
                bk = Blk(); engs = bk.engines(); y, g, t, v, s = engs
                g.wait(sm)
                g.dma_start(wbuf[:].rearrange("p (k c) -> p k c", k=16), ex["w1a"][oc]).then_inc(sd, 16)
                if oc < 18:
                    if oc >= 1:
                        y.wait(ss)
                    y.dma_start(cwb[:], ex["convw"][oc]).then_inc(sd, 16)
                    y.dma_start(cbb[:], ex["convb"][oc]).then_inc(sd, 16)
                if oc < 16:
                    dst = xs_c[:, oc * TP:(oc + 1) * TP]
                elif oc == 16:
                    dst = Bp[:]
                elif oc == 17:
                    dst = Cp[:]
                t.wait(sd)
                for tb in range(4):
                    par = tb % 2
                    t.wait(sv)
                    mmi = None
                    for kc in range(16):
                        mmi = t.matmul([pq0, pq1][par][:], wbuf[:, kc * 128:(kc + 1) * 128],
                                       xn2[:, kc * T + tb * 512:kc * T + (tb + 1) * 512],
                                       start=(kc == 0), stop=(kc == 15))
                    mmi.then_inc(sm, 1)
                    v.wait(sm)
                    if oc < 18:
                        v.tensor_copy(dst[:, 3 + tb * 512:3 + (tb + 1) * 512], [pq0, pq1][par][:]).then_inc(sv, 1)
                    else:
                        v.tensor_copy(r0[0:16, tb * 512:(tb + 1) * 512], [pq0, pq1][par][0:16, :]).then_inc(sv, 1)
                if oc < 18:
                    v.wait(sv)
                    v.wait(sd)
                    v.memset(dst[:, 0:3], 0.0)
                    v.tensor_scalar(t1f[:], dst[:, 0:T], cwb[:, 0:1], None, ALU.mult).then_inc(sv, 1)
                    g.wait(sv)
                    g.tensor_scalar(t2f[:], dst[:, 1:1 + T], cwb[:, 1:2], 1.0, ALU.mult, op1=ALU.mult).then_inc(sg, 1)
                    v.wait(sg)
                    v.tensor_tensor(t1f[:], t1f[:], t2f[:], ALU.add).then_inc(sv, 1)
                    g.wait(sv)
                    g.tensor_scalar(t2f[:], dst[:, 2:2 + T], cwb[:, 2:3], 1.0, ALU.mult, op1=ALU.mult).then_inc(sg, 1)
                    v.wait(sg)
                    v.tensor_tensor(t1f[:], t1f[:], t2f[:], ALU.add).then_inc(sv, 1)
                    g.wait(sv)
                    g.tensor_scalar(t2f[:], dst[:, 3:3 + T], cwb[:, 3:4], 1.0, ALU.mult, op1=ALU.mult).then_inc(sg, 1)
                    v.wait(sg)
                    v.tensor_tensor(t1f[:], t1f[:], t2f[:], ALU.add).then_inc(sv, 1)
                    s.wait(sv)
                    s.activation(dst[:, 3:3 + T], t1f[:], AF.Silu, bias=cbb[:]).then_inc(ss, 1)
                    v.wait(ss)
                bk.run(nc)

            for oc in range(16):
                bk = Blk(); engs = bk.engines(); y, g, t, v, s = engs
                g.wait(sm)
                g.dma_start(wbuf[:].rearrange("p (k c) -> p k c", k=16), ex["w1z"][oc]).then_inc(sd, 16)
                t.wait(sd)
                for tb in range(4):
                    par = tb % 2
                    t.wait(sv)
                    mmi = None
                    for kc in range(16):
                        mmi = t.matmul([pq0, pq1][par][:], wbuf[:, kc * 128:(kc + 1) * 128],
                                       xn2[:, kc * T + tb * 512:kc * T + (tb + 1) * 512],
                                       start=(kc == 0), stop=(kc == 15))
                    mmi.then_inc(sm, 1)
                    v.wait(sm)
                    if tb == 0:
                        v.wait(sd)
                    v.tensor_copy(stq[:, tb * 512:(tb + 1) * 512], [pq0, pq1][par][:]).then_inc(sv, 1)
                y.wait(sv)
                y.dma_start(sp["sp_z"][oc * 128:(oc + 1) * 128, :], stq[:]).then_inc(sd, 16)
                bk.run(nc)
            bk = Blk(); y, g, t, v, s = bk.engines()
            y.wait(sd)
            bk.run(nc)

    with nc.sbuf_tensor("r2C", [16, T], F32, side="right") as r2, \
         nc.sbuf_tensor("r3C", [16, T], F32, side="right") as r3:
        bk = Blk(); y, g, t, v, s = bk.engines()
        v.tensor_scalar(r1[0:16, :], r0[0:16, :], avec[0:16, 1:2], None, ALU.add).then_inc(sv, 1)
        g.wait(sv)
        g.tensor_scalar(r2[0:16, :], r1[0:16, :], 0.0, None, ALU.max).then_inc(sg, 1)
        v.wait(sg)
        v.tensor_scalar(r3[0:16, :], r2[0:16, :], 2.0, 1.0, ALU.mult, op1=ALU.mult).then_inc(sv, 1)
        g.wait(sv)
        g.tensor_tensor(r3[0:16, :], r1[0:16, :], r3[0:16, :], ALU.subtract).then_inc(sg, 1)
        s.wait(sg)
        s.activation(r3[0:16, :], r3[0:16, :], AF.Exp).then_inc(ss, 1)
        bk.run(nc)
        bk = Blk(); y, g, t, v, s = bk.engines()
        s.activation(r3[0:16, :], r3[0:16, :], AF.Ln, bias=epsb[0:16, 1:2]).then_inc(ss, 1)
        bk.run(nc)
        bk = Blk(); y, g, t, v, s = bk.engines()
        v.tensor_tensor(r0[0:16, :], r2[0:16, :], r3[0:16, :], ALU.add).then_inc(sv, 1)
        g.wait(sv)
        g.tensor_scalar(r1[0:16, :], r0[0:16, :], avec[0:16, 2:3], 1.0, ALU.mult, op1=ALU.mult).then_inc(sg, 1)
        bk.run(nc)
        src, dst_ = r1, r2
        for si, sh in enumerate((1, 2, 4, 8, 16, 32, 64, 128)):
            bk = Blk(); y, g, t, v, s = bk.engines()
            for c in range(8):
                base = c * 256
                v.tensor_tensor(dst_[0:16, base + sh:base + 256], src[0:16, base + sh:base + 256],
                                src[0:16, base:base + 256 - sh], ALU.add)
                v.tensor_copy(dst_[0:16, base:base + sh], src[0:16, base:base + sh])
            bk.run(nc)
            src, dst_ = dst_, src
        Acs = src

        with nc.psum_tensor("psTrC", [128, 64], F32) as ps_trf:
            bk = Blk(); y, g, t, v, s = bk.engines()
            for ts_ in range(16):
                if ts_ >= 1:
                    t.wait(sv)
                t.transpose(ps_trf[:, 0:16], Acs[0:16, ts_ * 128:(ts_ + 1) * 128], identf[0:16, 0:16])
                tri = t.transpose(ps_trf[:, 16:32], r0[0:16, ts_ * 128:(ts_ + 1) * 128], identf[0:16, 0:16])
                tri.then_inc(sm, 1)
                v.wait(sm)
                v.tensor_copy(AcsT[:, ts_ * 16:(ts_ + 1) * 16], ps_trf[:, 0:16])
                v.tensor_copy(dtT[:, ts_ * 16:(ts_ + 1) * 16], ps_trf[:, 16:32]).then_inc(sv, 1)
            bk.run(nc)
        bk = Blk(); y, g, t, v, s = bk.engines()
        v.tensor_scalar(nAcsT[:], AcsT[:], -1.0, None, ALU.mult).then_inc(sv, 1)
        bk.run(nc)



    y_buf = A("ybufC", (128, 16 * T), BF16)
    bk = Blk(); y, g, t, v, s = bk.engines()
    v.memset(S_prev[:], 0.0)
    v.memset(S_prev_bf[:], 0.0)
    bk.run(nc)

    with ExitStack() as st4, \
         nc.psum_tensor("psLC", [128, 512], F32) as ps_L, \
         nc.psum_tensor("psYC", [128, 256], F32) as ps_Y, \
         nc.psum_tensor("psYoC", [128, 256], F32) as ps_Yo, \
         nc.psum_tensor("psSC", [64, 128], F32) as ps_S, \
         nc.psum_tensor("psTbC", [128, 256], BF16) as ps_tb, \
         nc.psum_tensor("psMsC", [128, 64], F32) as ps_ms:
        A4 = lambda nm, sh, dt: st4.enter_context(nc.sbuf_tensor(nm, list(sh), dt, side="right"))
        B_tm = A4("btmC", (128, 2 * 64), BF16)
        CBt = A4("cbtC", (128, 512), BF16)
        L_sb = A4("lsbC", (128, 512), BF16)
        d_T = A4("dtTC2", (128, 512), BF16)
        xh_tm = A4("xhC", (128, 256), BF16)
        xd_tm = A4("xdC", (128, 256), BF16)
        xdd_tm = A4("xddC", (128, 256), BF16)
        decT = A4("decC", (128, 32), F32)
        cdec = A4("cdecC", (128, 16), F32)
        expT = A4("expTC", (128, 32), F32)
        yos = A4("yosC", (128, 256), F32)
        ycm = A4("ycmC", (128, 256), F32)
        xDq = A4("xdqC", (128, 256), BF16)
        st_sb = A4("stsbC", (64, 128), F32)

        for c in range(8):
            base = 3 + c * 256
            bk = Blk(); y, g, t, v, s = bk.engines()
            t.wait(sv); t.wait(ss)
            tri = None
            for kt in range(2):
                tri = t.transpose(ps_tb[:, kt * 64:(kt + 1) * 64],
                                  Bp[0:64, base + kt * 128:base + (kt + 1) * 128], identb[0:64, 0:64])
            tri.then_inc(sm, 1)
            v.wait(sm)
            v.tensor_copy(B_tm[:], ps_tb[:, 0:128]).then_inc(sv, 1)
            t.wait(sv)
            mmi = None
            for kt in range(2):
                mmi = t.matmul(ps_L[:, kt * 256:(kt + 1) * 256], Bp[0:64, base + kt * 128:base + (kt + 1) * 128],
                               Cp[0:64, base:base + 256], start=True, stop=True)
            mmi.then_inc(sm, 1)
            v.wait(sm)
            v.tensor_copy(CBt[:], ps_L[:]).then_inc(sv, 1)
            t.wait(sv)
            t.matmul(ps_ms[:, 0:16], sel127[:], AcsT[:, (2 * c + 1) * 16:(2 * c + 2) * 16],
                     start=True, stop=True).then_inc(sm, 1)
            v.wait(sm)
            for kt in range(2):
                v.tensor_tensor(decT[:, kt * 16:(kt + 1) * 16], ps_ms[:, 0:16],
                                AcsT[:, (2 * c + kt) * 16:(2 * c + kt + 1) * 16], ALU.subtract)
            v.tensor_copy(cdec[:], ps_ms[:, 0:16]).then_inc(sv, 1)
            s.wait(sv)
            s.activation(decT[:], decT[:], AF.Exp)
            s.activation(cdec[:], cdec[:], AF.Exp)
            for kt in range(2):
                s.activation(expT[:, kt * 16:(kt + 1) * 16], AcsT[:, (2 * c + kt) * 16:(2 * c + kt + 1) * 16],
                             AF.Exp)
            bk.run(nc)

            for h in range(16):
                bk = Blk(); y, g, t, v, s = bk.engines()
                t.wait(ss)
                mmi = None
                for kt in range(2):
                    t.matmul(ps_L[:, kt * 256:(kt + 1) * 256], sel16[:, h * 128:(h + 1) * 128],
                             Acs[0:16, c * 256:(c + 1) * 256], start=True, stop=False)
                    mmi = t.matmul(ps_L[:, kt * 256:(kt + 1) * 256], identf[:],
                                   maskT[:, kt * 256:(kt + 1) * 256], start=False, stop=True)
                mmi.then_inc(sm, 1)
                s.wait(sm)
                for kt in range(2):
                    s.activation(L_sb[:, kt * 256:(kt + 1) * 256], ps_L[:, kt * 256:(kt + 1) * 256], AF.Exp,
                                 bias=nAcsT[:, (2 * c + kt) * 16 + h:(2 * c + kt) * 16 + h + 1])
                s.activation(L_sb[0:1, 0:1], L_sb[0:1, 0:1], AF.Copy).then_inc(ss, 1)
                v.wait(ss)
                v.tensor_tensor(d_T[:], L_sb[:], CBt[:], ALU.mult).then_inc(sv, 1)
                t.wait(sg); t.wait(sv)
                tri = None
                for kt in range(2):
                    tri = t.transpose(ps_tb[:, kt * 128:(kt + 1) * 128],
                                      xs_c[:, h * TP + base + kt * 128:h * TP + base + (kt + 1) * 128], identb[:])
                tri.then_inc(sm, 1)
                v.wait(sm)
                v.tensor_copy(xh_tm[:], ps_tb[:, 0:256]).then_inc(sv, 1)
                g.wait(sv)
                for kt in range(2):
                    g.tensor_scalar(xd_tm[:, kt * 128:(kt + 1) * 128], xh_tm[:, kt * 128:(kt + 1) * 128],
                                    dtT[:, (2 * c + kt) * 16 + h:(2 * c + kt) * 16 + h + 1], 1.0, ALU.mult, op1=ALU.mult)
                bk.run(nc)
                bk = Blk(); y, g, t, v, s = bk.engines()
                for kt in range(2):
                    v.tensor_scalar(xdd_tm[:, kt * 128:(kt + 1) * 128], xd_tm[:, kt * 128:(kt + 1) * 128],
                                    decT[:, kt * 16 + h:kt * 16 + h + 1], None, ALU.mult)
                bk.run(nc)
                bk = Blk(); y, g, t, v, s = bk.engines()
                mmi = None
                for qt in range(2):
                    for kt in range(qt + 1):
                        mmi = t.matmul(ps_Y[:, qt * 128:(qt + 1) * 128],
                                       d_T[:, kt * 256 + qt * 128:kt * 256 + (qt + 1) * 128],
                                       xd_tm[:, kt * 128:(kt + 1) * 128],
                                       start=(kt == 0), stop=(kt == qt))
                mmi.then_inc(sm, 1)
                mmi = None
                for kt in range(2):
                    mmi = t.matmul(ps_S[:], B_tm[:, kt * 64:(kt + 1) * 64], xdd_tm[:, kt * 128:(kt + 1) * 128],
                                   start=(kt == 0), stop=(kt == 1))
                mmi.then_inc(sm, 1)
                mmi = None
                for qt in range(2):
                    mmi = t.matmul(ps_Yo[:, qt * 128:(qt + 1) * 128],
                                   Cp[0:64, base + qt * 128:base + (qt + 1) * 128],
                                   S_prev_bf[:, h * 128:(h + 1) * 128], start=True, stop=True)
                mmi.then_inc(sm, 1)
                s.wait(sm)
                for qt in range(2):
                    s.activation(yos[:, qt * 128:(qt + 1) * 128], ps_Yo[:, qt * 128:(qt + 1) * 128], AF.Copy,
                                 scale=expT[:, qt * 16 + h:qt * 16 + h + 1])
                s.activation(yos[0:1, 0:1], yos[0:1, 0:1], AF.Copy).then_inc(ss, 1)
                v.wait(ss)
                v.tensor_tensor(ycm[:], ps_Y[:], yos[:], ALU.add).then_inc(sv, 1)
                g.wait(sv)
                for kt in range(2):
                    g.tensor_scalar(xDq[:, kt * 128:(kt + 1) * 128], xh_tm[:, kt * 128:(kt + 1) * 128],
                                    DT_b[:, h:h + 1], 1.0, ALU.mult, op1=ALU.mult)
                g.tensor_scalar(xDq[0:1, 0:1], xDq[0:1, 0:1], 1.0, 1.0, ALU.mult, op1=ALU.mult).then_inc(sg, 1)
                v.wait(sg)
                for qt in range(2):
                    v.tensor_tensor(y_buf[:, (2 * c + qt) * T + h * 128:(2 * c + qt) * T + (h + 1) * 128],
                                    ycm[:, qt * 128:(qt + 1) * 128], xDq[:, qt * 128:(qt + 1) * 128], ALU.add)
                v.tensor_copy(st_sb[:], ps_S[:]).then_inc(sv, 1)
                g.wait(sv)
                g.tensor_scalar(S_tmp[:], S_prev[:, h * 128:(h + 1) * 128], cdec[0:64, h:h + 1], 1.0, ALU.mult, op1=ALU.mult).then_inc(sg, 1)
                v.wait(sg)
                v.tensor_tensor(S_prev[:, h * 128:(h + 1) * 128], S_tmp[:], st_sb[:], ALU.add)
                v.tensor_copy(S_prev_bf[:, h * 128:(h + 1) * 128], S_prev[:, h * 128:(h + 1) * 128]).then_inc(sv, 1)
                bk.run(nc)

    xsbc_stack.close()

    with nc.sbuf_tensor("ssqC", [128, 16], F32, side="right") as ssq_cols, \
         nc.sbuf_tensor("scr2C", [128, T], BF16, side="right") as scr2, \
         nc.sbuf_tensor("ssqoC", [16, 128], F32, side="right") as ssq_o, \
         nc.psum_tensor("psTr2C", [16, 128], F32) as ps_tr2:
        bk = Blk(); y, g, t, v, s = bk.engines()
        for tt in range(16):
            s.activation(scr2[:], y_buf[:, tt * T:(tt + 1) * T], AF.Square,
                         accum_out=ssq_cols[:, tt:tt + 1]).then_inc(ss, 1)
        t.wait(ss)
        t.transpose(ps_tr2[:], ssq_cols[:], identf[:]).then_inc(sm, 1)
        v.wait(sm)
        v.tensor_copy(ssq_o[:], ps_tr2[:]).then_inc(sv, 1)
        y.wait(sv)
        y.dma_start(ssq_out[:, :], ssq_o[:]).then_inc(sd, 16)
        bk.run(nc)

    if fused:
        ccq = Ct(nc.alloc_semaphore("ccqC"))
        bk = Blk(); y, g, t, v, s = bk.engines()
        g.wait(sd)
        g.collective_compute("AllGather", ALU.bypass, replica_groups=PAIRS,
                             ins=[ex["cc_ssq_in"]], outs=[ex["cc_ssq_out"]]).then_inc_default(ccq)
        g.wait(ccq, 1)
        bk.run(nc)
        rst_cols = st.enter_context(nc.sbuf_tensor("rstcC", [128, 16], F32, side="right"))
        with nc.sbuf_tensor("rstC", [16, 256], F32, side="right") as rst, \
             nc.psum_tensor("psRstC", [128, 16], F32) as ps_rst:
            bk = Blk(); y, g, t, v, s = bk.engines()
            y.dma_start(rst[0:16, 0:128], ex["ag3"][0:16, :]).then_inc(sd, 16)
            y.dma_start(rst[0:16, 128:256], ex["ag3"][16:32, :]).then_inc(sd, 16)
            v.wait(sd)
            v.tensor_tensor(rst[0:16, 0:128], rst[0:16, 0:128], rst[0:16, 128:256], ALU.add)
            v.tensor_scalar(rst[0:16, 0:128], rst[0:16, 0:128], 1.0 / 4096.0, None, ALU.mult).then_inc(sv, 1)
            s.wait(sv)
            s.activation(rst[0:16, 128:256], rst[0:16, 0:128], AF.Sqrt, bias=epsb[0:16, 2:3]).then_inc(ss, 1)
            v.wait(ss)
            v.reciprocal(rst[0:16, 0:128], rst[0:16, 128:256]).then_inc(sv, 1)
            t.wait(sv)
            t.transpose(ps_rst[:], rst[0:16, 0:128], identf[0:16, 0:16]).then_inc(sm, 1)
            v.wait(sm)
            v.tensor_copy(rst_cols[:], ps_rst[:]).then_inc(sv, 1)
            bk.run(nc)

        with nc.sbuf_tensor("ztmC", [128, 16 * T], BF16, side="right") as z_tm, \
             nc.sbuf_tensor("zldC", [128, T], BF16, side="right") as zld, \
             nc.sbuf_tensor("zslC", [128, T], BF16, side="right") as zsil, \
             nc.sbuf_tensor("stg2C", [128, T], BF16, side="right") as stg2, \
             nc.psum_tensor("psTr3C", [128, 512], BF16) as ps_tr3:
            for cc in range(16):
                bk = Blk(); y, g, t, v, s = bk.engines()
                y.dma_start(zld[:], sp["sp_z"][cc * 128:(cc + 1) * 128, :]).then_inc(sd, 16)
                t.wait(sd)
                for tg in range(4):
                    if tg > 0 or cc > 0:
                        t.wait(sv)
                    tri = None
                    for j2 in range(4):
                        tri = t.transpose(ps_tr3[:, j2 * 128:(j2 + 1) * 128],
                                          zld[:, (tg * 4 + j2) * 128:(tg * 4 + j2 + 1) * 128], identb[:])
                    tri.then_inc(sm, 1)
                    v.wait(sm)
                    cpi = None
                    for j2 in range(4):
                        tt = tg * 4 + j2
                        cpi = v.tensor_copy(z_tm[:, tt * T + cc * 128:tt * T + (cc + 1) * 128],
                                            ps_tr3[:, j2 * 128:(j2 + 1) * 128])
                    cpi.then_inc(sv, 1)
                bk.run(nc)
            for tt in range(16):
                bk = Blk(); y, g, t, v, s = bk.engines()
                s.activation(zsil[:], z_tm[:, tt * T:(tt + 1) * T], AF.Silu).then_inc(ss, 1)
                v.tensor_scalar(stg2[:], y_buf[:, tt * T:(tt + 1) * T], rst_cols[:, tt:tt + 1], None, ALU.mult).then_inc(sv, 1)
                g.wait(ss); g.wait(sv)
                g.tensor_tensor(y_buf[:, tt * T:(tt + 1) * T], stg2[:], zsil[:], ALU.mult).then_inc(sg, 1)
                bk.run(nc)

        with nc.sbuf_tensor("yfmC", [128, 16 * T], BF16, side="right") as y_fm, \
             nc.psum_tensor("psTr4C", [128, 512], BF16) as ps_tr4:
            for tt in range(16):
                bk = Blk(); y, g, t, v, s = bk.engines()
                for cg in range(4):
                    if tt > 0 or cg > 0:
                        t.wait(sv)
                    tri = None
                    for j2 in range(4):
                        cc = cg * 4 + j2
                        tri = t.transpose(ps_tr4[:, j2 * 128:(j2 + 1) * 128],
                                          y_buf[:, tt * T + cc * 128:tt * T + (cc + 1) * 128], identb[:])
                    tri.then_inc(sm, 1)
                    v.wait(sm)
                    cpi = None
                    for j2 in range(4):
                        cc = cg * 4 + j2
                        cpi = v.tensor_copy(y_fm[:, cc * T + tt * 128:cc * T + (tt + 1) * 128],
                                            ps_tr4[:, j2 * 128:(j2 + 1) * 128])
                    cpi.then_inc(sv, 1)
                bk.run(nc)
        with nc.sbuf_tensor("woC", [128, 16 * 512], BF16, side="right") as wo_sb, \
             nc.sbuf_tensor("stoC", [128, 2 * 512], BF16, side="right") as sto, \
             nc.psum_tensor("psOC", [128, 512], F32) as ps_o0, \
             nc.psum_tensor("psOC2", [128, 512], F32) as ps_o1:
            ps_os = [ps_o0, ps_o1]
            for db in range(4):
                bk = Blk(); y, g, t, v, s = bk.engines()
                g.wait(sm)
                g.dma_start(wo_sb[:].rearrange("p (k c) -> p k c", k=16), ex["wotC"][db]).then_inc(sd, 16)
                t.wait(sd)
                for qt in range(16):
                    par = qt % 2
                    t.wait(sv)
                    mmi = None
                    for cc in range(16):
                        mmi = t.matmul(ps_os[par][:], y_fm[:, cc * T + qt * 128:cc * T + (qt + 1) * 128],
                                       wo_sb[:, cc * 512:(cc + 1) * 512],
                                       start=(cc == 0), stop=(cc == 15))
                    mmi.then_inc(sm, 1)
                    v.wait(sm)
                    v.wait(sd)
                    v.tensor_copy(sto[:, par * 512:(par + 1) * 512], ps_os[par][:]).then_inc(sv, 1)
                    y.wait(sv)
                    y.dma_start(ex["rs2in"][qt * 128:(qt + 1) * 128, db * 512:(db + 1) * 512],
                                sto[:, par * 512:(par + 1) * 512]).then_inc(sd, 16)
                y.wait(sd)
                bk.run(nc)
    else:
        with nc.sbuf_tensor("ztmC", [128, 16 * T], BF16, side="right") as z_tm, \
             nc.sbuf_tensor("zldC", [128, T], BF16, side="right") as zld, \
             nc.sbuf_tensor("zslC", [128, T], BF16, side="right") as zsil, \
             nc.sbuf_tensor("stg2C", [128, T], BF16, side="right") as stg2, \
             nc.psum_tensor("psTr3C", [128, 512], BF16) as ps_tr3:
            for cc in range(16):
                bk = Blk(); y, g, t, v, s = bk.engines()
                y.dma_start(zld[:], sp["sp_z"][cc * 128:(cc + 1) * 128, :]).then_inc(sd, 16)
                t.wait(sd)
                for tg in range(4):
                    if tg > 0 or cc > 0:
                        t.wait(sv)
                    tri = None
                    for j in range(4):
                        tri = t.transpose(ps_tr3[:, j * 128:(j + 1) * 128],
                                          zld[:, (tg * 4 + j) * 128:(tg * 4 + j + 1) * 128], identb[:])
                    tri.then_inc(sm, 1)
                    v.wait(sm)
                    cpi = None
                    for j in range(4):
                        tt = tg * 4 + j
                        cpi = v.tensor_copy(z_tm[:, tt * T + cc * 128:tt * T + (cc + 1) * 128],
                                            ps_tr3[:, j * 128:(j + 1) * 128])
                    cpi.then_inc(sv, 1)
                bk.run(nc)
            for tt in range(16):
                bk = Blk(); y, g, t, v, s = bk.engines()
                s.activation(zsil[:], z_tm[:, tt * T:(tt + 1) * T], AF.Silu).then_inc(ss, 1)
                v.wait(ss)
                v.wait(sd)
                v.tensor_tensor(stg2[:], y_buf[:, tt * T:(tt + 1) * T], zsil[:], ALU.mult).then_inc(sv, 1)
                y.wait(sv)
                y.dma_start(yg_out[tt * 128:(tt + 1) * 128, :], stg2[:]).then_inc(sd, 16)
                bk.run(nc)
            bk = Blk(); y, g, t, v, s = bk.engines()
            y.wait(sd)
            bk.run(nc)



def build_stageC():
    nc = bass.Bass("TRN2", target_bir_lowering=False)
    ex = _declare(nc, STAGEC_IN, "ExternalInput")
    yg = nc.dram_tensor("ygated", [T, 2048], BF16, kind="ExternalOutput").ap()
    ssq = nc.dram_tensor("ssqout", [16, 128], F32, kind="ExternalOutput").ap()
    sp = {"sp_z": nc.dram_tensor("sp_z", [16 * 128, T], BF16).ap()}
    with ExitStack() as st:
        emit_stageC(nc, st, ex, yg, ssq, sp)
    nc.finalize()
    return nc



STAGED_IN = [
    ("x1rD", (D, 1024), F32),
    ("yq0", (1024, 2048), BF16),
    ("yq1", (1024, 2048), BF16),
    ("ssqa", (8, 128), F32),
    ("ssqb", (8, 128), F32),
    ("wot", (16, 128, 32, 128), F32),
    ("wg8", (16, 128, 16, 512), I8),
    ("wu8", (16, 128, 16, 512), I8),
    ("wd8", (16, 128, 64, 128), I8),
    ("sgt", (128, 64), F32),
    ("sut", (128, 64), F32),
    ("sdt", (128, 16), F32),
    ("sel8", (8, 8, 128), F32),
    ("ident_f32", (128, 128), F32),
    ("ones_bf", (128, 1), BF16),
    ("onesrow_f32", (1, 128), F32),
]


def emit_stageD(nc, st, ex, out_r, fused=False):
    A = lambda nm, sh, dt, side="left": st.enter_context(nc.sbuf_tensor(nm, list(sh), dt, side=side))
    identf = A("idfD", (128, 128), F32)
    onesb = A("onbD", (128, 1), BF16)
    onesrow = A("onrD", (1, 128), F32)
    epsb = A("epsbD", (128, 2), F32)
    sgt = A("sgtD", (128, 64), F32)
    sut = A("sutD", (128, 64), F32)
    sdt = A("sdtD", (128, 16), F32)
    ssq2 = A("ssq2D", (8, 256), F32)
    out_acc = A("oaccD", (128, 16 * 1024), F32)
    if not fused:
        d2_stack = ExitStack()
        sel8 = d2_stack.enter_context(nc.sbuf_tensor("sel8D", [8, 8 * 128], F32, side="right"))
        rstd_b = d2_stack.enter_context(nc.sbuf_tensor("rstdbD", [128, 1024], F32, side="right"))

    sd = Ct(nc.alloc_semaphore("sdD"))
    sm = Ct(nc.alloc_semaphore("smD"))
    sv = Ct(nc.alloc_semaphore("svD"))
    ss = Ct(nc.alloc_semaphore("ssD"))
    sg = Ct(nc.alloc_semaphore("sgD"))

    bk = Blk(); y, g, t, v, s = bk.engines()
    y.dma_start(identf[:], ex["ident_f32"][:]).then_inc(sd, 16)
    y.dma_start(onesb[:], ex["ones_bf"][:]).then_inc(sd, 16)
    y.dma_start(onesrow[:], ex["onesrow_f32"][:]).then_inc(sd, 16)
    if not fused:
        for f in range(8):
            y.dma_start(sel8[:, f * 128:(f + 1) * 128], ex["sel8"][f]).then_inc(sd, 16)
        y.dma_start(ssq2[0:8, 0:128], ex["ssqa"][:]).then_inc(sd, 16)
        y.dma_start(ssq2[0:8, 128:256], ex["ssqb"][:]).then_inc(sd, 16)
    y.dma_start(sgt[:], ex["sgt"][:]).then_inc(sd, 16)
    y.dma_start(sut[:], ex["sut"][:]).then_inc(sd, 16)
    y.dma_start(sdt[:], ex["sdt"][:]).then_inc(sd, 16)
    v.memset(epsb[:, 0:1], 1e-5)
    v.memset(epsb[:, 1:2], 1.1920929e-07)
    y.wait(sd)
    bk.run(nc)

    if fused:
        with nc.sbuf_tensor("tpD", [128, 2 * 1024], BF16, side="right") as tp, \
             nc.sbuf_tensor("xslD0", [128, 2 * 1024], F32, side="right") as xsl:
            bk = Blk(); y, g, t, v, s = bk.engines()
            for Dc in range(16):
                par = Dc % 2
                if Dc >= 2:
                    y.wait(sv)
                y.dma_start(tp[:, par * 1024:(par + 1) * 1024],
                            ex["rs2"][:, Dc * 128:(Dc + 1) * 128], transpose=True).then_inc(sd, 16)
                y.dma_start(xsl[:, par * 1024:(par + 1) * 1024],
                            ex["x1rD"][Dc * 128:(Dc + 1) * 128, :]).then_inc(sd, 16)
                v.wait(sd)
                v.tensor_tensor(out_acc[:, Dc * 1024:(Dc + 1) * 1024], tp[:, par * 1024:(par + 1) * 1024],
                                xsl[:, par * 1024:(par + 1) * 1024], ALU.add).then_inc(sv, 1)
            bk.run(nc)
    else:
        with nc.psum_tensor("psD1", [128, 128], F32) as ps1:
            bk = Blk(); y, g, t, v, s = bk.engines()
            v.tensor_tensor(ssq2[0:8, 0:128], ssq2[0:8, 0:128], ssq2[0:8, 128:256], ALU.add)
            v.tensor_scalar(ssq2[0:8, 0:128], ssq2[0:8, 0:128], 1.0 / 4096.0, None, ALU.mult).then_inc(sv, 1)
            s.wait(sv)
            s.activation(ssq2[0:8, 128:256], ssq2[0:8, 0:128], AF.Sqrt, bias=epsb[0:8, 1:2]).then_inc(ss, 1)
            v.wait(ss)
            v.reciprocal(ssq2[0:8, 0:128], ssq2[0:8, 128:256]).then_inc(sv, 1)
            for f in range(8):
                t.wait(sv)
                t.matmul(ps1[:], sel8[0:8, f * 128:(f + 1) * 128], ssq2[0:8, 0:128], start=True, stop=True).then_inc(sm, 1)
                v.wait(sm)
                v.tensor_copy(rstd_b[:, f * 128:(f + 1) * 128], ps1[:]).then_inc(sv, 1)
            bk.run(nc)

        with nc.sbuf_tensor("yfmD", [128, 32 * 1024], BF16, side="right") as y_fm, \
             nc.sbuf_tensor("wotD", [128, 32 * 128], BF16, side="right") as wbo, \
             nc.sbuf_tensor("xslD", [128, 1024], F32, side="right") as xsl, \
             nc.psum_tensor("psOD", [128, 512], F32) as ps_o0, \
             nc.psum_tensor("psOD2", [128, 512], F32) as ps_o1:
            bk = Blk(); y, g, t, v, s = bk.engines()
            for cc in range(32):
                src = ex["yq0"] if cc < 16 else ex["yq1"]
                y.dma_start(y_fm[:, cc * 1024:(cc + 1) * 1024], src[:, (cc % 16) * 128:((cc % 16) + 1) * 128],
                            transpose=True).then_inc(sd, 16)
            v.wait(sd)
            for cc in range(32):
                v.tensor_tensor(y_fm[:, cc * 1024:(cc + 1) * 1024], y_fm[:, cc * 1024:(cc + 1) * 1024],
                                rstd_b[:], ALU.mult)
            v.tensor_scalar(rstd_b[0:1, 0:1], rstd_b[0:1, 0:1], 1.0, None, ALU.mult).then_inc(sv, 1)
            bk.run(nc)
            for Dc in range(16):
                bk = Blk(); y, g, t, v, s = bk.engines()
                g.wait(sm)
                g.dma_start(wbo[:].rearrange("p (k c) -> p k c", k=32), ex["wot"][Dc]).then_inc(sd, 16)
                y.wait(sv)
                y.dma_start(xsl[:], ex["x1rD"][Dc * 128:(Dc + 1) * 128, :]).then_inc(sd, 16)
                t.wait(sd)
                for tb in range(2):
                    par = tb % 2
                    t.wait(sv)
                    mmi = None
                    for cc in range(32):
                        mmi = t.matmul([ps_o0, ps_o1][par][:], wbo[:, cc * 128:(cc + 1) * 128],
                                       y_fm[:, cc * 1024 + tb * 512:cc * 1024 + (tb + 1) * 512],
                                       start=(cc == 0), stop=(cc == 31))
                    mmi.then_inc(sm, 1)
                    v.wait(sm)
                    v.tensor_tensor(out_acc[:, Dc * 1024 + tb * 512:Dc * 1024 + (tb + 1) * 512],
                                    [ps_o0, ps_o1][par][:], xsl[:, tb * 512:(tb + 1) * 512], ALU.add).then_inc(sv, 1)
                bk.run(nc)
        d2_stack.close()


    xn2 = A("xn2D", (128, 16 * 1024), BF16)
    with nc.sbuf_tensor("mrD", [1, 1024], BF16, side="right") as mean_row, \
         nc.sbuf_tensor("orbD", [1, 128], BF16, side="right") as onesrow_bf, \
         nc.sbuf_tensor("swD", [1, 1024], F32, side="right") as srow, \
         nc.sbuf_tensor("bcAD", [128, 1024], BF16, side="right") as bcA, \
         nc.sbuf_tensor("bcBD", [128, 1024], BF16, side="right") as bcB, \
         nc.sbuf_tensor("xldD", [128, 1024], F32, side="right") as xld:
        with nc.psum_tensor("psSumD", [1, 1024], F32) as ps_sum, \
             nc.psum_tensor("psSqD", [1, 1024], F32) as ps_sq:
            bk = Blk(); y, g, t, v, s = bk.engines()
            g.dma_start(onesrow_bf[:], ex["onesrow_f32"][:]).then_inc(sd, 16)
            for cc in range(16):
                if cc >= 1:
                    y.wait(sv); y.wait(ss)
                y.dma_start(xld[:], ex["x1rD"][cc * 128:(cc + 1) * 128, :]).then_inc(sd, 16)
                v.wait(sd)
                if cc >= 1:
                    v.wait(sm)
                v.tensor_copy(bcB[:], xld[:]).then_inc(sv, 1)
                s.wait(sd)
                if cc >= 1:
                    s.wait(sm)
                s.activation(xn2[:, cc * 1024:(cc + 1) * 1024], xld[:], AF.Square).then_inc(ss, 1)
                t.wait(sv); t.wait(ss)
                for j in range(2):
                    t.matmul(ps_sum[0:1, j * 512:(j + 1) * 512], onesb[:], bcB[:, j * 512:(j + 1) * 512],
                             start=(cc == 0), stop=(cc == 15))
                mmi = None
                for j in range(2):
                    mmi = t.matmul(ps_sq[0:1, j * 512:(j + 1) * 512], onesb[:],
                                   xn2[:, cc * 1024 + j * 512:cc * 1024 + (j + 1) * 512],
                                   start=(cc == 0), stop=(cc == 15))
                mmi.then_inc(sm, 1)
            bk.run(nc)
            bk = Blk(); y, g, t, v, s = bk.engines()
            v.tensor_scalar(mean_row[:], ps_sum[0:1, :], 1.0 / 2048.0, None, ALU.mult)
            v.tensor_scalar(srow[:], ps_sq[0:1, :], 1.0 / 2048.0, None, ALU.mult).then_inc(sv, 1)
            bk.run(nc)
            bk = Blk(); y, g, t, v, s = bk.engines()
            v.tensor_tensor(ps_sum[0:1, :], mean_row[:], mean_row[:], ALU.mult).then_inc(sv, 1)
            bk.run(nc)
            bk = Blk(); y, g, t, v, s = bk.engines()
            v.tensor_tensor(srow[:], srow[:], ps_sum[0:1, :], ALU.subtract).then_inc(sv, 1)
            s.wait(sv)
            s.activation(ps_sum[0:1, :], srow[:], AF.Sqrt, bias=epsb[0:1, 0:1]).then_inc(ss, 1)
            bk.run(nc)
            bk = Blk(); y, g, t, v, s = bk.engines()
            v.reciprocal(srow[:], ps_sum[0:1, :]).then_inc(sv, 1)
            bk.run(nc)
        with nc.psum_tensor("psBcD", [128, 1024], F32) as ps_bc:
            bk = Blk(); y, g, t, v, s = bk.engines()
            for j in range(2):
                t.wait(sv)
                t.matmul(ps_bc[:, 0:512], onesrow_bf[:], mean_row[0:1, j * 512:(j + 1) * 512], start=True, stop=True)
                t.matmul(ps_bc[:, 512:1024], onesrow[:], srow[0:1, j * 512:(j + 1) * 512], start=True, stop=True).then_inc(sm, 1)
                v.wait(sm)
                v.tensor_copy(bcA[:, j * 512:(j + 1) * 512], ps_bc[:, 0:512])
                v.tensor_copy(bcB[:, j * 512:(j + 1) * 512], ps_bc[:, 512:1024]).then_inc(sv, 1)
            bk.run(nc)
            bk = Blk(); y, g, t, v, s = bk.engines()
            for cc in range(16):
                if cc >= 1:
                    y.wait(sg)
                y.dma_start(xld[:], ex["x1rD"][cc * 128:(cc + 1) * 128, :]).then_inc(sd, 16)
                v.wait(sd)
                v.tensor_tensor(xld[:], xld[:], bcA[:], ALU.subtract).then_inc(sv, 1)
                g.wait(sv)
                g.tensor_tensor(xn2[:, cc * 1024:(cc + 1) * 1024], xld[:], bcB[:], ALU.mult).then_inc(sg, 1)
            bk.run(nc)

    with nc.sbuf_tensor("fuseD", [128, 64 * 512], BF16, side="right") as fuse, \
         nc.sbuf_tensor("gsbD", [128, 512], BF16, side="right") as gate_sb, \
         nc.sbuf_tensor("usbD", [128, 512], BF16, side="right") as up_sb, \
         nc.sbuf_tensor("t0D", [128, 512], F32, side="right") as t0f, \
         nc.psum_tensor("psGD", [128, 512], F32) as ps_g, \
         nc.psum_tensor("psUD", [128, 512], F32) as ps_u, \
         nc.psum_tensor("psDD", [128, 512], F32) as ps_d0, \
         nc.psum_tensor("psDD2", [128, 512], F32) as ps_d1:
        for sl in range(2):
          with nc.sbuf_tensor(f"wgD{sl}", [128, 16 * 512], BF16, side="right") as wbg, \
               nc.sbuf_tensor(f"wuD{sl}", [128, 16 * 512], BF16, side="right") as wbu:
            bk = Blk(); y, g, t, v, s = bk.engines()
            mm_evt = []
            for og in range(16):
                if og >= 1:
                    g.wait(sm, mm_evt[og - 1])
                g.dma_start(wbg[:].rearrange("p (k c) -> p k c", k=16), ex["wg8"][og]).then_inc(sd, 16)
                g.dma_start(wbu[:].rearrange("p (k c) -> p k c", k=16), ex["wu8"][og]).then_inc(sd, 16)
                load_evt = sd.n
                t.wait(sd, load_evt)
                for oc4 in range(4):
                    oc = og * 4 + oc4
                    t.wait(ss)
                    mmi = None
                    for kc in range(16):
                        mmi = t.matmul(ps_g[:], wbg[:, kc * 512 + oc4 * 128:kc * 512 + (oc4 + 1) * 128],
                                       xn2[:, kc * 1024 + sl * 512:kc * 1024 + (sl + 1) * 512],
                                       start=(kc == 0), stop=(kc == 15))
                    mmi.then_inc(sm, 1)
                    t.wait(sv)
                    mmi = None
                    for kc in range(16):
                        mmi = t.matmul(ps_u[:], wbu[:, kc * 512 + oc4 * 128:kc * 512 + (oc4 + 1) * 128],
                                       xn2[:, kc * 1024 + sl * 512:kc * 1024 + (sl + 1) * 512],
                                       start=(kc == 0), stop=(kc == 15))
                    mmi.then_inc(sm, 1)
                    s.wait(sm)
                    s.activation(gate_sb[:], ps_g[:], AF.Silu, scale=sgt[:, oc:oc + 1]).then_inc(ss, 1)
                    v.wait(sm)
                    v.tensor_scalar(up_sb[:], ps_u[:], sut[:, oc:oc + 1], None, ALU.mult).then_inc(sv, 1)
                    g.wait(ss); g.wait(sv)
                    g.tensor_tensor(fuse[:, oc * 512:(oc + 1) * 512], gate_sb[:], up_sb[:], ALU.mult).then_inc(sg, 1)
                mm_evt.append(sm.n)
            bk.run(nc)
          with nc.sbuf_tensor(f"wdD{sl}", [128, 64 * 128], BF16, side="right") as wbd:
            for dc in range(16):
                bk = Blk(); y, g, t, v, s = bk.engines()
                g.wait(sm)
                g.dma_start(wbd[:].rearrange("p (k c) -> p k c", k=64), ex["wd8"][dc]).then_inc(sd, 16)
                par = dc % 2
                t.wait(sd)
                t.wait(sg)
                t.wait(sv)
                mmi = None
                for oc in range(64):
                    mmi = t.matmul([ps_d0, ps_d1][par][:], wbd[:, oc * 128:(oc + 1) * 128],
                                   fuse[:, oc * 512:(oc + 1) * 512],
                                   start=(oc == 0), stop=(oc == 63))
                mmi.then_inc(sm, 1)
                v.wait(sm)
                v.tensor_scalar(t0f[:], [ps_d0, ps_d1][par][:], sdt[:, dc:dc + 1], None, ALU.mult).then_inc(sv, 1)
                g.wait(sv)
                g.tensor_tensor(out_acc[:, dc * 1024 + sl * 512:dc * 1024 + (sl + 1) * 512],
                                out_acc[:, dc * 1024 + sl * 512:dc * 1024 + (sl + 1) * 512],
                                t0f[:], ALU.add).then_inc(sg, 1)
                bk.run(nc)

    with nc.sbuf_tensor("stgD", [128, 2048], F32, side="right") as stg, \
         nc.psum_tensor("psTD", [128, 256], F32) as ps_t:
        for tt in range(8):
            bk = Blk(); y, g, t, v, s = bk.engines()
            if tt > 0:
                t.wait(sd)
            for Dc in range(16):
                par = Dc % 2
                t.wait(sv)
                tri = t.transpose(ps_t[:, par * 128:(par + 1) * 128],
                                  out_acc[:, Dc * 1024 + tt * 128:Dc * 1024 + (tt + 1) * 128], identf[:])
                tri.then_inc(sm, 1)
                v.wait(sm)
                v.tensor_copy(stg[:, Dc * 128:(Dc + 1) * 128], ps_t[:, par * 128:(par + 1) * 128]).then_inc(sv, 1)
            y.wait(sv)
            y.dma_start(out_r[tt * 128:(tt + 1) * 128, :], stg[:]).then_inc(sd, 16)
            bk.run(nc)
        bk = Blk(); y, g, t, v, s = bk.engines()
        y.wait(sd)
        bk.run(nc)


def build_stageD():
    nc = bass.Bass("TRN2", target_bir_lowering=False)
    ex = _declare(nc, STAGED_IN, "ExternalInput")
    outr = nc.dram_tensor("outr", [1024, 2048], F32, kind="ExternalOutput").ap()
    with ExitStack() as st:
        emit_stageD(nc, st, ex, outr)
    nc.finalize()
    return nc


def _kernel_unfused_impl(**inputs):
    pc = host_prep(inputs)

    nc = build_stageA()
    names = [nm for nm, _, _ in STAGEA_IN]
    in_maps = [{nm: np.ascontiguousarray(pc[c][nm]) for nm in names} for c in range(NCORE)]
    resA = run_bass_kernel_spmd(nc, in_maps, core_ids=list(range(NCORE)))
    parts = [resA.results[c]["part"] for c in range(NCORE)]
    partFs = [resA.results[c]["partF"] for c in range(NCORE)]

    nc = build_stageB()
    names = [nm for nm, _, _ in STAGEB_IN]
    in_maps = []
    for c in range(NCORE):
        b, h2 = c // 2, c % 2
        d = {nm: np.ascontiguousarray(pc[c][nm]) for nm in names
             if nm not in ("partAr", "partBr", "pfAr", "pfBr")}
        rows = slice(h2 * 1024, (h2 + 1) * 1024)
        d["partAr"] = np.ascontiguousarray(parts[2 * b][rows])
        d["partBr"] = np.ascontiguousarray(parts[2 * b + 1][rows])
        d["pfAr"] = np.ascontiguousarray(partFs[2 * b][rows])
        d["pfBr"] = np.ascontiguousarray(partFs[2 * b + 1][rows])
        in_maps.append(d)
    resB = run_bass_kernel_spmd(nc, in_maps, core_ids=list(range(NCORE)))
    x1r = [resB.results[c]["x1r"] for c in range(NCORE)]

    nc = build_stageC()
    names = [nm for nm, _, _ in STAGEC_IN]
    in_maps = []
    for c in range(NCORE):
        b = c // 2
        d = {nm: np.ascontiguousarray(pc[c][nm]) for nm in names if nm != "x1T"}
        d["x1T"] = np.ascontiguousarray(np.concatenate([x1r[2 * b], x1r[2 * b + 1]], axis=1))
        in_maps.append(d)
    resC = run_bass_kernel_spmd(nc, in_maps, core_ids=list(range(NCORE)))
    ygs = [resC.results[c]["ygated"] for c in range(NCORE)]
    ssqs = [resC.results[c]["ssqout"] for c in range(NCORE)]

    nc = build_stageD()
    names = [nm for nm, _, _ in STAGED_IN]
    in_maps = []
    for c in range(NCORE):
        b, h2 = c // 2, c % 2
        d = {nm: np.ascontiguousarray(pc[c][nm]) for nm in names
             if nm not in ("x1rD", "yq0", "yq1", "ssqa", "ssqb")}
        rows = slice(h2 * 1024, (h2 + 1) * 1024)
        d["x1rD"] = np.ascontiguousarray(x1r[c])
        d["yq0"] = np.ascontiguousarray(ygs[2 * b][rows])
        d["yq1"] = np.ascontiguousarray(ygs[2 * b + 1][rows])
        d["ssqa"] = np.ascontiguousarray(ssqs[2 * b][h2 * 8:(h2 + 1) * 8])
        d["ssqb"] = np.ascontiguousarray(ssqs[2 * b + 1][h2 * 8:(h2 + 1) * 8])
        in_maps.append(d)
    resD = run_bass_kernel_spmd(nc, in_maps, core_ids=list(range(NCORE)))

    out = np.empty((4, T, D), np.float32)
    for c in range(NCORE):
        b, h2 = c // 2, c % 2
        out[b, h2 * 1024:(h2 + 1) * 1024, :] = resD.results[c]["outr"]
    return out


def build_fused(upto='D'):
    nc = bass.Bass("TRN2", target_bir_lowering=False)
    removed = {"partAr", "partBr", "pfAr", "pfBr", "x1T", "yq0", "yq1", "ssqa", "ssqb",
               "sel8", "wot", "x1rD"}
    seen = {}
    for lst in (STAGEA_IN, STAGEB_IN, STAGEC_IN, STAGED_IN):
        for nm, sh, dt in lst:
            if nm in removed or nm in seen:
                continue
            seen[nm] = (sh, dt)
    seen["wotC"] = ((4, 128, 16, 512), F32)
    ex = {}
    for nm, (sh, dt) in seen.items():
        ex[nm] = nc.dram_tensor(nm, list(sh), dt, kind="ExternalInput").ap()
    outr = nc.dram_tensor("outr", [1024, 2048], F32, kind="ExternalOutput").ap()

    part_i = nc.dram_tensor("part_i", [T, 2048], BF16)
    partF_i = nc.dram_tensor("partF_i", [T, 2048], BF16)
    rs_part = nc.dram_tensor("rs_part_o", [1024, 2048], BF16)
    rs_pf = nc.dram_tensor("rs_pf_o", [1024, 2048], BF16)
    x1r_i = nc.dram_tensor("x1r_i", [D, 1024], F32)
    x1bf_i = nc.dram_tensor("x1bf_i", [D, 1024], BF16)
    ag1_o = nc.dram_tensor("ag1_o", [2 * D, 1024], BF16)
    ssq_i = nc.dram_tensor("ssq_i", [16, 128], F32)
    ag3_o = nc.dram_tensor("ag3_o", [32, 128], F32)
    rs2_i = nc.dram_tensor("rs2_i", [T, 2048], BF16)
    rs2_o = nc.dram_tensor("rs2_o", [1024, 2048], BF16)
    sp = {
        "sp_qf": nc.dram_tensor("sp_qf", [8 * 128, T], BF16).ap(),
        "sp_kf": nc.dram_tensor("sp_kf", [8 * 128, T], BF16).ap(),
        "sp_vtm": nc.dram_tensor("sp_vtm", [16 * 128, 1024], BF16).ap(),
        "sp_avnT": nc.dram_tensor("sp_avnT", [16, 24, 128, 128], BF16).ap(),
        "sp_z": nc.dram_tensor("sp_z", [16 * 128, T], BF16).ap(),
    }

    ccn = [0]

    def collective(kind, tin, tout):
        ccn[0] += 1
        cc = Ct(nc.alloc_semaphore(f"ccF{ccn[0]}"))
        bk = Blk(); y, g, t, v, s = bk.engines()
        g.collective_compute(kind, ALU.add if kind == "ReduceScatter" else ALU.bypass,
                             replica_groups=PAIRS,
                             ins=[tin.ap().opt()], outs=[tout.ap().opt()]).then_inc_default(cc)
        g.wait(cc, 1)
        bk.run(nc)

    with ExitStack() as stA:
        emit_stageA(nc, stA, ex, part_i.ap(), partF_i.ap(), sp)
    collective("ReduceScatter", part_i, rs_part)
    collective("ReduceScatter", partF_i, rs_pf)
    if upto == 'A':
        nc.finalize()
        return nc

    exB = dict(ex)
    exB["rs_part"] = rs_part.ap()
    exB["rs_pf"] = rs_pf.ap()
    with ExitStack() as stB:
        emit_stageB(nc, stB, exB, x1r_i.ap(), fused=True, x1bf_out=x1bf_i.ap())
    collective("AllGather", x1bf_i, ag1_o)
    if upto == 'B':
        nc.finalize()
        return nc

    exC = dict(ex)
    exC["ag1"] = ag1_o.ap()
    exC["ag3"] = ag3_o.ap()
    exC["rs2in"] = rs2_i.ap()
    exC["cc_ssq_in"] = ssq_i.ap().opt()
    exC["cc_ssq_out"] = ag3_o.ap().opt()
    with ExitStack() as stC:
        emit_stageC(nc, stC, exC, None, ssq_i.ap(), sp, fused=True)
    collective("ReduceScatter", rs2_i, rs2_o)
    if upto == 'C':
        nc.finalize()
        return nc

    exD = dict(ex)
    exD["rs2"] = rs2_o.ap()
    exD["x1rD"] = x1r_i.ap()
    with ExitStack() as stD:
        emit_stageD(nc, stD, exD, outr, fused=True)
    nc.finalize()
    return nc


def kernel(**inputs):
    pc = host_prep(inputs)
    Wout = np.asarray(inputs["L1_attn_W_out"])
    for c in range(NCORE):
        h2 = c % 2
        wc = np.ascontiguousarray(Wout.T[2048 * h2:2048 * (h2 + 1)])
        pc[c]["wotC"] = wc.reshape(16, 128, 4, 512).transpose(2, 1, 0, 3).copy()
    nc = build_fused()
    names = [t.name for t in []]
    removed = {"partAr", "partBr", "pfAr", "pfBr", "x1T", "yq0", "yq1", "ssqa", "ssqb",
               "sel8", "wot", "x1rD"}
    innames = []
    seenn = set()
    for lst in (STAGEA_IN, STAGEB_IN, STAGEC_IN, STAGED_IN):
        for nm, sh, dt in lst:
            if nm in removed or nm in seenn:
                continue
            seenn.add(nm)
            innames.append(nm)
    innames.append("wotC")
    in_maps = [{nm: np.ascontiguousarray(pc[c][nm]) for nm in innames} for c in range(NCORE)]
    res = run_bass_kernel_spmd(nc, in_maps, core_ids=list(range(NCORE)))
    out = np.empty((4, T, D), np.float32)
    for c in range(NCORE):
        b, h2 = c // 2, c % 2
        out[b, h2 * 1024:(h2 + 1) * 1024, :] = res.results[c]["outr"]
    return out


def kernel_unfused(**inputs):
    return _kernel_unfused_impl(**inputs)


# version 06
# speedup vs baseline: 1.1852x; 1.0044x over previous
import numpy as np
import ml_dtypes
from contextlib import ExitStack
import concourse.bass as bass
import concourse.mybir as mybir
from concourse.bass_utils import run_bass_kernel_spmd

F32 = mybir.dt.float32
BF16 = mybir.dt.bfloat16
I8 = mybir.dt.int8
AF = mybir.ActivationFunctionType
ALU = mybir.AluOpType
AX = mybir.AxisListType
bfdt = ml_dtypes.bfloat16

D = 2048
T = 2048
NCORE = 8
SCALE = float(1.0 / np.sqrt(128.0))
NEG = -1e30
PAIRS = [[0, 1], [2, 3], [4, 5], [6, 7]]


class Ct:
    def __init__(self, sem):
        self.sem = sem
        self.n = 0
    def inc(self, k=1):
        self.n += k
        return self.n


class _RecInstr:
    __slots__ = ("rec",)
    def __init__(self, rec):
        self.rec = rec
    def then_inc(self, ct, k=1):
        self.rec[3].append((ct.sem, k))
        ct.inc(k)
        return self
    def then_inc_default(self, ct):
        self.rec[3].append((ct.sem, None))
        ct.inc(1)
        return self


class Rec:
    def __init__(self, sched, eng):
        self._q = sched.q[eng]
    def wait(self, ct, n=None):
        sem = ct.sem
        val = ct.n if n is None else n
        if val > 0:
            self._q.append(("w", sem, val, None))
    def __getattr__(self, name):
        q = self._q
        def call(*a, **kw):
            rec = ("c", name, (a, kw), [])
            q.append(rec)
            return _RecInstr(rec)
        return call


class Blk:
    ORDER = ("y", "g", "t", "v", "s")
    ATTR = {"y": "sync", "g": "gpsimd", "t": "tensor", "v": "vector", "s": "scalar"}
    def __init__(self):
        self.q = {e: [] for e in self.ORDER}
    def engines(self):
        return tuple(Rec(self, e) for e in ("y", "g", "t", "v", "s"))
    def run(self, nc):
        with nc.Block() as b:
            for e in self.ORDER:
                if not self.q[e]:
                    continue
                def make(recs):
                    def body(eng):
                        for r in recs:
                            if r[0] == "w":
                                eng.wait_ge(r[1], r[2])
                            else:
                                a, kw = r[2]
                                inst = getattr(eng, r[1])(*a, **kw)
                                for sem, k in r[3]:
                                    if k is None:
                                        inst.then_inc(sem)
                                    else:
                                        inst.then_inc(sem, k)
                    return body
                getattr(b, self.ATTR[e])(make(self.q[e]))



_PERM128 = np.concatenate([np.arange(0, 128, 2), np.arange(1, 128, 2)])

def _perm256(base):
    return np.concatenate([base + _PERM128, base + 128 + _PERM128])


def _host_consts():
    c = {}
    c["ident_bf"] = np.eye(128, dtype=np.float32).astype(bfdt)
    c["ident_f32"] = np.eye(128, dtype=np.float32)
    m = np.zeros((4, 128, 512), np.float32)
    for j in range(4):
        for r in range(128):
            m[j, r, 128 * j + r + 1:] = NEG
    c["maskA"] = m
    inv = 1.0 / 10000.0 ** (np.arange(0, 128, 2, dtype=np.float64) / 128.0)
    fr = np.arange(T, dtype=np.float64)[None, :] * inv[:, None]
    cosv = np.cos(fr).astype(np.float32)
    sinv = np.sin(fr).astype(np.float32)
    c["cs1T"] = np.concatenate([cosv, sinv], axis=0)
    c["cs2T"] = np.concatenate([sinv, cosv], axis=0)
    r1 = np.zeros((128, 128), np.float32)
    r2 = np.zeros((128, 128), np.float32)
    for m in range(64):
        r1[m, m] = 1.0
        r1[m + 64, m] = -1.0
        r2[m, m + 64] = 1.0
        r2[m + 64, m + 64] = 1.0
    c["R1"] = r1.astype(bfdt)
    c["R2"] = r2.astype(bfdt)
    mt = np.zeros((2, 128, 256), np.float32)
    for kt in range(2):
        for kk in range(128):
            mt[kt, kk, :kt * 128 + kk] = NEG
    c["maskT"] = mt
    sel = np.zeros((16, 16, 128), np.float32)
    for h in range(16):
        sel[h, h, :] = 1.0
    c["sel16"] = sel
    sel8 = np.zeros((8, 8, 128), np.float32)
    for f in range(8):
        sel8[f, f, :] = 1.0
    c["sel8"] = sel8
    s127 = np.zeros((128, 128), np.float32)
    s127[127, :] = 1.0
    c["sel127"] = s127
    c["ones_bf"] = np.ones((128, 1), np.float32).astype(bfdt)
    c["ones_f32"] = np.ones((128, 1), np.float32)
    c["onesrow_f32"] = np.ones((1, 128), np.float32)
    return c


def host_prep(inputs):
    consts = _host_consts()
    x = np.asarray(inputs["L0_attn_x"])
    pc = []
    for c in range(NCORE):
        b, h2 = c // 2, c % 2
        d = dict(consts)
        d["xT"] = np.ascontiguousarray(x[b].T)
        d["xTr"] = np.ascontiguousarray(x[b].T[:, h2 * 1024:(h2 + 1) * 1024])
        Wq = np.asarray(inputs["L0_attn_Wq"])
        cols = np.concatenate([_perm256(256 * h) for h in range(8 * h2, 8 * h2 + 8)])
        d["qw4"] = np.ascontiguousarray(Wq[:, cols]).reshape(16, 128, 4, 512).transpose(2, 1, 0, 3).copy()
        Wk = np.asarray(inputs["L0_attn_Wk"])
        kcols = np.concatenate([_perm256(256 * kv) for kv in (2 * h2, 2 * h2 + 1)])
        d["kw"] = np.ascontiguousarray(Wk[:, kcols]).reshape(16, 128, 512).transpose(1, 0, 2).copy()
        Wv = np.asarray(inputs["L0_attn_Wv"])
        d["vw"] = np.ascontiguousarray(Wv[:, 512 * h2:512 * h2 + 512]).reshape(16, 128, 512).transpose(1, 0, 2).copy()
        Wo = np.asarray(inputs["L0_attn_Wo"])
        woa = Wo[h2 * 2048:(h2 + 1) * 2048, :]
        d["gnw_row"] = np.ascontiguousarray(np.asarray(inputs["L0_attn_gn_w"])[h2 * 2048:(h2 + 1) * 2048].reshape(1, 2048))
        d["gnb_row"] = np.ascontiguousarray(np.asarray(inputs["L0_attn_gn_b"])[h2 * 2048:(h2 + 1) * 2048].reshape(1, 2048))
        for nm in ("lambda_q1", "lambda_k1", "lambda_q2", "lambda_k2"):
            d[nm.replace("lambda_", "lam")] = np.ascontiguousarray(np.asarray(inputs["L0_attn_" + nm]).reshape(128, 1))
        rows = np.arange(1024 * h2, 1024 * h2 + 1024)
        d["fqw"] = np.ascontiguousarray(np.asarray(inputs["L0_ffn_Wq"])[rows].T).reshape(16, 128, 2, 512).transpose(2, 1, 0, 3).copy()
        d["fkw"] = np.ascontiguousarray(np.asarray(inputs["L0_ffn_Wk"])[rows].T).reshape(16, 128, 2, 512).transpose(2, 1, 0, 3).copy()
        d["fvw"] = np.ascontiguousarray(np.asarray(inputs["L0_ffn_Wv"])[rows].T).reshape(16, 128, 2, 512).transpose(2, 1, 0, 3).copy()
        wof = np.ascontiguousarray(np.asarray(inputs["L0_ffn_Wo"]).T[rows])
        woall = np.concatenate([woa, wof], axis=0)
        d["wo4"] = woall.reshape(24, 128, 4, 512).transpose(2, 1, 0, 3).copy()
        d["ln1w_col"] = np.ascontiguousarray(np.asarray(inputs["L0_ffn_ln1_w"]).reshape(16, 128).T)
        d["ln2w_col"] = np.ascontiguousarray(np.asarray(inputs["L0_ffn_ln2_w"]).reshape(16, 128).T)
        d["A_t"] = np.asarray(inputs["L0_ffn_A"]).reshape(16, 128, 16).copy()
        d["Bmat"] = np.asarray(inputs["L0_ffn_B"]).copy()
        d["wi4"] = np.ascontiguousarray(np.asarray(inputs["L0_ffn_W_in_base"]).T).reshape(16, 128, 16, 512).transpose(2, 1, 0, 3).copy()
        d["wu4"] = np.ascontiguousarray(np.asarray(inputs["L0_ffn_W_up_base"]).T).reshape(16, 128, 16, 512).transpose(2, 1, 0, 3).copy()
        d["wd4"] = np.ascontiguousarray(np.asarray(inputs["L0_ffn_W_down"]).T).reshape(64, 128, 16, 128).transpose(2, 1, 0, 3).copy()
        Wi = np.asarray(inputs["L1_attn_W_in"])
        zrows = np.arange(2048 * h2, 2048 * h2 + 2048)
        zpad64 = np.zeros((64, 2048), Wi.dtype)
        w1a = np.concatenate([Wi[4096 + zrows],
                              Wi[8192:8256], zpad64,
                              Wi[8256:8320], zpad64,
                              Wi[8320 + 16 * h2:8320 + 16 * h2 + 16],
                              np.zeros((112, 2048), Wi.dtype)], axis=0)
        d["w1a"] = np.ascontiguousarray(w1a.T).reshape(16, 128, 19, 128).transpose(2, 1, 0, 3).copy()
        d["w1z"] = np.ascontiguousarray(Wi[zrows].T).reshape(16, 128, 16, 128).transpose(2, 1, 0, 3).copy()
        cw = np.asarray(inputs["L1_attn_conv_w"])
        cb = np.asarray(inputs["L1_attn_conv_b"])
        cwp = np.concatenate([cw[2048 * h2:2048 * h2 + 2048],
                              cw[4096:4160], np.zeros((64, 4), cw.dtype),
                              cw[4160:4224], np.zeros((64, 4), cw.dtype)], axis=0)
        cbp = np.concatenate([cb[2048 * h2:2048 * h2 + 2048],
                              cb[4096:4160], np.zeros(64, cb.dtype),
                              cb[4160:4224], np.zeros(64, cb.dtype)])
        d["convw"] = np.ascontiguousarray(cwp).reshape(18, 128, 4)
        d["convb"] = np.ascontiguousarray(cbp).reshape(18, 128, 1)
        hrows = np.arange(16 * h2, 16 * h2 + 16)
        d["Alog"] = np.ascontiguousarray(np.asarray(inputs["L1_attn_A_log"])[hrows].reshape(16, 1))
        d["dtb"] = np.ascontiguousarray(np.asarray(inputs["L1_attn_dt_bias"])[hrows].reshape(16, 1))
        d["Dvec"] = np.ascontiguousarray(np.asarray(inputs["L1_attn_D"])[hrows].reshape(16, 1))
        d["wot"] = np.ascontiguousarray(np.asarray(inputs["L1_attn_W_out"]).T).reshape(32, 128, 16, 128).transpose(2, 1, 0, 3).copy()
        d["wg8"] = np.asarray(inputs["L1_ffn_w_gate"]).reshape(16, 128, 16, 512).transpose(2, 1, 0, 3).copy()
        d["wu8"] = np.asarray(inputs["L1_ffn_w_up"]).reshape(16, 128, 16, 512).transpose(2, 1, 0, 3).copy()
        d["wd8"] = np.asarray(inputs["L1_ffn_w_down"]).reshape(64, 128, 16, 128).transpose(2, 1, 0, 3).copy()
        d["sgt"] = np.ascontiguousarray(np.asarray(inputs["L1_ffn_s_gate"]).reshape(64, 128).T)
        d["sut"] = np.ascontiguousarray(np.asarray(inputs["L1_ffn_s_up"]).reshape(64, 128).T)
        d["sdt"] = np.ascontiguousarray(np.asarray(inputs["L1_ffn_s_down"]).reshape(16, 128).T)
        pc.append(d)
    return pc


def _declare(nc, names_shapes, kind):
    out = {}
    for nm, shape, dt in names_shapes:
        out[nm] = nc.dram_tensor(nm, list(shape), dt, kind=kind).ap()
    return out



STAGEA_IN = [
    ("xT", (D, T), F32),
    ("qw4", (4, 128, 16, 512), F32),
    ("kw", (128, 16, 512), F32),
    ("vw", (128, 16, 512), F32),
    ("fqw", (2, 128, 16, 512), F32),
    ("fkw", (2, 128, 16, 512), F32),
    ("fvw", (2, 128, 16, 512), F32),
    ("wo4", (4, 128, 24, 512), F32),
    ("gnw_row", (1, 2048), F32), ("gnb_row", (1, 2048), F32),
    ("lamq1", (128, 1), F32), ("lamk1", (128, 1), F32),
    ("lamq2", (128, 1), F32), ("lamk2", (128, 1), F32),
    ("ln1w_col", (128, 16), F32),
    ("cs1T", (128, T), F32), ("cs2T", (128, T), F32),
    ("R1", (128, 128), BF16), ("R2", (128, 128), BF16),
    ("maskA", (4, 128, 512), F32),
    ("ident_bf", (128, 128), BF16),
    ("ones_bf", (128, 1), BF16),
    ("ones_f32", (128, 1), F32),
    ("onesrow_f32", (1, 128), F32),
]


def emit_stageA(nc, st, ex, part_out, partF_out, sp):
    A = lambda nm, sh, dt, side="left": st.enter_context(nc.sbuf_tensor(nm, list(sh), dt, side=side))

    ident = A("idA", (128, 128), BF16)
    onesb = A("onbA", (128, 1), BF16)
    onesf = A("onfA", (128, 1), F32)
    onesrow = A("onrA", (1, 128), F32)
    ln1w = A("l1wA", (128, 16), F32)
    lamv = A("lamA", (128, 4), F32)
    lam_b = A("lamB", (128, 1), F32)
    small = A("smlA", (1, 32), F32)
    epsb = A("epsbA", (128, 4), F32)
    bc0 = A("bc0A", (128, T), F32)

    sd = Ct(nc.alloc_semaphore("sdA"))
    sm = Ct(nc.alloc_semaphore("smA"))
    sv = Ct(nc.alloc_semaphore("svA"))
    ss = Ct(nc.alloc_semaphore("ssA"))
    sg = Ct(nc.alloc_semaphore("sgA"))

    bk = Blk(); y, g, t, v, s = bk.engines()
    y.dma_start(ident[:], ex["ident_bf"][:]).then_inc(sd, 16)
    y.dma_start(onesb[:], ex["ones_bf"][:]).then_inc(sd, 16)
    y.dma_start(onesf[:], ex["ones_f32"][:]).then_inc(sd, 16)
    y.dma_start(onesrow[:], ex["onesrow_f32"][:]).then_inc(sd, 16)
    y.dma_start(ln1w[:], ex["ln1w_col"][:]).then_inc(sd, 16)
    for j, nm in enumerate(("lamq1", "lamk1", "lamq2", "lamk2")):
        y.dma_start(lamv[:, j:j + 1], ex[nm][:]).then_inc(sd, 16)
    y.wait(sd)
    v.memset(epsb[:, 0:1], 1e-5)
    v.memset(epsb[:, 1:2], 1e-6)
    v.memset(epsb[:, 2:3], 1.1920929e-07)
    v.memset(epsb[:, 3:4], 0.0)
    bk.run(nc)

    with nc.psum_tensor("psA1", [128, 8], F32) as ps1:
        bk = Blk(); y, g, t, v, s = bk.engines()
        v.tensor_tensor(bc0[:, 0:1], lamv[:, 0:1], lamv[:, 1:2], ALU.mult)
        v.tensor_tensor(bc0[:, 1:2], lamv[:, 2:3], lamv[:, 3:4], ALU.mult).then_inc(sv, 1)
        t.wait(sv)
        t.matmul(ps1[0:1, 0:1], bc0[:, 0:1], onesf[:], start=True, stop=True)
        t.matmul(ps1[0:1, 1:2], bc0[:, 1:2], onesf[:], start=True, stop=True).then_inc(sm, 1)
        s.wait(sm)
        s.activation(small[0:1, 0:2], ps1[0:1, 0:2], AF.Exp).then_inc(ss, 1)
        v.wait(ss)
        v.tensor_tensor(small[0:1, 2:3], small[0:1, 0:1], small[0:1, 1:2], ALU.subtract).then_inc(sv, 1)
        g.wait(sv)
        g.tensor_scalar(small[0:1, 3:4], small[0:1, 2:3], 0.2, 1.0, ALU.add, op1=ALU.mult).then_inc(sg, 1)
        t.wait(sg)
        t.matmul(ps1[:, 2:3], onesrow[:], small[0:1, 3:4], start=True, stop=True).then_inc(sm, 1)
        v.wait(sm)
        v.tensor_copy(lam_b[:], ps1[:, 2:3])
        bk.run(nc)

    qkv_stack = ExitStack()
    q_r = qkv_stack.enter_context(nc.sbuf_tensor("qrA", [128, 16 * T], BF16, side="left"))
    k_r = qkv_stack.enter_context(nc.sbuf_tensor("krA", [128, 4 * T], BF16, side="left"))
    v_tm = qkv_stack.enter_context(nc.sbuf_tensor("vtA", [128, 16 * 512], BF16, side="left"))

    with nc.sbuf_tensor("xnA", [128, 16 * T], BF16, side="right") as xn, \
         nc.sbuf_tensor("swA", [1, T], F32, side="right") as srow, \
         nc.sbuf_tensor("scr0A", [128, T], BF16, side="right") as scr0:

      with nc.sbuf_tensor("mrA", [1, T], F32, side="right") as mean_row, \
           nc.sbuf_tensor("bc1A", [128, T], BF16, side="right") as bc1, \
           nc.sbuf_tensor("xldA", [128, T], F32, side="right") as xld:
        with nc.psum_tensor("psSumA", [1, T], F32) as ps_sum, \
             nc.psum_tensor("psSqA", [1, T], F32) as ps_sq:
            bk = Blk(); y, g, t, v, s = bk.engines()
            for cc in range(16):
                if cc >= 1:
                    y.wait(sv); y.wait(ss)
                y.dma_start(xld[:], ex["xT"][cc * 128:(cc + 1) * 128, :]).then_inc(sd, 16)
                v.wait(sd)
                if cc >= 1:
                    v.wait(sm)
                v.tensor_copy(scr0[:], xld[:]).then_inc(sv, 1)
                s.wait(sd)
                s.activation(xn[:, cc * T:(cc + 1) * T], xld[:], AF.Square).then_inc(ss, 1)
                t.wait(sv); t.wait(ss)
                xb = scr0
                for j in range(4):
                    t.matmul(ps_sum[0:1, j * 512:(j + 1) * 512], onesb[:], xb[:, j * 512:(j + 1) * 512],
                             start=(cc == 0), stop=(cc == 15))
                mmi = None
                for j in range(4):
                    mmi = t.matmul(ps_sq[0:1, j * 512:(j + 1) * 512], onesb[:],
                                   xn[:, cc * T + j * 512:cc * T + (j + 1) * 512],
                                   start=(cc == 0), stop=(cc == 15))
                mmi.then_inc(sm, 1)
            bk.run(nc)

            bk = Blk(); y, g, t, v, s = bk.engines()
            v.tensor_scalar(mean_row[:], ps_sum[0:1, 0:T], 1.0 / 2048.0, None, ALU.mult)
            v.tensor_scalar(srow[:], ps_sq[0:1, 0:T], 1.0 / 2048.0, None, ALU.mult).then_inc(sv, 1)
            bk.run(nc)
            bk = Blk(); y, g, t, v, s = bk.engines()
            v.tensor_tensor(ps_sum[0:1, 0:T], mean_row[:], mean_row[:], ALU.mult).then_inc(sv, 1)
            bk.run(nc)
            bk = Blk(); y, g, t, v, s = bk.engines()
            v.tensor_tensor(srow[:], srow[:], ps_sum[0:1, 0:T], ALU.subtract).then_inc(sv, 1)
            s.wait(sv)
            s.activation(ps_sum[0:1, 0:T], srow[:], AF.Sqrt, bias=epsb[0:1, 0:1]).then_inc(ss, 1)
            bk.run(nc)
            bk = Blk(); y, g, t, v, s = bk.engines()
            v.reciprocal(srow[:], ps_sum[0:1, 0:T]).then_inc(sv, 1)
            bk.run(nc)

        with nc.psum_tensor("psBcA", [128, 1024], F32) as ps_bc:
            bk = Blk(); y, g, t, v, s = bk.engines()
            for j in range(4):
                t.wait(sv)
                t.matmul(ps_bc[:, 0:512], onesrow[:], mean_row[0:1, j * 512:(j + 1) * 512], start=True, stop=True)
                t.matmul(ps_bc[:, 512:1024], onesrow[:], srow[0:1, j * 512:(j + 1) * 512], start=True, stop=True).then_inc(sm, 1)
                v.wait(sm)
                v.tensor_copy(bc0[:, j * 512:(j + 1) * 512], ps_bc[:, 0:512])
                v.tensor_copy(bc1[:, j * 512:(j + 1) * 512], ps_bc[:, 512:1024]).then_inc(sv, 1)
            bk.run(nc)

            bk = Blk(); y, g, t, v, s = bk.engines()
            for cc in range(16):
                if cc >= 1:
                    y.wait(sg)
                y.dma_start(xld[:], ex["xT"][cc * 128:(cc + 1) * 128, :]).then_inc(sd, 16)
                v.wait(sd)
                v.tensor_tensor(xld[:], xld[:], bc0[:], ALU.subtract).then_inc(sv, 1)
                g.wait(sv)
                g.tensor_tensor(xn[:, cc * T:(cc + 1) * T], xld[:], bc1[:], ALU.mult).then_inc(sg, 1)
            bk.run(nc)

      if True:
        with ExitStack() as st3, \
             nc.psum_tensor("psqA0", [128, 512], F32) as pq0, \
             nc.psum_tensor("psqA1", [128, 512], F32) as pq1, \
             nc.psum_tensor("pstrA", [128, 512], BF16) as pstr, \
             nc.psum_tensor("psS1A", [1, T], F32) as ps_s1:
            A3 = lambda nm, sh, dt: st3.enter_context(nc.sbuf_tensor(nm, list(sh), dt, side="right"))
            wbuf = A3("wbA", (128, 16 * 512), BF16)
            vtmp = A3("vtmpA", (128, 512), BF16)
            stq = A3("stqA", (128, 2048), BF16)
            psq = [pq0, pq1]

            def proj_block(bk_eng, grp_aps, oc_width, dst_fn):
                y, g, t, v, s = bk_eng
                evts = []
                for gi, w_ap in enumerate(grp_aps):
                    wb = wbuf[:, 0:16 * oc_width]
                    if gi >= 1:
                        g.wait(sm, evts[gi - 1])
                    les = []
                    for q4 in range(4):
                        g.dma_start(wb[:, q4 * 4 * oc_width:(q4 + 1) * 4 * oc_width]
                                    .rearrange("p (k c) -> p k c", k=4),
                                    w_ap[:, q4 * 4:(q4 + 1) * 4, :]).then_inc(sd, 16)
                        les.append(sd.n)
                    noc = oc_width // 128
                    for oc in range(noc):
                        for tb in range(4):
                            par = (oc * 4 + tb) % 2
                            t.wait(sv)
                            mmi = None
                            for kc in range(16):
                                if kc % 4 == 0:
                                    t.wait(sd, les[kc // 4])
                                base = kc * oc_width + oc * 128
                                mmi = t.matmul(psq[par][:], wb[:, base:base + 128],
                                               xn[:, kc * T + tb * 512:kc * T + (tb + 1) * 512],
                                               start=(kc == 0), stop=(kc == 15))
                            mmi.then_inc(sm, 1)
                            dst_fn(gi, oc, tb, psq[par])
                    evts.append(sm.n)

            bk = Blk(); engs = bk.engines(); y, g, t, v, s = engs
            def dq(gi, oc, tb, ps, v=v):
                v.wait(sm)
                cc = gi * 4 + oc
                v.tensor_copy(q_r[:, cc * T + tb * 512:cc * T + (tb + 1) * 512], ps[:]).then_inc(sv, 1)
            proj_block(engs, [ex["qw4"][grp] for grp in range(4)], 512, dq)
            bk.run(nc)
            bk = Blk(); engs = bk.engines(); y, g, t, v, s = engs
            def dk(gi, oc, tb, ps, v=v):
                v.wait(sm)
                v.tensor_copy(k_r[:, oc * T + tb * 512:oc * T + (tb + 1) * 512], ps[:]).then_inc(sv, 1)
            proj_block(engs, [ex["kw"][:]], 512, dk)
            bk.run(nc)
            bk = Blk(); engs = bk.engines(); y, g, t, v, s = engs
            def dv(gi, oc, tb, ps, t=t, v=v):
                v.wait(sm)
                v.tensor_copy(vtmp[:], ps[:]).then_inc(sv, 1)
                t.wait(sv)
                tri = None
                for j in range(4):
                    tri = t.transpose(pstr[:, j * 128:(j + 1) * 128], vtmp[:, j * 128:(j + 1) * 128], ident[:])
                tri.then_inc(sm, 1)
                v.wait(sm)
                cpi = None
                for j in range(4):
                    kt = tb * 4 + j
                    cpi = v.tensor_copy(v_tm[:, kt * 512 + oc * 128:kt * 512 + (oc + 1) * 128],
                                        pstr[:, j * 128:(j + 1) * 128])
                cpi.then_inc(sv, 1)
            proj_block(engs, [ex["vw"][:]], 512, dv)
            bk.run(nc)

            bk = Blk(); y, g, t, v, s = bk.engines()
            sqm_evt = []
            for cc in range(16):
                if cc >= 1:
                    s.wait(sm, sqm_evt[cc - 1])
                sb = scr0
                s.activation(sb[:], xn[:, cc * T:(cc + 1) * T], AF.Square).then_inc(ss, 1)
                t.wait(ss)
                mmi = None
                for j in range(4):
                    mmi = t.matmul(ps_s1[0:1, j * 512:(j + 1) * 512], onesb[:], sb[:, j * 512:(j + 1) * 512],
                                   start=(cc == 0), stop=(cc == 15))
                mmi.then_inc(sm, 1)
                sqm_evt.append(sm.n)
            bk.run(nc)
            bk = Blk(); y, g, t, v, s = bk.engines()
            v.tensor_scalar(srow[:], ps_s1[0:1, 0:T], 1.0 / 2048.0, None, ALU.mult).then_inc(sv, 1)
            s.wait(sv)
            s.activation(ps_s1[0:1, 0:T], srow[:], AF.Sqrt, bias=epsb[0:1, 1:2]).then_inc(ss, 1)
            bk.run(nc)
            bk = Blk(); y, g, t, v, s = bk.engines()
            v.wait(ss)
            v.reciprocal(srow[:], ps_s1[0:1, 0:T]).then_inc(sv, 1)
            for j in range(4):
                t.wait(sv)
                t.matmul(psq[1][:], onesrow[:], srow[0:1, j * 512:(j + 1) * 512], start=True, stop=True).then_inc(sm, 1)
                v.wait(sm)
                v.tensor_copy(bc0[:, j * 512:(j + 1) * 512], psq[1][:]).then_inc(sv, 1)
            bk.run(nc)
            bk = Blk(); y, g, t, v, s = bk.engines()
            for cc in range(16):
                sb = scr0
                if cc >= 1:
                    v.wait(sg)
                v.tensor_scalar(sb[:], xn[:, cc * T:(cc + 1) * T], ln1w[:, cc:cc + 1], None, ALU.mult).then_inc(sv, 1)
                g.wait(sv)
                g.tensor_tensor(xn[:, cc * T:(cc + 1) * T], sb[:], bc0[:], ALU.mult).then_inc(sg, 1)
            bk.run(nc)

            for wnm, spnm in (("fqw", "sp_qf"), ("fkw", "sp_kf")):
                bk = Blk(); engs = bk.engines(); y2, g2, t2, v2, s2 = engs
                def dqf(gi, oc, tb, ps, spnm=spnm, y2=y2, v2=v2):
                    v2.wait(sm)
                    if tb == 0:
                        v2.wait(sd)
                    v2.tensor_copy(stq[:, tb * 512:(tb + 1) * 512], ps[:]).then_inc(sv, 1)
                    if tb == 3:
                        y2.wait(sv)
                        row = (gi * 4 + oc) * 128
                        y2.dma_start(sp[spnm][row:row + 128, :], stq[:]).then_inc(sd, 16)
                proj_block(engs, [ex[wnm][grp] for grp in range(2)], 512, dqf)
                bk.run(nc)
            bk = Blk(); engs = bk.engines(); y2, g2, t2, v2, s2 = engs
            def dvf(gi, oc, tb, ps, t2=t2, v2=v2, y2=y2):
                h = gi * 4 + oc
                v2.wait(sm)
                v2.tensor_copy(vtmp[:], ps[:]).then_inc(sv, 1)
                t2.wait(sv)
                tri = None
                for j in range(4):
                    tri = t2.transpose(pstr[:, j * 128:(j + 1) * 128], vtmp[:, j * 128:(j + 1) * 128], ident[:])
                tri.then_inc(sm, 1)
                v2.wait(sm)
                v2.wait(sd)
                cpi = None
                for j in range(4):
                    cpi = v2.tensor_copy(stq[:, j * 128:(j + 1) * 128], pstr[:, j * 128:(j + 1) * 128])
                cpi.then_inc(sv, 1)
                y2.wait(sv)
                for j in range(4):
                    kt = tb * 4 + j
                    y2.dma_start(sp["sp_vtm"][kt * 128:(kt + 1) * 128, h * 128:(h + 1) * 128],
                                 stq[:, j * 128:(j + 1) * 128]).then_inc(sd, 16)
            proj_block(engs, [ex["fvw"][grp] for grp in range(2)], 512, dvf)
            bk.run(nc)
            bk = Blk(); y, g, t, v, s = bk.engines()
            y.wait(sd)
            bk.run(nc)

    with nc.sbuf_tensor("cs1A", [128, T], F32, side="right") as cs1, \
         nc.sbuf_tensor("cs2A", [128, T], F32, side="right") as cs2, \
         nc.sbuf_tensor("rR1A", [128, 128], BF16, side="right") as R1, \
         nc.sbuf_tensor("rR2A", [128, 128], BF16, side="right") as R2, \
         nc.sbuf_tensor("rpA", [128, T], BF16, side="right") as scrA, \
         nc.sbuf_tensor("rpB", [128, T], BF16, side="right") as scrB, \
         nc.psum_tensor("psRopeA", [128, 2048], F32) as ps_ro:
        bk = Blk(); y, g, t, v, s = bk.engines()
        y.dma_start(cs1[:], ex["cs1T"][:]).then_inc(sd, 16)
        y.dma_start(cs2[:], ex["cs2T"][:]).then_inc(sd, 16)
        y.dma_start(R1[:], ex["R1"][:]).then_inc(sd, 16)
        y.dma_start(R2[:], ex["R2"][:]).then_inc(sd, 16)
        for i in range(20):
            buf = q_r if i < 16 else k_r
            cc = i if i < 16 else i - 16
            ch = buf[:, cc * T:(cc + 1) * T]
            if i == 0:
                v.wait(sd); g.wait(sd)
            if i > 0:
                v.wait(sm)
                g.wait(sm)
            v.tensor_tensor(scrA[:], ch, cs1[:], ALU.mult).then_inc(sv, 1)
            g.tensor_tensor(scrB[:], ch, cs2[:], ALU.mult).then_inc(sg, 1)
            t.wait(sv); t.wait(sg)
            if i > 0:
                t.wait(sv)
            mmi = None
            for tb in range(4):
                t.matmul(ps_ro[:, tb * 512:(tb + 1) * 512], R1[:], scrA[:, tb * 512:(tb + 1) * 512],
                         start=True, stop=False)
                mmi = t.matmul(ps_ro[:, tb * 512:(tb + 1) * 512], R2[:], scrB[:, tb * 512:(tb + 1) * 512],
                               start=False, stop=True)
            mmi.then_inc(sm, 1)
            v.wait(sm)
            cpi = None
            for tb in range(4):
                cpi = v.tensor_copy(ch[:, tb * 512:(tb + 1) * 512] if False else buf[:, cc * T + tb * 512:cc * T + (tb + 1) * 512], ps_ro[:, tb * 512:(tb + 1) * 512])
            cpi.then_inc(sv, 1)
        bk.run(nc)

    with nc.sbuf_tensor("mskA", [128, 4 * 512], BF16, side="right") as masks:
        bk = Blk(); y, g, t, v, s = bk.engines()
        for j in range(4):
            g.dma_start(masks[:, j * 512:(j + 1) * 512], ex["maskA"][j]).then_inc(sd, 16)
        g.wait(sd)
        bk.run(nc)

        with ExitStack() as st5, \
             nc.psum_tensor("psScA", [128, 2048], F32) as ps_sc, \
             nc.psum_tensor("psTrA", [128, 512], BF16) as ps_tr, \
             nc.psum_tensor("psAv0", [128, 256], F32) as ps_av1, \
             nc.psum_tensor("psAv1", [128, 256], F32) as ps_av2, \
             nc.psum_tensor("psEpA", [128, 512], F32) as ps_ep:
            A5 = lambda nm, sh, dt: st5.enter_context(nc.sbuf_tensor(nm, list(sh), dt, side="right"))
            e_sb = A5("esbA", (128, 2048), BF16)
            eT_sb = A5("etbA", (128, 2048), BF16)
            avd = A5("avdA", (128, 16 * 256), F32)
            mxn = A5("mxnA", (128, 1), F32)
            biasq = A5("biasA", (128, 1), F32)
            ssum = A5("ssumA", (128, 2), F32)
            rrec = A5("rrecA", (128, 3), F32)
            sums_b = A5("sumsA", (128, 16), F32)
            sqs_b = A5("sqsA", (128, 16), F32)
            avc = A5("avcA", (128, 512), F32)
            abrow = A5("abrowA", (1, 512), F32)
            marow = A5("marowA", (1, 256), F32)
            gnrow = A5("gnrowA", (1, 512), F32)
            scrq = A5("scrqA", (128, 256), BF16)
            avn_tm = A5("avntmA", (128, 256), BF16)
            stow = A5("stowA", (128, 256), BF16)

            for h in range(8):
                kvl = h // 4
                bk = Blk(); y, g, t, v, s = bk.engines()
                y.dma_start(gnrow[0:1, 0:256], ex["gnw_row"][0:1, h * 256:(h + 1) * 256]).then_inc(sd, 16)
                y.dma_start(gnrow[0:1, 256:512], ex["gnb_row"][0:1, h * 256:(h + 1) * 256]).then_inc(sd, 16)
                for qt in range(16):
                    nkb = qt // 4 + 1
                    for m in (1, 2):
                        qcc = 2 * h + (m - 1)
                        kcc = 2 * kvl + (m - 1)
                        t.wait(ss)
                        mmi = None
                        for kb in range(nkb):
                            mmi = t.matmul(ps_sc[:, kb * 512:(kb + 1) * 512],
                                           q_r[:, qcc * T + qt * 128:qcc * T + (qt + 1) * 128],
                                           k_r[:, kcc * T + kb * 512:kcc * T + (kb + 1) * 512],
                                           start=True, stop=True)
                        mmi = t.matmul(ps_sc[:, (nkb - 1) * 512:nkb * 512], ident[:],
                                       masks[:, (qt % 4) * 512:(qt % 4 + 1) * 512], start=False, stop=True)
                        mmi.then_inc(sm, 1)
                        v.wait(sm)
                        v.reduce_max(mxn[:], ps_sc[:, 0:nkb * 512], axis=AX.X, negate=True).then_inc(sv, 1)
                        g.wait(sv)
                        g.tensor_scalar(biasq[:], mxn[:], SCALE, 1.0, ALU.mult, op1=ALU.mult).then_inc(sg, 1)
                        s.wait(sg)
                        s.wait(sm)
                        s.activation(e_sb[:, 0:nkb * 512], ps_sc[:, 0:nkb * 512], AF.Exp,
                                     bias=biasq[:], scale=SCALE,
                                     accum_out=ssum[:, m - 1:m]).then_inc(ss, 1)
                        for grp in range(nkb):
                            t.wait(ss)
                            t.wait(sv)
                            tri = None
                            for j in range(4):
                                tri = t.transpose(ps_tr[:, j * 128:(j + 1) * 128],
                                                  e_sb[:, grp * 512 + j * 128:grp * 512 + (j + 1) * 128], ident[:])
                            tri.then_inc(sm, 1)
                            v.wait(sm)
                            v.tensor_copy(eT_sb[:, grp * 512:(grp + 1) * 512], ps_tr[:]).then_inc(sv, 1)
                        ps_av = ps_av1 if m == 1 else ps_av2
                        t.wait(sv)
                        t.wait(ss)
                        mmi = None
                        for kt in range(nkb * 4):
                            mmi = t.matmul(ps_av[:], eT_sb[:, kt * 128:(kt + 1) * 128],
                                           v_tm[:, kt * 512 + kvl * 256:kt * 512 + (kvl + 1) * 256],
                                           start=(kt == 0), stop=(kt == nkb * 4 - 1))
                        mmi.then_inc(sm, 1)
                    v.wait(ss)
                    v.reciprocal(rrec[:, 0:1], ssum[:, 0:1])
                    v.reciprocal(rrec[:, 1:2], ssum[:, 1:2]).then_inc(sv, 1)
                    g.wait(sv)
                    g.tensor_tensor(rrec[:, 2:3], rrec[:, 1:2], lam_b[:], ALU.mult).then_inc(sg, 1)
                    s.wait(sm); s.wait(sg)
                    s.activation(avc[:, 0:256], ps_av1[:], AF.Copy, scale=rrec[:, 0:1])
                    s.activation(avc[:, 256:512], ps_av2[:], AF.Copy, scale=rrec[:, 2:3]).then_inc(ss, 1)
                    v.wait(ss)
                    v.tensor_tensor(avd[:, qt * 256:(qt + 1) * 256], avc[:, 0:256], avc[:, 256:512],
                                    ALU.subtract).then_inc(sv, 1)
                    v.tensor_scalar(avc[:, 0:256], avd[:, qt * 256:(qt + 1) * 256], 1.0, None, ALU.mult,
                                    op1=ALU.add, accum_out=sums_b[:, qt:qt + 1]).then_inc(sv, 1)
                    s.wait(sv)
                    s.activation(scrq[:], avd[:, qt * 256:(qt + 1) * 256], AF.Square,
                                 accum_out=sqs_b[:, qt:qt + 1]).then_inc(ss, 1)
                bk.run(nc)

                bk = Blk(); y, g, t, v, s = bk.engines()
                v.wait(ss)
                v.reduce_sum(rrec[:, 0:1], sums_b[:], axis=AX.X)
                v.reduce_sum(rrec[:, 1:2], sqs_b[:], axis=AX.X).then_inc(sv, 1)
                t.wait(sv)
                t.matmul(ps_ep[0:1, 0:1], rrec[:, 0:1], onesf[:], start=True, stop=True)
                t.matmul(ps_ep[0:1, 1:2], rrec[:, 1:2], onesf[:], start=True, stop=True).then_inc(sm, 1)
                v.wait(sm)
                v.tensor_scalar(small[0:1, 4:5], ps_ep[0:1, 0:1], 1.0 / (2048.0 * 256.0), None, ALU.mult)
                v.tensor_scalar(small[0:1, 5:6], ps_ep[0:1, 1:2], 1.0 / (2048.0 * 256.0), None, ALU.mult).then_inc(sv, 1)
                g.wait(sv)
                g.tensor_tensor(small[0:1, 6:7], small[0:1, 4:5], small[0:1, 4:5], ALU.mult).then_inc(sg, 1)
                v.wait(sg)
                v.tensor_tensor(small[0:1, 7:8], small[0:1, 5:6], small[0:1, 6:7], ALU.subtract).then_inc(sv, 1)
                s.wait(sv)
                s.activation(small[0:1, 8:9], small[0:1, 7:8], AF.Sqrt, bias=epsb[0:1, 0:1]).then_inc(ss, 1)
                v.wait(ss)
                v.reciprocal(small[0:1, 9:10], small[0:1, 8:9]).then_inc(sv, 1)
                v.wait(sd)
                v.tensor_scalar(abrow[0:1, 0:256], gnrow[0:1, 0:256], small[0:1, 9:10], 0.8,
                                ALU.mult, op1=ALU.mult).then_inc(sv, 1)
                g.wait(sv)
                g.tensor_scalar(marow[0:1, 0:256], abrow[0:1, 0:256], small[0:1, 4:5], 1.0, ALU.mult, op1=ALU.mult).then_inc(sg, 1)
                v.tensor_scalar(abrow[0:1, 256:512], gnrow[0:1, 256:512], 0.8, None, ALU.mult)
                v.wait(sg)
                v.tensor_tensor(abrow[0:1, 256:512], abrow[0:1, 256:512], marow[0:1, 0:256], ALU.subtract).then_inc(sv, 1)
                t.wait(sv)
                t.matmul(ps_ep[:, 0:512], onesrow[:], abrow[0:1, 0:512], start=True, stop=True).then_inc(sm, 1)
                v.wait(sm)
                v.tensor_copy(bc0[:, 0:512], ps_ep[:, 0:512]).then_inc(sv, 1)
                bk.run(nc)

                bk = Blk(); y, g, t, v, s = bk.engines()
                for qt in range(16):
                    if qt > 0:
                        v.wait(sg)
                    v.tensor_tensor(avc[:, 0:256], avd[:, qt * 256:(qt + 1) * 256], bc0[:, 0:256], ALU.mult).then_inc(sv, 1)
                    g.wait(sv)
                    g.tensor_tensor(avn_tm[:], avc[:, 0:256], bc0[:, 256:512], ALU.add).then_inc(sg, 1)
                    t.wait(sg)
                    tri = None
                    for j in range(2):
                        tri = t.transpose(ps_tr[:, j * 128:(j + 1) * 128], avn_tm[:, j * 128:(j + 1) * 128], ident[:])
                    tri.then_inc(sm, 1)
                    v.wait(sm)
                    v.wait(sd)
                    v.tensor_copy(stow[:], ps_tr[:, 0:256]).then_inc(sv, 1)
                    y.wait(sv)
                    y.dma_start(sp["sp_avnT"][qt, 2 * h], stow[:, 0:128]).then_inc(sd, 16)
                    y.dma_start(sp["sp_avnT"][qt, 2 * h + 1], stow[:, 128:256]).then_inc(sd, 16)
                bk.run(nc)
    qkv_stack.close()

    with nc.sbuf_tensor("mskF", [128, 4 * 512], BF16, side="right") as masks:
        bk = Blk(); y, g, t, v, s = bk.engines()
        for j in range(4):
            g.dma_start(masks[:, j * 512:(j + 1) * 512], ex["maskA"][j]).then_inc(sd, 16)
        g.wait(sd)
        bk.run(nc)
        with nc.sbuf_tensor("qfA", [128, 8 * T], BF16, side="right") as q_f, \
             nc.sbuf_tensor("kfA", [128, 8 * T], BF16, side="right") as k_f, \
             nc.sbuf_tensor("vfA", [128, 16 * 1024], BF16, side="right") as v_tf, \
             ExitStack() as st6, \
             nc.psum_tensor("psScF", [128, 2048], F32) as ps_sc, \
             nc.psum_tensor("psTrF", [128, 512], BF16) as ps_tr, \
             nc.psum_tensor("psAvF", [128, 128], F32) as ps_av:
            A6 = lambda nm, sh, dt: st6.enter_context(nc.sbuf_tensor(nm, list(sh), dt, side="right"))
            e_sb = A6("esbF", (128, 2048), BF16)
            eT_sb = A6("etbF", (128, 2048), BF16)
            mxn = A6("mxnF", (128, 1), F32)
            biasq = A6("biasF", (128, 1), F32)
            ssum = A6("ssumF", (128, 1), F32)
            rrec = A6("rrecF", (128, 1), F32)
            avn_tm = A6("avntmF", (128, 128), BF16)
            stow = A6("stowF", (128, 128), BF16)

            bk = Blk(); y, g, t, v, s = bk.engines()
            for cc in range(8):
                y.dma_start(q_f[:, cc * T:(cc + 1) * T], sp["sp_qf"][cc * 128:(cc + 1) * 128, :]).then_inc(sd, 16)
                y.dma_start(k_f[:, cc * T:(cc + 1) * T], sp["sp_kf"][cc * 128:(cc + 1) * 128, :]).then_inc(sd, 16)
            for kt in range(16):
                y.dma_start(v_tf[:, kt * 1024:(kt + 1) * 1024], sp["sp_vtm"][kt * 128:(kt + 1) * 128, :]).then_inc(sd, 16)
            y.wait(sd)
            bk.run(nc)

            for h in range(8):
                bk = Blk(); y, g, t, v, s = bk.engines()
                for qt in range(16):
                    nkb = qt // 4 + 1
                    t.wait(ss)
                    mmi = None
                    for kb in range(nkb):
                        mmi = t.matmul(ps_sc[:, kb * 512:(kb + 1) * 512],
                                       q_f[:, h * T + qt * 128:h * T + (qt + 1) * 128],
                                       k_f[:, h * T + kb * 512:h * T + (kb + 1) * 512],
                                       start=True, stop=True)
                    mmi = t.matmul(ps_sc[:, (nkb - 1) * 512:nkb * 512], ident[:],
                                   masks[:, (qt % 4) * 512:(qt % 4 + 1) * 512], start=False, stop=True)
                    mmi.then_inc(sm, 1)
                    v.wait(sm)
                    v.reduce_max(mxn[:], ps_sc[:, 0:nkb * 512], axis=AX.X, negate=True).then_inc(sv, 1)
                    g.wait(sv)
                    g.tensor_scalar(biasq[:], mxn[:], SCALE, 1.0, ALU.mult, op1=ALU.mult).then_inc(sg, 1)
                    s.wait(sg)
                    s.wait(sm)
                    s.activation(e_sb[:, 0:nkb * 512], ps_sc[:, 0:nkb * 512], AF.Exp,
                                 bias=biasq[:], scale=SCALE, accum_out=ssum[:]).then_inc(ss, 1)
                    for grp in range(nkb):
                        t.wait(ss)
                        t.wait(sv)
                        tri = None
                        for j in range(4):
                            tri = t.transpose(ps_tr[:, j * 128:(j + 1) * 128],
                                              e_sb[:, grp * 512 + j * 128:grp * 512 + (j + 1) * 128], ident[:])
                        tri.then_inc(sm, 1)
                        v.wait(sm)
                        v.tensor_copy(eT_sb[:, grp * 512:(grp + 1) * 512], ps_tr[:]).then_inc(sv, 1)
                    t.wait(sv)
                    t.wait(ss)
                    mmi = None
                    for kt in range(nkb * 4):
                        mmi = t.matmul(ps_av[:], eT_sb[:, kt * 128:(kt + 1) * 128],
                                       v_tf[:, kt * 1024 + h * 128:kt * 1024 + (h + 1) * 128],
                                       start=(kt == 0), stop=(kt == nkb * 4 - 1))
                    mmi.then_inc(sm, 1)
                    v.wait(ss)
                    v.reciprocal(rrec[:], ssum[:]).then_inc(sv, 1)
                    s.wait(sm); s.wait(sv)
                    s.activation(avn_tm[:], ps_av[:], AF.Copy, scale=rrec[:]).then_inc(ss, 1)
                    t.wait(ss)
                    tri = t.transpose(ps_tr[:, 0:128], avn_tm[:], ident[:])
                    tri.then_inc(sm, 1)
                    v.wait(sm)
                    v.wait(sd)
                    v.tensor_copy(stow[:], ps_tr[:, 0:128]).then_inc(sv, 1)
                    y.wait(sv)
                    y.dma_start(sp["sp_avnT"][qt, 16 + h], stow[:]).then_inc(sd, 16)
                bk.run(nc)

    with nc.sbuf_tensor("woA", [128, 24 * 512], BF16, side="right") as wo_sb, \
         nc.sbuf_tensor("avqA", [128, 2 * 24 * 128], BF16, side="right") as avq, \
         nc.sbuf_tensor("stoA", [128, 2 * 512], BF16, side="right") as sto, \
         nc.sbuf_tensor("stoF", [128, 2 * 512], BF16, side="right") as stoF, \
         nc.psum_tensor("psOA", [128, 512], F32) as ps_o0, \
         nc.psum_tensor("psOB", [128, 512], F32) as ps_o1, \
         nc.psum_tensor("psFA", [128, 512], F32) as ps_f0, \
         nc.psum_tensor("psFB", [128, 512], F32) as ps_f1:
        ps_os = [ps_o0, ps_o1]
        ps_fs = [ps_f0, ps_f1]
        for db in range(4):
            bk = Blk(); y, g, t, v, s = bk.engines()
            g.wait(sm)
            g.dma_start(wo_sb[:].rearrange("p (k c) -> p k c", k=24), ex["wo4"][db]).then_inc(sd, 16)
            for qt in range(16):
                par = qt % 2
                abase = par * 24 * 128
                y.wait(sm)
                for cc in range(24):
                    y.dma_start(avq[:, abase + cc * 128:abase + (cc + 1) * 128],
                                sp["sp_avnT"][qt, cc]).then_inc(sd, 16)
                t.wait(sd)
                t.wait(sv)
                mmi = None
                for cc in range(16):
                    mmi = t.matmul(ps_os[par][:], avq[:, abase + cc * 128:abase + (cc + 1) * 128],
                                   wo_sb[:, cc * 512:(cc + 1) * 512],
                                   start=(cc == 0), stop=(cc == 15))
                mmi.then_inc(sm, 1)
                mmi = None
                for cc in range(16, 24):
                    mmi = t.matmul(ps_fs[par][:], avq[:, abase + cc * 128:abase + (cc + 1) * 128],
                                   wo_sb[:, cc * 512:(cc + 1) * 512],
                                   start=(cc == 16), stop=(cc == 23))
                mmi.then_inc(sm, 1)
                v.wait(sm)
                v.wait(sd)
                v.tensor_copy(stoF[:, par * 512:(par + 1) * 512], ps_fs[par][:]).then_inc(sv, 1)
                v.tensor_copy(sto[:, par * 512:(par + 1) * 512], ps_os[par][:]).then_inc(sv, 1)
                y.wait(sv)
                y.dma_start(partF_out[qt * 128:(qt + 1) * 128, db * 512:(db + 1) * 512],
                            stoF[:, par * 512:(par + 1) * 512]).then_inc(sd, 16)
                y.dma_start(part_out[qt * 128:(qt + 1) * 128, db * 512:(db + 1) * 512],
                            sto[:, par * 512:(par + 1) * 512]).then_inc(sd, 16)
            y.wait(sd)
            bk.run(nc)



def build_stageA():
    nc = bass.Bass("TRN2", target_bir_lowering=False)
    ex = _declare(nc, STAGEA_IN, "ExternalInput")
    part = nc.dram_tensor("part", [T, 2048], BF16, kind="ExternalOutput").ap()
    partF = nc.dram_tensor("partF", [T, 2048], BF16, kind="ExternalOutput").ap()
    sp = {
        "sp_qf": nc.dram_tensor("sp_qf", [8 * 128, T], BF16).ap(),
        "sp_kf": nc.dram_tensor("sp_kf", [8 * 128, T], BF16).ap(),
        "sp_vtm": nc.dram_tensor("sp_vtm", [16 * 128, 1024], BF16).ap(),
        "sp_avnT": nc.dram_tensor("sp_avnT", [16, 24, 128, 128], BF16).ap(),
    }
    with ExitStack() as st:
        emit_stageA(nc, st, ex, part, partF, sp)
    nc.finalize()
    return nc




STAGEB_IN = [
    ("xTr", (D, 1024), F32),
    ("partAr", (1024, 2048), BF16),
    ("partBr", (1024, 2048), BF16),
    ("pfAr", (1024, 2048), BF16),
    ("pfBr", (1024, 2048), BF16),
    ("A_t", (16, 128, 16), F32),
    ("Bmat", (16, 8192), F32),
    ("ln2w_col", (128, 16), F32),
    ("wi4", (16, 128, 16, 512), F32),
    ("wu4", (16, 128, 16, 512), F32),
    ("wd4", (16, 128, 64, 128), F32),
    ("ones_bf", (128, 1), BF16),
    ("ones_f32", (128, 1), F32),
    ("onesrow_f32", (1, 128), F32),
]

TB = 1024
B_UPTO = 99


def emit_stageB(nc, st, ex, x1_out, fused=False, x1bf_out=None):
    A = lambda nm, sh, dt, side="left": st.enter_context(nc.sbuf_tensor(nm, list(sh), dt, side=side))
    onesb = A("onbB", (128, 1), BF16)
    onesf = A("onfB", (128, 1), F32)
    onesrow = A("onrB", (1, 128), F32)
    epsb = A("epsbB", (128, 2), F32)
    ln2w = A("l2wB", (128, 16), F32)
    mean_row = A("mrB", (1, TB), F32)
    rstd_row = A("rstdB", (1, TB), F32)
    srow = A("srowB", (1, TB), F32)
    mod_row = A("modB", (16, TB), F32)
    modp = A("modpB", (16, TB), BF16)
    h_bf = A("hB", (128, 16 * TB), BF16)
    bcA = A("bcAB", (128, TB), F32)
    bcB = A("bcBB", (128, TB), F32)
    bcC = A("bcCB", (128, TB), F32)

    sd = Ct(nc.alloc_semaphore("sdB"))
    sm = Ct(nc.alloc_semaphore("smB"))
    sv = Ct(nc.alloc_semaphore("svB"))
    ss = Ct(nc.alloc_semaphore("ssB"))
    sg = Ct(nc.alloc_semaphore("sgB"))

    bk = Blk(); y, g, t, v, s = bk.engines()
    y.dma_start(onesb[:], ex["ones_bf"][:]).then_inc(sd, 16)
    y.dma_start(onesf[:], ex["ones_f32"][:]).then_inc(sd, 16)
    y.dma_start(onesrow[:], ex["onesrow_f32"][:]).then_inc(sd, 16)
    y.dma_start(ln2w[:], ex["ln2w_col"][:]).then_inc(sd, 16)
    v.memset(epsb[:, 0:1], 1e-5)
    v.memset(epsb[:, 1:2], 1e-6)
    y.wait(sd)
    bk.run(nc)

    if B_UPTO <= 1:
        return
    with nc.sbuf_tensor("xldB", [128, TB], F32, side="right") as xld, \
         nc.sbuf_tensor("scrB0", [128, TB], BF16, side="right") as scr0, \
         nc.sbuf_tensor("scrB1", [128, TB], BF16, side="right") as scr1:
        with nc.psum_tensor("psSumB", [1, TB], F32) as ps_sum, \
             nc.psum_tensor("psSqB", [1, TB], F32) as ps_sq:
            bk = Blk(); y, g, t, v, s = bk.engines()
            for cc in range(16):
                if cc >= 1:
                    y.wait(sv); y.wait(ss)
                y.dma_start(xld[:], ex["xTr"][cc * 128:(cc + 1) * 128, :]).then_inc(sd, 16)
                v.wait(sd)
                if cc >= 1:
                    v.wait(sm)
                v.tensor_copy(scr0[:], xld[:]).then_inc(sv, 1)
                s.wait(sd)
                if cc >= 1:
                    s.wait(sm)
                s.activation(scr1[:], xld[:], AF.Square).then_inc(ss, 1)
                t.wait(sv); t.wait(ss)
                for j in range(2):
                    t.matmul(ps_sum[0:1, j * 512:(j + 1) * 512], onesb[:], scr0[:, j * 512:(j + 1) * 512],
                             start=(cc == 0), stop=(cc == 15))
                mmi = None
                for j in range(2):
                    mmi = t.matmul(ps_sq[0:1, j * 512:(j + 1) * 512], onesb[:], scr1[:, j * 512:(j + 1) * 512],
                                   start=(cc == 0), stop=(cc == 15))
                mmi.then_inc(sm, 1)
            bk.run(nc)
            bk = Blk(); y, g, t, v, s = bk.engines()
            v.tensor_scalar(mean_row[:], ps_sum[0:1, :], 1.0 / 2048.0, None, ALU.mult)
            v.tensor_scalar(srow[:], ps_sq[0:1, :], 1.0 / 2048.0, None, ALU.mult).then_inc(sv, 1)
            bk.run(nc)
            bk = Blk(); y, g, t, v, s = bk.engines()
            v.tensor_tensor(ps_sum[0:1, :], mean_row[:], mean_row[:], ALU.mult).then_inc(sv, 1)
            bk.run(nc)
            bk = Blk(); y, g, t, v, s = bk.engines()
            v.tensor_tensor(srow[:], srow[:], ps_sum[0:1, :], ALU.subtract).then_inc(sv, 1)
            s.wait(sv)
            s.activation(ps_sum[0:1, :], srow[:], AF.Sqrt, bias=epsb[0:1, 0:1]).then_inc(ss, 1)
            bk.run(nc)
            bk = Blk(); y, g, t, v, s = bk.engines()
            v.reciprocal(rstd_row[:], ps_sum[0:1, :]).then_inc(sv, 1)
            bk.run(nc)
        with nc.psum_tensor("psBcB", [128, 1024], F32) as ps_bc:
            bk = Blk(); y, g, t, v, s = bk.engines()
            for j in range(2):
                t.wait(sv)
                t.matmul(ps_bc[:, 0:512], onesrow[:], mean_row[0:1, j * 512:(j + 1) * 512], start=True, stop=True)
                t.matmul(ps_bc[:, 512:1024], onesrow[:], rstd_row[0:1, j * 512:(j + 1) * 512], start=True, stop=True).then_inc(sm, 1)
                v.wait(sm)
                v.tensor_copy(bcA[:, j * 512:(j + 1) * 512], ps_bc[:, 0:512])
                v.tensor_copy(bcB[:, j * 512:(j + 1) * 512], ps_bc[:, 512:1024]).then_inc(sv, 1)
            bk.run(nc)

            with nc.psum_tensor("psS2B", [1, TB], F32) as ps_s2:
                bk = Blk(); y, g, t, v, s = bk.engines()
                for cc in range(16):
                    if cc >= 1:
                        y.wait(sg)
                    y.dma_start(xld[:], ex["xTr"][cc * 128:(cc + 1) * 128, :]).then_inc(sd, 16)
                    v.wait(sd)
                    if cc >= 1:
                        v.wait(sm)
                    v.tensor_tensor(xld[:], xld[:], bcA[:], ALU.subtract).then_inc(sv, 1)
                    g.wait(sv)
                    if cc >= 1:
                        g.wait(ss)
                    g.tensor_tensor(scr0[:], xld[:], bcB[:], ALU.mult).then_inc(sg, 1)
                    s.wait(sg)
                    if cc >= 1:
                        s.wait(sm)
                    s.activation(scr1[:], scr0[:], AF.Square).then_inc(ss, 1)
                    t.wait(ss)
                    mmi = None
                    for j in range(2):
                        mmi = t.matmul(ps_s2[0:1, j * 512:(j + 1) * 512], onesb[:], scr1[:, j * 512:(j + 1) * 512],
                                       start=(cc == 0), stop=(cc == 15))
                    mmi.then_inc(sm, 1)
                bk.run(nc)
                bk = Blk(); y, g, t, v, s = bk.engines()
                v.tensor_scalar(srow[:], ps_s2[0:1, :], 1.0 / 2048.0, None, ALU.mult).then_inc(sv, 1)
                s.wait(sv)
                s.activation(ps_s2[0:1, :], srow[:], AF.Sqrt, bias=epsb[0:1, 1:2]).then_inc(ss, 1)
                bk.run(nc)
                bk = Blk(); y, g, t, v, s = bk.engines()
                v.reciprocal(srow[:], ps_s2[0:1, :]).then_inc(sv, 1)
                for j in range(2):
                    t.wait(sv)
                    t.matmul(ps_bc[:, 0:512], onesrow[:], srow[0:1, j * 512:(j + 1) * 512], start=True, stop=True).then_inc(sm, 1)
                    v.wait(sm)
                    v.tensor_copy(bcC[:, j * 512:(j + 1) * 512], ps_bc[:, 0:512]).then_inc(sv, 1)
                bk.run(nc)

            bk = Blk(); y, g, t, v, s = bk.engines()
            for cc in range(16):
                if cc >= 1:
                    y.wait(sg)
                y.dma_start(xld[:], ex["xTr"][cc * 128:(cc + 1) * 128, :]).then_inc(sd, 16)
                v.wait(sd)
                v.tensor_tensor(xld[:], xld[:], bcA[:], ALU.subtract).then_inc(sv, 1)
                g.wait(sv)
                g.tensor_tensor(scr0[:], xld[:], bcB[:], ALU.mult).then_inc(sg, 1)
                v.wait(sg)
                v.tensor_scalar(scr1[:], scr0[:], ln2w[:, cc:cc + 1], None, ALU.mult).then_inc(sv, 1)
                g.wait(sv)
                g.tensor_tensor(h_bf[:, cc * TB:(cc + 1) * TB], scr1[:], bcC[:], ALU.mult).then_inc(sg, 1)
            bk.run(nc)

        if B_UPTO <= 14:
            return
        bk = Blk(); y, g, t, v, s = bk.engines()
        v.tensor_scalar(srow[:], rstd_row[:], 1.0, 1.0, ALU.mult, op1=ALU.add).then_inc(sv, 1)
        g.tensor_tensor(mean_row[:], mean_row[:], rstd_row[:], ALU.mult).then_inc(sg, 1)
        bk.run(nc)
        with nc.psum_tensor("psBcB2", [128, 1024], F32) as ps_bc2:
            bk = Blk(); y, g, t, v, s = bk.engines()
            for j in range(2):
                t.matmul(ps_bc2[:, 0:512], onesrow[:], srow[0:1, j * 512:(j + 1) * 512], start=True, stop=True)
                t.matmul(ps_bc2[:, 512:1024], onesrow[:], mean_row[0:1, j * 512:(j + 1) * 512], start=True, stop=True).then_inc(sm, 1)
                v.wait(sm)
                v.tensor_copy(bcA[:, j * 512:(j + 1) * 512], ps_bc2[:, 0:512])
                v.tensor_copy(bcB[:, j * 512:(j + 1) * 512], ps_bc2[:, 512:1024]).then_inc(sv, 1)
                if j == 0:
                    t.wait(sv)
            bk.run(nc)

    with nc.sbuf_tensor("tp0B", [128, TB], BF16, side="right") as tp0, \
         nc.sbuf_tensor("tp1B", [128, TB], BF16, side="right") as tp1, \
         nc.sbuf_tensor("tpFB", [128, TB], BF16, side="right") as tpF, \
         nc.sbuf_tensor("AtB", [128, 16 * 16], BF16, side="right") as A_sb, \
         nc.psum_tensor("psModB", [16, TB], F32) as ps_mod:
        bk = Blk(); y, g, t, v, s = bk.engines()
        for cc in range(16):
            g.dma_start(A_sb[:, cc * 16:(cc + 1) * 16], ex["A_t"][cc]).then_inc(sd, 16)
        for cc in range(16):
            if cc >= 1:
                y.wait(sm)
            if fused:
                y.dma_start(tpF[:], ex["rs_pf"][:, cc * 128:(cc + 1) * 128], transpose=True).then_inc(sd, 16)
                t.wait(sd)
            else:
                y.dma_start(tp0[:], ex["pfAr"][:, cc * 128:(cc + 1) * 128], transpose=True).then_inc(sd, 16)
                y.dma_start(tp1[:], ex["pfBr"][:, cc * 128:(cc + 1) * 128], transpose=True).then_inc(sd, 16)
                v.wait(sd)
                if cc >= 1:
                    v.wait(sm)
                v.tensor_tensor(tpF[:], tp0[:], tp1[:], ALU.add).then_inc(sv, 1)
                t.wait(sv)
            mmi = None
            for half in range(2):
                mmi = t.matmul(ps_mod[:, half * 512:(half + 1) * 512],
                               A_sb[:, cc * 16:(cc + 1) * 16],
                               tpF[:, half * 512:(half + 1) * 512],
                               start=(cc == 0), stop=(cc == 15))
            mmi.then_inc(sm, 1)
        bk.run(nc)
        bk = Blk(); y, g, t, v, s = bk.engines()
        s.activation(mod_row[:], ps_mod[:], AF.Tanh).then_inc(ss, 1)
        g.wait(ss)
        g.tensor_scalar(mod_row[:], mod_row[:], 0.1, 1.0, ALU.mult, op1=ALU.mult).then_inc(sg, 1)
        bk.run(nc)

    if B_UPTO <= 2:
        return
    with nc.sbuf_tensor("AtB2", [128, 16 * 16], BF16, side="right") as A_sb, \
         nc.psum_tensor("psHaB", [16, TB], F32) as ps_ha:
        bk = Blk(); y, g, t, v, s = bk.engines()
        for cc in range(16):
            g.dma_start(A_sb[:, cc * 16:(cc + 1) * 16], ex["A_t"][cc]).then_inc(sd, 16)
        t.wait(sd)
        mmi = None
        for half in range(2):
            for cc in range(16):
                mmi = t.matmul(ps_ha[:, half * 512:(half + 1) * 512],
                               A_sb[:, cc * 16:(cc + 1) * 16],
                               h_bf[:, cc * TB + half * 512:cc * TB + (half + 1) * 512],
                               start=(cc == 0), stop=(cc == 15))
        mmi.then_inc(sm, 1)
        v.wait(sm)
        v.tensor_tensor(modp[:], mod_row[:], ps_ha[:], ALU.mult).then_inc(sv, 1)
        bk.run(nc)

    if B_UPTO <= 3:
        return
    with nc.sbuf_tensor("fuseB", [128, 64 * 512], BF16, side="right") as fuse, \
         nc.sbuf_tensor("gsbB", [128, 512], BF16, side="right") as gate_sb, \
         nc.psum_tensor("psGB", [128, 512], F32) as ps_g, \
         nc.psum_tensor("psUB", [128, 512], F32) as ps_u, \
         nc.psum_tensor("psDB", [128, 512], F32) as ps_d0, \
         nc.psum_tensor("psDB2", [128, 512], F32) as ps_d1:
        ps_ds = [ps_d0, ps_d1]
        for sl in range(2):
          with nc.sbuf_tensor(f"wiB{sl}", [128, 2 * 16 * 512], BF16, side="right") as wbi2, \
               nc.sbuf_tensor(f"wuB{sl}", [128, 2 * 16 * 512], BF16, side="right") as wbu2, \
               nc.sbuf_tensor(f"BmB{sl}", [16, 2 * 512], BF16, side="right") as Bm2:
              bk = Blk(); y, g, t, v, s = bk.engines()
              mm_evt = []
              for og in range(16):
                  par = og % 2
                  wbi = wbi2[:, par * 16 * 512:(par + 1) * 16 * 512]
                  wbu = wbu2[:, par * 16 * 512:(par + 1) * 16 * 512]
                  Bm_sb = Bm2[0:16, par * 512:(par + 1) * 512]
                  if og >= 2:
                      g.wait(sm, mm_evt[og - 2])
                  g.dma_start(wbi.rearrange("p (k c) -> p k c", k=16), ex["wi4"][og]).then_inc(sd, 16)
                  g.dma_start(wbu.rearrange("p (k c) -> p k c", k=16), ex["wu4"][og]).then_inc(sd, 16)
                  g.dma_start(Bm2[0:16, par * 512:(par + 1) * 512], ex["Bmat"][:, og * 512:(og + 1) * 512]).then_inc(sd, 16)
                  load_evt = sd.n
                  t.wait(sd, load_evt)
                  for oc4 in range(4):
                      oc = og * 4 + oc4
                      t.wait(ss)
                      mmi = None
                      for kc in range(16):
                          mmi = t.matmul(ps_g[:], wbi[:, kc * 512 + oc4 * 128:kc * 512 + (oc4 + 1) * 128],
                                         h_bf[:, kc * TB + sl * 512:kc * TB + (sl + 1) * 512],
                                         start=(kc == 0), stop=False)
                      mmi = t.matmul(ps_g[:], Bm_sb[0:16, oc4 * 128:(oc4 + 1) * 128],
                                     modp[0:16, sl * 512:(sl + 1) * 512], start=False, stop=True)
                      mmi.then_inc(sm, 1)
                      t.wait(sv)
                      mmi = None
                      for kc in range(16):
                          mmi = t.matmul(ps_u[:], wbu[:, kc * 512 + oc4 * 128:kc * 512 + (oc4 + 1) * 128],
                                         h_bf[:, kc * TB + sl * 512:kc * TB + (sl + 1) * 512],
                                         start=(kc == 0), stop=(kc == 15))
                      mmi.then_inc(sm, 1)
                      s.wait(sm)
                      s.activation(gate_sb[:], ps_g[:], AF.Silu).then_inc(ss, 1)
                      v.wait(ss)
                      v.tensor_tensor(fuse[:, oc * 512:(oc + 1) * 512], gate_sb[:], ps_u[:], ALU.mult).then_inc(sv, 1)
                  mm_evt.append(sm.n)
              bk.run(nc)
          with nc.sbuf_tensor(f"wdB{sl}", [128, 64 * 128], BF16, side="right") as wbd, \
               nc.sbuf_tensor(f"t0B{sl}", [128, 512], F32, side="right") as t0f, \
               nc.sbuf_tensor(f"t1B{sl}", [128, 512], F32, side="right") as t1f, \
               nc.sbuf_tensor(f"xslB{sl}", [128, 512], F32, side="right") as xsl, \
               nc.sbuf_tensor(f"atB{sl}", [128, 2 * 512], BF16, side="right") as atp, \
               nc.sbuf_tensor(f"stg1B{sl}", [128, 512], F32, side="right") as stg1:
            for dc in range(16):
                bk = Blk(); y, g, t, v, s = bk.engines()
                g.wait(sm)
                g.dma_start(wbd[:].rearrange("p (k c) -> p k c", k=64), ex["wd4"][dc]).then_inc(sd, 16)
                y.dma_start(xsl[:], ex["xTr"][dc * 128:(dc + 1) * 128, sl * 512:(sl + 1) * 512]).then_inc(sd, 16)
                if fused:
                    y.dma_start(atp[:, 0:512], ex["rs_part"][sl * 512:(sl + 1) * 512, dc * 128:(dc + 1) * 128],
                                transpose=True).then_inc(sd, 16)
                else:
                    y.dma_start(atp[:, 0:512], ex["partAr"][sl * 512:(sl + 1) * 512, dc * 128:(dc + 1) * 128],
                                transpose=True).then_inc(sd, 16)
                    y.dma_start(atp[:, 512:1024], ex["partBr"][sl * 512:(sl + 1) * 512, dc * 128:(dc + 1) * 128],
                                transpose=True).then_inc(sd, 16)
                par = dc % 2
                t.wait(sd)
                t.wait(sv)
                mmi = None
                for oc in range(64):
                    mmi = t.matmul(ps_ds[par][:], wbd[:, oc * 128:(oc + 1) * 128],
                                   fuse[:, oc * 512:(oc + 1) * 512],
                                   start=(oc == 0), stop=(oc == 63))
                mmi.then_inc(sm, 1)
                v.wait(sm)
                v.tensor_tensor(t0f[:], ps_ds[par][:], atp[:, 0:512], ALU.add).then_inc(sv, 1)
                if not fused:
                    g.wait(sv)
                    g.tensor_tensor(t0f[:], t0f[:], atp[:, 512:1024], ALU.add).then_inc(sg, 1)
                g.wait(sv)
                g.tensor_tensor(t1f[:], xsl[:], bcA[:, sl * 512:(sl + 1) * 512], ALU.mult).then_inc(sg, 1)
                v.wait(sg)
                v.tensor_tensor(t0f[:], t0f[:], t1f[:], ALU.add).then_inc(sv, 1)
                g.wait(sv)
                g.wait(sd)
                g.tensor_tensor(stg1[:], t0f[:], bcB[:, sl * 512:(sl + 1) * 512], ALU.subtract).then_inc(sg, 1)
                y.wait(sg)
                y.dma_start(x1_out[dc * 128:(dc + 1) * 128, sl * 512:(sl + 1) * 512], stg1[:]).then_inc(sd, 16)
                if fused:
                    v.wait(sg)
                    v.wait(sd)
                    v.tensor_copy(atp[:, 512:1024], stg1[:]).then_inc(sv, 1)
                    y.wait(sv)
                    y.dma_start(x1bf_out[dc * 128:(dc + 1) * 128, sl * 512:(sl + 1) * 512],
                                atp[:, 512:1024]).then_inc(sd, 16)
                bk.run(nc)
        bk = Blk(); y, g, t, v, s = bk.engines()
        y.wait(sd)
        bk.run(nc)


def build_stageB():
    nc = bass.Bass("TRN2", target_bir_lowering=False)
    ex = _declare(nc, STAGEB_IN, "ExternalInput")
    x1r = nc.dram_tensor("x1r", [D, 1024], F32, kind="ExternalOutput").ap()
    with ExitStack() as st:
        emit_stageB(nc, st, ex, x1r)
    nc.finalize()
    return nc



STAGEC_IN = [
    ("x1T", (D, T), F32),
    ("w1a", (19, 128, 16, 128), F32),
    ("w1z", (16, 128, 16, 128), F32),
    ("convw", (18, 128, 4), F32),
    ("convb", (18, 128, 1), F32),
    ("Alog", (16, 1), F32),
    ("dtb", (16, 1), F32),
    ("Dvec", (16, 1), F32),
    ("sel16", (16, 16, 128), F32),
    ("sel127", (128, 128), F32),
    ("maskT", (2, 128, 256), F32),
    ("ident_bf", (128, 128), BF16),
    ("ident_f32", (128, 128), F32),
    ("ones_bf", (128, 1), BF16),
    ("ones_f32", (128, 1), F32),
    ("onesrow_f32", (1, 128), F32),
]

TP = 2051


def emit_stageC(nc, st, ex, yg_out, ssq_out, sp, fused=False):
    A = lambda nm, sh, dt, side="left": st.enter_context(nc.sbuf_tensor(nm, list(sh), dt, side=side))
    identb = A("idbC", (128, 128), BF16)
    identf = A("idfC", (128, 128), F32)
    onesb = A("onbC", (128, 1), BF16)
    onesf = A("onfC", (128, 1), F32)
    onesrow = A("onrC", (1, 128), F32)
    epsb = A("epsbC", (128, 4), F32)
    sel16 = A("selC", (16, 16 * 128), F32)
    sel127 = A("s127C", (128, 128), F32)
    maskT = A("mskC", (128, 2 * 256), F32)
    avec = A("avecC", (16, 4), F32)
    DT_b = A("dtbbC", (128, 16), F32)
    r0 = A("r0C", (16, T), F32)
    r1 = A("r1C", (16, T), F32)
    AcsT = A("acstC", (128, 256), F32)
    nAcsT = A("nacstC", (128, 256), F32)
    dtT = A("dttC", (128, 256), F32)
    S_prev = A("sprevC", (64, 16 * 128), F32)
    S_prev_bf = A("sprevbC", (64, 16 * 128), BF16)
    S_tmp = A("stmpC", (64, 128), F32)

    sd = Ct(nc.alloc_semaphore("sdC"))
    sm = Ct(nc.alloc_semaphore("smC"))
    sv = Ct(nc.alloc_semaphore("svC"))
    ss = Ct(nc.alloc_semaphore("ssC"))
    sg = Ct(nc.alloc_semaphore("sgC"))

    bk = Blk(); y, g, t, v, s = bk.engines()
    y.dma_start(identb[:], ex["ident_bf"][:]).then_inc(sd, 16)
    y.dma_start(identf[:], ex["ident_f32"][:]).then_inc(sd, 16)
    y.dma_start(onesb[:], ex["ones_bf"][:]).then_inc(sd, 16)
    y.dma_start(onesf[:], ex["ones_f32"][:]).then_inc(sd, 16)
    y.dma_start(onesrow[:], ex["onesrow_f32"][:]).then_inc(sd, 16)
    for h in range(16):
        y.dma_start(sel16[:, h * 128:(h + 1) * 128], ex["sel16"][h]).then_inc(sd, 16)
    y.dma_start(sel127[:], ex["sel127"][:]).then_inc(sd, 16)
    y.dma_start(maskT[:, 0:256], ex["maskT"][0]).then_inc(sd, 16)
    y.dma_start(maskT[:, 256:512], ex["maskT"][1]).then_inc(sd, 16)
    y.dma_start(avec[:, 0:1], ex["Alog"][:]).then_inc(sd, 16)
    y.dma_start(avec[:, 1:2], ex["dtb"][:]).then_inc(sd, 16)
    y.dma_start(avec[:, 3:4], ex["Dvec"][:]).then_inc(sd, 16)
    v.memset(epsb[:, 0:1], 1e-5)
    v.memset(epsb[:, 1:2], 1.0)
    v.memset(epsb[:, 2:3], 1.1920929e-07)
    y.wait(sd)
    bk.run(nc)

    with nc.psum_tensor("psC0", [128, 32], F32) as ps0:
        bk = Blk(); y, g, t, v, s = bk.engines()
        s.activation(avec[0:16, 2:3], avec[0:16, 0:1], AF.Exp).then_inc(ss, 1)
        g.wait(ss)
        g.tensor_scalar(avec[0:16, 2:3], avec[0:16, 2:3], -1.0, 1.0, ALU.mult, op1=ALU.mult).then_inc(sg, 1)
        t.wait(sg)
        t.transpose(ps0[0:1, 0:16], avec[0:16, 3:4], identf[0:16, 0:16]).then_inc(sm, 1)
        v.wait(sm)
        v.tensor_copy(r0[0:1, 0:16], ps0[0:1, 0:16]).then_inc(sv, 1)
        t.wait(sv)
        t.matmul(ps0[:, 16:32], onesrow[:], r0[0:1, 0:16], start=True, stop=True).then_inc(sm, 1)
        v.wait(sm)
        v.tensor_copy(DT_b[:], ps0[:, 16:32]).then_inc(sv, 1)
        bk.run(nc)

    xsbc_stack = ExitStack()
    xs_c = xsbc_stack.enter_context(nc.sbuf_tensor("xscC", [128, 16 * TP], BF16, side="right"))
    Bp = xsbc_stack.enter_context(nc.sbuf_tensor("bpC", [128, TP], BF16, side="right"))
    Cp = xsbc_stack.enter_context(nc.sbuf_tensor("cpC", [128, TP], BF16, side="right"))

    with nc.sbuf_tensor("xn2C", [128, 16 * T], BF16, side="right") as xn2:
        with nc.sbuf_tensor("mrC", [1, T], BF16, side="right") as mean_row, \
             nc.sbuf_tensor("orbC", [1, 128], BF16, side="right") as onesrow_bf, \
             nc.sbuf_tensor("swC", [1, T], F32, side="right") as srow, \
             nc.sbuf_tensor("bcAC", [128, T], BF16, side="right") as bcA, \
             nc.sbuf_tensor("bcBC", [128, T], BF16, side="right") as bcB, \
             nc.sbuf_tensor("xldC", [128, T], BF16 if fused else F32, side="right") as xld:
            with nc.psum_tensor("psSumC", [1, T], F32) as ps_sum, \
                 nc.psum_tensor("psSqC", [1, T], F32) as ps_sq:
                bk = Blk(); y, g, t, v, s = bk.engines()
                g.dma_start(onesrow_bf[:], ex["onesrow_f32"][:]).then_inc(sd, 16)
                for cc in range(16):
                    if cc >= 1:
                        y.wait(sv); y.wait(ss)
                    if fused:
                        y.dma_start(xld[:, 0:1024], ex["ag1"][cc * 128:(cc + 1) * 128, :]).then_inc(sd, 16)
                        y.dma_start(xld[:, 1024:2048], ex["ag1"][2048 + cc * 128:2048 + (cc + 1) * 128, :]).then_inc(sd, 16)
                    else:
                        y.dma_start(xld[:], ex["x1T"][cc * 128:(cc + 1) * 128, :]).then_inc(sd, 16)
                    v.wait(sd)
                    if cc >= 1:
                        v.wait(sm)
                    v.tensor_copy(bcB[:], xld[:]).then_inc(sv, 1)
                    s.wait(sd)
                    if cc >= 1:
                        s.wait(sm)
                    s.activation(xn2[:, cc * T:(cc + 1) * T], xld[:], AF.Square).then_inc(ss, 1)
                    t.wait(sv); t.wait(ss)
                    for j in range(4):
                        t.matmul(ps_sum[0:1, j * 512:(j + 1) * 512], onesb[:], bcB[:, j * 512:(j + 1) * 512],
                                 start=(cc == 0), stop=(cc == 15))
                    mmi = None
                    for j in range(4):
                        mmi = t.matmul(ps_sq[0:1, j * 512:(j + 1) * 512], onesb[:],
                                       xn2[:, cc * T + j * 512:cc * T + (j + 1) * 512],
                                       start=(cc == 0), stop=(cc == 15))
                    mmi.then_inc(sm, 1)
                bk.run(nc)
                bk = Blk(); y, g, t, v, s = bk.engines()
                v.tensor_scalar(mean_row[:], ps_sum[0:1, 0:T], 1.0 / 2048.0, None, ALU.mult)
                v.tensor_scalar(srow[:], ps_sq[0:1, 0:T], 1.0 / 2048.0, None, ALU.mult).then_inc(sv, 1)
                bk.run(nc)
                bk = Blk(); y, g, t, v, s = bk.engines()
                v.tensor_tensor(ps_sum[0:1, 0:T], mean_row[:], mean_row[:], ALU.mult).then_inc(sv, 1)
                bk.run(nc)
                bk = Blk(); y, g, t, v, s = bk.engines()
                v.tensor_tensor(srow[:], srow[:], ps_sum[0:1, 0:T], ALU.subtract).then_inc(sv, 1)
                s.wait(sv)
                s.activation(ps_sum[0:1, 0:T], srow[:], AF.Sqrt, bias=epsb[0:1, 0:1]).then_inc(ss, 1)
                bk.run(nc)
                bk = Blk(); y, g, t, v, s = bk.engines()
                v.reciprocal(srow[:], ps_sum[0:1, 0:T]).then_inc(sv, 1)
                bk.run(nc)
            with nc.psum_tensor("psBcC", [128, 1024], F32) as ps_bc:
                bk = Blk(); y, g, t, v, s = bk.engines()
                for j in range(4):
                    t.wait(sv)
                    t.matmul(ps_bc[:, 0:512], onesrow_bf[:], mean_row[0:1, j * 512:(j + 1) * 512], start=True, stop=True)
                    t.matmul(ps_bc[:, 512:1024], onesrow[:], srow[0:1, j * 512:(j + 1) * 512], start=True, stop=True).then_inc(sm, 1)
                    v.wait(sm)
                    v.tensor_copy(bcA[:, j * 512:(j + 1) * 512], ps_bc[:, 0:512])
                    v.tensor_copy(bcB[:, j * 512:(j + 1) * 512], ps_bc[:, 512:1024]).then_inc(sv, 1)
                bk.run(nc)
                bk = Blk(); y, g, t, v, s = bk.engines()
                for cc in range(16):
                    if cc >= 1:
                        y.wait(sg)
                    if fused:
                        y.dma_start(xld[:, 0:1024], ex["ag1"][cc * 128:(cc + 1) * 128, :]).then_inc(sd, 16)
                        y.dma_start(xld[:, 1024:2048], ex["ag1"][2048 + cc * 128:2048 + (cc + 1) * 128, :]).then_inc(sd, 16)
                    else:
                        y.dma_start(xld[:], ex["x1T"][cc * 128:(cc + 1) * 128, :]).then_inc(sd, 16)
                    v.wait(sd)
                    v.tensor_tensor(xld[:], xld[:], bcA[:], ALU.subtract).then_inc(sv, 1)
                    g.wait(sv)
                    g.tensor_tensor(xn2[:, cc * T:(cc + 1) * T], xld[:], bcB[:], ALU.mult).then_inc(sg, 1)
                bk.run(nc)

        with ExitStack() as st3, \
             nc.psum_tensor("psqC0", [128, 512], F32) as pq0, \
             nc.psum_tensor("psqC1", [128, 512], F32) as pq1:
            A3 = lambda nm, sh, dt: st3.enter_context(nc.sbuf_tensor(nm, list(sh), dt, side="right"))
            wbuf = A3("wbC", (128, 16 * 128), BF16)
            cwb = A3("cwbC", (128, 4), F32)
            cbb = A3("cbbC", (128, 1), F32)
            t1f = A3("t1fC", (128, T), F32)
            t2f = A3("t2fC", (128, T), F32)
            stq = A3("stqC", (128, T), BF16)

            for oc in range(19):
                bk = Blk(); engs = bk.engines(); y, g, t, v, s = engs
                g.wait(sm)
                g.dma_start(wbuf[:].rearrange("p (k c) -> p k c", k=16), ex["w1a"][oc]).then_inc(sd, 16)
                if oc < 18:
                    if oc >= 1:
                        y.wait(ss)
                    y.dma_start(cwb[:], ex["convw"][oc]).then_inc(sd, 16)
                    y.dma_start(cbb[:], ex["convb"][oc]).then_inc(sd, 16)
                if oc < 16:
                    dst = xs_c[:, oc * TP:(oc + 1) * TP]
                elif oc == 16:
                    dst = Bp[:]
                elif oc == 17:
                    dst = Cp[:]
                t.wait(sd)
                for tb in range(4):
                    par = tb % 2
                    t.wait(sv)
                    mmi = None
                    for kc in range(16):
                        mmi = t.matmul([pq0, pq1][par][:], wbuf[:, kc * 128:(kc + 1) * 128],
                                       xn2[:, kc * T + tb * 512:kc * T + (tb + 1) * 512],
                                       start=(kc == 0), stop=(kc == 15))
                    mmi.then_inc(sm, 1)
                    v.wait(sm)
                    if oc < 18:
                        v.tensor_copy(dst[:, 3 + tb * 512:3 + (tb + 1) * 512], [pq0, pq1][par][:]).then_inc(sv, 1)
                    else:
                        v.tensor_copy(r0[0:16, tb * 512:(tb + 1) * 512], [pq0, pq1][par][0:16, :]).then_inc(sv, 1)
                if oc < 18:
                    v.wait(sv)
                    v.wait(sd)
                    v.memset(dst[:, 0:3], 0.0)
                    v.tensor_scalar(t1f[:], dst[:, 0:T], cwb[:, 0:1], None, ALU.mult).then_inc(sv, 1)
                    g.wait(sv)
                    g.tensor_scalar(t2f[:], dst[:, 1:1 + T], cwb[:, 1:2], 1.0, ALU.mult, op1=ALU.mult).then_inc(sg, 1)
                    v.wait(sg)
                    v.tensor_tensor(t1f[:], t1f[:], t2f[:], ALU.add).then_inc(sv, 1)
                    g.wait(sv)
                    g.tensor_scalar(t2f[:], dst[:, 2:2 + T], cwb[:, 2:3], 1.0, ALU.mult, op1=ALU.mult).then_inc(sg, 1)
                    v.wait(sg)
                    v.tensor_tensor(t1f[:], t1f[:], t2f[:], ALU.add).then_inc(sv, 1)
                    g.wait(sv)
                    g.tensor_scalar(t2f[:], dst[:, 3:3 + T], cwb[:, 3:4], 1.0, ALU.mult, op1=ALU.mult).then_inc(sg, 1)
                    v.wait(sg)
                    v.tensor_tensor(t1f[:], t1f[:], t2f[:], ALU.add).then_inc(sv, 1)
                    s.wait(sv)
                    s.activation(dst[:, 3:3 + T], t1f[:], AF.Silu, bias=cbb[:]).then_inc(ss, 1)
                    v.wait(ss)
                bk.run(nc)

            for oc in range(16):
                bk = Blk(); engs = bk.engines(); y, g, t, v, s = engs
                g.wait(sm)
                g.dma_start(wbuf[:].rearrange("p (k c) -> p k c", k=16), ex["w1z"][oc]).then_inc(sd, 16)
                t.wait(sd)
                for tb in range(4):
                    par = tb % 2
                    t.wait(sv)
                    mmi = None
                    for kc in range(16):
                        mmi = t.matmul([pq0, pq1][par][:], wbuf[:, kc * 128:(kc + 1) * 128],
                                       xn2[:, kc * T + tb * 512:kc * T + (tb + 1) * 512],
                                       start=(kc == 0), stop=(kc == 15))
                    mmi.then_inc(sm, 1)
                    v.wait(sm)
                    if tb == 0:
                        v.wait(sd)
                    v.tensor_copy(stq[:, tb * 512:(tb + 1) * 512], [pq0, pq1][par][:]).then_inc(sv, 1)
                y.wait(sv)
                y.dma_start(sp["sp_z"][oc * 128:(oc + 1) * 128, :], stq[:]).then_inc(sd, 16)
                bk.run(nc)
            bk = Blk(); y, g, t, v, s = bk.engines()
            y.wait(sd)
            bk.run(nc)

    with nc.sbuf_tensor("r2C", [16, T], F32, side="right") as r2, \
         nc.sbuf_tensor("r3C", [16, T], F32, side="right") as r3:
        bk = Blk(); y, g, t, v, s = bk.engines()
        v.tensor_scalar(r1[0:16, :], r0[0:16, :], avec[0:16, 1:2], None, ALU.add).then_inc(sv, 1)
        g.wait(sv)
        g.tensor_scalar(r2[0:16, :], r1[0:16, :], 0.0, None, ALU.max).then_inc(sg, 1)
        v.wait(sg)
        v.tensor_scalar(r3[0:16, :], r2[0:16, :], 2.0, 1.0, ALU.mult, op1=ALU.mult).then_inc(sv, 1)
        g.wait(sv)
        g.tensor_tensor(r3[0:16, :], r1[0:16, :], r3[0:16, :], ALU.subtract).then_inc(sg, 1)
        s.wait(sg)
        s.activation(r3[0:16, :], r3[0:16, :], AF.Exp).then_inc(ss, 1)
        bk.run(nc)
        bk = Blk(); y, g, t, v, s = bk.engines()
        s.activation(r3[0:16, :], r3[0:16, :], AF.Ln, bias=epsb[0:16, 1:2]).then_inc(ss, 1)
        bk.run(nc)
        bk = Blk(); y, g, t, v, s = bk.engines()
        v.tensor_tensor(r0[0:16, :], r2[0:16, :], r3[0:16, :], ALU.add).then_inc(sv, 1)
        g.wait(sv)
        g.tensor_scalar(r1[0:16, :], r0[0:16, :], avec[0:16, 2:3], 1.0, ALU.mult, op1=ALU.mult).then_inc(sg, 1)
        bk.run(nc)
        src, dst_ = r1, r2
        for si, sh in enumerate((1, 2, 4, 8, 16, 32, 64, 128)):
            bk = Blk(); y, g, t, v, s = bk.engines()
            for c in range(8):
                base = c * 256
                v.tensor_tensor(dst_[0:16, base + sh:base + 256], src[0:16, base + sh:base + 256],
                                src[0:16, base:base + 256 - sh], ALU.add)
                v.tensor_copy(dst_[0:16, base:base + sh], src[0:16, base:base + sh])
            bk.run(nc)
            src, dst_ = dst_, src
        Acs = src

        with nc.psum_tensor("psTrC", [128, 64], F32) as ps_trf:
            bk = Blk(); y, g, t, v, s = bk.engines()
            for ts_ in range(16):
                if ts_ >= 1:
                    t.wait(sv)
                t.transpose(ps_trf[:, 0:16], Acs[0:16, ts_ * 128:(ts_ + 1) * 128], identf[0:16, 0:16])
                tri = t.transpose(ps_trf[:, 16:32], r0[0:16, ts_ * 128:(ts_ + 1) * 128], identf[0:16, 0:16])
                tri.then_inc(sm, 1)
                v.wait(sm)
                v.tensor_copy(AcsT[:, ts_ * 16:(ts_ + 1) * 16], ps_trf[:, 0:16])
                v.tensor_copy(dtT[:, ts_ * 16:(ts_ + 1) * 16], ps_trf[:, 16:32]).then_inc(sv, 1)
            bk.run(nc)
        bk = Blk(); y, g, t, v, s = bk.engines()
        v.tensor_scalar(nAcsT[:], AcsT[:], -1.0, None, ALU.mult).then_inc(sv, 1)
        bk.run(nc)



    y_buf = A("ybufC", (128, 16 * T), BF16)
    bk = Blk(); y, g, t, v, s = bk.engines()
    v.memset(S_prev[:], 0.0)
    v.memset(S_prev_bf[:], 0.0)
    bk.run(nc)

    with ExitStack() as st4, \
         nc.psum_tensor("psLC", [128, 512], F32) as ps_L, \
         nc.psum_tensor("psYC", [128, 256], F32) as ps_Y, \
         nc.psum_tensor("psYoC", [128, 256], F32) as ps_Yo, \
         nc.psum_tensor("psSC", [64, 128], F32) as ps_S, \
         nc.psum_tensor("psTbC", [128, 256], BF16) as ps_tb, \
         nc.psum_tensor("psMsC", [128, 64], F32) as ps_ms:
        A4 = lambda nm, sh, dt: st4.enter_context(nc.sbuf_tensor(nm, list(sh), dt, side="right"))
        B_tm = A4("btmC", (128, 2 * 64), BF16)
        CBt = A4("cbtC", (128, 512), BF16)
        L_sb = A4("lsbC", (128, 512), BF16)
        d_T = A4("dtTC2", (128, 512), BF16)
        xh_tm = A4("xhC", (128, 256), BF16)
        xd_tm = A4("xdC", (128, 256), BF16)
        xdd_tm = A4("xddC", (128, 256), BF16)
        decT = A4("decC", (128, 32), F32)
        cdec = A4("cdecC", (128, 16), F32)
        expT = A4("expTC", (128, 32), F32)
        yos = A4("yosC", (128, 256), F32)
        ycm = A4("ycmC", (128, 256), F32)
        xDq = A4("xdqC", (128, 256), BF16)
        st_sb = A4("stsbC", (64, 128), F32)

        for c in range(8):
            base = 3 + c * 256
            bk = Blk(); y, g, t, v, s = bk.engines()
            t.wait(sv); t.wait(ss)
            tri = None
            for kt in range(2):
                tri = t.transpose(ps_tb[:, kt * 64:(kt + 1) * 64],
                                  Bp[0:64, base + kt * 128:base + (kt + 1) * 128], identb[0:64, 0:64])
            tri.then_inc(sm, 1)
            v.wait(sm)
            v.tensor_copy(B_tm[:], ps_tb[:, 0:128]).then_inc(sv, 1)
            t.wait(sv)
            mmi = None
            for kt in range(2):
                mmi = t.matmul(ps_L[:, kt * 256:(kt + 1) * 256], Bp[0:64, base + kt * 128:base + (kt + 1) * 128],
                               Cp[0:64, base:base + 256], start=True, stop=True)
            mmi.then_inc(sm, 1)
            v.wait(sm)
            v.tensor_copy(CBt[:], ps_L[:]).then_inc(sv, 1)
            t.wait(sv)
            t.matmul(ps_ms[:, 0:16], sel127[:], AcsT[:, (2 * c + 1) * 16:(2 * c + 2) * 16],
                     start=True, stop=True).then_inc(sm, 1)
            v.wait(sm)
            for kt in range(2):
                v.tensor_tensor(decT[:, kt * 16:(kt + 1) * 16], ps_ms[:, 0:16],
                                AcsT[:, (2 * c + kt) * 16:(2 * c + kt + 1) * 16], ALU.subtract)
            v.tensor_copy(cdec[:], ps_ms[:, 0:16]).then_inc(sv, 1)
            s.wait(sv)
            s.activation(decT[:], decT[:], AF.Exp)
            s.activation(cdec[:], cdec[:], AF.Exp)
            for kt in range(2):
                s.activation(expT[:, kt * 16:(kt + 1) * 16], AcsT[:, (2 * c + kt) * 16:(2 * c + kt + 1) * 16],
                             AF.Exp)
            bk.run(nc)

            for h in range(16):
                bk = Blk(); y, g, t, v, s = bk.engines()
                t.wait(ss)
                mmi = None
                for kt in range(2):
                    t.matmul(ps_L[:, kt * 256:(kt + 1) * 256], sel16[:, h * 128:(h + 1) * 128],
                             Acs[0:16, c * 256:(c + 1) * 256], start=True, stop=False)
                    mmi = t.matmul(ps_L[:, kt * 256:(kt + 1) * 256], identf[:],
                                   maskT[:, kt * 256:(kt + 1) * 256], start=False, stop=True)
                mmi.then_inc(sm, 1)
                s.wait(sm)
                for kt in range(2):
                    s.activation(L_sb[:, kt * 256:(kt + 1) * 256], ps_L[:, kt * 256:(kt + 1) * 256], AF.Exp,
                                 bias=nAcsT[:, (2 * c + kt) * 16 + h:(2 * c + kt) * 16 + h + 1])
                s.activation(L_sb[0:1, 0:1], L_sb[0:1, 0:1], AF.Copy).then_inc(ss, 1)
                v.wait(ss)
                v.tensor_tensor(d_T[:], L_sb[:], CBt[:], ALU.mult).then_inc(sv, 1)
                t.wait(sg); t.wait(sv)
                tri = None
                for kt in range(2):
                    tri = t.transpose(ps_tb[:, kt * 128:(kt + 1) * 128],
                                      xs_c[:, h * TP + base + kt * 128:h * TP + base + (kt + 1) * 128], identb[:])
                tri.then_inc(sm, 1)
                v.wait(sm)
                v.tensor_copy(xh_tm[:], ps_tb[:, 0:256]).then_inc(sv, 1)
                g.wait(sv)
                gi_ = None
                for kt in range(2):
                    gi_ = g.tensor_scalar(xd_tm[:, kt * 128:(kt + 1) * 128], xh_tm[:, kt * 128:(kt + 1) * 128],
                                    dtT[:, (2 * c + kt) * 16 + h:(2 * c + kt) * 16 + h + 1], 1.0, ALU.mult, op1=ALU.mult)
                gi_.then_inc(sg, 1)
                v.wait(sg)
                vi_ = None
                for kt in range(2):
                    vi_ = v.tensor_scalar(xdd_tm[:, kt * 128:(kt + 1) * 128], xd_tm[:, kt * 128:(kt + 1) * 128],
                                    decT[:, kt * 16 + h:kt * 16 + h + 1], None, ALU.mult)
                vi_.then_inc(sv, 1)
                t.wait(sv); t.wait(sg); t.wait(ss)
                mmi = None
                for qt in range(2):
                    for kt in range(qt + 1):
                        mmi = t.matmul(ps_Y[:, qt * 128:(qt + 1) * 128],
                                       d_T[:, kt * 256 + qt * 128:kt * 256 + (qt + 1) * 128],
                                       xd_tm[:, kt * 128:(kt + 1) * 128],
                                       start=(kt == 0), stop=(kt == qt))
                mmi.then_inc(sm, 1)
                mmi = None
                for kt in range(2):
                    mmi = t.matmul(ps_S[:], B_tm[:, kt * 64:(kt + 1) * 64], xdd_tm[:, kt * 128:(kt + 1) * 128],
                                   start=(kt == 0), stop=(kt == 1))
                mmi.then_inc(sm, 1)
                mmi = None
                for qt in range(2):
                    mmi = t.matmul(ps_Yo[:, qt * 128:(qt + 1) * 128],
                                   Cp[0:64, base + qt * 128:base + (qt + 1) * 128],
                                   S_prev_bf[:, h * 128:(h + 1) * 128], start=True, stop=True)
                mmi.then_inc(sm, 1)
                s.wait(sm)
                for qt in range(2):
                    s.activation(yos[:, qt * 128:(qt + 1) * 128], ps_Yo[:, qt * 128:(qt + 1) * 128], AF.Copy,
                                 scale=expT[:, qt * 16 + h:qt * 16 + h + 1])
                s.activation(yos[0:1, 0:1], yos[0:1, 0:1], AF.Copy).then_inc(ss, 1)
                v.wait(ss)
                v.tensor_tensor(ycm[:], ps_Y[:], yos[:], ALU.add).then_inc(sv, 1)
                g.wait(sv)
                for kt in range(2):
                    g.tensor_scalar(xDq[:, kt * 128:(kt + 1) * 128], xh_tm[:, kt * 128:(kt + 1) * 128],
                                    DT_b[:, h:h + 1], 1.0, ALU.mult, op1=ALU.mult)
                g.tensor_scalar(xDq[0:1, 0:1], xDq[0:1, 0:1], 1.0, 1.0, ALU.mult, op1=ALU.mult).then_inc(sg, 1)
                v.wait(sg)
                for qt in range(2):
                    v.tensor_tensor(y_buf[:, (2 * c + qt) * T + h * 128:(2 * c + qt) * T + (h + 1) * 128],
                                    ycm[:, qt * 128:(qt + 1) * 128], xDq[:, qt * 128:(qt + 1) * 128], ALU.add)
                v.tensor_copy(st_sb[:], ps_S[:]).then_inc(sv, 1)
                g.wait(sv)
                g.tensor_scalar(S_tmp[:], S_prev[:, h * 128:(h + 1) * 128], cdec[0:64, h:h + 1], 1.0, ALU.mult, op1=ALU.mult).then_inc(sg, 1)
                v.wait(sg)
                v.tensor_tensor(S_prev[:, h * 128:(h + 1) * 128], S_tmp[:], st_sb[:], ALU.add)
                v.tensor_copy(S_prev_bf[:, h * 128:(h + 1) * 128], S_prev[:, h * 128:(h + 1) * 128]).then_inc(sv, 1)
                bk.run(nc)

    xsbc_stack.close()

    with nc.sbuf_tensor("ssqC", [128, 16], F32, side="right") as ssq_cols, \
         nc.sbuf_tensor("scr2C", [128, T], BF16, side="right") as scr2, \
         nc.sbuf_tensor("ssqoC", [16, 128], F32, side="right") as ssq_o, \
         nc.psum_tensor("psTr2C", [16, 128], F32) as ps_tr2:
        bk = Blk(); y, g, t, v, s = bk.engines()
        for tt in range(16):
            s.activation(scr2[:], y_buf[:, tt * T:(tt + 1) * T], AF.Square,
                         accum_out=ssq_cols[:, tt:tt + 1]).then_inc(ss, 1)
        t.wait(ss)
        t.transpose(ps_tr2[:], ssq_cols[:], identf[:]).then_inc(sm, 1)
        v.wait(sm)
        v.tensor_copy(ssq_o[:], ps_tr2[:]).then_inc(sv, 1)
        y.wait(sv)
        y.dma_start(ssq_out[:, :], ssq_o[:]).then_inc(sd, 16)
        bk.run(nc)

    if fused:
        ccq = Ct(nc.alloc_semaphore("ccqC"))
        bk = Blk(); y, g, t, v, s = bk.engines()
        g.wait(sd)
        g.collective_compute("AllGather", ALU.bypass, replica_groups=PAIRS,
                             ins=[ex["cc_ssq_in"]], outs=[ex["cc_ssq_out"]]).then_inc_default(ccq)
        g.wait(ccq, 1)
        bk.run(nc)
        rst_cols = st.enter_context(nc.sbuf_tensor("rstcC", [128, 16], F32, side="right"))
        with nc.sbuf_tensor("rstC", [16, 256], F32, side="right") as rst, \
             nc.psum_tensor("psRstC", [128, 16], F32) as ps_rst:
            bk = Blk(); y, g, t, v, s = bk.engines()
            y.dma_start(rst[0:16, 0:128], ex["ag3"][0:16, :]).then_inc(sd, 16)
            y.dma_start(rst[0:16, 128:256], ex["ag3"][16:32, :]).then_inc(sd, 16)
            v.wait(sd)
            v.tensor_tensor(rst[0:16, 0:128], rst[0:16, 0:128], rst[0:16, 128:256], ALU.add)
            v.tensor_scalar(rst[0:16, 0:128], rst[0:16, 0:128], 1.0 / 4096.0, None, ALU.mult).then_inc(sv, 1)
            s.wait(sv)
            s.activation(rst[0:16, 128:256], rst[0:16, 0:128], AF.Sqrt, bias=epsb[0:16, 2:3]).then_inc(ss, 1)
            v.wait(ss)
            v.reciprocal(rst[0:16, 0:128], rst[0:16, 128:256]).then_inc(sv, 1)
            t.wait(sv)
            t.transpose(ps_rst[:], rst[0:16, 0:128], identf[0:16, 0:16]).then_inc(sm, 1)
            v.wait(sm)
            v.tensor_copy(rst_cols[:], ps_rst[:]).then_inc(sv, 1)
            bk.run(nc)

        with nc.sbuf_tensor("ztmC", [128, 16 * T], BF16, side="right") as z_tm, \
             nc.sbuf_tensor("zldC", [128, T], BF16, side="right") as zld, \
             nc.sbuf_tensor("zslC", [128, T], BF16, side="right") as zsil, \
             nc.sbuf_tensor("stg2C", [128, T], BF16, side="right") as stg2, \
             nc.psum_tensor("psTr3C", [128, 512], BF16) as ps_tr3:
            for cc in range(16):
                bk = Blk(); y, g, t, v, s = bk.engines()
                y.dma_start(zld[:], sp["sp_z"][cc * 128:(cc + 1) * 128, :]).then_inc(sd, 16)
                t.wait(sd)
                for tg in range(4):
                    if tg > 0 or cc > 0:
                        t.wait(sv)
                    tri = None
                    for j2 in range(4):
                        tri = t.transpose(ps_tr3[:, j2 * 128:(j2 + 1) * 128],
                                          zld[:, (tg * 4 + j2) * 128:(tg * 4 + j2 + 1) * 128], identb[:])
                    tri.then_inc(sm, 1)
                    v.wait(sm)
                    cpi = None
                    for j2 in range(4):
                        tt = tg * 4 + j2
                        cpi = v.tensor_copy(z_tm[:, tt * T + cc * 128:tt * T + (cc + 1) * 128],
                                            ps_tr3[:, j2 * 128:(j2 + 1) * 128])
                    cpi.then_inc(sv, 1)
                bk.run(nc)
            for tt in range(16):
                bk = Blk(); y, g, t, v, s = bk.engines()
                s.activation(zsil[:], z_tm[:, tt * T:(tt + 1) * T], AF.Silu).then_inc(ss, 1)
                v.tensor_scalar(stg2[:], y_buf[:, tt * T:(tt + 1) * T], rst_cols[:, tt:tt + 1], None, ALU.mult).then_inc(sv, 1)
                g.wait(ss); g.wait(sv)
                g.tensor_tensor(y_buf[:, tt * T:(tt + 1) * T], stg2[:], zsil[:], ALU.mult).then_inc(sg, 1)
                bk.run(nc)

        with nc.sbuf_tensor("yfmC", [128, 16 * T], BF16, side="right") as y_fm, \
             nc.psum_tensor("psTr4C", [128, 512], BF16) as ps_tr4:
            for tt in range(16):
                bk = Blk(); y, g, t, v, s = bk.engines()
                for cg in range(4):
                    if tt > 0 or cg > 0:
                        t.wait(sv)
                    tri = None
                    for j2 in range(4):
                        cc = cg * 4 + j2
                        tri = t.transpose(ps_tr4[:, j2 * 128:(j2 + 1) * 128],
                                          y_buf[:, tt * T + cc * 128:tt * T + (cc + 1) * 128], identb[:])
                    tri.then_inc(sm, 1)
                    v.wait(sm)
                    cpi = None
                    for j2 in range(4):
                        cc = cg * 4 + j2
                        cpi = v.tensor_copy(y_fm[:, cc * T + tt * 128:cc * T + (tt + 1) * 128],
                                            ps_tr4[:, j2 * 128:(j2 + 1) * 128])
                    cpi.then_inc(sv, 1)
                bk.run(nc)
        with nc.sbuf_tensor("woC", [128, 16 * 512], BF16, side="right") as wo_sb, \
             nc.sbuf_tensor("stoC", [128, 2 * 512], BF16, side="right") as sto, \
             nc.psum_tensor("psOC", [128, 512], F32) as ps_o0, \
             nc.psum_tensor("psOC2", [128, 512], F32) as ps_o1:
            ps_os = [ps_o0, ps_o1]
            for db in range(4):
                bk = Blk(); y, g, t, v, s = bk.engines()
                g.wait(sm)
                g.dma_start(wo_sb[:].rearrange("p (k c) -> p k c", k=16), ex["wotC"][db]).then_inc(sd, 16)
                t.wait(sd)
                for qt in range(16):
                    par = qt % 2
                    t.wait(sv)
                    mmi = None
                    for cc in range(16):
                        mmi = t.matmul(ps_os[par][:], y_fm[:, cc * T + qt * 128:cc * T + (qt + 1) * 128],
                                       wo_sb[:, cc * 512:(cc + 1) * 512],
                                       start=(cc == 0), stop=(cc == 15))
                    mmi.then_inc(sm, 1)
                    v.wait(sm)
                    v.wait(sd)
                    v.tensor_copy(sto[:, par * 512:(par + 1) * 512], ps_os[par][:]).then_inc(sv, 1)
                    y.wait(sv)
                    y.dma_start(ex["rs2in"][qt * 128:(qt + 1) * 128, db * 512:(db + 1) * 512],
                                sto[:, par * 512:(par + 1) * 512]).then_inc(sd, 16)
                y.wait(sd)
                bk.run(nc)
    else:
        with nc.sbuf_tensor("ztmC", [128, 16 * T], BF16, side="right") as z_tm, \
             nc.sbuf_tensor("zldC", [128, T], BF16, side="right") as zld, \
             nc.sbuf_tensor("zslC", [128, T], BF16, side="right") as zsil, \
             nc.sbuf_tensor("stg2C", [128, T], BF16, side="right") as stg2, \
             nc.psum_tensor("psTr3C", [128, 512], BF16) as ps_tr3:
            for cc in range(16):
                bk = Blk(); y, g, t, v, s = bk.engines()
                y.dma_start(zld[:], sp["sp_z"][cc * 128:(cc + 1) * 128, :]).then_inc(sd, 16)
                t.wait(sd)
                for tg in range(4):
                    if tg > 0 or cc > 0:
                        t.wait(sv)
                    tri = None
                    for j in range(4):
                        tri = t.transpose(ps_tr3[:, j * 128:(j + 1) * 128],
                                          zld[:, (tg * 4 + j) * 128:(tg * 4 + j + 1) * 128], identb[:])
                    tri.then_inc(sm, 1)
                    v.wait(sm)
                    cpi = None
                    for j in range(4):
                        tt = tg * 4 + j
                        cpi = v.tensor_copy(z_tm[:, tt * T + cc * 128:tt * T + (cc + 1) * 128],
                                            ps_tr3[:, j * 128:(j + 1) * 128])
                    cpi.then_inc(sv, 1)
                bk.run(nc)
            for tt in range(16):
                bk = Blk(); y, g, t, v, s = bk.engines()
                s.activation(zsil[:], z_tm[:, tt * T:(tt + 1) * T], AF.Silu).then_inc(ss, 1)
                v.wait(ss)
                v.wait(sd)
                v.tensor_tensor(stg2[:], y_buf[:, tt * T:(tt + 1) * T], zsil[:], ALU.mult).then_inc(sv, 1)
                y.wait(sv)
                y.dma_start(yg_out[tt * 128:(tt + 1) * 128, :], stg2[:]).then_inc(sd, 16)
                bk.run(nc)
            bk = Blk(); y, g, t, v, s = bk.engines()
            y.wait(sd)
            bk.run(nc)



def build_stageC():
    nc = bass.Bass("TRN2", target_bir_lowering=False)
    ex = _declare(nc, STAGEC_IN, "ExternalInput")
    yg = nc.dram_tensor("ygated", [T, 2048], BF16, kind="ExternalOutput").ap()
    ssq = nc.dram_tensor("ssqout", [16, 128], F32, kind="ExternalOutput").ap()
    sp = {"sp_z": nc.dram_tensor("sp_z", [16 * 128, T], BF16).ap()}
    with ExitStack() as st:
        emit_stageC(nc, st, ex, yg, ssq, sp)
    nc.finalize()
    return nc



STAGED_IN = [
    ("x1rD", (D, 1024), F32),
    ("yq0", (1024, 2048), BF16),
    ("yq1", (1024, 2048), BF16),
    ("ssqa", (8, 128), F32),
    ("ssqb", (8, 128), F32),
    ("wot", (16, 128, 32, 128), F32),
    ("wg8", (16, 128, 16, 512), I8),
    ("wu8", (16, 128, 16, 512), I8),
    ("wd8", (16, 128, 64, 128), I8),
    ("sgt", (128, 64), F32),
    ("sut", (128, 64), F32),
    ("sdt", (128, 16), F32),
    ("sel8", (8, 8, 128), F32),
    ("ident_f32", (128, 128), F32),
    ("ones_bf", (128, 1), BF16),
    ("onesrow_f32", (1, 128), F32),
]


def emit_stageD(nc, st, ex, out_r, fused=False):
    A = lambda nm, sh, dt, side="left": st.enter_context(nc.sbuf_tensor(nm, list(sh), dt, side=side))
    identf = A("idfD", (128, 128), F32)
    onesb = A("onbD", (128, 1), BF16)
    onesrow = A("onrD", (1, 128), F32)
    epsb = A("epsbD", (128, 2), F32)
    sgt = A("sgtD", (128, 64), F32)
    sut = A("sutD", (128, 64), F32)
    sdt = A("sdtD", (128, 16), F32)
    ssq2 = A("ssq2D", (8, 256), F32)
    out_acc = A("oaccD", (128, 16 * 1024), F32)
    if not fused:
        d2_stack = ExitStack()
        sel8 = d2_stack.enter_context(nc.sbuf_tensor("sel8D", [8, 8 * 128], F32, side="right"))
        rstd_b = d2_stack.enter_context(nc.sbuf_tensor("rstdbD", [128, 1024], F32, side="right"))

    sd = Ct(nc.alloc_semaphore("sdD"))
    sm = Ct(nc.alloc_semaphore("smD"))
    sv = Ct(nc.alloc_semaphore("svD"))
    ss = Ct(nc.alloc_semaphore("ssD"))
    sg = Ct(nc.alloc_semaphore("sgD"))

    bk = Blk(); y, g, t, v, s = bk.engines()
    y.dma_start(identf[:], ex["ident_f32"][:]).then_inc(sd, 16)
    y.dma_start(onesb[:], ex["ones_bf"][:]).then_inc(sd, 16)
    y.dma_start(onesrow[:], ex["onesrow_f32"][:]).then_inc(sd, 16)
    if not fused:
        for f in range(8):
            y.dma_start(sel8[:, f * 128:(f + 1) * 128], ex["sel8"][f]).then_inc(sd, 16)
        y.dma_start(ssq2[0:8, 0:128], ex["ssqa"][:]).then_inc(sd, 16)
        y.dma_start(ssq2[0:8, 128:256], ex["ssqb"][:]).then_inc(sd, 16)
    y.dma_start(sgt[:], ex["sgt"][:]).then_inc(sd, 16)
    y.dma_start(sut[:], ex["sut"][:]).then_inc(sd, 16)
    y.dma_start(sdt[:], ex["sdt"][:]).then_inc(sd, 16)
    v.memset(epsb[:, 0:1], 1e-5)
    v.memset(epsb[:, 1:2], 1.1920929e-07)
    y.wait(sd)
    bk.run(nc)

    xn2 = A("xn2D", (128, 16 * 1024), BF16)
    with nc.sbuf_tensor("mrD", [1, 1024], BF16, side="right") as mean_row, \
         nc.sbuf_tensor("orbD", [1, 128], BF16, side="right") as onesrow_bf, \
         nc.sbuf_tensor("swD", [1, 1024], F32, side="right") as srow, \
         nc.sbuf_tensor("bcAD", [128, 1024], BF16, side="right") as bcA, \
         nc.sbuf_tensor("bcBD", [128, 1024], BF16, side="right") as bcB, \
         nc.sbuf_tensor("xldD", [128, 1024], F32, side="right") as xld:
        with nc.psum_tensor("psSumD", [1, 1024], F32) as ps_sum, \
             nc.psum_tensor("psSqD", [1, 1024], F32) as ps_sq:
            bk = Blk(); y, g, t, v, s = bk.engines()
            g.dma_start(onesrow_bf[:], ex["onesrow_f32"][:]).then_inc(sd, 16)
            for cc in range(16):
                if cc >= 1:
                    y.wait(sv); y.wait(ss)
                y.dma_start(xld[:], ex["x1rD"][cc * 128:(cc + 1) * 128, :]).then_inc(sd, 16)
                v.wait(sd)
                if cc >= 1:
                    v.wait(sm)
                v.tensor_copy(bcB[:], xld[:]).then_inc(sv, 1)
                s.wait(sd)
                if cc >= 1:
                    s.wait(sm)
                s.activation(xn2[:, cc * 1024:(cc + 1) * 1024], xld[:], AF.Square).then_inc(ss, 1)
                t.wait(sv); t.wait(ss)
                for j in range(2):
                    t.matmul(ps_sum[0:1, j * 512:(j + 1) * 512], onesb[:], bcB[:, j * 512:(j + 1) * 512],
                             start=(cc == 0), stop=(cc == 15))
                mmi = None
                for j in range(2):
                    mmi = t.matmul(ps_sq[0:1, j * 512:(j + 1) * 512], onesb[:],
                                   xn2[:, cc * 1024 + j * 512:cc * 1024 + (j + 1) * 512],
                                   start=(cc == 0), stop=(cc == 15))
                mmi.then_inc(sm, 1)
            bk.run(nc)
            bk = Blk(); y, g, t, v, s = bk.engines()
            v.tensor_scalar(mean_row[:], ps_sum[0:1, :], 1.0 / 2048.0, None, ALU.mult)
            v.tensor_scalar(srow[:], ps_sq[0:1, :], 1.0 / 2048.0, None, ALU.mult).then_inc(sv, 1)
            bk.run(nc)
            bk = Blk(); y, g, t, v, s = bk.engines()
            v.tensor_tensor(ps_sum[0:1, :], mean_row[:], mean_row[:], ALU.mult).then_inc(sv, 1)
            bk.run(nc)
            bk = Blk(); y, g, t, v, s = bk.engines()
            v.tensor_tensor(srow[:], srow[:], ps_sum[0:1, :], ALU.subtract).then_inc(sv, 1)
            s.wait(sv)
            s.activation(ps_sum[0:1, :], srow[:], AF.Sqrt, bias=epsb[0:1, 0:1]).then_inc(ss, 1)
            bk.run(nc)
            bk = Blk(); y, g, t, v, s = bk.engines()
            v.reciprocal(srow[:], ps_sum[0:1, :]).then_inc(sv, 1)
            bk.run(nc)
        with nc.psum_tensor("psBcD", [128, 1024], F32) as ps_bc:
            bk = Blk(); y, g, t, v, s = bk.engines()
            for j in range(2):
                t.wait(sv)
                t.matmul(ps_bc[:, 0:512], onesrow_bf[:], mean_row[0:1, j * 512:(j + 1) * 512], start=True, stop=True)
                t.matmul(ps_bc[:, 512:1024], onesrow[:], srow[0:1, j * 512:(j + 1) * 512], start=True, stop=True).then_inc(sm, 1)
                v.wait(sm)
                v.tensor_copy(bcA[:, j * 512:(j + 1) * 512], ps_bc[:, 0:512])
                v.tensor_copy(bcB[:, j * 512:(j + 1) * 512], ps_bc[:, 512:1024]).then_inc(sv, 1)
            bk.run(nc)
            bk = Blk(); y, g, t, v, s = bk.engines()
            for cc in range(16):
                if cc >= 1:
                    y.wait(sg)
                y.dma_start(xld[:], ex["x1rD"][cc * 128:(cc + 1) * 128, :]).then_inc(sd, 16)
                v.wait(sd)
                v.tensor_tensor(xld[:], xld[:], bcA[:], ALU.subtract).then_inc(sv, 1)
                g.wait(sv)
                g.tensor_tensor(xn2[:, cc * 1024:(cc + 1) * 1024], xld[:], bcB[:], ALU.mult).then_inc(sg, 1)
            bk.run(nc)

    if fused:
        with nc.sbuf_tensor("tpD", [128, 2 * 1024], BF16, side="right") as tp, \
             nc.sbuf_tensor("xslD0", [128, 2 * 1024], F32, side="right") as xsl:
            bk = Blk(); y, g, t, v, s = bk.engines()
            for Dc in range(16):
                par = Dc % 2
                if Dc >= 2:
                    y.wait(sv)
                y.dma_start(tp[:, par * 1024:(par + 1) * 1024],
                            ex["rs2"][:, Dc * 128:(Dc + 1) * 128], transpose=True).then_inc(sd, 16)
                y.dma_start(xsl[:, par * 1024:(par + 1) * 1024],
                            ex["x1rD"][Dc * 128:(Dc + 1) * 128, :]).then_inc(sd, 16)
                v.wait(sd)
                v.tensor_tensor(out_acc[:, Dc * 1024:(Dc + 1) * 1024], tp[:, par * 1024:(par + 1) * 1024],
                                xsl[:, par * 1024:(par + 1) * 1024], ALU.add).then_inc(sv, 1)
            bk.run(nc)
    else:
        with nc.psum_tensor("psD1", [128, 128], F32) as ps1:
            bk = Blk(); y, g, t, v, s = bk.engines()
            v.tensor_tensor(ssq2[0:8, 0:128], ssq2[0:8, 0:128], ssq2[0:8, 128:256], ALU.add)
            v.tensor_scalar(ssq2[0:8, 0:128], ssq2[0:8, 0:128], 1.0 / 4096.0, None, ALU.mult).then_inc(sv, 1)
            s.wait(sv)
            s.activation(ssq2[0:8, 128:256], ssq2[0:8, 0:128], AF.Sqrt, bias=epsb[0:8, 1:2]).then_inc(ss, 1)
            v.wait(ss)
            v.reciprocal(ssq2[0:8, 0:128], ssq2[0:8, 128:256]).then_inc(sv, 1)
            for f in range(8):
                t.wait(sv)
                t.matmul(ps1[:], sel8[0:8, f * 128:(f + 1) * 128], ssq2[0:8, 0:128], start=True, stop=True).then_inc(sm, 1)
                v.wait(sm)
                v.tensor_copy(rstd_b[:, f * 128:(f + 1) * 128], ps1[:]).then_inc(sv, 1)
            bk.run(nc)

        with nc.sbuf_tensor("yfmD", [128, 32 * 1024], BF16, side="right") as y_fm, \
             nc.sbuf_tensor("wotD", [128, 32 * 128], BF16, side="right") as wbo, \
             nc.sbuf_tensor("xslD", [128, 1024], F32, side="right") as xsl, \
             nc.psum_tensor("psOD", [128, 512], F32) as ps_o0, \
             nc.psum_tensor("psOD2", [128, 512], F32) as ps_o1:
            bk = Blk(); y, g, t, v, s = bk.engines()
            for cc in range(32):
                src = ex["yq0"] if cc < 16 else ex["yq1"]
                y.dma_start(y_fm[:, cc * 1024:(cc + 1) * 1024], src[:, (cc % 16) * 128:((cc % 16) + 1) * 128],
                            transpose=True).then_inc(sd, 16)
            v.wait(sd)
            for cc in range(32):
                v.tensor_tensor(y_fm[:, cc * 1024:(cc + 1) * 1024], y_fm[:, cc * 1024:(cc + 1) * 1024],
                                rstd_b[:], ALU.mult)
            v.tensor_scalar(rstd_b[0:1, 0:1], rstd_b[0:1, 0:1], 1.0, None, ALU.mult).then_inc(sv, 1)
            bk.run(nc)
            for Dc in range(16):
                bk = Blk(); y, g, t, v, s = bk.engines()
                g.wait(sm)
                g.dma_start(wbo[:].rearrange("p (k c) -> p k c", k=32), ex["wot"][Dc]).then_inc(sd, 16)
                y.wait(sv)
                y.dma_start(xsl[:], ex["x1rD"][Dc * 128:(Dc + 1) * 128, :]).then_inc(sd, 16)
                t.wait(sd)
                for tb in range(2):
                    par = tb % 2
                    t.wait(sv)
                    mmi = None
                    for cc in range(32):
                        mmi = t.matmul([ps_o0, ps_o1][par][:], wbo[:, cc * 128:(cc + 1) * 128],
                                       y_fm[:, cc * 1024 + tb * 512:cc * 1024 + (tb + 1) * 512],
                                       start=(cc == 0), stop=(cc == 31))
                    mmi.then_inc(sm, 1)
                    v.wait(sm)
                    v.tensor_tensor(out_acc[:, Dc * 1024 + tb * 512:Dc * 1024 + (tb + 1) * 512],
                                    [ps_o0, ps_o1][par][:], xsl[:, tb * 512:(tb + 1) * 512], ALU.add).then_inc(sv, 1)
                bk.run(nc)
        d2_stack.close()


    with nc.sbuf_tensor("fuseD", [128, 64 * 512], BF16, side="right") as fuse, \
         nc.sbuf_tensor("gsbD", [128, 512], BF16, side="right") as gate_sb, \
         nc.sbuf_tensor("usbD", [128, 512], BF16, side="right") as up_sb, \
         nc.sbuf_tensor("t0D", [128, 512], F32, side="right") as t0f, \
         nc.psum_tensor("psGD", [128, 512], F32) as ps_g, \
         nc.psum_tensor("psUD", [128, 512], F32) as ps_u, \
         nc.psum_tensor("psDD", [128, 512], F32) as ps_d0, \
         nc.psum_tensor("psDD2", [128, 512], F32) as ps_d1:
        for sl in range(2):
          with nc.sbuf_tensor(f"wgD{sl}", [128, 16 * 512], BF16, side="right") as wbg, \
               nc.sbuf_tensor(f"wuD{sl}", [128, 16 * 512], BF16, side="right") as wbu:
            bk = Blk(); y, g, t, v, s = bk.engines()
            mm_evt = []
            for og in range(16):
                if og >= 1:
                    g.wait(sm, mm_evt[og - 1])
                g.dma_start(wbg[:].rearrange("p (k c) -> p k c", k=16), ex["wg8"][og]).then_inc(sd, 16)
                g.dma_start(wbu[:].rearrange("p (k c) -> p k c", k=16), ex["wu8"][og]).then_inc(sd, 16)
                load_evt = sd.n
                t.wait(sd, load_evt)
                for oc4 in range(4):
                    oc = og * 4 + oc4
                    t.wait(ss)
                    mmi = None
                    for kc in range(16):
                        mmi = t.matmul(ps_g[:], wbg[:, kc * 512 + oc4 * 128:kc * 512 + (oc4 + 1) * 128],
                                       xn2[:, kc * 1024 + sl * 512:kc * 1024 + (sl + 1) * 512],
                                       start=(kc == 0), stop=(kc == 15))
                    mmi.then_inc(sm, 1)
                    t.wait(sv)
                    mmi = None
                    for kc in range(16):
                        mmi = t.matmul(ps_u[:], wbu[:, kc * 512 + oc4 * 128:kc * 512 + (oc4 + 1) * 128],
                                       xn2[:, kc * 1024 + sl * 512:kc * 1024 + (sl + 1) * 512],
                                       start=(kc == 0), stop=(kc == 15))
                    mmi.then_inc(sm, 1)
                    s.wait(sm)
                    s.activation(gate_sb[:], ps_g[:], AF.Silu, scale=sgt[:, oc:oc + 1]).then_inc(ss, 1)
                    v.wait(sm)
                    v.tensor_scalar(up_sb[:], ps_u[:], sut[:, oc:oc + 1], None, ALU.mult).then_inc(sv, 1)
                    g.wait(ss); g.wait(sv)
                    g.tensor_tensor(fuse[:, oc * 512:(oc + 1) * 512], gate_sb[:], up_sb[:], ALU.mult).then_inc(sg, 1)
                mm_evt.append(sm.n)
            bk.run(nc)
          with nc.sbuf_tensor(f"wdD{sl}", [128, 64 * 128], BF16, side="right") as wbd:
            for dc in range(16):
                bk = Blk(); y, g, t, v, s = bk.engines()
                g.wait(sm)
                g.dma_start(wbd[:].rearrange("p (k c) -> p k c", k=64), ex["wd8"][dc]).then_inc(sd, 16)
                par = dc % 2
                t.wait(sd)
                t.wait(sg)
                t.wait(sv)
                mmi = None
                for oc in range(64):
                    mmi = t.matmul([ps_d0, ps_d1][par][:], wbd[:, oc * 128:(oc + 1) * 128],
                                   fuse[:, oc * 512:(oc + 1) * 512],
                                   start=(oc == 0), stop=(oc == 63))
                mmi.then_inc(sm, 1)
                v.wait(sm)
                v.tensor_scalar(t0f[:], [ps_d0, ps_d1][par][:], sdt[:, dc:dc + 1], None, ALU.mult).then_inc(sv, 1)
                g.wait(sv)
                g.tensor_tensor(out_acc[:, dc * 1024 + sl * 512:dc * 1024 + (sl + 1) * 512],
                                out_acc[:, dc * 1024 + sl * 512:dc * 1024 + (sl + 1) * 512],
                                t0f[:], ALU.add).then_inc(sg, 1)
                bk.run(nc)

    with nc.sbuf_tensor("stgD", [128, 2048], F32, side="right") as stg, \
         nc.psum_tensor("psTD", [128, 256], F32) as ps_t:
        for tt in range(8):
            bk = Blk(); y, g, t, v, s = bk.engines()
            if tt > 0:
                t.wait(sd)
            for Dc in range(16):
                par = Dc % 2
                t.wait(sv)
                tri = t.transpose(ps_t[:, par * 128:(par + 1) * 128],
                                  out_acc[:, Dc * 1024 + tt * 128:Dc * 1024 + (tt + 1) * 128], identf[:])
                tri.then_inc(sm, 1)
                v.wait(sm)
                v.tensor_copy(stg[:, Dc * 128:(Dc + 1) * 128], ps_t[:, par * 128:(par + 1) * 128]).then_inc(sv, 1)
            y.wait(sv)
            y.dma_start(out_r[tt * 128:(tt + 1) * 128, :], stg[:]).then_inc(sd, 16)
            bk.run(nc)
        bk = Blk(); y, g, t, v, s = bk.engines()
        y.wait(sd)
        bk.run(nc)


def build_stageD():
    nc = bass.Bass("TRN2", target_bir_lowering=False)
    ex = _declare(nc, STAGED_IN, "ExternalInput")
    outr = nc.dram_tensor("outr", [1024, 2048], F32, kind="ExternalOutput").ap()
    with ExitStack() as st:
        emit_stageD(nc, st, ex, outr)
    nc.finalize()
    return nc


def _kernel_unfused_impl(**inputs):
    pc = host_prep(inputs)

    nc = build_stageA()
    names = [nm for nm, _, _ in STAGEA_IN]
    in_maps = [{nm: np.ascontiguousarray(pc[c][nm]) for nm in names} for c in range(NCORE)]
    resA = run_bass_kernel_spmd(nc, in_maps, core_ids=list(range(NCORE)))
    parts = [resA.results[c]["part"] for c in range(NCORE)]
    partFs = [resA.results[c]["partF"] for c in range(NCORE)]

    nc = build_stageB()
    names = [nm for nm, _, _ in STAGEB_IN]
    in_maps = []
    for c in range(NCORE):
        b, h2 = c // 2, c % 2
        d = {nm: np.ascontiguousarray(pc[c][nm]) for nm in names
             if nm not in ("partAr", "partBr", "pfAr", "pfBr")}
        rows = slice(h2 * 1024, (h2 + 1) * 1024)
        d["partAr"] = np.ascontiguousarray(parts[2 * b][rows])
        d["partBr"] = np.ascontiguousarray(parts[2 * b + 1][rows])
        d["pfAr"] = np.ascontiguousarray(partFs[2 * b][rows])
        d["pfBr"] = np.ascontiguousarray(partFs[2 * b + 1][rows])
        in_maps.append(d)
    resB = run_bass_kernel_spmd(nc, in_maps, core_ids=list(range(NCORE)))
    x1r = [resB.results[c]["x1r"] for c in range(NCORE)]

    nc = build_stageC()
    names = [nm for nm, _, _ in STAGEC_IN]
    in_maps = []
    for c in range(NCORE):
        b = c // 2
        d = {nm: np.ascontiguousarray(pc[c][nm]) for nm in names if nm != "x1T"}
        d["x1T"] = np.ascontiguousarray(np.concatenate([x1r[2 * b], x1r[2 * b + 1]], axis=1))
        in_maps.append(d)
    resC = run_bass_kernel_spmd(nc, in_maps, core_ids=list(range(NCORE)))
    ygs = [resC.results[c]["ygated"] for c in range(NCORE)]
    ssqs = [resC.results[c]["ssqout"] for c in range(NCORE)]

    nc = build_stageD()
    names = [nm for nm, _, _ in STAGED_IN]
    in_maps = []
    for c in range(NCORE):
        b, h2 = c // 2, c % 2
        d = {nm: np.ascontiguousarray(pc[c][nm]) for nm in names
             if nm not in ("x1rD", "yq0", "yq1", "ssqa", "ssqb")}
        rows = slice(h2 * 1024, (h2 + 1) * 1024)
        d["x1rD"] = np.ascontiguousarray(x1r[c])
        d["yq0"] = np.ascontiguousarray(ygs[2 * b][rows])
        d["yq1"] = np.ascontiguousarray(ygs[2 * b + 1][rows])
        d["ssqa"] = np.ascontiguousarray(ssqs[2 * b][h2 * 8:(h2 + 1) * 8])
        d["ssqb"] = np.ascontiguousarray(ssqs[2 * b + 1][h2 * 8:(h2 + 1) * 8])
        in_maps.append(d)
    resD = run_bass_kernel_spmd(nc, in_maps, core_ids=list(range(NCORE)))

    out = np.empty((4, T, D), np.float32)
    for c in range(NCORE):
        b, h2 = c // 2, c % 2
        out[b, h2 * 1024:(h2 + 1) * 1024, :] = resD.results[c]["outr"]
    return out


def build_fused(upto='D'):
    nc = bass.Bass("TRN2", target_bir_lowering=False)
    removed = {"partAr", "partBr", "pfAr", "pfBr", "x1T", "yq0", "yq1", "ssqa", "ssqb",
               "sel8", "wot", "x1rD"}
    seen = {}
    for lst in (STAGEA_IN, STAGEB_IN, STAGEC_IN, STAGED_IN):
        for nm, sh, dt in lst:
            if nm in removed or nm in seen:
                continue
            seen[nm] = (sh, dt)
    seen["wotC"] = ((4, 128, 16, 512), F32)
    ex = {}
    for nm, (sh, dt) in seen.items():
        ex[nm] = nc.dram_tensor(nm, list(sh), dt, kind="ExternalInput").ap()
    outr = nc.dram_tensor("outr", [1024, 2048], F32, kind="ExternalOutput").ap()

    part_i = nc.dram_tensor("part_i", [T, 2048], BF16)
    partF_i = nc.dram_tensor("partF_i", [T, 2048], BF16)
    rs_part = nc.dram_tensor("rs_part_o", [1024, 2048], BF16)
    rs_pf = nc.dram_tensor("rs_pf_o", [1024, 2048], BF16)
    x1r_i = nc.dram_tensor("x1r_i", [D, 1024], F32)
    x1bf_i = nc.dram_tensor("x1bf_i", [D, 1024], BF16)
    ag1_o = nc.dram_tensor("ag1_o", [2 * D, 1024], BF16)
    ssq_i = nc.dram_tensor("ssq_i", [16, 128], F32)
    ag3_o = nc.dram_tensor("ag3_o", [32, 128], F32)
    rs2_i = nc.dram_tensor("rs2_i", [T, 2048], BF16)
    rs2_o = nc.dram_tensor("rs2_o", [1024, 2048], BF16)
    sp = {
        "sp_qf": nc.dram_tensor("sp_qf", [8 * 128, T], BF16).ap(),
        "sp_kf": nc.dram_tensor("sp_kf", [8 * 128, T], BF16).ap(),
        "sp_vtm": nc.dram_tensor("sp_vtm", [16 * 128, 1024], BF16).ap(),
        "sp_avnT": nc.dram_tensor("sp_avnT", [16, 24, 128, 128], BF16).ap(),
        "sp_z": nc.dram_tensor("sp_z", [16 * 128, T], BF16).ap(),
    }

    ccn = [0]

    def collective(kind, tin, tout, more=()):
        ccn[0] += 1
        cc = Ct(nc.alloc_semaphore(f"ccF{ccn[0]}"))
        bk = Blk(); y, g, t, v, s = bk.engines()
        g.collective_compute(kind, ALU.add if kind == "ReduceScatter" else ALU.bypass,
                             replica_groups=PAIRS,
                             ins=[tin.ap().opt()], outs=[tout.ap().opt()]).then_inc_default(cc)
        for kind2, tin2, tout2 in more:
            g.collective_compute(kind2, ALU.add if kind2 == "ReduceScatter" else ALU.bypass,
                                 replica_groups=PAIRS,
                                 ins=[tin2.ap().opt()], outs=[tout2.ap().opt()]).then_inc_default(cc)
        g.wait(cc, 1 + len(more))
        bk.run(nc)

    with ExitStack() as stA:
        emit_stageA(nc, stA, ex, part_i.ap(), partF_i.ap(), sp)
    collective("ReduceScatter", part_i, rs_part, more=[("ReduceScatter", partF_i, rs_pf)])
    if upto == 'A':
        nc.finalize()
        return nc

    exB = dict(ex)
    exB["rs_part"] = rs_part.ap()
    exB["rs_pf"] = rs_pf.ap()
    with ExitStack() as stB:
        emit_stageB(nc, stB, exB, x1r_i.ap(), fused=True, x1bf_out=x1bf_i.ap())
    collective("AllGather", x1bf_i, ag1_o)
    if upto == 'B':
        nc.finalize()
        return nc

    exC = dict(ex)
    exC["ag1"] = ag1_o.ap()
    exC["ag3"] = ag3_o.ap()
    exC["rs2in"] = rs2_i.ap()
    exC["cc_ssq_in"] = ssq_i.ap().opt()
    exC["cc_ssq_out"] = ag3_o.ap().opt()
    with ExitStack() as stC:
        emit_stageC(nc, stC, exC, None, ssq_i.ap(), sp, fused=True)
    collective("ReduceScatter", rs2_i, rs2_o)
    if upto == 'C':
        nc.finalize()
        return nc

    exD = dict(ex)
    exD["rs2"] = rs2_o.ap()
    exD["x1rD"] = x1r_i.ap()
    with ExitStack() as stD:
        emit_stageD(nc, stD, exD, outr, fused=True)
    nc.finalize()
    return nc


def kernel(**inputs):
    pc = host_prep(inputs)
    Wout = np.asarray(inputs["L1_attn_W_out"])
    for c in range(NCORE):
        h2 = c % 2
        wc = np.ascontiguousarray(Wout.T[2048 * h2:2048 * (h2 + 1)])
        pc[c]["wotC"] = wc.reshape(16, 128, 4, 512).transpose(2, 1, 0, 3).copy()
    nc = build_fused()
    names = [t.name for t in []]
    removed = {"partAr", "partBr", "pfAr", "pfBr", "x1T", "yq0", "yq1", "ssqa", "ssqb",
               "sel8", "wot", "x1rD"}
    innames = []
    seenn = set()
    for lst in (STAGEA_IN, STAGEB_IN, STAGEC_IN, STAGED_IN):
        for nm, sh, dt in lst:
            if nm in removed or nm in seenn:
                continue
            seenn.add(nm)
            innames.append(nm)
    innames.append("wotC")
    in_maps = [{nm: np.ascontiguousarray(pc[c][nm]) for nm in innames} for c in range(NCORE)]
    res = run_bass_kernel_spmd(nc, in_maps, core_ids=list(range(NCORE)))
    out = np.empty((4, T, D), np.float32)
    for c in range(NCORE):
        b, h2 = c // 2, c % 2
        out[b, h2 * 1024:(h2 + 1) * 1024, :] = res.results[c]["outr"]
    return out


def kernel_unfused(**inputs):
    return _kernel_unfused_impl(**inputs)
